# version 08
# baseline (speedup 1.0000x reference)
import numpy as np
import ml_dtypes
from contextlib import ExitStack
import concourse.bass as bass
import concourse.mybir as mybir
from concourse.bass_utils import run_bass_kernel_spmd

F32 = mybir.dt.float32
BF16 = mybir.dt.bfloat16
AF = mybir.ActivationFunctionType
ALU = mybir.AluOpType
AX = mybir.AxisListType

C = 1024
TLOC = 2048
NTB = 4
TB = 512
LCH = 64
NCHB = TB // LCH
NCH = TLOC // LCH
DIN = 4096
NWAVE = 4
CRND = 12582912.0
EPS_GN = 0.00064
EPS_LN = 1e-5
ENGS = ("sync", "scalar", "vector", "tensor", "gpsimd")


class Emitter:

    def __init__(self):
        self.q = {e: [] for e in ENGS}
        self.cnt = {}

    def newsem(self, name):
        self.cnt[name] = 0
        return name

    def mark(self, s):
        return self.cnt[s]

    def op(self, eng, fn, waits=(), incs=()):
        for s, k in incs:
            self.cnt[s] += k
        self.q[eng].append((fn, list(waits), list(incs)))

    def dma(self, eng, sem, out, in_, waits=(), **kw):
        def fn(e, out=out, in_=in_, kw=kw):
            return e.dma_start(out=out, in_=in_, **kw)
        self.op(eng, fn, waits=waits, incs=[(sem, 16)])
        return self.cnt[sem]

    def flush(self, block, semh):
        for eng in ENGS:
            ops = self.q[eng]
            if not ops:
                continue

            def body(e, ops=ops):
                for fn, waits, incs in ops:
                    for s, v in waits:
                        if v > 0:
                            e.wait_ge(semh[s], v)
                    inst = fn(e)
                    for s, k in incs:
                        assert inst is not None, "inc without instruction"
                        inst.then_inc(semh[s], k)
            getattr(block, eng)(body)


class KB:
    def __init__(self, taps=(), stop_after=None):
        self.taps = list(taps)
        self.stop_after = stop_after
        self.nc = bass.Bass("TRN2", target_bir_lowering=False)
        self.es = ExitStack()
        self.E = Emitter()
        self.M = {}

    def sb(self, name, shape, dt):
        return self.es.enter_context(self.nc.sbuf_tensor(name, shape, dt))

    def psum(self, name, shape, dt=F32):
        return self.es.enter_context(self.nc.psum_tensor(name, shape, dt))

    def build(self):
        nc = self.nc
        D = {}
        D["x"] = nc.dram_tensor("x", [TLOC, C], F32, kind="ExternalInput").ap()
        D["xprev"] = nc.dram_tensor("xprev", [1, C], F32, kind="ExternalInput").ap()
        for n1, shp in [("maa_x", [C]), ("maa_w", [C]), ("maa_k", [C]), ("maa_v", [C]),
                        ("maa_r", [C]), ("maa_g", [C]), ("time_decay", [C]),
                        ("ln_g", [C]), ("ln_b", [C])]:
            D[n1] = nc.dram_tensor(n1, shp, F32, kind="ExternalInput").ap()
        D["maa_w1"] = nc.dram_tensor("maa_w1", [C, 160], F32, kind="ExternalInput").ap()
        D["maa_w2"] = nc.dram_tensor("maa_w2", [5, 32, C], F32, kind="ExternalInput").ap()
        for n1 in ["Wr", "Wk", "Wv", "Wo"]:
            D[n1] = nc.dram_tensor(n1, [C, C], F32, kind="ExternalInput").ap()
        D["gate_w1"] = nc.dram_tensor("gate_w1", [C, 64], F32, kind="ExternalInput").ap()
        D["gate_w2"] = nc.dram_tensor("gate_w2", [64, C], F32, kind="ExternalInput").ap()
        D["decay_w1"] = nc.dram_tensor("decay_w1", [C, 64], F32, kind="ExternalInput").ap()
        D["decay_w2"] = nc.dram_tensor("decay_w2", [64, C], F32, kind="ExternalInput").ap()
        D["u"] = nc.dram_tensor("u", [16, 64], F32, kind="ExternalInput").ap()
        D["wg_sh"] = nc.dram_tensor("wg_sh", [512, C], F32, kind="ExternalInput").ap()
        D["wu_sh"] = nc.dram_tensor("wu_sh", [512, C], F32, kind="ExternalInput").ap()
        D["wd_sh"] = nc.dram_tensor("wd_sh", [128, DIN], F32, kind="ExternalInput").ap()
        D["wsel"] = nc.dram_tensor("wsel", [8, 8], F32, kind="ExternalInput").ap()
        D["wbias"] = nc.dram_tensor("wbias", [1, 8], F32, kind="ExternalInput").ap()
        D["out"] = nc.dram_tensor("out", [TLOC, C], F32, kind="ExternalOutput").ap()

        tset = set(self.taps)

        def dscr(name, shape, dt):
            kind = "ExternalOutput" if name.lstrip("c_") in tset or name in tset \
                else "Internal"
            return nc.dram_tensor(name, shape, dt, kind=kind).ap()
        S = {}
        S["abs_in"] = dscr("c_abs_in", [1, 4], F32)
        S["abs_out"] = dscr("c_abs_out", [1, 4], F32)
        S["wgT_sh"] = dscr("c_wgT_sh", [C, 512], BF16)
        S["wuT_sh"] = dscr("c_wuT_sh", [C, 512], BF16)
        S["wdT_sh"] = dscr("c_wdT_sh", [DIN, 128], BF16)
        S["wgT"] = dscr("c_wgT", [8 * C, 512], BF16)
        S["wuT"] = dscr("c_wuT", [8 * C, 512], BF16)
        S["wdT"] = dscr("c_wdT", [8 * DIN, 128], BF16)
        S["sum_in"] = dscr("c_sum_in", [128, 520], F32)
        S["sum_out"] = dscr("c_sum_out", [1024, 520], F32)
        S["xfm"] = dscr("c_xfm", [8 * 128, TLOC], F32)
        S["A"] = dscr("c_A", [NCH * 8 * 64, 128], BF16)
        S["vtm"] = dscr("c_vtm", [NCH * 8 * 64, 128], BF16)
        S["rt"] = dscr("c_rt", [NCH * 8 * 128, 64], BF16)
        S["sl"] = dscr("c_sl", [NCH * 8 * 128, 64], BF16)
        S["yd"] = dscr("c_yd", [NCH * 8 * 128, 64], BF16)
        S["g"] = dscr("c_g", [NCH * 8 * 128, 64], BF16)
        S["x1n"] = dscr("c_x1n", [8 * 128, TLOC], F32)
        S["hq"] = dscr("c_hq", [TLOC, DIN], BF16)
        S["dbg"] = dscr("c_dbg", [128, 40960], F32)
        S["wkb"] = dscr("c_wkb", [C, C], BF16)
        S["wvb"] = dscr("c_wvb", [C, C], BF16)
        S["wrb"] = dscr("c_wrb", [C, C], BF16)
        S["wob"] = dscr("c_wob", [C, C], BF16)
        S["w2b"] = dscr("c_w2b", [5 * 32, C], BF16)
        self.D, self.S = D, S

        E = self.E
        for s in ["sSY", "sSC", "sGP", "sCC", "sPE", "sDV", "sAC",
                  "cA", "cV", "cP", "cS", "gE", "sOUT"]:
            E.newsem(s)

        with self.es:
            self.arF = self.sb("arF", [128, 25600], F32)
            self.arB = self.sb("arB", [128, 35840], BF16)
            self._alloc_small()
            self.pb = [self.psum("pb%d" % i, [128, 512]) for i in range(6)]
            self.pba = self.psum("pba", [128, 512])
            self.pbt = self.psum("pbt", [64, 512], BF16)
            semh = {}
            for s in E.cnt:
                semh[s] = self.es.enter_context(nc.semaphore(s))
            self.block = self.es.enter_context(nc.Block())
            phases = [("ph0", self.ph0), ("ffnw", self.ph0_ffnw),
                      ("loop1", self.loop1), ("exchange", self.exchange),
                      ("loop2", self.loop2), ("ffn", self.ffn)]
            for nm, fn in phases:
                fn()
                sa = self.stop_after
                if sa and (sa == nm or sa.startswith(nm + "-")):
                    break
            self.finish()
            E.flush(self.block, semh)
        return nc

    def cF(self, off, cols):
        assert off + cols <= 25600, (off, cols)
        return self.arF[:, off:off + cols]

    def cB(self, off, cols):
        assert off + cols <= 35840, (off, cols)
        return self.arB[:, off:off + cols]

    def _alloc_small(self):
        sb = self.sb
        for nm in ["maax", "maaw", "maak", "maav", "maar", "maag", "td", "u", "lnb"]:
            setattr(self, "v_" + nm, sb("v_" + nm, [128, 8], F32))
        self.r_lng = sb("r_lng", [1, C], F32)
        self.r_lnb = sb("r_lnb", [1, C], F32)
        self.r_ones = sb("r_ones", [1, 128], F32)
        self.ones16 = sb("ones16", [1, 16], F32)
        self.onesc = sb("onesc", [128, 1], F32)
        self.identf = sb("identf", [128, 128], F32)
        self.identb = sb("identb", [128, 128], BF16)
        self.ones512 = sb("ones512", [128, TB], BF16)
        self.mask4 = sb("mask4", [128, 512], F32)
        self.mask4u = sb("mask4u", [128, 512], mybir.dt.uint8)
        self.zeros_bf = sb("zeros_bf", [128, 512], BF16)
        self.ind_cc = sb("ind_cc", [16, 8 * 128], F32)
        self.gin_cc = self.ind_cc
        self.ione_cc = sb("ione_cc", [128, 8 * 16], F32)
        self.ione_b = sb("ione_b", [128, 8 * 16], BF16)
        self.scols = sb("scols", [128, 8 * NCH], F32)
        self.carryB = sb("carryB", [128, 8], F32)
        self.xcarry = sb("xcarry", [128, 8], F32)
        self.S32 = sb("S32", [128, 512], F32)
        self.Sin = sb("Sin", [128, 512], F32)
        self.dsx = sb("dsx", [128, 16], F32)
        self.wsc = sb("wsc", [1, 4], F32)
        self.wsc_r = sb("wsc_r", [1, 4], F32)
        self.wsel_sb = sb("wsel_sb", [8, 8], F32)
        self.wbias_sb = sb("wbias_sb", [1, 8], F32)
        self.swcol = sb("swcol", [128, 4], F32)
        self.idxI = self.cF(14336, 128)
        self.idxA = self.cF(14464, 64)[0:64, :]
        self.vv16 = self.cF(14528, 128)[0:16, :]
        self.vv128 = self.cF(14656, 16)
        self.logdt = sb("logdt", [128, 64], F32)
        self.sdcol = sb("sdcol", [128, 16], F32)
        self.ddrow = sb("ddrow", [1, 512], F32)
        self.amaxw = sb("amaxw", [128, 16], F32)
        self.swv = sb("swv", [128, 4], F32)
        self.epsg = sb("epsg", [128, 1], F32)
        self.epsl = sb("epsl", [128, 1], F32)
    def ph0(self):
        E, D, M = self.E, self.D, self.M

        for vt, key in [(self.v_maax, "maa_x"), (self.v_maaw, "maa_w"),
                        (self.v_maak, "maa_k"), (self.v_maav, "maa_v"),
                        (self.v_maar, "maa_r"), (self.v_maag, "maa_g"),
                        (self.v_td, "time_decay"), (self.v_lnb, "ln_b")]:
            E.dma("sync", "sSY", vt[:], D[key].rearrange("(c p) -> p c", p=128),
                  allow_slow_non_contiguous=True)
        E.dma("sync", "sSY", self.v_u[:],
              D["u"].rearrange("h n -> (h n)").rearrange("(c p) -> p c", p=128),
              allow_slow_non_contiguous=True)
        E.dma("sync", "sSY", self.r_lng[:], D["ln_g"].rearrange("(o c) -> o c", o=1))
        E.dma("sync", "sSY", self.r_lnb[:], D["ln_b"].rearrange("(o c) -> o c", o=1))
        E.dma("sync", "sSY", self.xcarry[:],
              D["xprev"].rearrange("o (c p) -> p (o c)", p=128),
              allow_slow_non_contiguous=True)
        E.dma("sync", "sSY", self.wsel_sb[:], D["wsel"][:, :])
        E.dma("sync", "sSY", self.wbias_sb[:], D["wbias"][:, :])
        M["vecs"] = E.mark("sSY")

        S = self.S
        self.w_l1 = self.sb("w_l1", [128, 8 * 160], BF16)
        self.w_g1 = self.sb("w_g1", [128, 8 * 64], BF16)
        self.w_g2 = self.sb("w_g2", [64, C], BF16)
        self.w_d1 = self.sb("w_d1", [128, 8 * 64], BF16)
        self.w_d2 = self.sb("w_d2", [64, C], BF16)
        for skey, key in [("wkb", "Wk"), ("wvb", "Wv"), ("wrb", "Wr"),
                          ("wob", "Wo")]:
            E.dma("gpsimd", "sGP", S[skey][:, :], D[key][:, :])
        E.dma("gpsimd", "sGP", S["w2b"].rearrange("(f r) c -> f r c", r=32),
              D["maa_w2"][:, :, :])
        M["wcast"] = E.mark("sGP")
        E.dma("gpsimd", "sGP", self.w_l1[:].rearrange("p (a d) -> p a d", d=160),
              D["maa_w1"].rearrange("(a p) d -> p a d", p=128))
        E.dma("gpsimd", "sGP", self.w_g1[:].rearrange("p (a d) -> p a d", d=64),
              D["gate_w1"].rearrange("(a p) d -> p a d", p=128))
        E.dma("gpsimd", "sGP", self.w_g2[:], D["gate_w2"][:, :])
        E.dma("gpsimd", "sGP", self.w_d1[:].rearrange("p (a d) -> p a d", d=64),
              D["decay_w1"].rearrange("(a p) d -> p a d", p=128))
        E.dma("gpsimd", "sGP", self.w_d2[:], D["decay_w2"][:, :])
        M["wload"] = E.mark("sGP")

        def iotas(e):
            e.iota(self.idxA[:], [[1, 64]], base=0, channel_multiplier=-1,
                   allow_small_or_imprecise_dtypes=True)
            e.iota(self.vv16[:, 0:64], [[0, 64]], base=0, channel_multiplier=1,
                   allow_small_or_imprecise_dtypes=True)
            e.iota(self.vv16[:, 64:128], [[0, 64]], base=-1, channel_multiplier=1,
                   allow_small_or_imprecise_dtypes=True)
            e.iota(self.vv128[0:64, :], [[1, 16]], base=0, channel_multiplier=0,
                   allow_small_or_imprecise_dtypes=True)
            e.iota(self.vv128[64:128, :], [[1, 16]], base=-1, channel_multiplier=0,
                   allow_small_or_imprecise_dtypes=True)
            return e.iota(self.idxI[:], [[1, 128]], base=0, channel_multiplier=-1,
                          allow_small_or_imprecise_dtypes=True)
        E.op("gpsimd", iotas, incs=[("sGP", 1)])
        M["iota"] = E.mark("sGP")

        def consts(e):
            for blk in range(8):
                e.tensor_single_scalar(
                    self.mask4[0:64, blk * 64:(blk + 1) * 64],
                    self.idxA[:], 0.0, ALU.is_gt)
                e.tensor_single_scalar(
                    self.mask4[64:128, blk * 64:(blk + 1) * 64],
                    self.idxA[:], 0.0, ALU.is_gt)
            e.memset(self.zeros_bf[:], 0.0)
            e.memset(self.r_ones[:], 1.0)
            e.memset(self.ones16[:], 1.0)
            e.memset(self.onesc[:], 1.0)
            e.memset(self.ones512[:], 1.0)
            e.memset(self.carryB[:], 0.0)
            e.memset(self.epsg[:], EPS_GN)
            e.memset(self.epsl[:], EPS_LN)
            e.memset(self.S32[:], 0.0)
            for cc in range(8):
                e.tensor_single_scalar(self.ind_cc[:, cc * 128:(cc + 1) * 128],
                                       self.vv16[:], float(2 * cc), ALU.is_equal)
                e.tensor_single_scalar(self.ione_cc[:, cc * 16:(cc + 1) * 16],
                                       self.vv128[:], float(2 * cc), ALU.is_equal)
            e.tensor_single_scalar(self.identf[:], self.idxI[:], 0.0, ALU.is_equal)
            e.tensor_copy(self.ione_b[:], self.ione_cc[:])
            e.tensor_copy(self.mask4u[:], self.mask4[:])
            return e.tensor_copy(self.identb[:], self.identf[:])
        E.op("vector", consts, waits=[("sGP", M["iota"])], incs=[("sDV", 1)])
        M["consts"] = E.mark("sDV")

        M["gin"] = 0

    def ph0_ffnw(self):
        E, D, S, M = self.E, self.D, self.S, self.M
        sh_g = self.cF(0, 4096)
        sh_u = self.cF(4096, 4096)
        sh_d = self.cF(8192, 4096)
        dmp = self.cF(12288, 1024)
        shq = self.cB(24576, 4096)
        trT = self.cB(28672, 512)
        E.dma("sync", "sSY", sh_g.rearrange("p (a c) -> p a c", c=C),
              D["wg_sh"].rearrange("(a p) c -> p a c", p=128))
        E.dma("sync", "sSY", sh_u.rearrange("p (a c) -> p a c", c=C),
              D["wu_sh"].rearrange("(a p) c -> p a c", p=128))
        E.dma("sync", "sSY", sh_d, D["wd_sh"][:, :])
        M["shards"] = E.mark("sSY")

        def absf(e):
            acc = self.sdcol
            for i in range(4):
                e.activation(dmp, sh_g[:, i * C:(i + 1) * C], AF.Abs,
                             accum_out=acc[:, i:i + 1])
            for i in range(4):
                e.activation(dmp, sh_u[:, i * C:(i + 1) * C], AF.Abs,
                             accum_out=acc[:, 4 + i:5 + i])
            for i in range(4):
                inst = e.activation(dmp, sh_d[:, i * C:(i + 1) * C], AF.Abs,
                                    accum_out=acc[:, 8 + i:9 + i])
            return inst
        E.op("scalar", absf, waits=[("sSY", M["shards"])], incs=[("sAC", 1)])
        M["abs"] = E.mark("sAC")

        E.op("tensor",
             lambda e: e.matmul(self.pb[4][0:1, 0:12], self.onesc[:],
                                self.sdcol[:, 0:12], start=True, stop=True),
             waits=[("sAC", M["abs"]), ("sDV", M["consts"])], incs=[("sPE", 1)])
        M["fold"] = E.mark("sPE")

        absum = self.sb("absum", [1, 4], F32)

        def sumf(e):
            e.tensor_reduce(absum[:, 0:3].rearrange("o (n u) -> o n u", u=1),
                            self.pb[4][0:1, 0:12].rearrange("o (g i) -> o g i", g=3),
                            AX.X, ALU.add)
            return e.memset(absum[:, 3:4], 0.0)
        E.op("vector", sumf, waits=[("sPE", M["fold"])], incs=[("sDV", 1)])
        M["absum"] = E.mark("sDV")

        E.dma("scalar", "sSC", S["abs_in"][:, :], absum[:],
              waits=[("sDV", M["absum"])])
        M["absst"] = E.mark("sSC")

        E.op("gpsimd",
             lambda e: e.collective_compute(
                 "AllReduce", ALU.add, replica_groups=[list(range(8))],
                 ins=[S["abs_in"].opt()], outs=[S["abs_out"].opt()]),
             waits=[("sSC", M["absst"])], incs=[("sCC", 1)])
        M["ccabs"] = E.mark("sCC")

        E.dma("sync", "sSY", self.wsc[:], S["abs_out"][:, :],
              waits=[("sCC", M["ccabs"])])
        M["wscl"] = E.mark("sSY")

        E.op("gpsimd",
             lambda e: e.tensor_scalar(self.wsc[:], self.wsc[:],
                                       1.0 / (4096.0 * 1024.0), 1e-5,
                                       ALU.mult, ALU.max),
             waits=[("sSY", M["wscl"])], incs=[("sGP", 1)])
        E.op("vector",
             lambda e: e.reciprocal(self.wsc_r[:], self.wsc[:]),
             waits=[("sGP", E.mark("sGP"))], incs=[("sDV", 1)])
        M["wscr"] = E.mark("sDV")

        E.op("tensor",
             lambda e: e.matmul(self.pb[4][:, 16:20], self.r_ones[0:1, 0:128],
                                self.wsc_r[:], start=True, stop=True),
             waits=[("sDV", M["wscr"])], incs=[("sPE", 1)])
        M["swbc"] = E.mark("sPE")

        E.op("vector", lambda e: e.tensor_copy(self.swcol[:], self.pb[4][:, 16:20]),
             waits=[("sPE", M["swbc"])], incs=[("sDV", 1)])
        M["swcol"] = E.mark("sDV")

        E.op("tensor",
             lambda e: e.matmul(self.pb[4][:, 20:24], self.r_ones[0:1, 0:128],
                                self.wsc[:], start=True, stop=True),
             waits=[("sDV", M["swcol"])], incs=[("sPE", 1)])
        M["swbc2"] = E.mark("sPE")
        E.op("vector", lambda e: e.tensor_copy(self.swv[:], self.pb[4][:, 20:24]),
             waits=[("sPE", M["swbc2"])], incs=[("sDV", 1)])
        M["swv"] = E.mark("sDV")

        def quant(src, dstq, col, ntiles):
            def f(e):
                for i in range(ntiles):
                    e.tensor_scalar(dmp, src[:, i * C:(i + 1) * C],
                                    self.swcol[:, col:col + 1], CRND, ALU.mult, ALU.add)
                    e.tensor_scalar(dmp, dmp, CRND, 1.0, ALU.subtract, ALU.min)
                    inst = e.tensor_scalar(dstq[:, i * C:(i + 1) * C], dmp, -1.0,
                                           None, ALU.max)
                return inst
            E.op("vector", f, waits=[("sDV", M["swv"]), ("sPE", E.mark("sPE"))],
                 incs=[("sDV", 1)])

        def transpose_store(srcq, tiles, dst):
            for (t, dr, dc) in tiles:
                dep_dv = E.mark("sDV")
                dep_sc = E.mark("sSC")
                E.op("tensor",
                     (lambda e, t=t: [
                         e.transpose(self.pbt[:, 0:128],
                                     srcq[:, t * 128:t * 128 + 64], self.identb[:]),
                         e.transpose(self.pbt[:, 128:256],
                                     srcq[:, t * 128 + 64:t * 128 + 128], self.identb[:])][-1]),
                     waits=[("sDV", dep_dv), ("sSC", dep_sc)], incs=[("sPE", 1)])
                E.op("vector",
                     (lambda e: [
                         e.tensor_copy(trT[0:64, 0:128], self.pbt[:, 0:128]),
                         e.tensor_copy(trT[64:128, 0:128], self.pbt[:, 128:256])][-1]),
                     waits=[("sPE", E.mark("sPE")), ("sSC", E.mark("sSC"))],
                     incs=[("sDV", 1)])
                E.dma("scalar", "sSC", dst[dr[0]:dr[1], dc[0]:dc[1]], trT[:, 0:128],
                      waits=[("sDV", E.mark("sDV"))])

        quant(sh_g, shq, 0, 4)
        transpose_store(shq, [(a * 8 + ci, (ci * 128, ci * 128 + 128),
                               (a * 128, a * 128 + 128))
                              for a in range(4) for ci in range(8)], S["wgT_sh"])
        quant(sh_u, shq, 1, 4)
        transpose_store(shq, [(a * 8 + ci, (ci * 128, ci * 128 + 128),
                               (a * 128, a * 128 + 128))
                              for a in range(4) for ci in range(8)], S["wuT_sh"])
        quant(sh_d, shq, 2, 4)
        transpose_store(shq, [(t, (t * 128, t * 128 + 128), (0, 128))
                              for t in range(32)], S["wdT_sh"])
        M["wqst"] = E.mark("sSC")

        for pair in [("wgT_sh", "wgT"), ("wuT_sh", "wuT"), ("wdT_sh", "wdT")]:
            E.op("gpsimd",
                 (lambda e, pair=pair: e.collective_compute(
                     "AllGather", ALU.bypass, replica_groups=[list(range(8))],
                     ins=[S[pair[0]].opt()], outs=[S[pair[1]].opt()])),
                 waits=[("sSC", M["wqst"])], incs=[("sCC", 1)])
        M["ccw"] = E.mark("sCC")
    def _l1stop(self, part, tb):
        sa = self.stop_after or ""
        if not sa.startswith("loop1-"):
            return False
        _, p, n = sa.split("-")
        return part == p and tb == int(n) - 1

    def loop1(self):
        E, D, S, M = self.E, self.D, self.S, self.M
        l1stopped = False
        cF, cB = self.cF, self.cB
        pb = self.pb

        xtm = cF(0, 4096)
        xfm = cF(4096, 4104)
        xv = xfm.rearrange("p (c n) -> p c n", n=513)
        xx = cF(8200, 4096)
        xxv = xx.rearrange("p (c n) -> p c n", n=TB)
        self.ek = [cF(8200, 512), cF(8712, 512)]
        self.er = [cF(9224, 512), cF(9736, 512)]
        B_ = cF(12304, 4104)
        Bv = B_.rearrange("p (c n) -> p c n", n=513)
        self.Bv = Bv
        self.ktb = cF(16408, 4096)
        self.rtb = cF(20504, 4096)
        tmp = cF(24600, 512)
        ew = tmp
        self.dBt = cF(25112, 64)
        qct = cF(25176, 64)
        self.bc2t = cF(25240, 64)
        self.bc1nt = cF(25304, 64)
        self.ehmt = cF(25368, 64)
        self.Dtt = cF(25432, 64)

        self.wstr8 = [cB(0, 8192), cB(8192, 8192)]
        mixbuf = cB(16384, 4096)
        self.vtb = cB(20480, 4096)
        lt = {f: cB(24576 + i * 512, 512) for i, f in enumerate("wkvrg")}
        self.dg1t = cB(27136, 512)
        gcc = cB(27648, 512)
        ydcc = cB(28160, 512)
        p3c = cB(28672, 512)
        self.ktil = [cB(29184, 512), cB(29696, 512)]
        self.rtil = [cB(30208, 512), cB(30720, 512)]
        self.khat = cB(31232, 512)
        self.slbf = cB(31744, 512)
        self.asel = cB(32256, 512)
        self.vtms = cB(32768, 1024)
        self.ktms = cB(33792, 1024)
        self.mixbuf = mixbuf
        self._gcc = gcc
        self.tmp = tmp
        self.ew = ew
        self.lt = lt

        fprev = {"w": None, "k": "w", "v": "k", "r": "v", "g": "r"}
        fnum = {"w": 0, "k": 1, "v": 2, "r": 3, "g": 4}
        w_l2f = self.sb("w_l2f", [32, C], BF16)

        E.op("vector", lambda e: e.memset(Bv[:, :, 0:1], 0.0),
             waits=[], incs=[("sDV", 1)])

        for tb in range(NTB):
            t0 = tb * TB
            chg0 = tb * NCHB

            xw = [("sPE", M.get("xtr_%d" % (tb - 1), 0))]
            if tb == 0:
                xw = [("sDV", E.mark("sDV")), ("sPE", E.mark("sPE"))]
            E.dma("sync", "sSY", xtm.rearrange("p (s c) -> p s c", c=C),
                  D["x"][t0:t0 + TB, :].rearrange("(s p) c -> p s c", p=128),
                  waits=xw)
            M["xload_%d" % tb] = E.mark("sSY")

            wkw = [("sPE", M.get("proj_r_%d" % (tb - 1), 0))]
            if tb == 0:
                wkw.append(("sGP", M["wcast"]))
            E.dma("sync", "sSY", self.wstr8[0].rearrange("p (a co) -> p a co", co=C),
                  S["wkb"].rearrange("(a p) co -> p a co", p=128),
                  waits=wkw)
            M["wld_k_%d" % tb] = E.mark("sSY")

            pems = []
            for cc in range(8):
                bank = pb[0] if cc % 2 == 0 else pb[1]
                w = [("sSY", M["xload_%d" % tb]),
                     ("sAC", E.mark("sAC"))]
                if tb == 0 and cc == 0:
                    w.append(("sDV", M["consts"]))

                def trf(e, cc=cc, bank=bank):
                    for s1 in range(4):
                        inst = e.transpose(bank[:, s1 * 128:(s1 + 1) * 128],
                                           xtm[:, s1 * C + cc * 128: s1 * C + cc * 128 + 128],
                                           self.identf[:])
                    return inst
                E.op("tensor", trf, waits=w, incs=[("sPE", 1)])
                pems.append(E.mark("sPE"))
                E.op("scalar",
                     (lambda e, cc=cc, bank=bank: e.copy(xv[:, cc, 1:513], bank[:, 0:512])),
                     waits=[("sPE", pems[cc]),
                            ("sSC", M.get("xspill_%d" % (tb - 1), 0)),
                            ("sDV", M.get("mix_g_%d" % (tb - 1), 0))],
                     incs=[("sAC", 1)])
                M["xev_%d_%d" % (tb, cc)] = E.mark("sAC")
            M["xtr_%d" % tb] = E.mark("sPE")

            def xxf(e, tb=tb):
                e.tensor_copy(xv[:, :, 0:1], self.xcarry[:].rearrange("p (c u) -> p c u", u=1))
                e.tensor_tensor(xxv[:, :, :], xv[:, :, 0:512], xv[:, :, 1:513], ALU.subtract)
                e.tensor_copy(self.xcarry[:].rearrange("p (c u) -> p c u", u=1), xv[:, :, 512:513])
                for cc in range(8):
                    inst = e.scalar_tensor_tensor(
                        mixbuf[:, cc * TB:(cc + 1) * TB],
                        xxv[:, cc, :], self.v_maax[:, cc:cc + 1],
                        xv[:, cc, 1:513], ALU.mult, ALU.add)
                return inst
            w = [("sAC", M["xev_%d_%d" % (tb, 7)]),
                 ("sPE", M.get("proj_g_%d" % (tb - 1), 0)),
                 ("cA", chg0)]
            if tb == 0:
                w += [("sSY", M["vecs"])]
            E.op("vector", xxf, waits=w, incs=[("sDV", 1)])
            M["xx_%d" % tb] = E.mark("sDV")

            for cc in range(8):
                E.dma("scalar", "sSC",
                      S["xfm"][cc * 128:(cc + 1) * 128, t0:t0 + TB], xv[:, cc, 1:513],
                      waits=[("sDV", M["xx_%d" % tb])] if cc == 0 else [])
            M["xspill_%d" % tb] = E.mark("sSC")
            if self._l1stop("xx", tb):
                l1stopped = True
                break

            def loraA(e):
                for ci in range(8):
                    inst = e.matmul(pb[4][0:96, :], self.w_l1[:, ci * 160:ci * 160 + 96],
                                    mixbuf[:, ci * TB:(ci + 1) * TB],
                                    start=(ci == 0), stop=(ci == 7))
                return inst
            w = [("sDV", M["xx_%d" % tb]), ("sAC", E.mark("sAC")),
                 ("cS", tb * NCHB)]
            if tb == 0:
                w += [("sGP", M["wload"])]
            E.op("tensor", loraA, waits=w, incs=[("sPE", 1)])
            mk_la = E.mark("sPE")

            def loraB(e):
                for ci in range(8):
                    inst = e.matmul(pb[5][0:64, :], self.w_l1[:, ci * 160 + 96:ci * 160 + 160],
                                    mixbuf[:, ci * TB:(ci + 1) * TB],
                                    start=(ci == 0), stop=(ci == 7))
                return inst
            E.op("tensor", loraB, waits=[("sDV", E.mark("sDV"))], incs=[("sPE", 1)])
            mk_lb = E.mark("sPE")

            def ltanh(e):
                e.activation(lt["w"][0:32, :], pb[4][0:32, :], AF.Tanh)
                e.activation(lt["k"][0:32, :], pb[4][32:64, :], AF.Tanh)
                e.activation(lt["v"][0:32, :], pb[4][64:96, :], AF.Tanh)
                e.activation(lt["r"][0:32, :], pb[5][0:32, :], AF.Tanh)
                return e.activation(lt["g"][0:32, :], pb[5][32:64, :], AF.Tanh)
            E.op("scalar", ltanh, waits=[("sPE", mk_lb), ("sPE", mk_la)],
                 incs=[("sAC", 1)])
            M["lt_%d" % tb] = E.mark("sAC")

            for f in "wkvrg":
                wl2w = [("sPE", M.get("mf_%s" % fprev[f], 0) if fprev[f] else 0)]
                if tb == 0 and f == "w":
                    wl2w.append(("sGP", M["wcast"]))
                E.dma("sync", "sSY", w_l2f[:],
                      S["w2b"][fnum[f] * 32:(fnum[f] + 1) * 32, :], waits=wl2w)
                mk_wl2 = E.mark("sSY")

                dvs = []
                for cc in range(8):
                    bank = pb[4] if cc % 2 == 0 else pb[5]
                    w = [("sAC", M["lt_%d" % tb]), ("sSY", mk_wl2)]
                    if cc >= 2:
                        w.append(("sDV", dvs[cc - 2]))
                    elif fprev[f]:
                        w.append(("sDV", M["mix_%s_%d" % (fprev[f], tb)]))
                    E.op("tensor",
                         (lambda e, cc=cc, bank=bank, f=f: e.matmul(
                             bank[:, :], w_l2f[:, cc * 128:(cc + 1) * 128],
                             lt[f][0:32, :], start=True, stop=True)),
                         waits=w, incs=[("sPE", 1)])
                    mkp = E.mark("sPE")

                    wmix = [("sPE", mkp)]
                    if cc == 0:
                        if f == "w":
                            wmix.append(("sPE", mk_lb))
                        else:
                            wmix.append(("sPE", M["proj_%s_%d" % (fprev[f], tb)]))

                    def mixf(e, cc=cc, bank=bank, f=f):
                        e.scalar_tensor_tensor(
                            tmp, bank[:, :], getattr(self, "v_maa" + f)[:, cc:cc + 1],
                            xxv[:, cc, :], ALU.add, ALU.mult)
                        return e.tensor_tensor(mixbuf[:, cc * TB:(cc + 1) * TB],
                                               tmp, xv[:, cc, 1:513], ALU.add)
                    E.op("vector", mixf, waits=wmix, incs=[("sDV", 1)])
                    dvs.append(E.mark("sDV"))
                M["mf_%s" % f] = E.mark("sPE")
                M["mix_%s_%d" % (f, tb)] = E.mark("sDV")

                self._proj(f, tb, mixbuf, lt)
                if self._l1stop("proj" + f, tb):
                    l1stopped = True
                    break
            if l1stopped:
                break

            def colf(e, tb=tb):
                for cc in range(8):
                    e.tensor_tensor(self.dBt[:, cc * 8:(cc + 1) * 8],
                                    Bv[:, cc, 64::64], Bv[:, cc, 0::64][:, 0:8],
                                    ALU.subtract)
                e.tensor_scalar(qct, self.dBt, 0.25, None, ALU.mult)
                for cc in range(8):
                    e.scalar_tensor_tensor(self.bc2t[:, cc * 8:(cc + 1) * 8],
                                           Bv[:, cc, 0::64][:, 0:8], 0.5,
                                           qct[:, cc * 8:(cc + 1) * 8],
                                           ALU.mult, ALU.add)
                e.tensor_scalar(self.bc1nt, self.bc2t, -1.0, None, ALU.mult)
                for cc in range(8):
                    e.tensor_scalar(
                        self.scols[:, cc * NCH + tb * 8: cc * NCH + tb * 8 + 8],
                        self.bc2t[:, cc * 8:(cc + 1) * 8], 2.0,
                        self.carryB[:, cc:cc + 1], ALU.mult, ALU.add)
                return e.tensor_tensor(self.carryB[:], self.carryB[:],
                                       Bv[:, :, 512], ALU.add)
            E.op("vector", colf,
                 waits=[("sDV", M["scan_%d" % tb]), ("cA", chg0), ("cV", chg0)],
                 incs=[("sDV", 1)])
            mk_cols = E.mark("sDV")

            E.op("scalar", lambda e: e.activation(self.ehmt, self.dBt, AF.Exp, scale=-0.5),
                 waits=[("sDV", mk_cols), ("cV", chg0)], incs=[("sAC", 1)])
            mk_ehm = E.mark("sAC")

            E.op("vector", lambda e: e.tensor_tensor(self.Dtt, self.ehmt, self.ehmt, ALU.mult),
                 waits=[("sAC", mk_ehm)], incs=[("sDV", 1)])
            M["cols_%d" % tb] = E.mark("sDV")
            if self._l1stop("cols", tb):
                l1stopped = True
                break

            for loc in range(NCHB):
                self._chunk(tb, loc, chg0 + loc)
                if getattr(self, "chunk_stop", None):
                    l1stopped = True
                    break
            if l1stopped:
                break

            if self._l1stop("chunks", tb):
                l1stopped = True
                break

            self._sdiag(tb, p3c, gcc, ydcc)
            if self._l1stop("sdiag", tb):
                l1stopped = True
                break

        M["l1pe_done"] = E.mark("sPE")
        if l1stopped:
            return
        def sumf(e):
            return e.tensor_scalar(tmp[:, 0:8], self.carryB[:], -1.0, None, ALU.mult)
        E.op("vector", sumf, waits=[("cS", NCH)], incs=[("sDV", 1)])
        mk_logd = E.mark("sDV")
        E.dma("scalar", "sSC", S["sum_in"][:, 0:512], self.S32[:],
              waits=[("sDV", mk_logd)])
        E.dma("scalar", "sSC", S["sum_in"][:, 512:520], tmp[:, 0:8])
        M["sumspill"] = E.mark("sSC")

    def _proj(self, f, tb, mixbuf, lt):
        E, D, S, M = self.E, self.D, self.S, self.M
        pb = self.pb
        mixmark = M["mix_%s_%d" % (f, tb)]
        Bv, ew = self.Bv, self.ew

        if f == "w":
            def d1f(e):
                for ci in range(8):
                    inst = e.matmul(pb[4][0:64, :], self.w_d1[:, ci * 64:(ci + 1) * 64],
                                    mixbuf[:, ci * TB:(ci + 1) * TB],
                                    start=(ci == 0), stop=(ci == 7))
                return inst
            E.op("tensor", d1f, waits=[("sDV", mixmark), ("sAC", E.mark("sAC"))],
                 incs=[("sPE", 1)])
            mk = E.mark("sPE")
            E.op("scalar", (lambda e: e.activation(self.dg1t[0:64, :], pb[4][0:64, :],
                                                   AF.Tanh)),
                 waits=[("sPE", mk)], incs=[("sAC", 1)])
            mk_t = E.mark("sAC")
            for cc in range(8):
                bank = pb[0] if cc % 2 == 0 else pb[1]
                E.op("tensor",
                     (lambda e, cc=cc, bank=bank: e.matmul(
                         bank[:, :], self.w_d2[:, cc * 128:(cc + 1) * 128],
                         self.dg1t[0:64, :], start=True, stop=True)),
                     waits=[("sAC", mk_t if cc < 2 else M["ewev_%d_%d" % (tb, cc - 2)])],
                     incs=[("sPE", 1)])
                mkp = E.mark("sPE")
                E.op("scalar",
                     (lambda e, cc=cc, bank=bank: e.activation(
                         ew, bank[:, :], AF.Exp, bias=self.v_td[:, cc:cc + 1])),
                     waits=[("sPE", mkp),
                            ("sDV", M.get("scan_c_%d_%d" % (tb, cc - 1), 0))],
                     incs=[("sAC", 1)])
                M["ewev_%d_%d" % (tb, cc)] = E.mark("sAC")
                E.op("vector",
                     (lambda e, cc=cc: e.tensor_tensor_scan(
                         Bv[:, cc, 1:513], ew, self.ones512[:], 0.0,
                         ALU.add, ALU.mult)),
                     waits=[("sAC", M["ewev_%d_%d" % (tb, cc)])], incs=[("sDV", 1)])
                M["scan_c_%d_%d" % (tb, cc)] = E.mark("sDV")
            M["scan_%d" % tb] = E.mark("sDV")
            M["proj_w_%d" % tb] = E.mark("sPE")
            return

        if f == "g":
            def g1f(e):
                for ci in range(8):
                    inst = e.matmul(pb[4][0:64, :], self.w_g1[:, ci * 64:(ci + 1) * 64],
                                    mixbuf[:, ci * TB:(ci + 1) * TB],
                                    start=(ci == 0), stop=(ci == 7))
                return inst
            E.op("tensor", g1f, waits=[("sDV", mixmark), ("sAC", E.mark("sAC"))],
                 incs=[("sPE", 1)])
            mk = E.mark("sPE")
            E.op("scalar", (lambda e: e.activation(self.dg1t[0:64, :], pb[4][0:64, :],
                                                   AF.Tanh)),
                 waits=[("sPE", mk)], incs=[("sAC", 1)])
            mk_t = E.mark("sAC")
            for cc in range(8):
                bank = pb[0] if cc % 2 == 0 else pb[1]
                E.op("tensor",
                     (lambda e, cc=cc, bank=bank: e.matmul(
                         bank[:, :], self.w_g2[:, cc * 128:(cc + 1) * 128],
                         self.dg1t[0:64, :], start=True, stop=True)),
                     waits=[("sAC", mk_t if cc < 2 else M["gev_%d_%d" % (tb, cc - 2)])],
                     incs=[("sPE", 1)])
                mkp = E.mark("sPE")
                E.op("scalar",
                     (lambda e, cc=cc, bank=bank: e.copy(self._gcc[:], bank[:, :])),
                     waits=[("sPE", mkp),
                            ("sSC", M.get("gsp_%d_%d" % (tb, cc - 1), 0))],
                     incs=[("sAC", 1)])
                M["gev_%d_%d" % (tb, cc)] = E.mark("sAC")
                E.dma("scalar", "sSC",
                      S["g"].rearrange("(ch e p) n -> ch e p n", e=8, p=128)[
                          tb * 8:tb * 8 + 8, cc, :, :].transpose([1, 0, 2]),
                      self._gcc[:].rearrange("p (ch n) -> p ch n", n=64),
                      waits=[("sAC", M["gev_%d_%d" % (tb, cc)])])
                M["gsp_%d_%d" % (tb, cc)] = E.mark("sSC")
            M["proj_g_%d" % tb] = E.mark("sPE")
            return

        wkey = {"k": "Wk", "v": "Wv", "r": "Wr"}[f]
        dst = {"k": self.ktb, "v": self.vtb, "r": self.rtb}[f]
        bufi = {"k": 0, "v": 1, "r": 0}[f]
        buf = self.wstr8[bufi]
        nxt = {"k": ("Wv", 1, "v"), "v": ("Wr", 0, "r"), "r": (None, None, None)}[f]
        if f == "k":
            pass
        if nxt[0] is not None:
            if nxt[1] == 0:
                wldw = [("sPE", M["proj_k_%d" % tb])]
            else:
                wldw = [("sPE", M.get("proj_v_%d" % (tb - 1), 0))]
            skey = {"Wv": "wvb", "Wr": "wrb"}[nxt[0]]
            E.dma("sync", "sSY",
                  self.wstr8[nxt[1]].rearrange("p (a co) -> p a co", co=C),
                  S[skey].rearrange("(a p) co -> p a co", p=128),
                  waits=wldw)
            M["wld_%s_%d" % (nxt[2], tb)] = E.mark("sSY")
        mk_w = M["wld_%s_%d" % (f, tb)]

        for half in range(2):
            hs = half * 256

            def pf(e, hs=hs, buf=buf):
                for co in range(8):
                    bank = pb[co // 2]
                    for ci in range(8):
                        inst = e.matmul(bank[:, (co % 2) * 256:(co % 2) * 256 + 256],
                                        buf[:, ci * C + co * 128: ci * C + co * 128 + 128],
                                        mixbuf[:, ci * TB + hs: ci * TB + hs + 256],
                                        start=(ci == 0), stop=(ci == 7))
                return inst
            E.op("tensor", pf,
                 waits=[("sSY", mk_w), ("sDV", mixmark), ("sAC", E.mark("sAC"))],
                 incs=[("sPE", 1)])
            mk_proj = E.mark("sPE")

            def evf(e, hs=hs, dst=dst):
                for co in range(8):
                    bank = pb[co // 2]
                    inst = e.copy(dst[:, co * TB + hs: co * TB + hs + 256],
                                  bank[:, (co % 2) * 256:(co % 2) * 256 + 256])
                return inst
            E.op("scalar", evf,
                 waits=[("sPE", mk_proj), ("cV", tb * NCHB),
                        ("sDV", M.get("sdiag_%d" % (tb - 1), 0))],
                 incs=[("sAC", 1)])
        M["proj_%s_%d" % (f, tb)] = E.mark("sPE")
        M["pvev_%s_%d" % (f, tb)] = E.mark("sAC")
    def _chunk(self, tb, loc, chg):
        E, S, M = self.E, self.S, self.M
        par = chg % 2
        pb4 = self.pb[4]
        Bv = self.Bv
        ek, er = self.ek[par], self.er[par]
        ktil, rtil = self.ktil[par], self.rtil[par]
        kv3 = self.ktb.rearrange("p (c n) -> p c n", n=TB)
        rv3 = self.rtb.rearrange("p (c n) -> p c n", n=TB)
        vv3 = self.vtb.rearrange("p (c n) -> p c n", n=TB)
        col = lambda t, cc: t[:, cc * 8 + loc: cc * 8 + loc + 1]

        def act_f(e):
            for cc in range(8):
                e.activation(ek[:, cc * 64:(cc + 1) * 64],
                             Bv[:, cc, loc * 64 + 1: loc * 64 + 65],
                             AF.Exp, scale=0.5, bias=col(self.bc1nt, cc))
                inst = e.activation(er[:, cc * 64:(cc + 1) * 64],
                                    Bv[:, cc, loc * 64: loc * 64 + 64],
                                    AF.Exp, scale=-0.5, bias=col(self.bc2t, cc))
            return inst
        E.op("scalar", act_f,
             waits=[("sDV", M["cols_%d" % tb]), ("cV", max(0, chg - 1))],
             incs=[("cA", 1)])
        if getattr(self, "chunk_stop", None) == "act":
            return

        def dve_f(e):
            e.tensor_tensor(ek, ek, ek, ALU.mult)
            e.tensor_tensor(er, er, er, ALU.mult)
            e.tensor_tensor(ktil.rearrange("p (c n) -> p c n", n=64),
                            kv3[:, :, loc * 64:(loc + 1) * 64],
                            ek.rearrange("p (c n) -> p c n", n=64), ALU.mult)
            e.tensor_tensor(rtil.rearrange("p (c n) -> p c n", n=64),
                            rv3[:, :, loc * 64:(loc + 1) * 64],
                            er.rearrange("p (c n) -> p c n", n=64), ALU.mult)
            for cc in range(8):
                e.tensor_scalar(self.khat[:, cc * 64:(cc + 1) * 64],
                                ktil[:, cc * 64:(cc + 1) * 64],
                                col(self.ehmt, cc), None, ALU.mult)
                inst = e.tensor_scalar(self.slbf[:, cc * 64:(cc + 1) * 64],
                                       self.S32[:, cc * 64:(cc + 1) * 64],
                                       col(self.ehmt, cc), None, ALU.mult)
            return inst
        E.op("vector", dve_f,
             waits=[("cA", chg + 1), ("cP", chg), ("cS", chg),
                    ("sSC", M.get("slsp_%d" % (chg - 1), 0))],
             incs=[("cV", 1)])
        mk_cv = E.mark("cV")
        if getattr(self, "chunk_stop", None) == "dve":
            return

        E.dma("scalar", "sSC",
              S["rt"].rearrange("(ch e p) n -> ch e p n", e=8, p=128)[
                  chg, :, :, :].transpose([1, 0, 2]),
              rtil.rearrange("p (e n) -> p e n", n=64),
              waits=[("cV", chg + 1)])
        M["rsp_%d" % chg] = E.mark("sSC")
        E.dma("scalar", "sSC",
              S["sl"].rearrange("(ch e p) n -> ch e p n", e=8, p=128)[
                  chg, :, :, :].transpose([1, 0, 2]),
              self.slbf.rearrange("p (e n) -> p e n", n=64))
        M["slsp_%d" % chg] = E.mark("sSC")
        if getattr(self, "chunk_stop", None) == "spill":
            return

        def a_mm(e):
            for cc in range(8):
                e.matmul(self.pba[0:64, cc * 64:(cc + 1) * 64],
                         ktil[0:64, cc * 64:(cc + 1) * 64],
                         rtil[0:64, cc * 64:(cc + 1) * 64],
                         start=True, stop=True, tile_position=(0, 0))
                inst = e.matmul(self.pba[64:128, cc * 64:(cc + 1) * 64],
                                ktil[64:128, cc * 64:(cc + 1) * 64],
                                rtil[64:128, cc * 64:(cc + 1) * 64],
                                start=True, stop=True, tile_position=(64, 64))
            return inst
        E.op("tensor", a_mm,
             waits=[("cV", chg + 1), ("gE", chg)],
             incs=[("sPE", 1)])
        mk_a = E.mark("sPE")

        E.op("vector",
             (lambda e: e.select(self.asel[:], self.mask4u[:],
                                 self.pba[:, :], self.zeros_bf[:])),
             waits=[("sPE", mk_a),
                    ("sSC", M.get("asp_%d" % (chg - 1), 0))],
             incs=[("gE", 1)])
        mk_sel = E.mark("gE")
        E.dma("scalar", "sSC",
              S["A"].rearrange("(ch e s) n -> ch e s n", e=8, s=64)[
                  chg, :, :, 0:64].transpose([1, 0, 2]),
              self.asel[0:64, :].rearrange("s (e n) -> s e n", n=64),
              waits=[("gE", mk_sel)])
        E.dma("scalar", "sSC",
              S["A"].rearrange("(ch e s) n -> ch e s n", e=8, s=64)[
                  chg, :, :, 64:128].transpose([1, 0, 2]),
              self.asel[64:128, :].rearrange("s (e n) -> s e n", n=64))
        M["asp_%d" % chg] = E.mark("sSC")

        for (srcname, dstt, mkey) in [("v", self.vtms, "vtm"), ("kh", self.ktms, "ktm")]:
            for half in range(2):
                def tr_f(e, srcname=srcname, half=half):
                    for i in range(4):
                        cc = half * 4 + i
                        if srcname == "v":
                            src = vv3[:, cc, loc * 64:(loc + 1) * 64]
                        else:
                            src = self.khat[:, cc * 64:(cc + 1) * 64]
                        inst = e.transpose(self.pbt[:, i * 128:(i + 1) * 128],
                                           src, self.identb[:])
                    return inst
                E.op("tensor", tr_f,
                     waits=[("cV", chg + 1),
                            ("sAC", M.get("trev_prev", 0)),
                            ("sSC", M.get("vtmsp_%d" % (chg - 1), 0)
                             if srcname == "v" and half == 0 else 0)],
                     incs=[("sPE", 1)])
                mk_tr = E.mark("sPE")
                E.op("scalar",
                     (lambda e, dstt=dstt, half=half: e.copy(
                         dstt[0:64, half * 512:(half + 1) * 512], self.pbt[:, :])),
                     waits=[("sPE", mk_tr)], incs=[("sAC", 1)])
                M["trev_prev"] = E.mark("sAC")

        E.dma("scalar", "sSC",
              S["vtm"].rearrange("(ch e s) n -> ch e s n", e=8, s=64)[
                  chg, :, :, :].transpose([1, 0, 2]),
              self.vtms[0:64, :].rearrange("s (e n) -> s e n", n=128),
              waits=[("sAC", M["trev_prev"])])
        M["vtmsp_%d" % chg] = E.mark("sSC")

        if getattr(self, "chunk_stop", None) == "tr":
            return

        def st_mm(e):
            for cc in range(8):
                e.matmul(pb4[0:64, cc * 64:(cc + 1) * 64],
                         self.ktms[0:64, cc * 128: cc * 128 + 64],
                         self.vtms[0:64, cc * 128: cc * 128 + 64],
                         start=True, stop=True, tile_position=(0, 0))
                inst = e.matmul(pb4[64:128, cc * 64:(cc + 1) * 64],
                                self.ktms[0:64, cc * 128 + 64: cc * 128 + 128],
                                self.vtms[0:64, cc * 128 + 64: cc * 128 + 128],
                                start=True, stop=True, tile_position=(0, 64))
            return inst
        E.op("tensor", st_mm,
             waits=[("sAC", M["trev_prev"]), ("cS", chg)],
             incs=[("cP", 1)])

        def st_up(e):
            for cc in range(8):
                inst = e.scalar_tensor_tensor(
                    self.S32[:, cc * 64:(cc + 1) * 64],
                    self.S32[:, cc * 64:(cc + 1) * 64], col(self.Dtt, cc),
                    pb4[:, cc * 64:(cc + 1) * 64], ALU.mult, ALU.add)
            return inst
        E.op("vector", st_up, waits=[("cP", chg + 1)], incs=[("cS", 1)])

    def _sdiag(self, tb, p3c, gcc, ydcc):
        E, S, M = self.E, self.S, self.M
        kv3 = self.ktb.rearrange("p (c n) -> p c n", n=TB)
        rv3 = self.rtb.rearrange("p (c n) -> p c n", n=TB)
        vv3 = self.vtb.rearrange("p (c n) -> p c n", n=TB)
        ssl = self.tmp

        mks = []
        for cc in range(8):
            wp3 = [("cS", (tb + 1) * NCHB)] if cc == 0 else \
                [("sPE", mks[cc - 1])]
            E.op("vector",
                 (lambda e, cc=cc: e.scalar_tensor_tensor(
                     p3c[:], rv3[:, cc, :], self.v_u[:, cc:cc + 1],
                     kv3[:, cc, :], ALU.mult, ALU.mult)),
                 waits=wp3,
                 incs=[("sDV", 1)])
            mkd = E.mark("sDV")
            E.op("tensor",
                 (lambda e, cc=cc: e.matmul(
                     self.pb[5][0:16, :], self.ione_b[:, cc * 16:(cc + 1) * 16],
                     p3c[:], start=(cc == 0), stop=(cc == 7))),
                 waits=[("sDV", mkd)] + ([("sAC", E.mark("sAC"))] if cc == 0 else []),
                 incs=[("sPE", 1)])
            mks.append(E.mark("sPE"))

        E.op("scalar", (lambda e: e.copy(ssl[0:16, :], self.pb[5][0:16, :])),
             waits=[("sPE", mks[-1])], incs=[("sAC", 1)])
        mk_ssl = E.mark("sAC")

        for cc in range(8):
            bank = self.pb[2] if cc % 2 == 0 else self.pb[3]
            w = [("sAC", mk_ssl)]
            if cc >= 2:
                w.append(("sDV", M["ydtt_%d_%d" % (tb, cc - 2)]))
            E.op("tensor",
                 (lambda e, cc=cc, bank=bank: e.matmul(
                     bank[:, :], self.ind_cc[:, cc * 128:(cc + 1) * 128],
                     ssl[0:16, :], start=True, stop=True)),
                 waits=w, incs=[("sPE", 1)])
            mkb = E.mark("sPE")
            E.op("vector",
                 (lambda e, cc=cc, bank=bank: e.tensor_tensor(
                     ydcc[:], vv3[:, cc, :], bank[:, :], ALU.mult)),
                 waits=[("sPE", mkb),
                        ("sSC", M.get("ydsp_%d_%d" % (tb, cc - 1), 0))],
                 incs=[("sDV", 1)])
            M["ydtt_%d_%d" % (tb, cc)] = E.mark("sDV")
            E.dma("scalar", "sSC",
                  S["yd"].rearrange("(ch e p) n -> ch e p n", e=8, p=128)[
                      tb * 8:tb * 8 + 8, cc, :, :].transpose([1, 0, 2]),
                  ydcc[:].rearrange("p (ch n) -> p ch n", n=64),
                  waits=[("sDV", M["ydtt_%d_%d" % (tb, cc)])])
            M["ydsp_%d_%d" % (tb, cc)] = E.mark("sSC")
        M["sdiag_%d" % tb] = E.mark("sDV")
    def exchange(self):
        E, S, M = self.E, self.S, self.M
        for cc in range(8):
            E.op("tensor",
                 (lambda e, cc=cc: e.matmul(self.pb[5][0:16, 0:128], self.ones16[:],
                                            self.r_lng[:, cc * 128:(cc + 1) * 128],
                                            start=True, stop=True)),
                 waits=[("sAC", E.mark("sAC")), ("sDV", E.mark("sDV")),
                        ("sPE", E.mark("sPE"))] if cc == 0 else
                       [("sDV", E.mark("sDV"))],
                 incs=[("sPE", 1)])
            E.op("vector",
                 (lambda e, cc=cc: e.tensor_tensor(
                     self.gin_cc[:, cc * 128:(cc + 1) * 128],
                     self.ind_cc[:, cc * 128:(cc + 1) * 128],
                     self.pb[5][0:16, 0:128], ALU.mult)),
                 waits=[("sPE", E.mark("sPE"))], incs=[("sDV", 1)])
        E.op("gpsimd",
             lambda e: e.collective_compute(
                 "AllGather", ALU.bypass, replica_groups=[list(range(8))],
                 ins=[S["sum_in"].opt()], outs=[S["sum_out"].opt()]),
             waits=[("sSC", M["sumspill"])], incs=[("sCC", 1)])
        M["ccsum"] = E.mark("sCC")

        E.dma("sync", "sSY", self.logdt[:].rearrange("p (m c) -> p m c", c=8),
              S["sum_out"][:, 512:520].rearrange("(m p) c -> p m c", p=128),
              waits=[("sCC", M["ccsum"])], allow_slow_non_contiguous=True)
        mk_ld = E.mark("sSY")
        ldv = self.logdt.rearrange("p (m c) -> p m c", c=8)

        self.ldT = self.cF(14144, 128)[0:8, :]
        self.wfac = self.cF(14272, 64)
        psin = self.cF(0, 512)
        facts = self.cF(13696, 256)

        for cc in range(8):
            E.op("tensor",
                 (lambda e, cc=cc: e.transpose(self.pb[5][0:8, 0:128],
                                               ldv[:, :, cc], self.identf[:])),
                 waits=[("sSY", mk_ld)] + ([("sAC", E.mark("sAC"))] if cc == 0 else
                                           [("sAC", M["ldT_%d" % (cc - 1)])]),
                 incs=[("sPE", 1)])
            mkp = E.mark("sPE")
            E.op("scalar",
                 (lambda e: e.copy(self.ldT[:], self.pb[5][0:8, 0:128])),
                 waits=[("sPE", mkp)], incs=[("sAC", 1)])
            M["ldT_%d" % cc] = E.mark("sAC")

            def emm(e, cc=cc):
                e.matmul(self.pb[4][:, 0:8], self.ldT[:], self.wsel_sb[:],
                         start=True, stop=False)
                return e.matmul(self.pb[4][:, 0:8], self.r_ones[0:1, 0:128],
                                self.wbias_sb[:], start=False, stop=True)
            E.op("tensor", emm, waits=[("sAC", M["ldT_%d" % cc]),
                                       ("sDV", M.get("wfac_%d" % (cc - 1), 0))],
                 incs=[("sPE", 1)])
            mkp2 = E.mark("sPE")
            E.op("scalar",
                 (lambda e, cc=cc: e.activation(self.wfac[:, cc * 8:(cc + 1) * 8],
                                                self.pb[4][:, 0:8], AF.Exp)),
                 waits=[("sPE", mkp2)], incs=[("sAC", 1)])
            M["wfac_%d" % cc] = E.mark("sAC")
        mk_wf = M["wfac_7"]

        E.op("vector", lambda e: e.memset(self.Sin[:], 0.0),
             waits=[], incs=[("sDV", 1)])
        for j in range(8):
            E.dma("sync", "sSY", psin,
                  S["sum_out"][j * 128:(j + 1) * 128, 0:512],
                  waits=[("sDV", E.mark("sDV"))] if j >= 2 else [])
            mk_pl = E.mark("sSY")

            def acc(e, j=j):
                for cc in range(8):
                    inst = e.scalar_tensor_tensor(
                        self.Sin[:, cc * 64:(cc + 1) * 64],
                        psin[:, cc * 64:(cc + 1) * 64],
                        self.wfac[:, cc * 8 + j: cc * 8 + j + 1],
                        self.Sin[:, cc * 64:(cc + 1) * 64], ALU.mult, ALU.add)
                return inst
            E.op("vector", acc, waits=[("sSY", mk_pl), ("sAC", mk_wf)],
                 incs=[("sDV", 1)])
        M["sin"] = E.mark("sDV")

        E.op("scalar", lambda e: e.activation(facts, self.scols[:], AF.Exp, scale=-1.0),
             waits=[("cV", NCH)], incs=[("sAC", 1)])
        M["facts"] = E.mark("sAC")
        self.facts = facts

    def _l2stop(self, part):
        return getattr(self, "l2_stop", None) == part

    def loop2(self):
        E, D, S, M = self.E, self.D, self.S, self.M
        l2stopped = False
        cF, cB = self.cF, self.cB
        pb = self.pb
        xfms = cF(0, 4096)
        xfv = xfms.rearrange("p (c n) -> p c n", n=TB)
        x1 = cF(4096, 4096)
        x1v = x1.rearrange("p (c n) -> p c n", n=TB)
        x1tm = cF(8192, 4096)
        ychp = [cF(14336, 512), cF(14848, 512)]
        ysqp = [cF(15360, 512), cF(15872, 512)]
        stp = [cF(16384, 384), cF(16768, 384)]
        lncol = cF(14080, 64)
        xq = cB(0, 16384)
        wo = cB(16384, 8192)
        a_s = cB(24576, 1024)
        v_s = cB(25600, 1024)
        r_s = cB(26624, 1024)
        sl_sp = [cB(27648, 512), cB(34304, 512)]
        yd_sp = [cB(28160, 512), cB(34816, 512)]
        g_sp = [cB(28672, 512), cB(35328, 512)]
        woin = cB(29184, 4096)
        sfull = cB(33280, 1024)
        self.xq = xq

        E.dma("sync", "sSY", wo.rearrange("p (a co) -> p a co", co=C),
              S["wob"].rearrange("(a p) co -> p a co", p=128),
              waits=[("sPE", M.get("l1pe_done", 0))])
        M["wo_ld"] = E.mark("sSY")

        for tb in range(NTB):
            t0 = tb * TB
            E.dma("sync", "sSY", xfms.rearrange("p (c n) -> p c n", n=TB),
                  S["xfm"][:, t0:t0 + TB].rearrange("(c p) n -> p c n", p=128),
                  waits=[("sSC", M["sumspill"]), ("sDV", M["sin"])] if tb == 0
                  else [("sSC", M["ln1_%d" % (tb - 1)])])
            M["xs_%d" % tb] = E.mark("sSY")

            for loc in range(NCHB):
                chg = tb * NCHB + loc
                p = chg % 2
                pbase = p * 64
                sl_s, yd_s, g_s = sl_sp[p], yd_sp[p], g_sp[p]
                ych, ysq, st = ychp[p], ysqp[p], stp[p]
                mu2 = st[0:16, 0:64]
                msq = st[0:16, 64:128]
                var = st[0:16, 128:192]
                lnv = st[0:16, 192:256]
                rstd = st[0:16, 256:320]
                qng = st[0:16, 320:384]
                pbY = pb[p]
                pbP = pb[2 + 2 * p]
                pbQ = pb[3 + 2 * p]
                sbase = p * 128

                wld = [("sSC", M["sumspill"])] if chg == 0 else \
                    [("sPE", M.get("ymm_%d" % (chg - 2), 0))]
                E.dma("sync", "sSY",
                      a_s[pbase:pbase + 64, :].rearrange("s (e n) -> s e n", n=128),
                      S["A"].rearrange("(ch e s) n -> ch e s n", e=8, s=64)[
                          chg].transpose([1, 0, 2]),
                      waits=wld)
                E.dma("sync", "sSY",
                      v_s[pbase:pbase + 64, :].rearrange("s (e n) -> s e n", n=128),
                      S["vtm"].rearrange("(ch e s) n -> ch e s n", e=8, s=64)[
                          chg].transpose([1, 0, 2]))
                rv4 = S["rt"].rearrange("(ch e p) n -> ch e p n", e=8, p=128)
                E.dma("sync", "sSY",
                      r_s[pbase:pbase + 64, 0:512].rearrange("p (e n) -> p e n", n=64),
                      rv4[chg, :, 0:64, :].transpose([1, 0, 2]))
                E.dma("sync", "sSY",
                      r_s[pbase:pbase + 64, 512:1024].rearrange("p (e n) -> p e n", n=64),
                      rv4[chg, :, 64:128, :].transpose([1, 0, 2]))
                E.dma("sync", "sSY", sl_s.rearrange("p (e n) -> p e n", n=64),
                      S["sl"].rearrange("(ch e p) n -> ch e p n", e=8, p=128)[
                          chg].transpose([1, 0, 2]))
                mk_ld = E.mark("sSY")
                E.dma("sync", "sSY", yd_s.rearrange("p (e n) -> p e n", n=64),
                      S["yd"].rearrange("(ch e p) n -> ch e p n", e=8, p=128)[
                          chg].transpose([1, 0, 2]),
                      waits=[("sDV", M.get("yev_%d" % (chg - 2), 0))])
                mk_ldyd = E.mark("sSY")
                E.dma("sync", "sSY", g_s.rearrange("p (e n) -> p e n", n=64),
                      S["g"].rearrange("(ch e p) n -> ch e p n", e=8, p=128)[
                          chg].transpose([1, 0, 2]),
                      waits=[("sDV", M.get("app_%d" % (chg - 2), 0))])
                mk_ldg = E.mark("sSY")

                def sff(e, chg=chg, pbase=pbase, sl_s=sl_s):
                    for cc in range(8):
                        e.scalar_tensor_tensor(
                            sfull[pbase:pbase + 64, cc * 64:(cc + 1) * 64],
                            self.Sin[0:64, cc * 64:(cc + 1) * 64],
                            self.facts[0:64, cc * 32 + chg: cc * 32 + chg + 1],
                            sl_s[0:64, cc * 64:(cc + 1) * 64], ALU.mult, ALU.add)
                        inst = e.scalar_tensor_tensor(
                            sfull[pbase:pbase + 64, 512 + cc * 64: 512 + (cc + 1) * 64],
                            self.Sin[64:128, cc * 64:(cc + 1) * 64],
                            self.facts[64:128, cc * 32 + chg: cc * 32 + chg + 1],
                            sl_s[64:128, cc * 64:(cc + 1) * 64], ALU.mult, ALU.add)
                    return inst
                E.op("vector", sff,
                     waits=[("sSY", mk_ld), ("sAC", M["facts"]),
                            ("sPE", M.get("ymm_%d" % (chg - 2), 0))],
                     incs=[("sDV", 1)])
                mk_sf = E.mark("sDV")

                def ymm(e, chg=chg, pbase=pbase, pbY=pbY):
                    tpA = (pbase, 0)
                    tpB = (pbase, 64)
                    for cc in range(8):
                        e.matmul(pbY[0:64, cc * 64:(cc + 1) * 64],
                                 v_s[pbase:pbase + 64, cc * 128: cc * 128 + 64],
                                 a_s[pbase:pbase + 64, cc * 128: cc * 128 + 64],
                                 start=True, stop=False, tile_position=tpA)
                        e.matmul(pbY[0:64, cc * 64:(cc + 1) * 64],
                                 sfull[pbase:pbase + 64, cc * 64:(cc + 1) * 64],
                                 r_s[pbase:pbase + 64, cc * 64:(cc + 1) * 64],
                                 start=False, stop=True, tile_position=tpA)
                        e.matmul(pbY[64:128, cc * 64:(cc + 1) * 64],
                                 v_s[pbase:pbase + 64, cc * 128 + 64: cc * 128 + 128],
                                 a_s[pbase:pbase + 64, cc * 128 + 64: cc * 128 + 128],
                                 start=True, stop=False, tile_position=tpB)
                        inst = e.matmul(
                            pbY[64:128, cc * 64:(cc + 1) * 64],
                            sfull[pbase:pbase + 64, 512 + cc * 64: 512 + (cc + 1) * 64],
                            r_s[pbase:pbase + 64, 512 + cc * 64: 512 + (cc + 1) * 64],
                            start=False, stop=True, tile_position=tpB)
                    return inst
                ymw = [("sDV", mk_sf), ("sDV", M.get("yev_%d" % (chg - 2), 0))]
                if loc <= 1:
                    ymw.append(("sDV", M.get("x1_1_%d" % (tb - 1), 0)))
                E.op("tensor", ymm, waits=ymw, incs=[("sPE", 1)])
                M["ymm_%d" % chg] = E.mark("sPE")
                if self._l2stop("ymm"):
                    l2stopped = True
                    break

                E.op("vector",
                     (lambda e, ych=ych, yd_s=yd_s, pbY=pbY: e.tensor_tensor(
                         ych, pbY[:, :], yd_s[:], ALU.add)),
                     waits=[("sPE", M["ymm_%d" % chg]), ("sSY", mk_ldyd)],
                     incs=[("sDV", 1)])
                M["yev_%d" % chg] = E.mark("sDV")
                if "dbg" in self.taps:
                    E.dma("scalar", "sSC", S["dbg"][:, chg * 512:(chg + 1) * 512],
                          ych, waits=[("sDV", M["yev_%d" % chg])])
                if self._l2stop("yev"):
                    l2stopped = True
                    break

                E.op("scalar",
                     (lambda e, ych=ych, ysq=ysq: e.activation(ysq, ych, AF.Square)),
                     waits=[("sDV", M["yev_%d" % chg]),
                            ("sPE", M.get("smm_%d" % (chg - 2), 0))],
                     incs=[("sAC", 1)])
                mk_sq = E.mark("sAC")

                def smm(e, ych=ych, ysq=ysq, sbase=sbase):
                    for cc in range(8):
                        e.matmul(self.pba[0:16, sbase:sbase + 64],
                                 self.ione_cc[:, cc * 16:(cc + 1) * 16],
                                 ych[:, cc * 64:(cc + 1) * 64],
                                 start=(cc == 0), stop=False)
                    for cc in range(8):
                        inst = e.matmul(self.pba[0:16, sbase + 64:sbase + 128],
                                        self.ione_cc[:, cc * 16:(cc + 1) * 16],
                                        ysq[:, cc * 64:(cc + 1) * 64],
                                        start=False, stop=(cc == 7))
                    return inst
                E.op("tensor", smm,
                     waits=[("sAC", mk_sq), ("sDV", M.get("statf_%d" % (chg - 2), 0))],
                     incs=[("sPE", 1)])
                M["smm_%d" % chg] = E.mark("sPE")
                mk_smm = M["smm_%d" % chg]

                def statf(e, mu2=mu2, msq=msq, var=var, sbase=sbase):
                    e.tensor_scalar(mu2, self.pba[0:16, sbase:sbase + 64],
                                    1.0 / 64.0, None, ALU.mult)
                    e.tensor_tensor(msq, mu2, mu2, ALU.mult)
                    return e.scalar_tensor_tensor(
                        var, self.pba[0:16, sbase + 64:sbase + 128], 1.0 / 64.0,
                        msq, ALU.mult, ALU.subtract)
                E.op("vector", statf, waits=[("sPE", mk_smm)], incs=[("sDV", 1)])
                M["statf_%d" % chg] = E.mark("sDV")
                mk_st = M["statf_%d" % chg]

                def lnf(e, lnv=lnv, var=var, rstd=rstd):
                    e.activation(lnv, var, AF.Ln, bias=self.epsg[0:16, :])
                    return e.activation(rstd, lnv, AF.Exp, scale=-0.5)
                E.op("scalar", lnf, waits=[("sDV", mk_st)], incs=[("sAC", 1)])
                mk_ln = E.mark("sAC")

                E.op("vector",
                     (lambda e, qng=qng, mu2=mu2, rstd=rstd: e.scalar_tensor_tensor(
                         qng, mu2, -1.0, rstd, ALU.mult, ALU.mult)),
                     waits=[("sAC", mk_ln)], incs=[("sDV", 1)])
                M["stats_%d" % chg] = E.mark("sDV")
                if "dbg4" in self.taps:
                    E.dma("scalar", "sSC", S["dbg"][0:16, chg * 384:(chg + 1) * 384],
                          st[0:16, :], waits=[("sDV", M["stats_%d" % chg])])
                if self._l2stop("stats"):
                    l2stopped = True
                    break

                def pqmm(e, pbP=pbP, pbQ=pbQ, rstd=rstd, qng=qng):
                    for cc in range(8):
                        e.matmul(pbP[:, cc * 64:(cc + 1) * 64],
                                 self.gin_cc[:, cc * 128:(cc + 1) * 128], rstd,
                                 start=True, stop=True)
                        e.matmul(pbQ[:, cc * 64:(cc + 1) * 64],
                                 self.gin_cc[:, cc * 128:(cc + 1) * 128], qng,
                                 start=True, stop=False)
                        inst = e.matmul(pbQ[:, cc * 64:(cc + 1) * 64],
                                        self.r_lnb[0:1, cc * 128:(cc + 1) * 128],
                                        self.r_ones[0:1, 0:64],
                                        start=False, stop=True)
                    return inst
                wpq = [("sDV", M["stats_%d" % chg]),
                       ("sDV", M.get("app_%d" % (chg - 2), 0))]
                if loc == 0:
                    wpq += [("sAC", M.get("l1q_ac_%d" % (tb - 1), 0)),
                            ("sDV", M.get("x1_1_%d" % (tb - 1), 0))]
                elif loc == 1:
                    wpq.append(("sAC", M.get("l1q_ac_%d" % (tb - 1), 0)))
                E.op("tensor", pqmm, waits=wpq, incs=[("sPE", 1)])
                mk_pq = E.mark("sPE")

                def appf(e, loc=loc, ych=ych, ysq=ysq, g_s=g_s, pbP=pbP, pbQ=pbQ):
                    e.tensor_tensor(ysq, ych, pbP[:, :], ALU.mult)
                    e.tensor_tensor(ych, ysq, pbQ[:, :], ALU.add)
                    wv = woin.rearrange("p (c n) -> p c n", n=TB)[:, :, loc * 64:(loc + 1) * 64]
                    return e.tensor_tensor(
                        wv, ych.rearrange("p (c n) -> p c n", n=64),
                        g_s.rearrange("p (c n) -> p c n", n=64), ALU.mult)
                wapp = [("sPE", mk_pq), ("sSY", mk_ldg)]
                if loc == 0:
                    wapp.append(("sPE", M.get("xqpe_%d" % (tb - 1), 0)))
                E.op("vector", appf, waits=wapp, incs=[("sDV", 1)])
                M["app_%d" % chg] = E.mark("sDV")
                if "dbg5" in self.taps:
                    E.dma("scalar", "sSC", S["dbg"][:, chg * 512:(chg + 1) * 512],
                          ych, waits=[("sDV", M["app_%d" % chg])])
                if self._l2stop("app"):
                    l2stopped = True
                    break

            if "dbg2" in self.taps:
                E.dma("scalar", "sSC",
                      S["dbg"].bitcast(BF16)[:, tb * 8192:tb * 8192 + 4096],
                      woin, waits=[("sDV", M["app_%d" % (tb * NCHB + NCHB - 1)])])
            if l2stopped:
                break
            for half in range(2):
                hs = half * 256

                def womm(e, hs=hs):
                    for co in range(8):
                        bank = pb[co // 2]
                        for ci in range(8):
                            inst = e.matmul(
                                bank[:, (co % 2) * 256:(co % 2) * 256 + 256],
                                wo[:, ci * C + co * 128: ci * C + co * 128 + 128],
                                woin[:, ci * TB + hs: ci * TB + hs + 256],
                                start=(ci == 0), stop=(ci == 7))
                    return inst
                w = [("sDV", M["app_%d" % (tb * NCHB + NCHB - 1)]),
                     ("sSY", M["wo_ld"]),
                     ("sDV", M.get("yev_%d" % (tb * NCHB + NCHB - 1), 0))]
                if half == 0:
                    w.append(("sDV", M.get("x1_%s_%d" % (1, tb - 1), 0)))
                else:
                    w.append(("sDV", M.get("x1_%s_%d" % (0, tb), 0)))
                E.op("tensor", womm, waits=w, incs=[("sPE", 1)])
                mk_wo = E.mark("sPE")

                def x1f(e, hs=hs):
                    for co in range(8):
                        bank = pb[co // 2]
                        inst = e.tensor_tensor(
                            x1v[:, co, hs:hs + 256], xfv[:, co, hs:hs + 256],
                            bank[:, (co % 2) * 256:(co % 2) * 256 + 256], ALU.add)
                    return inst
                x1w = [("sPE", mk_wo), ("sSY", M["xs_%d" % tb])]
                if half == 0:
                    x1w.append(("sSC", M.get("x1nsp_%d_3" % (tb - 1), 0)))
                E.op("vector", x1f, waits=x1w, incs=[("sDV", 1)])
                M["x1_%s_%d" % (half, tb)] = E.mark("sDV")
            M["x1_%d" % tb] = E.mark("sDV")
            if "dbg3" in self.taps:
                E.dma("scalar", "sSC", S["dbg"][:, tb * 4096:(tb + 1) * 4096],
                      x1, waits=[("sDV", M["x1_%d" % tb])])
            if self._l2stop("x1"):
                l2stopped = True
                break

            self._ln1_quant(tb, x1, x1v, x1tm, xfms, woin, xq)
        M["wo_free"] = E.mark("sPE")

    def _ln1_quant(self, tb, x1, x1v, x1tm, xfms, woin, xq):
        E, S, M = self.E, self.S, self.M
        pb = self.pb
        lncol = self.cF(14080, 64)
        mucol = lncol[:, 0:1]
        sqcol = lncol[:, 1:2]
        varcol = lncol[:, 2:3]
        rstdc = lncol[:, 4:5]
        lnvc = lncol[:, 3:4]
        amaxc = lncol[:, 5:6]
        sclc = lncol[:, 6:7]
        mu_s = lncol[:, 8:9]
        msqc = lncol[:, 9:10]
        amax2c = lncol[:, 10:11]
        lnsc = lncol[:, 11:12]

        for s1 in range(4):
            pem = []
            for cc in range(8):
                bank = pb[4] if cc % 2 == 0 else pb[5]
                w = [("sDV", M["x1_%d" % tb]), ("sAC", E.mark("sAC"))]
                E.op("tensor",
                     (lambda e, cc=cc, s1=s1, bank=bank: e.transpose(
                         bank[:, 0:128], x1v[:, cc, s1 * 128:(s1 + 1) * 128],
                         self.identf[:])),
                     waits=w, incs=[("sPE", 1)])
                pem.append(E.mark("sPE"))
                E.op("scalar",
                     (lambda e, cc=cc, s1=s1, bank=bank: e.copy(
                         x1tm[:, s1 * 1024 + cc * 128: s1 * 1024 + cc * 128 + 128],
                         bank[:, 0:128])),
                     waits=[("sPE", pem[-1])], incs=[("sAC", 1)])
            mk_tm = E.mark("sAC")

            sl = x1tm[:, s1 * 1024:(s1 + 1) * 1024]
            sq = xfms[:, 0:1024]

            def stat1(e, sl=sl):
                return e.tensor_reduce(mucol, sl, AX.X, ALU.add)
            E.op("vector", stat1, waits=[("sAC", mk_tm)], incs=[("sDV", 1)])
            E.op("scalar", (lambda e, sl=sl, sq=sq: e.activation(sq, sl, AF.Square)),
                 waits=[("sAC", mk_tm), ("sDV", E.mark("sDV"))], incs=[("sAC", 1)])
            mk_sq = E.mark("sAC")
            E.op("vector", (lambda e, sq=sq: e.tensor_reduce(sqcol, sq, AX.X, ALU.add)),
                 waits=[("sAC", mk_sq)], incs=[("sDV", 1)])

            def stat2a(e):
                e.activation(mu_s, mucol, AF.Copy, scale=1.0 / 1024.0)
                e.activation(msqc, mu_s, AF.Square)
                return e.activation(lnvc, sqcol, AF.Copy, scale=1.0 / 1024.0)
            E.op("scalar", stat2a, waits=[("sDV", E.mark("sDV"))], incs=[("sAC", 1)])
            mk_sa = E.mark("sAC")
            E.op("vector",
                 (lambda e: e.tensor_tensor(varcol, lnvc, msqc, ALU.subtract)),
                 waits=[("sAC", mk_sa)], incs=[("sDV", 1)])
            mk_v = E.mark("sDV")

            def lnf2(e):
                e.activation(lnvc, varcol, AF.Ln, bias=self.epsl[:])
                return e.activation(rstdc, lnvc, AF.Exp, scale=-0.5)
            E.op("scalar", lnf2, waits=[("sDV", mk_v)], incs=[("sAC", 1)])
            mk_r = E.mark("sAC")

            E.op("vector",
                 (lambda e, sl=sl: e.tensor_scalar(sl, sl, mu_s, rstdc,
                                                   ALU.subtract, ALU.mult)),
                 waits=[("sAC", mk_r)], incs=[("sDV", 1)])
            mk_n = E.mark("sDV")

            if "dbg6" in self.taps:
                sidx = tb * 4 + s1
                E.dma("scalar", "sSC", S["dbg"][:, sidx * 1024:(sidx + 1) * 1024],
                      sl, waits=[("sDV", mk_n)])
                E.dma("scalar", "sSC",
                      S["dbg"][0:128, 32768 + sidx * 8: 32768 + sidx * 8 + 8],
                      lncol[:, 0:8])
            for cc in range(8):
                bank = pb[4] if cc % 2 == 0 else pb[5]
                E.op("tensor",
                     (lambda e, cc=cc, bank=bank, sl=sl: e.transpose(
                         bank[:, 0:128], sl[:, cc * 128:(cc + 1) * 128], self.identf[:])),
                     waits=[("sDV", mk_n), ("sAC", E.mark("sAC"))],
                     incs=[("sPE", 1)])
                mk_t3 = E.mark("sPE")
                E.op("scalar",
                     (lambda e, cc=cc, s1=s1, bank=bank: e.copy(
                         x1v[:, cc, s1 * 128:(s1 + 1) * 128], bank[:, 0:128])),
                     waits=[("sPE", mk_t3)], incs=[("sAC", 1)])
            mk_fm = E.mark("sAC")
            mk_fm_pe = E.mark("sPE")
            for cc in range(8):
                E.dma("scalar", "sSC",
                      S["x1n"][cc * 128:(cc + 1) * 128,
                               tb * TB + s1 * 128: tb * TB + (s1 + 1) * 128],
                      x1v[:, cc, s1 * 128:(s1 + 1) * 128],
                      waits=[("sAC", mk_fm)] if cc == 0 else [])
            M["x1nsp_%d_%d" % (tb, s1)] = E.mark("sSC")

            E.op("vector",
                 (lambda e, sl=sl: e.tensor_reduce(
                     amaxc, sl, AX.X, ALU.max, apply_absolute_value=True)),
                 waits=[("sDV", mk_n)], incs=[("sDV", 1)])
            E.op("gpsimd",
                 (lambda e: e.tensor_scalar(amax2c, amaxc, 1.0 / 127.0,
                                            1e-5 / 127.0, ALU.mult, ALU.max)),
                 waits=[("sDV", E.mark("sDV"))], incs=[("sGP", 1)])

            def qs2(e, tb=tb, s1=s1):
                e.activation(self.dsx[:, tb * 4 + s1: tb * 4 + s1 + 1],
                             amax2c, AF.Copy)
                e.activation(lnsc, amax2c, AF.Ln)
                return e.activation(sclc, lnsc, AF.Exp, scale=-1.0)
            E.op("scalar", qs2, waits=[("sGP", E.mark("sGP"))], incs=[("sAC", 1)])
            mk_s = E.mark("sAC")
            M["dsxac"] = mk_s

            xqtm = woin[:, 0:1024]

            def qr(e, sl=sl, xqtm=xqtm):
                e.tensor_scalar(sl, sl, sclc, CRND, ALU.mult, ALU.add)
                return e.tensor_scalar(xqtm, sl, CRND, None, ALU.subtract)
            E.op("vector", qr, waits=[("sAC", mk_s), ("sPE", mk_fm_pe)],
                 incs=[("sDV", 1)])
            mk_q = E.mark("sDV")

            for cc in range(8):
                def trq(e, cc=cc):
                    e.transpose(self.pbt[:, 0:128], xqtm[:, cc * 128: cc * 128 + 64],
                                self.identb[:])
                    return e.transpose(self.pbt[:, 128:256],
                                       xqtm[:, cc * 128 + 64: cc * 128 + 128],
                                       self.identb[:])
                E.op("tensor", trq,
                     waits=[("sDV", mk_q), ("sAC", E.mark("sAC"))],
                     incs=[("sPE", 1)])
                mk_t2 = E.mark("sPE")
                tcol = tb * TB + s1 * 128

                def evq(e, cc=cc, tcol=tcol):
                    e.copy(xq.rearrange("p (c n) -> p c n", n=TLOC)[0:64, cc, tcol:tcol + 128],
                           self.pbt[:, 0:128])
                    return e.copy(
                        xq.rearrange("p (c n) -> p c n", n=TLOC)[64:128, cc, tcol:tcol + 128],
                        self.pbt[:, 128:256])
                E.op("scalar", evq, waits=[("sPE", mk_t2)], incs=[("sAC", 1)])
            M["xq_%d_%d" % (tb, s1)] = E.mark("sAC")
            M["xqpe_%d" % tb] = E.mark("sPE")
        M["l1q_ac_%d" % tb] = E.mark("sAC")
        M["ln1_%d" % tb] = E.mark("sSC")
    def ffn(self):
        E, D, S, M = self.E, self.D, self.S, self.M
        cF, cB = self.cF, self.cB
        pb = self.pb
        hw = cF(0, 16384)
        x1ns = cF(16384, 512)
        ddbs = cF(16896, 512)
        x2f = cF(17408, 512)
        x2tm = cF(17920, 4096)
        sq2 = cF(22016, 1024)
        gs = cF(23040, 512)
        fc = cF(23552, 64)
        dsg = fc[:, 0:16]
        dsu = fc[:, 16:32]
        ddc = fc[:, 32:48]
        csq = fc[:, 48:64]
        l2c = cF(23616, 16)
        mucol = l2c[:, 0:1]
        sqcol = l2c[:, 1:2]
        varcol = l2c[:, 2:3]
        lnvc = l2c[:, 3:4]
        rstdc = l2c[:, 4:5]
        mu_s2 = l2c[:, 5:6]
        msqc2 = l2c[:, 6:7]
        xq = self.xq
        wgq = cB(16384, 4096)
        wuq = cB(20480, 4096)
        hqs = cB(24576, 4096)
        wdt4 = cB(16384, 16384)
        hqT = [cB(32768, 512), cB(33280, 512)]
        amaxw = self.amaxw

        hmxq = cF(23632, 64)

        def dsf(e):
            e.tensor_scalar(dsg, self.dsx[:], self.swv[:, 0:1], None, ALU.mult)
            return e.tensor_scalar(dsu, self.dsx[:], self.swv[:, 1:2], None, ALU.mult)
        E.op("vector", dsf, waits=[("sSC", M["ln1_%d" % (NTB - 1)]),
                                   ("sAC", M["dsxac"])],
             incs=[("sDV", 1)])
        M["dsf"] = E.mark("sDV")

        for w in range(NWAVE):
            wt0 = w * TB
            for q in range(8):
                wld = [("sCC", M["ccw"]), ("sPE", M.get("dn_%d" % (w - 1), 0)),
                       ("sPE", M.get("wo_free", 0))]
                if q > 0:
                    wld.append(("sPE", M.get("gu_%d_%d" % (w, q - 1), 0)))
                E.dma("sync", "sSY", wgq.rearrange("p (a n) -> p a n", n=512),
                      S["wgT"].rearrange("(a p) n -> a p n", p=128)[
                          q * 8:(q + 1) * 8].transpose([1, 0, 2]),
                      waits=wld)
                E.dma("sync", "sSY", wuq.rearrange("p (a n) -> p a n", n=512),
                      S["wuT"].rearrange("(a p) n -> a p n", p=128)[
                          q * 8:(q + 1) * 8].transpose([1, 0, 2]))
                mk_w = E.mark("sSY")
                for sl in range(4):
                    gbank = pb[sl % 2]
                    ubank = pb[2 + sl % 2]
                    ts = wt0 + sl * 128
                    sidx = w * 4 + sl

                    def gumm(e, ts=ts, gbank=gbank, ubank=ubank):
                        xqv = xq.rearrange("p (c n) -> p c n", n=TLOC)
                        for ci in range(8):
                            e.matmul(gbank[:, :], xqv[:, ci, ts:ts + 128],
                                     wgq[:, ci * 512:(ci + 1) * 512],
                                     start=(ci == 0), stop=(ci == 7))
                        for ci in range(8):
                            inst = e.matmul(ubank[:, :], xqv[:, ci, ts:ts + 128],
                                            wuq[:, ci * 512:(ci + 1) * 512],
                                            start=(ci == 0), stop=(ci == 7))
                        return inst
                    if sl >= 2:
                        bw = ("sDV", M.get("hup_%d_%d_%d" % (w, q, sl - 2), 0))
                    else:
                        bw = ("sDV", M.get("hup_%d_%d_%d" % (w, q - 1, sl + 2), 0))
                    wmm = [("sSY", mk_w), bw]
                    if w == 0 and q == 0 and sl == 0:
                        wmm.append(("sAC", M["xq_%d_%d" % (NTB - 1, 3)]))
                    E.op("tensor", gumm, waits=wmm, incs=[("sPE", 1)])
                    mk_g = E.mark("sPE")
                    if sl == 3:
                        M["gu_%d_%d" % (w, q)] = mk_g

                    E.op("scalar",
                         (lambda e, gbank=gbank, sidx=sidx: e.activation(
                             gs, gbank[:, :], AF.Silu, scale=dsg[:, sidx:sidx + 1])),
                         waits=[("sPE", mk_g), ("sDV", M["dsf"]),
                                ("sDV", M.get("hup_%d_%d_%d" % (w, q, sl - 1), 0)
                                 if sl else M.get("hup_%d_%d_3" % (w, q - 1), 0))],
                         incs=[("sAC", 1)])
                    mk_si = E.mark("sAC")

                    def hupf(e, sl=sl, q=q, ubank=ubank):
                        hsl = hw[:, sl * 4096 + q * 512: sl * 4096 + (q + 1) * 512]
                        e.tensor_tensor(hsl, gs, ubank[:, :], ALU.mult)
                        return e.tensor_reduce(
                            hmxq[:, sl * 8 + q: sl * 8 + q + 1], hsl,
                            AX.X, ALU.max, apply_absolute_value=True)
                    E.op("vector", hupf, waits=[("sAC", mk_si)], incs=[("sDV", 1)])
                    M["hup_%d_%d_%d" % (w, q, sl)] = E.mark("sDV")

            for sl in range(4):
                sidx = w * 4 + sl

                hmax_s = hmxq[:, 32 + sl: 33 + sl]
                tmpc_s = hmxq[:, 36 + sl: 37 + sl]
                tmp2_s = hmxq[:, 40 + sl: 41 + sl]
                rec_s = hmxq[:, 44 + sl: 45 + sl]
                ddt_s = hmxq[:, 48 + sl: 49 + sl]

                E.op("vector",
                     (lambda e, sl=sl, hmax_s=hmax_s: e.tensor_reduce(
                         hmax_s, hmxq[:, sl * 8: sl * 8 + 8], AX.X, ALU.max)),
                     waits=[("sDV", M["hup_%d_%d_%d" % (w, 7, sl)])],
                     incs=[("sDV", 1)])
                E.op("gpsimd",
                     (lambda e, sidx=sidx, hmax_s=hmax_s, tmpc_s=tmpc_s:
                      e.tensor_tensor(tmpc_s, hmax_s, dsu[:, sidx:sidx + 1],
                                      ALU.mult)),
                     waits=[("sDV", E.mark("sDV"))], incs=[("sGP", 1)])
                E.op("vector",
                     (lambda e, tmpc_s=tmpc_s, tmp2_s=tmp2_s: e.tensor_scalar(
                         tmp2_s, tmpc_s, 1e-5, None, ALU.max)),
                     waits=[("sGP", E.mark("sGP"))], incs=[("sDV", 1)])

                lnt_s = hmxq[:, 52 + sl: 53 + sl]

                def qsha(e, tmp2_s=tmp2_s, rec_s=rec_s, ddt_s=ddt_s,
                         lnt_s=lnt_s, sidx=sidx):
                    e.activation(lnt_s, tmp2_s, AF.Ln)
                    e.activation(rec_s, lnt_s, AF.Exp, scale=-1.0)
                    e.activation(ddt_s, tmp2_s, AF.Copy, scale=1.0 / 127.0)
                    return e.activation(ddc[:, sidx:sidx + 1], ddt_s, AF.Copy,
                                        scale=self.swv[:, 2:3])
                E.op("scalar", qsha, waits=[("sDV", E.mark("sDV"))],
                     incs=[("sAC", 1)])
                E.op("gpsimd",
                     (lambda e, rec_s=rec_s, sidx=sidx: e.tensor_scalar(
                         csq[:, sidx:sidx + 1], rec_s, 127.0,
                         dsu[:, sidx:sidx + 1], ALU.mult, ALU.mult)),
                     waits=[("sAC", E.mark("sAC"))], incs=[("sGP", 1)])
                mk_csq = E.mark("sGP")

                qrw = [("sSC", M.get("hqsp_%d_%d" % (w, sl - 1), 0)),
                       ("sPE", M.get("dn_%d" % (w - 1), 0)),
                       ("sGP", mk_csq)]
                if "dbg7" in self.taps and w == 0:
                    E.dma("scalar", "sSC", S["dbg"][:, sl * 4096:(sl + 1) * 4096],
                          hw[:, sl * 4096:(sl + 1) * 4096],
                          waits=[("sDV", M["hup_%d_%d_%d" % (w, 7, sl)])])
                    if sl == 3:
                        E.dma("scalar", "sSC", S["dbg"][:, 16384:16448], fc,
                              waits=[("sGP", mk_csq), ("sAC", E.mark("sAC"))])
                        E.dma("scalar", "sSC", S["dbg"][:, 16448:16512], hmxq)
                    qrw.append(("sSC", E.mark("sSC")))

                def qrh(e, sl=sl, sidx=sidx):
                    e.tensor_scalar(hw[:, sl * 4096:(sl + 1) * 4096],
                                    hw[:, sl * 4096:(sl + 1) * 4096],
                                    csq[:, sidx:sidx + 1], CRND, ALU.mult, ALU.add)
                    return e.tensor_scalar(hqs, hw[:, sl * 4096:(sl + 1) * 4096],
                                           CRND, None, ALU.subtract)
                E.op("vector", qrh, waits=qrw, incs=[("sDV", 1)])
                mk_hq = E.mark("sDV")
                E.dma("scalar", "sSC", S["hq"][wt0 + sl * 128: wt0 + (sl + 1) * 128, :],
                      hqs, waits=[("sDV", mk_hq)])
                M["hqsp_%d_%d" % (w, sl)] = E.mark("sSC")

            for sl in range(4):
                sidx = w * 4 + sl
                E.op("tensor",
                     (lambda e, sidx=sidx: e.transpose(
                         pb[5][0:1, 0:128], ddc[:, sidx:sidx + 1], self.identf[:])),
                     waits=[("sDV", E.mark("sDV")), ("sAC", E.mark("sAC"))],
                     incs=[("sPE", 1)])
                mk_dt = E.mark("sPE")
                E.op("scalar",
                     (lambda e, sl=sl: e.copy(self.ddrow[0:1, sl * 128:(sl + 1) * 128],
                                              pb[5][0:1, 0:128])),
                     waits=[("sPE", mk_dt)], incs=[("sAC", 1)])
            mk_dr = E.mark("sAC")
            E.op("tensor",
                 (lambda e: e.matmul(pb[4][:, :], self.r_ones[0:1, 0:128],
                                     self.ddrow[0:1, :], start=True, stop=True)),
                 waits=[("sAC", mk_dr), ("sDV", M.get("ddbev_%d" % (w - 1), 0))],
                 incs=[("sPE", 1)])
            mk_db = E.mark("sPE")
            E.op("vector", (lambda e: e.tensor_copy(ddbs, pb[4][:, :])),
                 waits=[("sPE", mk_db)], incs=[("sDV", 1)])
            M["ddbev_%d" % w] = E.mark("sDV")

            wdtv = wdt4.rearrange("p (a c n) -> p a c n", a=4, n=128)

            for chalf in range(2):
                for i4 in range(4):
                    blk = chalf * 4 + i4
                    wldd = [("sSC", M["hqsp_%d_3" % w]), ("sPE", M["gu_%d_7" % w])]
                    if chalf == 1:
                        wldd.append(("sPE", M.get("dmm_%d_0_31" % w, 0)))
                    E.dma("sync", "sSY",
                          wdt4.rearrange("p (q a n) -> p q a n", q=4, n=128)[:, i4],
                          S["wdT"].rearrange("(a p) n -> a p n", p=128)[
                              blk * 32:(blk + 1) * 32].transpose([1, 0, 2]),
                          waits=wldd if i4 == 0 else [])
                mk_wd = E.mark("sSY")
                for coch in range(32):
                    par2 = coch % 2
                    qeng = "sync" if coch % 2 == 0 else "scalar"
                    qsem = "sSY" if coch % 2 == 0 else "sSC"
                    E.dma(qeng, qsem, hqT[par2],
                          S["hq"][wt0:wt0 + TB, coch * 128:(coch + 1) * 128],
                          transpose=True,
                          waits=[("sSC", M["hqsp_%d_3" % w]),
                                 ("sPE", M.get("dmm_%d_%d_%d" % (w, chalf, coch - 2),
                                               M.get("dmm_%d_%d_31" % (w, chalf - 1), 0)))])
                    mk_h = E.mark(qsem)

                    def dmm(e, coch=coch, chalf=chalf, par2=par2):
                        for i in range(4):
                            cch = chalf * 4 + i
                            inst = e.matmul(pb[i][:, :], wdtv[:, i, coch, :],
                                            hqT[par2][:, :],
                                            start=(coch == 0), stop=(coch == 31))
                        return inst
                    wmm = [(qsem, mk_h), ("sSY", mk_wd)]
                    if coch == 0:
                        wmm.append(("sDV", M.get("x2f_%d_%d" % (w, chalf * 4 - 1),
                                                 M.get("x2f_%d_7" % (w - 1), 0))))
                    E.op("tensor", dmm, waits=wmm, incs=[("sPE", 1)])
                    M["dmm_%d_%d_%d" % (w, chalf, coch)] = E.mark("sPE")
                mk_dn = E.mark("sPE")

                for i in range(4):
                    cch = chalf * 4 + i
                    E.dma("sync", "sSY", x1ns,
                          S["x1n"][cch * 128:(cch + 1) * 128, wt0:wt0 + TB],
                          waits=[("sDV", M.get("x2f_%d_%d" % (w, cch - 1), 0))])
                    mk_x1 = E.mark("sSY")

                    def x2ev(e, i=i):
                        e.tensor_tensor(x2f, pb[i][:, :], ddbs, ALU.mult)
                        return e.tensor_tensor(x2f, x2f, x1ns, ALU.add)
                    E.op("vector", x2ev,
                         waits=[("sPE", mk_dn), ("sSY", mk_x1),
                                ("sDV", M["ddbev_%d" % w]),
                                ("sAC", M.get("x2tr_%d_%d" % (w, cch - 1), 0))],
                         incs=[("sDV", 1)])
                    M["x2f_%d_%d" % (w, cch)] = E.mark("sDV")

                    for sl in range(4):
                        E.op("tensor",
                             (lambda e, sl=sl: e.transpose(
                                 pb[5][:, 0:128], x2f[:, sl * 128:(sl + 1) * 128],
                                 self.identf[:])),
                             waits=[("sDV", M["x2f_%d_%d" % (w, cch)]),
                                    ("sAC", E.mark("sAC"))],
                             incs=[("sPE", 1)])
                        mk_t = E.mark("sPE")
                        E.op("scalar",
                             (lambda e, sl=sl, cch=cch: e.copy(
                                 x2tm[:, sl * 1024 + cch * 128:
                                      sl * 1024 + cch * 128 + 128],
                                 pb[5][:, 0:128])),
                             waits=[("sPE", mk_t)], incs=[("sAC", 1)])
                    M["x2tr_%d_%d" % (w, cch)] = E.mark("sAC")
            M["dn_%d" % w] = E.mark("sPE")

            for sl in range(4):
                slv = x2tm[:, sl * 1024:(sl + 1) * 1024]
                E.op("vector", (lambda e, slv=slv: e.tensor_reduce(
                    mucol, slv, AX.X, ALU.add)),
                     waits=[("sAC", M["x2tr_%d_%d" % (w, 7)])], incs=[("sDV", 1)])
                E.op("scalar", (lambda e, slv=slv: e.activation(sq2, slv, AF.Square)),
                     waits=[("sAC", M["x2tr_%d_%d" % (w, 7)]),
                            ("sDV", E.mark("sDV"))], incs=[("sAC", 1)])
                mk_s2 = E.mark("sAC")
                E.op("vector", (lambda e: e.tensor_reduce(sqcol, sq2, AX.X, ALU.add)),
                     waits=[("sAC", mk_s2)], incs=[("sDV", 1)])

                def l2sa(e):
                    e.activation(mu_s2, mucol, AF.Copy, scale=1.0 / 1024.0)
                    e.activation(msqc2, mu_s2, AF.Square)
                    return e.activation(lnvc, sqcol, AF.Copy, scale=1.0 / 1024.0)
                E.op("scalar", l2sa, waits=[("sDV", E.mark("sDV"))],
                     incs=[("sAC", 1)])
                mk_a2 = E.mark("sAC")
                E.op("vector",
                     (lambda e: e.tensor_tensor(varcol, lnvc, msqc2, ALU.subtract)),
                     waits=[("sAC", mk_a2)], incs=[("sDV", 1)])
                mk_v2 = E.mark("sDV")

                def l2ln(e):
                    e.activation(lnvc, varcol, AF.Ln, bias=self.epsl[:])
                    return e.activation(rstdc, lnvc, AF.Exp, scale=-0.5)
                E.op("scalar", l2ln, waits=[("sDV", mk_v2)], incs=[("sAC", 1)])
                mk_r2 = E.mark("sAC")

                E.op("vector",
                     (lambda e, slv=slv: e.tensor_scalar(slv, slv, mu_s2, rstdc,
                                                         ALU.subtract, ALU.mult)),
                     waits=[("sAC", mk_r2)], incs=[("sDV", 1)])
                mk_o = E.mark("sDV")
                E.dma("scalar", "sOUT",
                      D["out"][wt0 + sl * 128: wt0 + (sl + 1) * 128, :], slv,
                      waits=[("sDV", mk_o)])

    def finish(self):
        E = self.E
        if E.mark("sOUT") == 0:
            w = [(s, E.mark(s)) for s in ["sSY", "sSC", "sGP", "sCC", "sPE",
                                          "sDV", "sAC", "cS", "gE", "sOUT"]]
            E.op("sync", lambda e: None, waits=w)
            return
        total = E.mark("sOUT")
        E.op("sync", lambda e: None, waits=[("sOUT", total)])


_IN_MAP = [
    ("maa_x", "L0_attn_maa_x"), ("maa_w", "L0_attn_maa_w"), ("maa_k", "L0_attn_maa_k"),
    ("maa_v", "L0_attn_maa_v"), ("maa_r", "L0_attn_maa_r"), ("maa_g", "L0_attn_maa_g"),
    ("maa_w1", "L0_attn_maa_w1"), ("maa_w2", "L0_attn_maa_w2"),
    ("Wr", "L0_attn_Wr"), ("Wk", "L0_attn_Wk"), ("Wv", "L0_attn_Wv"),
    ("Wo", "L0_attn_Wo"), ("gate_w1", "L0_attn_gate_w1"), ("gate_w2", "L0_attn_gate_w2"),
    ("time_decay", "L0_attn_time_decay"), ("decay_w1", "L0_attn_decay_w1"),
    ("decay_w2", "L0_attn_decay_w2"), ("u", "L0_attn_u"),
    ("ln_g", "L0_attn_ln_g"), ("ln_b", "L0_attn_ln_b"),
]


def build_program(taps=(), stop_after=None, chunk_stop=None, l2_stop=None):
    kb = KB(taps=taps, stop_after=stop_after)
    kb.chunk_stop = chunk_stop
    kb.l2_stop = l2_stop
    nc = kb.build()
    return nc


def make_in_maps(inputs):
    x = np.ascontiguousarray(np.asarray(inputs["L0_attn_x"], dtype=np.float32))
    wg = np.ascontiguousarray(np.asarray(inputs["L0_ffn_w_gate"], dtype=np.float32))
    wu = np.ascontiguousarray(np.asarray(inputs["L0_ffn_w_up"], dtype=np.float32))
    wd = np.ascontiguousarray(np.asarray(inputs["L0_ffn_w_down"], dtype=np.float32))
    common = {}
    for knew, kold in _IN_MAP:
        common[knew] = np.ascontiguousarray(np.asarray(inputs[kold], dtype=np.float32))
    in_maps = []
    for c in range(8):
        b, q = c // 4, c % 4
        im = dict(common)
        im["x"] = np.ascontiguousarray(x[b, q * TLOC:(q + 1) * TLOC, :])
        if q == 0:
            im["xprev"] = np.zeros((1, C), np.float32)
        else:
            im["xprev"] = np.ascontiguousarray(x[b, q * TLOC - 1: q * TLOC, :])
        im["wg_sh"] = np.ascontiguousarray(wg[c * 512:(c + 1) * 512, :])
        im["wu_sh"] = np.ascontiguousarray(wu[c * 512:(c + 1) * 512, :])
        im["wd_sh"] = np.ascontiguousarray(wd[c * 128:(c + 1) * 128, :])
        wsel = np.zeros((8, 8), np.float32)
        wbias = np.full((1, 8), -1e30, np.float32)
        g0 = b * 4
        for j in range(g0, c):
            wbias[0, j] = 0.0
            for m in range(j + 1, c):
                wsel[m, j] = 1.0
        im["wsel"] = wsel
        im["wbias"] = wbias
        in_maps.append(im)
    return in_maps


def kernel(**inputs):
    nc = build_program()
    in_maps = make_in_maps(inputs)
    res = run_bass_kernel_spmd(nc, in_maps, core_ids=list(range(8)))
    out = np.empty((2, 8192, C), np.float32)
    for c in range(8):
        b, q = c // 4, c % 4
        out[b, q * TLOC:(q + 1) * TLOC, :] = res.results[c]["out"]
    return out


# version 09
# speedup vs baseline: 1.0010x; 1.0010x over previous
import numpy as np
import ml_dtypes
from contextlib import ExitStack
import concourse.bass as bass
import concourse.mybir as mybir
from concourse.bass_utils import run_bass_kernel_spmd

F32 = mybir.dt.float32
BF16 = mybir.dt.bfloat16
AF = mybir.ActivationFunctionType
ALU = mybir.AluOpType
AX = mybir.AxisListType

C = 1024
TLOC = 2048
NTB = 4
TB = 512
LCH = 64
NCHB = TB // LCH
NCH = TLOC // LCH
DIN = 4096
NWAVE = 4
CRND = 12582912.0
EPS_GN = 0.00064
EPS_LN = 1e-5
ENGS = ("sync", "scalar", "vector", "tensor", "gpsimd")


class Emitter:

    def __init__(self):
        self.q = {e: [] for e in ENGS}
        self.cnt = {}

    def newsem(self, name):
        self.cnt[name] = 0
        return name

    def mark(self, s):
        return self.cnt[s]

    def op(self, eng, fn, waits=(), incs=()):
        for s, k in incs:
            self.cnt[s] += k
        self.q[eng].append((fn, list(waits), list(incs)))

    def dma(self, eng, sem, out, in_, waits=(), **kw):
        def fn(e, out=out, in_=in_, kw=kw):
            return e.dma_start(out=out, in_=in_, **kw)
        self.op(eng, fn, waits=waits, incs=[(sem, 16)])
        return self.cnt[sem]

    def flush(self, block, semh):
        for eng in ENGS:
            ops = self.q[eng]
            if not ops:
                continue

            def body(e, ops=ops):
                for fn, waits, incs in ops:
                    for s, v in waits:
                        if v > 0:
                            e.wait_ge(semh[s], v)
                    inst = fn(e)
                    for s, k in incs:
                        assert inst is not None, "inc without instruction"
                        inst.then_inc(semh[s], k)
            getattr(block, eng)(body)


class KB:
    def __init__(self, taps=(), stop_after=None):
        self.taps = list(taps)
        self.stop_after = stop_after
        self.nc = bass.Bass("TRN2", target_bir_lowering=False)
        self.es = ExitStack()
        self.E = Emitter()
        self.M = {}

    def sb(self, name, shape, dt):
        return self.es.enter_context(self.nc.sbuf_tensor(name, shape, dt))

    def psum(self, name, shape, dt=F32):
        return self.es.enter_context(self.nc.psum_tensor(name, shape, dt))

    def build(self):
        nc = self.nc
        D = {}
        D["x"] = nc.dram_tensor("x", [TLOC, C], F32, kind="ExternalInput").ap()
        D["xprev"] = nc.dram_tensor("xprev", [1, C], F32, kind="ExternalInput").ap()
        for n1, shp in [("maa_x", [C]), ("maa_w", [C]), ("maa_k", [C]), ("maa_v", [C]),
                        ("maa_r", [C]), ("maa_g", [C]), ("time_decay", [C]),
                        ("ln_g", [C]), ("ln_b", [C])]:
            D[n1] = nc.dram_tensor(n1, shp, F32, kind="ExternalInput").ap()
        D["maa_w1"] = nc.dram_tensor("maa_w1", [C, 160], F32, kind="ExternalInput").ap()
        D["maa_w2"] = nc.dram_tensor("maa_w2", [5, 32, C], F32, kind="ExternalInput").ap()
        for n1 in ["Wr", "Wk", "Wv", "Wo"]:
            D[n1] = nc.dram_tensor(n1, [C, C], F32, kind="ExternalInput").ap()
        D["gate_w1"] = nc.dram_tensor("gate_w1", [C, 64], F32, kind="ExternalInput").ap()
        D["gate_w2"] = nc.dram_tensor("gate_w2", [64, C], F32, kind="ExternalInput").ap()
        D["decay_w1"] = nc.dram_tensor("decay_w1", [C, 64], F32, kind="ExternalInput").ap()
        D["decay_w2"] = nc.dram_tensor("decay_w2", [64, C], F32, kind="ExternalInput").ap()
        D["u"] = nc.dram_tensor("u", [16, 64], F32, kind="ExternalInput").ap()
        D["wg_sh"] = nc.dram_tensor("wg_sh", [512, C], F32, kind="ExternalInput").ap()
        D["wu_sh"] = nc.dram_tensor("wu_sh", [512, C], F32, kind="ExternalInput").ap()
        D["wd_sh"] = nc.dram_tensor("wd_sh", [128, DIN], F32, kind="ExternalInput").ap()
        D["wsel"] = nc.dram_tensor("wsel", [8, 8], F32, kind="ExternalInput").ap()
        D["wbias"] = nc.dram_tensor("wbias", [1, 8], F32, kind="ExternalInput").ap()
        D["out"] = nc.dram_tensor("out", [TLOC, C], F32, kind="ExternalOutput").ap()

        tset = set(self.taps)

        def dscr(name, shape, dt):
            kind = "ExternalOutput" if name.lstrip("c_") in tset or name in tset \
                else "Internal"
            return nc.dram_tensor(name, shape, dt, kind=kind).ap()
        S = {}
        S["abs_in"] = dscr("c_abs_in", [1, 4], F32)
        S["abs_out"] = dscr("c_abs_out", [1, 4], F32)
        S["wgT_sh"] = dscr("c_wgT_sh", [C, 512], BF16)
        S["wuT_sh"] = dscr("c_wuT_sh", [C, 512], BF16)
        S["wdT_sh"] = dscr("c_wdT_sh", [DIN, 128], BF16)
        S["wgT"] = dscr("c_wgT", [8 * C, 512], BF16)
        S["wuT"] = dscr("c_wuT", [8 * C, 512], BF16)
        S["wdT"] = dscr("c_wdT", [8 * DIN, 128], BF16)
        S["sum_in"] = dscr("c_sum_in", [128, 520], F32)
        S["sum_out"] = dscr("c_sum_out", [1024, 520], F32)
        S["xfm"] = dscr("c_xfm", [8 * 128, TLOC], F32)
        S["A"] = dscr("c_A", [NCH * 64, 1024], BF16)
        S["vtm"] = dscr("c_vtm", [NCH * 64, 1024], BF16)
        S["rt"] = dscr("c_rt", [NCH * 128, 512], BF16)
        S["sl"] = dscr("c_sl", [NCH * 128, 512], BF16)
        S["yd"] = dscr("c_yd", [NCH * 128, 512], BF16)
        S["g"] = dscr("c_g", [NCH * 128, 512], BF16)
        S["x1n"] = dscr("c_x1n", [8 * 128, TLOC], F32)
        S["hq"] = dscr("c_hq", [TLOC, DIN], BF16)
        S["dbg"] = dscr("c_dbg", [128, 40960], F32)
        S["wkb"] = dscr("c_wkb", [C, C], BF16)
        S["wvb"] = dscr("c_wvb", [C, C], BF16)
        S["wrb"] = dscr("c_wrb", [C, C], BF16)
        S["wob"] = dscr("c_wob", [C, C], BF16)
        S["w2b"] = dscr("c_w2b", [5 * 32, C], BF16)
        self.D, self.S = D, S

        E = self.E
        for s in ["sSY", "sSC", "sGP", "sCC", "sPE", "sDV", "sAC",
                  "cA", "cV", "cP", "cS", "gE", "sOUT"]:
            E.newsem(s)

        with self.es:
            self.arF = self.sb("arF", [128, 25600], F32)
            self.arB = self.sb("arB", [128, 35840], BF16)
            self._alloc_small()
            self.pb = [self.psum("pb%d" % i, [128, 512]) for i in range(6)]
            self.pba = self.psum("pba", [128, 512])
            self.pbt = self.psum("pbt", [64, 512], BF16)
            semh = {}
            for s in E.cnt:
                semh[s] = self.es.enter_context(nc.semaphore(s))
            self.block = self.es.enter_context(nc.Block())
            phases = [("ph0", self.ph0), ("ffnw", self.ph0_ffnw),
                      ("loop1", self.loop1), ("exchange", self.exchange),
                      ("loop2", self.loop2), ("ffn", self.ffn)]
            for nm, fn in phases:
                fn()
                sa = self.stop_after
                if sa and (sa == nm or sa.startswith(nm + "-")):
                    break
            self.finish()
            E.flush(self.block, semh)
        return nc

    def cF(self, off, cols):
        assert off + cols <= 25600, (off, cols)
        return self.arF[:, off:off + cols]

    def cB(self, off, cols):
        assert off + cols <= 35840, (off, cols)
        return self.arB[:, off:off + cols]

    def _alloc_small(self):
        sb = self.sb
        for nm in ["maax", "maaw", "maak", "maav", "maar", "maag", "td", "u", "lnb"]:
            setattr(self, "v_" + nm, sb("v_" + nm, [128, 8], F32))
        self.r_lng = sb("r_lng", [1, C], F32)
        self.r_lnb = sb("r_lnb", [1, C], F32)
        self.r_ones = sb("r_ones", [1, 128], F32)
        self.ones16 = sb("ones16", [1, 16], F32)
        self.onesc = sb("onesc", [128, 1], F32)
        self.identf = sb("identf", [128, 128], F32)
        self.identb = sb("identb", [128, 128], BF16)
        self.ones512 = sb("ones512", [128, TB], BF16)
        self.mask4 = sb("mask4", [128, 512], F32)
        self.mask4u = sb("mask4u", [128, 512], mybir.dt.uint8)
        self.zeros_bf = sb("zeros_bf", [128, 512], BF16)
        self.ind_cc = sb("ind_cc", [16, 8 * 128], F32)
        self.gin_cc = self.ind_cc
        self.ione_cc = sb("ione_cc", [128, 8 * 16], F32)
        self.ione_b = sb("ione_b", [128, 8 * 16], BF16)
        self.scols = sb("scols", [128, 8 * NCH], F32)
        self.carryB = sb("carryB", [128, 8], F32)
        self.xcarry = sb("xcarry", [128, 8], F32)
        self.S32 = sb("S32", [128, 512], F32)
        self.Sin = sb("Sin", [128, 512], F32)
        self.dsx = sb("dsx", [128, 16], F32)
        self.wsc = sb("wsc", [1, 4], F32)
        self.wsc_r = sb("wsc_r", [1, 4], F32)
        self.wsel_sb = sb("wsel_sb", [8, 8], F32)
        self.wbias_sb = sb("wbias_sb", [1, 8], F32)
        self.swcol = sb("swcol", [128, 4], F32)
        self.idxI = self.cF(14336, 128)
        self.idxA = self.cF(14464, 64)[0:64, :]
        self.vv16 = self.cF(14528, 128)[0:16, :]
        self.vv128 = self.cF(14656, 16)
        self.logdt = sb("logdt", [128, 64], F32)
        self.sdcol = sb("sdcol", [128, 16], F32)
        self.ddrow = sb("ddrow", [1, 512], F32)
        self.amaxw = sb("amaxw", [128, 16], F32)
        self.swv = sb("swv", [128, 4], F32)
        self.epsg = sb("epsg", [128, 1], F32)
        self.epsl = sb("epsl", [128, 1], F32)
    def ph0(self):
        E, D, M = self.E, self.D, self.M

        for vt, key in [(self.v_maax, "maa_x"), (self.v_maaw, "maa_w"),
                        (self.v_maak, "maa_k"), (self.v_maav, "maa_v"),
                        (self.v_maar, "maa_r"), (self.v_maag, "maa_g"),
                        (self.v_td, "time_decay"), (self.v_lnb, "ln_b")]:
            E.dma("sync", "sSY", vt[:], D[key].rearrange("(c p) -> p c", p=128),
                  allow_slow_non_contiguous=True)
        E.dma("sync", "sSY", self.v_u[:],
              D["u"].rearrange("h n -> (h n)").rearrange("(c p) -> p c", p=128),
              allow_slow_non_contiguous=True)
        E.dma("sync", "sSY", self.r_lng[:], D["ln_g"].rearrange("(o c) -> o c", o=1))
        E.dma("sync", "sSY", self.r_lnb[:], D["ln_b"].rearrange("(o c) -> o c", o=1))
        E.dma("sync", "sSY", self.xcarry[:],
              D["xprev"].rearrange("o (c p) -> p (o c)", p=128),
              allow_slow_non_contiguous=True)
        E.dma("sync", "sSY", self.wsel_sb[:], D["wsel"][:, :])
        E.dma("sync", "sSY", self.wbias_sb[:], D["wbias"][:, :])
        M["vecs"] = E.mark("sSY")

        S = self.S
        self.w_l1 = self.sb("w_l1", [128, 8 * 160], BF16)
        self.w_g1 = self.sb("w_g1", [128, 8 * 64], BF16)
        self.w_g2 = self.sb("w_g2", [64, C], BF16)
        self.w_d1 = self.sb("w_d1", [128, 8 * 64], BF16)
        self.w_d2 = self.sb("w_d2", [64, C], BF16)
        for skey, key in [("wkb", "Wk"), ("wvb", "Wv"), ("wrb", "Wr"),
                          ("wob", "Wo")]:
            E.dma("gpsimd", "sGP", S[skey][:, :], D[key][:, :])
        E.dma("gpsimd", "sGP", S["w2b"].rearrange("(f r) c -> f r c", r=32),
              D["maa_w2"][:, :, :])
        M["wcast"] = E.mark("sGP")
        E.dma("gpsimd", "sGP", self.w_l1[:].rearrange("p (a d) -> p a d", d=160),
              D["maa_w1"].rearrange("(a p) d -> p a d", p=128))
        E.dma("gpsimd", "sGP", self.w_g1[:].rearrange("p (a d) -> p a d", d=64),
              D["gate_w1"].rearrange("(a p) d -> p a d", p=128))
        E.dma("gpsimd", "sGP", self.w_g2[:], D["gate_w2"][:, :])
        E.dma("gpsimd", "sGP", self.w_d1[:].rearrange("p (a d) -> p a d", d=64),
              D["decay_w1"].rearrange("(a p) d -> p a d", p=128))
        E.dma("gpsimd", "sGP", self.w_d2[:], D["decay_w2"][:, :])
        M["wload"] = E.mark("sGP")

        def iotas(e):
            e.iota(self.idxA[:], [[1, 64]], base=0, channel_multiplier=-1,
                   allow_small_or_imprecise_dtypes=True)
            e.iota(self.vv16[:, 0:64], [[0, 64]], base=0, channel_multiplier=1,
                   allow_small_or_imprecise_dtypes=True)
            e.iota(self.vv16[:, 64:128], [[0, 64]], base=-1, channel_multiplier=1,
                   allow_small_or_imprecise_dtypes=True)
            e.iota(self.vv128[0:64, :], [[1, 16]], base=0, channel_multiplier=0,
                   allow_small_or_imprecise_dtypes=True)
            e.iota(self.vv128[64:128, :], [[1, 16]], base=-1, channel_multiplier=0,
                   allow_small_or_imprecise_dtypes=True)
            return e.iota(self.idxI[:], [[1, 128]], base=0, channel_multiplier=-1,
                          allow_small_or_imprecise_dtypes=True)
        E.op("gpsimd", iotas, incs=[("sGP", 1)])
        M["iota"] = E.mark("sGP")

        def consts(e):
            for blk in range(8):
                e.tensor_single_scalar(
                    self.mask4[0:64, blk * 64:(blk + 1) * 64],
                    self.idxA[:], 0.0, ALU.is_gt)
                e.tensor_single_scalar(
                    self.mask4[64:128, blk * 64:(blk + 1) * 64],
                    self.idxA[:], 0.0, ALU.is_gt)
            e.memset(self.zeros_bf[:], 0.0)
            e.memset(self.r_ones[:], 1.0)
            e.memset(self.ones16[:], 1.0)
            e.memset(self.onesc[:], 1.0)
            e.memset(self.ones512[:], 1.0)
            e.memset(self.carryB[:], 0.0)
            e.memset(self.epsg[:], EPS_GN)
            e.memset(self.epsl[:], EPS_LN)
            e.memset(self.S32[:], 0.0)
            for cc in range(8):
                e.tensor_single_scalar(self.ind_cc[:, cc * 128:(cc + 1) * 128],
                                       self.vv16[:], float(2 * cc), ALU.is_equal)
                e.tensor_single_scalar(self.ione_cc[:, cc * 16:(cc + 1) * 16],
                                       self.vv128[:], float(2 * cc), ALU.is_equal)
            e.tensor_single_scalar(self.identf[:], self.idxI[:], 0.0, ALU.is_equal)
            e.tensor_copy(self.ione_b[:], self.ione_cc[:])
            e.tensor_copy(self.mask4u[:], self.mask4[:])
            return e.tensor_copy(self.identb[:], self.identf[:])
        E.op("vector", consts, waits=[("sGP", M["iota"])], incs=[("sDV", 1)])
        M["consts"] = E.mark("sDV")

        M["gin"] = 0

    def ph0_ffnw(self):
        E, D, S, M = self.E, self.D, self.S, self.M
        sh_g = self.cF(0, 4096)
        sh_u = self.cF(4096, 4096)
        sh_d = self.cF(8192, 4096)
        dmp = self.cF(12288, 1024)
        shq = self.cB(24576, 4096)
        trT = self.cB(28672, 512)
        E.dma("sync", "sSY", sh_g.rearrange("p (a c) -> p a c", c=C),
              D["wg_sh"].rearrange("(a p) c -> p a c", p=128))
        E.dma("sync", "sSY", sh_u.rearrange("p (a c) -> p a c", c=C),
              D["wu_sh"].rearrange("(a p) c -> p a c", p=128))
        E.dma("sync", "sSY", sh_d, D["wd_sh"][:, :])
        M["shards"] = E.mark("sSY")

        def absf(e):
            acc = self.sdcol
            for i in range(4):
                e.activation(dmp, sh_g[:, i * C:(i + 1) * C], AF.Abs,
                             accum_out=acc[:, i:i + 1])
            for i in range(4):
                e.activation(dmp, sh_u[:, i * C:(i + 1) * C], AF.Abs,
                             accum_out=acc[:, 4 + i:5 + i])
            for i in range(4):
                inst = e.activation(dmp, sh_d[:, i * C:(i + 1) * C], AF.Abs,
                                    accum_out=acc[:, 8 + i:9 + i])
            return inst
        E.op("scalar", absf, waits=[("sSY", M["shards"])], incs=[("sAC", 1)])
        M["abs"] = E.mark("sAC")

        E.op("tensor",
             lambda e: e.matmul(self.pb[4][0:1, 0:12], self.onesc[:],
                                self.sdcol[:, 0:12], start=True, stop=True),
             waits=[("sAC", M["abs"]), ("sDV", M["consts"])], incs=[("sPE", 1)])
        M["fold"] = E.mark("sPE")

        absum = self.sb("absum", [1, 4], F32)

        def sumf(e):
            e.tensor_reduce(absum[:, 0:3].rearrange("o (n u) -> o n u", u=1),
                            self.pb[4][0:1, 0:12].rearrange("o (g i) -> o g i", g=3),
                            AX.X, ALU.add)
            return e.memset(absum[:, 3:4], 0.0)
        E.op("vector", sumf, waits=[("sPE", M["fold"])], incs=[("sDV", 1)])
        M["absum"] = E.mark("sDV")

        E.dma("scalar", "sSC", S["abs_in"][:, :], absum[:],
              waits=[("sDV", M["absum"])])
        M["absst"] = E.mark("sSC")

        E.op("gpsimd",
             lambda e: e.collective_compute(
                 "AllReduce", ALU.add, replica_groups=[list(range(8))],
                 ins=[S["abs_in"].opt()], outs=[S["abs_out"].opt()]),
             waits=[("sSC", M["absst"])], incs=[("sCC", 1)])
        M["ccabs"] = E.mark("sCC")

        E.dma("sync", "sSY", self.wsc[:], S["abs_out"][:, :],
              waits=[("sCC", M["ccabs"])])
        M["wscl"] = E.mark("sSY")

        E.op("gpsimd",
             lambda e: e.tensor_scalar(self.wsc[:], self.wsc[:],
                                       1.0 / (4096.0 * 1024.0), 1e-5,
                                       ALU.mult, ALU.max),
             waits=[("sSY", M["wscl"])], incs=[("sGP", 1)])
        E.op("vector",
             lambda e: e.reciprocal(self.wsc_r[:], self.wsc[:]),
             waits=[("sGP", E.mark("sGP"))], incs=[("sDV", 1)])
        M["wscr"] = E.mark("sDV")

        E.op("tensor",
             lambda e: e.matmul(self.pb[4][:, 16:20], self.r_ones[0:1, 0:128],
                                self.wsc_r[:], start=True, stop=True),
             waits=[("sDV", M["wscr"])], incs=[("sPE", 1)])
        M["swbc"] = E.mark("sPE")

        E.op("vector", lambda e: e.tensor_copy(self.swcol[:], self.pb[4][:, 16:20]),
             waits=[("sPE", M["swbc"])], incs=[("sDV", 1)])
        M["swcol"] = E.mark("sDV")

        E.op("tensor",
             lambda e: e.matmul(self.pb[4][:, 20:24], self.r_ones[0:1, 0:128],
                                self.wsc[:], start=True, stop=True),
             waits=[("sDV", M["swcol"])], incs=[("sPE", 1)])
        M["swbc2"] = E.mark("sPE")
        E.op("vector", lambda e: e.tensor_copy(self.swv[:], self.pb[4][:, 20:24]),
             waits=[("sPE", M["swbc2"])], incs=[("sDV", 1)])
        M["swv"] = E.mark("sDV")

        def quant(src, dstq, col, ntiles):
            def f(e):
                for i in range(ntiles):
                    e.tensor_scalar(dmp, src[:, i * C:(i + 1) * C],
                                    self.swcol[:, col:col + 1], CRND, ALU.mult, ALU.add)
                    e.tensor_scalar(dmp, dmp, CRND, 1.0, ALU.subtract, ALU.min)
                    inst = e.tensor_scalar(dstq[:, i * C:(i + 1) * C], dmp, -1.0,
                                           None, ALU.max)
                return inst
            E.op("vector", f, waits=[("sDV", M["swv"]), ("sPE", E.mark("sPE"))],
                 incs=[("sDV", 1)])

        def transpose_store(srcq, tiles, dst):
            for (t, dr, dc) in tiles:
                dep_dv = E.mark("sDV")
                dep_sc = E.mark("sSC")
                E.op("tensor",
                     (lambda e, t=t: [
                         e.transpose(self.pbt[:, 0:128],
                                     srcq[:, t * 128:t * 128 + 64], self.identb[:]),
                         e.transpose(self.pbt[:, 128:256],
                                     srcq[:, t * 128 + 64:t * 128 + 128], self.identb[:])][-1]),
                     waits=[("sDV", dep_dv), ("sSC", dep_sc)], incs=[("sPE", 1)])
                E.op("vector",
                     (lambda e: [
                         e.tensor_copy(trT[0:64, 0:128], self.pbt[:, 0:128]),
                         e.tensor_copy(trT[64:128, 0:128], self.pbt[:, 128:256])][-1]),
                     waits=[("sPE", E.mark("sPE")), ("sSC", E.mark("sSC"))],
                     incs=[("sDV", 1)])
                E.dma("scalar", "sSC", dst[dr[0]:dr[1], dc[0]:dc[1]], trT[:, 0:128],
                      waits=[("sDV", E.mark("sDV"))])

        quant(sh_g, shq, 0, 4)
        transpose_store(shq, [(a * 8 + ci, (ci * 128, ci * 128 + 128),
                               (a * 128, a * 128 + 128))
                              for a in range(4) for ci in range(8)], S["wgT_sh"])
        quant(sh_u, shq, 1, 4)
        transpose_store(shq, [(a * 8 + ci, (ci * 128, ci * 128 + 128),
                               (a * 128, a * 128 + 128))
                              for a in range(4) for ci in range(8)], S["wuT_sh"])
        quant(sh_d, shq, 2, 4)
        transpose_store(shq, [(t, (t * 128, t * 128 + 128), (0, 128))
                              for t in range(32)], S["wdT_sh"])
        M["wqst"] = E.mark("sSC")

        for pair in [("wgT_sh", "wgT"), ("wuT_sh", "wuT"), ("wdT_sh", "wdT")]:
            E.op("gpsimd",
                 (lambda e, pair=pair: e.collective_compute(
                     "AllGather", ALU.bypass, replica_groups=[list(range(8))],
                     ins=[S[pair[0]].opt()], outs=[S[pair[1]].opt()])),
                 waits=[("sSC", M["wqst"])], incs=[("sCC", 1)])
        M["ccw"] = E.mark("sCC")
    def _l1stop(self, part, tb):
        sa = self.stop_after or ""
        if not sa.startswith("loop1-"):
            return False
        _, p, n = sa.split("-")
        return part == p and tb == int(n) - 1

    def loop1(self):
        E, D, S, M = self.E, self.D, self.S, self.M
        l1stopped = False
        cF, cB = self.cF, self.cB
        pb = self.pb

        xtm = cF(0, 4096)
        xfm = cF(4096, 4104)
        xv = xfm.rearrange("p (c n) -> p c n", n=513)
        xx = cF(8200, 4096)
        xxv = xx.rearrange("p (c n) -> p c n", n=TB)
        self.ek = [cF(8200, 512), cF(8712, 512)]
        self.er = [cF(9224, 512), cF(9736, 512)]
        B_ = cF(12304, 4104)
        Bv = B_.rearrange("p (c n) -> p c n", n=513)
        self.Bv = Bv
        self.ktb = cF(16408, 4096)
        self.rtb = cF(20504, 4096)
        tmp = cF(24600, 512)
        ew = tmp
        self.dBt = cF(25112, 64)
        qct = cF(25176, 64)
        self.bc2t = cF(25240, 64)
        self.bc1nt = cF(25304, 64)
        self.ehmt = cF(25368, 64)
        self.Dtt = cF(25432, 64)

        self.wstr8 = [cB(0, 8192), cB(8192, 8192)]
        mixbuf = cB(16384, 4096)
        self.vtb = cB(20480, 4096)
        lt = {f: cB(24576 + i * 512, 512) for i, f in enumerate("wkvrg")}
        self.dg1t = cB(27136, 512)
        gcc = cB(27648, 512)
        ydcc = cB(28160, 512)
        p3c = cB(28672, 512)
        self.ktil = [cB(29184, 512), cB(29696, 512)]
        self.rtil = [cB(30208, 512), cB(30720, 512)]
        self.khat = cB(31232, 512)
        self.slbf = cB(31744, 512)
        self.asel = cB(32256, 512)
        self.vtms = cB(32768, 1024)
        self.ktms = cB(33792, 1024)
        self.mixbuf = mixbuf
        self._gcc = gcc
        self.tmp = tmp
        self.ew = ew
        self.lt = lt

        fprev = {"w": None, "k": "w", "v": "k", "r": "v", "g": "r"}
        fnum = {"w": 0, "k": 1, "v": 2, "r": 3, "g": 4}
        w_l2f = self.sb("w_l2f", [32, C], BF16)

        E.op("vector", lambda e: e.memset(Bv[:, :, 0:1], 0.0),
             waits=[], incs=[("sDV", 1)])

        for tb in range(NTB):
            t0 = tb * TB
            chg0 = tb * NCHB

            xw = [("sPE", M.get("xtr_%d" % (tb - 1), 0))]
            if tb == 0:
                xw = [("sDV", E.mark("sDV")), ("sPE", E.mark("sPE"))]
            E.dma("sync", "sSY", xtm.rearrange("p (s c) -> p s c", c=C),
                  D["x"][t0:t0 + TB, :].rearrange("(s p) c -> p s c", p=128),
                  waits=xw)
            M["xload_%d" % tb] = E.mark("sSY")

            wkw = [("sPE", M.get("proj_r_%d" % (tb - 1), 0))]
            if tb == 0:
                wkw.append(("sGP", M["wcast"]))
            E.dma("sync", "sSY", self.wstr8[0].rearrange("p (a co) -> p a co", co=C),
                  S["wkb"].rearrange("(a p) co -> p a co", p=128),
                  waits=wkw)
            M["wld_k_%d" % tb] = E.mark("sSY")

            pems = []
            for cc in range(8):
                bank = pb[0] if cc % 2 == 0 else pb[1]
                w = [("sSY", M["xload_%d" % tb]),
                     ("sAC", E.mark("sAC"))]
                if tb == 0 and cc == 0:
                    w.append(("sDV", M["consts"]))

                def trf(e, cc=cc, bank=bank):
                    for s1 in range(4):
                        inst = e.transpose(bank[:, s1 * 128:(s1 + 1) * 128],
                                           xtm[:, s1 * C + cc * 128: s1 * C + cc * 128 + 128],
                                           self.identf[:])
                    return inst
                E.op("tensor", trf, waits=w, incs=[("sPE", 1)])
                pems.append(E.mark("sPE"))
                E.op("scalar",
                     (lambda e, cc=cc, bank=bank: e.copy(xv[:, cc, 1:513], bank[:, 0:512])),
                     waits=[("sPE", pems[cc]),
                            ("sSC", M.get("xspill_%d" % (tb - 1), 0)),
                            ("sDV", M.get("mix_g_%d" % (tb - 1), 0))],
                     incs=[("sAC", 1)])
                M["xev_%d_%d" % (tb, cc)] = E.mark("sAC")
            M["xtr_%d" % tb] = E.mark("sPE")

            def xxf(e, tb=tb):
                e.tensor_copy(xv[:, :, 0:1], self.xcarry[:].rearrange("p (c u) -> p c u", u=1))
                e.tensor_tensor(xxv[:, :, :], xv[:, :, 0:512], xv[:, :, 1:513], ALU.subtract)
                e.tensor_copy(self.xcarry[:].rearrange("p (c u) -> p c u", u=1), xv[:, :, 512:513])
                for cc in range(8):
                    inst = e.scalar_tensor_tensor(
                        mixbuf[:, cc * TB:(cc + 1) * TB],
                        xxv[:, cc, :], self.v_maax[:, cc:cc + 1],
                        xv[:, cc, 1:513], ALU.mult, ALU.add)
                return inst
            w = [("sAC", M["xev_%d_%d" % (tb, 7)]),
                 ("sPE", M.get("proj_g_%d" % (tb - 1), 0)),
                 ("cA", chg0)]
            if tb == 0:
                w += [("sSY", M["vecs"])]
            E.op("vector", xxf, waits=w, incs=[("sDV", 1)])
            M["xx_%d" % tb] = E.mark("sDV")

            for cc in range(8):
                E.dma("scalar", "sSC",
                      S["xfm"][cc * 128:(cc + 1) * 128, t0:t0 + TB], xv[:, cc, 1:513],
                      waits=[("sDV", M["xx_%d" % tb])] if cc == 0 else [])
            M["xspill_%d" % tb] = E.mark("sSC")
            if self._l1stop("xx", tb):
                l1stopped = True
                break

            def loraA(e):
                for ci in range(8):
                    inst = e.matmul(pb[4][0:96, :], self.w_l1[:, ci * 160:ci * 160 + 96],
                                    mixbuf[:, ci * TB:(ci + 1) * TB],
                                    start=(ci == 0), stop=(ci == 7))
                return inst
            w = [("sDV", M["xx_%d" % tb]), ("sAC", E.mark("sAC")),
                 ("cS", tb * NCHB)]
            if tb == 0:
                w += [("sGP", M["wload"])]
            E.op("tensor", loraA, waits=w, incs=[("sPE", 1)])
            mk_la = E.mark("sPE")

            def loraB(e):
                for ci in range(8):
                    inst = e.matmul(pb[5][0:64, :], self.w_l1[:, ci * 160 + 96:ci * 160 + 160],
                                    mixbuf[:, ci * TB:(ci + 1) * TB],
                                    start=(ci == 0), stop=(ci == 7))
                return inst
            E.op("tensor", loraB, waits=[("sDV", E.mark("sDV"))], incs=[("sPE", 1)])
            mk_lb = E.mark("sPE")

            def ltanh(e):
                e.activation(lt["w"][0:32, :], pb[4][0:32, :], AF.Tanh)
                e.activation(lt["k"][0:32, :], pb[4][32:64, :], AF.Tanh)
                e.activation(lt["v"][0:32, :], pb[4][64:96, :], AF.Tanh)
                e.activation(lt["r"][0:32, :], pb[5][0:32, :], AF.Tanh)
                return e.activation(lt["g"][0:32, :], pb[5][32:64, :], AF.Tanh)
            E.op("scalar", ltanh, waits=[("sPE", mk_lb), ("sPE", mk_la)],
                 incs=[("sAC", 1)])
            M["lt_%d" % tb] = E.mark("sAC")

            for f in "wkvrg":
                wl2w = [("sPE", M.get("mf_%s" % fprev[f], 0) if fprev[f] else 0)]
                if tb == 0 and f == "w":
                    wl2w.append(("sGP", M["wcast"]))
                E.dma("sync", "sSY", w_l2f[:],
                      S["w2b"][fnum[f] * 32:(fnum[f] + 1) * 32, :], waits=wl2w)
                mk_wl2 = E.mark("sSY")

                dvs = []
                for cc in range(8):
                    bank = pb[4] if cc % 2 == 0 else pb[5]
                    w = [("sAC", M["lt_%d" % tb]), ("sSY", mk_wl2)]
                    if cc >= 2:
                        w.append(("sDV", dvs[cc - 2]))
                    elif fprev[f]:
                        w.append(("sDV", M["mix_%s_%d" % (fprev[f], tb)]))
                    E.op("tensor",
                         (lambda e, cc=cc, bank=bank, f=f: e.matmul(
                             bank[:, :], w_l2f[:, cc * 128:(cc + 1) * 128],
                             lt[f][0:32, :], start=True, stop=True)),
                         waits=w, incs=[("sPE", 1)])
                    mkp = E.mark("sPE")

                    wmix = [("sPE", mkp)]
                    if cc == 0:
                        if f == "w":
                            wmix.append(("sPE", mk_lb))
                        else:
                            wmix.append(("sPE", M["proj_%s_%d" % (fprev[f], tb)]))

                    def mixf(e, cc=cc, bank=bank, f=f):
                        e.scalar_tensor_tensor(
                            tmp, bank[:, :], getattr(self, "v_maa" + f)[:, cc:cc + 1],
                            xxv[:, cc, :], ALU.add, ALU.mult)
                        return e.tensor_tensor(mixbuf[:, cc * TB:(cc + 1) * TB],
                                               tmp, xv[:, cc, 1:513], ALU.add)
                    E.op("vector", mixf, waits=wmix, incs=[("sDV", 1)])
                    dvs.append(E.mark("sDV"))
                M["mf_%s" % f] = E.mark("sPE")
                M["mix_%s_%d" % (f, tb)] = E.mark("sDV")

                self._proj(f, tb, mixbuf, lt)
                if self._l1stop("proj" + f, tb):
                    l1stopped = True
                    break
            if l1stopped:
                break

            def colf(e, tb=tb):
                for cc in range(8):
                    e.tensor_tensor(self.dBt[:, cc * 8:(cc + 1) * 8],
                                    Bv[:, cc, 64::64], Bv[:, cc, 0::64][:, 0:8],
                                    ALU.subtract)
                e.tensor_scalar(qct, self.dBt, 0.25, None, ALU.mult)
                for cc in range(8):
                    e.scalar_tensor_tensor(self.bc2t[:, cc * 8:(cc + 1) * 8],
                                           Bv[:, cc, 0::64][:, 0:8], 0.5,
                                           qct[:, cc * 8:(cc + 1) * 8],
                                           ALU.mult, ALU.add)
                e.tensor_scalar(self.bc1nt, self.bc2t, -1.0, None, ALU.mult)
                for cc in range(8):
                    e.tensor_scalar(
                        self.scols[:, cc * NCH + tb * 8: cc * NCH + tb * 8 + 8],
                        self.bc2t[:, cc * 8:(cc + 1) * 8], 2.0,
                        self.carryB[:, cc:cc + 1], ALU.mult, ALU.add)
                return e.tensor_tensor(self.carryB[:], self.carryB[:],
                                       Bv[:, :, 512], ALU.add)
            E.op("vector", colf,
                 waits=[("sDV", M["scan_%d" % tb]), ("cA", chg0), ("cV", chg0)],
                 incs=[("sDV", 1)])
            mk_cols = E.mark("sDV")

            E.op("scalar", lambda e: e.activation(self.ehmt, self.dBt, AF.Exp, scale=-0.5),
                 waits=[("sDV", mk_cols), ("cV", chg0)], incs=[("sAC", 1)])
            mk_ehm = E.mark("sAC")

            E.op("vector", lambda e: e.tensor_tensor(self.Dtt, self.ehmt, self.ehmt, ALU.mult),
                 waits=[("sAC", mk_ehm)], incs=[("sDV", 1)])
            M["cols_%d" % tb] = E.mark("sDV")
            if self._l1stop("cols", tb):
                l1stopped = True
                break

            for loc in range(NCHB):
                self._chunk(tb, loc, chg0 + loc)
                if getattr(self, "chunk_stop", None):
                    l1stopped = True
                    break
            if l1stopped:
                break

            if self._l1stop("chunks", tb):
                l1stopped = True
                break

            self._sdiag(tb, p3c, gcc, ydcc)
            if self._l1stop("sdiag", tb):
                l1stopped = True
                break

        M["l1pe_done"] = E.mark("sPE")
        if l1stopped:
            return
        def sumf(e):
            return e.tensor_scalar(tmp[:, 0:8], self.carryB[:], -1.0, None, ALU.mult)
        E.op("vector", sumf, waits=[("cS", NCH)], incs=[("sDV", 1)])
        mk_logd = E.mark("sDV")
        E.dma("scalar", "sSC", S["sum_in"][:, 0:512], self.S32[:],
              waits=[("sDV", mk_logd)])
        E.dma("scalar", "sSC", S["sum_in"][:, 512:520], tmp[:, 0:8])
        M["sumspill"] = E.mark("sSC")

    def _proj(self, f, tb, mixbuf, lt):
        E, D, S, M = self.E, self.D, self.S, self.M
        pb = self.pb
        mixmark = M["mix_%s_%d" % (f, tb)]
        Bv, ew = self.Bv, self.ew

        if f == "w":
            def d1f(e):
                for ci in range(8):
                    inst = e.matmul(pb[4][0:64, :], self.w_d1[:, ci * 64:(ci + 1) * 64],
                                    mixbuf[:, ci * TB:(ci + 1) * TB],
                                    start=(ci == 0), stop=(ci == 7))
                return inst
            E.op("tensor", d1f, waits=[("sDV", mixmark), ("sAC", E.mark("sAC"))],
                 incs=[("sPE", 1)])
            mk = E.mark("sPE")
            E.op("scalar", (lambda e: e.activation(self.dg1t[0:64, :], pb[4][0:64, :],
                                                   AF.Tanh)),
                 waits=[("sPE", mk)], incs=[("sAC", 1)])
            mk_t = E.mark("sAC")
            for cc in range(8):
                bank = pb[0] if cc % 2 == 0 else pb[1]
                E.op("tensor",
                     (lambda e, cc=cc, bank=bank: e.matmul(
                         bank[:, :], self.w_d2[:, cc * 128:(cc + 1) * 128],
                         self.dg1t[0:64, :], start=True, stop=True)),
                     waits=[("sAC", mk_t if cc < 2 else M["ewev_%d_%d" % (tb, cc - 2)])],
                     incs=[("sPE", 1)])
                mkp = E.mark("sPE")
                E.op("scalar",
                     (lambda e, cc=cc, bank=bank: e.activation(
                         ew, bank[:, :], AF.Exp, bias=self.v_td[:, cc:cc + 1])),
                     waits=[("sPE", mkp),
                            ("sDV", M.get("scan_c_%d_%d" % (tb, cc - 1), 0))],
                     incs=[("sAC", 1)])
                M["ewev_%d_%d" % (tb, cc)] = E.mark("sAC")
                E.op("vector",
                     (lambda e, cc=cc: e.tensor_tensor_scan(
                         Bv[:, cc, 1:513], ew, self.ones512[:], 0.0,
                         ALU.add, ALU.mult)),
                     waits=[("sAC", M["ewev_%d_%d" % (tb, cc)])], incs=[("sDV", 1)])
                M["scan_c_%d_%d" % (tb, cc)] = E.mark("sDV")
            M["scan_%d" % tb] = E.mark("sDV")
            M["proj_w_%d" % tb] = E.mark("sPE")
            return

        if f == "g":
            def g1f(e):
                for ci in range(8):
                    inst = e.matmul(pb[4][0:64, :], self.w_g1[:, ci * 64:(ci + 1) * 64],
                                    mixbuf[:, ci * TB:(ci + 1) * TB],
                                    start=(ci == 0), stop=(ci == 7))
                return inst
            E.op("tensor", g1f, waits=[("sDV", mixmark), ("sAC", E.mark("sAC"))],
                 incs=[("sPE", 1)])
            mk = E.mark("sPE")
            E.op("scalar", (lambda e: e.activation(self.dg1t[0:64, :], pb[4][0:64, :],
                                                   AF.Tanh)),
                 waits=[("sPE", mk)], incs=[("sAC", 1)])
            mk_t = E.mark("sAC")
            for cc in range(8):
                bank = pb[0] if cc % 2 == 0 else pb[1]
                E.op("tensor",
                     (lambda e, cc=cc, bank=bank: e.matmul(
                         bank[:, :], self.w_g2[:, cc * 128:(cc + 1) * 128],
                         self.dg1t[0:64, :], start=True, stop=True)),
                     waits=[("sAC", mk_t if cc < 2 else M["gev_%d_%d" % (tb, cc - 2)])],
                     incs=[("sPE", 1)])
                mkp = E.mark("sPE")
                E.op("scalar",
                     (lambda e, cc=cc, bank=bank: e.copy(self._gcc[:], bank[:, :])),
                     waits=[("sPE", mkp),
                            ("sSC", M.get("gsp_%d_%d" % (tb, cc - 1), 0))],
                     incs=[("sAC", 1)])
                M["gev_%d_%d" % (tb, cc)] = E.mark("sAC")
                E.dma("scalar", "sSC",
                      S["g"].rearrange("(ch p) n -> ch p n", p=128)[
                          tb * 8:tb * 8 + 8, :, cc * 64:(cc + 1) * 64].transpose([1, 0, 2]),
                      self._gcc[:].rearrange("p (ch n) -> p ch n", n=64),
                      waits=[("sAC", M["gev_%d_%d" % (tb, cc)])])
                M["gsp_%d_%d" % (tb, cc)] = E.mark("sSC")
            M["proj_g_%d" % tb] = E.mark("sPE")
            return

        wkey = {"k": "Wk", "v": "Wv", "r": "Wr"}[f]
        dst = {"k": self.ktb, "v": self.vtb, "r": self.rtb}[f]
        bufi = {"k": 0, "v": 1, "r": 0}[f]
        buf = self.wstr8[bufi]
        nxt = {"k": ("Wv", 1, "v"), "v": ("Wr", 0, "r"), "r": (None, None, None)}[f]
        if f == "k":
            pass
        if nxt[0] is not None:
            if nxt[1] == 0:
                wldw = [("sPE", M["proj_k_%d" % tb])]
            else:
                wldw = [("sPE", M.get("proj_v_%d" % (tb - 1), 0))]
            skey = {"Wv": "wvb", "Wr": "wrb"}[nxt[0]]
            E.dma("sync", "sSY",
                  self.wstr8[nxt[1]].rearrange("p (a co) -> p a co", co=C),
                  S[skey].rearrange("(a p) co -> p a co", p=128),
                  waits=wldw)
            M["wld_%s_%d" % (nxt[2], tb)] = E.mark("sSY")
        mk_w = M["wld_%s_%d" % (f, tb)]

        for half in range(2):
            hs = half * 256

            def pf(e, hs=hs, buf=buf):
                for co in range(8):
                    bank = pb[co // 2]
                    for ci in range(8):
                        inst = e.matmul(bank[:, (co % 2) * 256:(co % 2) * 256 + 256],
                                        buf[:, ci * C + co * 128: ci * C + co * 128 + 128],
                                        mixbuf[:, ci * TB + hs: ci * TB + hs + 256],
                                        start=(ci == 0), stop=(ci == 7))
                return inst
            E.op("tensor", pf,
                 waits=[("sSY", mk_w), ("sDV", mixmark), ("sAC", E.mark("sAC"))],
                 incs=[("sPE", 1)])
            mk_proj = E.mark("sPE")

            def evf(e, hs=hs, dst=dst):
                for co in range(8):
                    bank = pb[co // 2]
                    inst = e.copy(dst[:, co * TB + hs: co * TB + hs + 256],
                                  bank[:, (co % 2) * 256:(co % 2) * 256 + 256])
                return inst
            E.op("scalar", evf,
                 waits=[("sPE", mk_proj), ("cV", tb * NCHB),
                        ("sDV", M.get("sdiag_%d" % (tb - 1), 0))],
                 incs=[("sAC", 1)])
        M["proj_%s_%d" % (f, tb)] = E.mark("sPE")
        M["pvev_%s_%d" % (f, tb)] = E.mark("sAC")
    def _chunk(self, tb, loc, chg):
        E, S, M = self.E, self.S, self.M
        par = chg % 2
        pb4 = self.pb[4]
        Bv = self.Bv
        ek, er = self.ek[par], self.er[par]
        ktil, rtil = self.ktil[par], self.rtil[par]
        kv3 = self.ktb.rearrange("p (c n) -> p c n", n=TB)
        rv3 = self.rtb.rearrange("p (c n) -> p c n", n=TB)
        vv3 = self.vtb.rearrange("p (c n) -> p c n", n=TB)
        col = lambda t, cc: t[:, cc * 8 + loc: cc * 8 + loc + 1]

        def act_f(e):
            for cc in range(8):
                e.activation(ek[:, cc * 64:(cc + 1) * 64],
                             Bv[:, cc, loc * 64 + 1: loc * 64 + 65],
                             AF.Exp, scale=0.5, bias=col(self.bc1nt, cc))
                inst = e.activation(er[:, cc * 64:(cc + 1) * 64],
                                    Bv[:, cc, loc * 64: loc * 64 + 64],
                                    AF.Exp, scale=-0.5, bias=col(self.bc2t, cc))
            return inst
        E.op("scalar", act_f,
             waits=[("sDV", M["cols_%d" % tb]), ("cV", max(0, chg - 1))],
             incs=[("cA", 1)])
        if getattr(self, "chunk_stop", None) == "act":
            return

        def dve_f(e):
            e.tensor_tensor(ek, ek, ek, ALU.mult)
            e.tensor_tensor(er, er, er, ALU.mult)
            e.tensor_tensor(ktil.rearrange("p (c n) -> p c n", n=64),
                            kv3[:, :, loc * 64:(loc + 1) * 64],
                            ek.rearrange("p (c n) -> p c n", n=64), ALU.mult)
            e.tensor_tensor(rtil.rearrange("p (c n) -> p c n", n=64),
                            rv3[:, :, loc * 64:(loc + 1) * 64],
                            er.rearrange("p (c n) -> p c n", n=64), ALU.mult)
            for cc in range(8):
                e.tensor_scalar(self.khat[:, cc * 64:(cc + 1) * 64],
                                ktil[:, cc * 64:(cc + 1) * 64],
                                col(self.ehmt, cc), None, ALU.mult)
                inst = e.tensor_scalar(self.slbf[:, cc * 64:(cc + 1) * 64],
                                       self.S32[:, cc * 64:(cc + 1) * 64],
                                       col(self.ehmt, cc), None, ALU.mult)
            return inst
        E.op("vector", dve_f,
             waits=[("cA", chg + 1), ("cP", chg), ("cS", chg),
                    ("sSC", M.get("slsp_%d" % (chg - 1), 0))],
             incs=[("cV", 1)])
        mk_cv = E.mark("cV")
        if getattr(self, "chunk_stop", None) == "dve":
            return

        E.dma("scalar", "sSC",
              S["rt"].rearrange("(ch p) n -> ch p n", p=128)[chg],
              rtil[:], waits=[("cV", chg + 1)])
        M["rsp_%d" % chg] = E.mark("sSC")
        E.dma("scalar", "sSC",
              S["sl"].rearrange("(ch p) n -> ch p n", p=128)[chg],
              self.slbf[:])
        M["slsp_%d" % chg] = E.mark("sSC")
        if getattr(self, "chunk_stop", None) == "spill":
            return

        def a_mm(e):
            for cc in range(8):
                e.matmul(self.pba[0:64, cc * 64:(cc + 1) * 64],
                         ktil[0:64, cc * 64:(cc + 1) * 64],
                         rtil[0:64, cc * 64:(cc + 1) * 64],
                         start=True, stop=True, tile_position=(0, 0))
                inst = e.matmul(self.pba[64:128, cc * 64:(cc + 1) * 64],
                                ktil[64:128, cc * 64:(cc + 1) * 64],
                                rtil[64:128, cc * 64:(cc + 1) * 64],
                                start=True, stop=True, tile_position=(64, 64))
            return inst
        E.op("tensor", a_mm,
             waits=[("cV", chg + 1), ("gE", chg)],
             incs=[("sPE", 1)])
        mk_a = E.mark("sPE")

        E.op("vector",
             (lambda e: e.select(self.asel[:], self.mask4u[:],
                                 self.pba[:, :], self.zeros_bf[:])),
             waits=[("sPE", mk_a),
                    ("sSC", M.get("asp_%d" % (chg - 1), 0))],
             incs=[("gE", 1)])
        mk_sel = E.mark("gE")
        av = S["A"].rearrange("(ch s) n -> ch s n", s=64)
        E.dma("scalar", "sSC", av[chg, :, 0:512], self.asel[0:64, :],
              waits=[("gE", mk_sel)])
        E.dma("scalar", "sSC", av[chg, :, 512:1024], self.asel[64:128, :])
        M["asp_%d" % chg] = E.mark("sSC")

        for (srcname, dstt, mkey) in [("v", self.vtms, "vtm"), ("kh", self.ktms, "ktm")]:
            for half in range(2):
                def tr_f(e, srcname=srcname, half=half):
                    for i in range(4):
                        cc = half * 4 + i
                        if srcname == "v":
                            src = vv3[:, cc, loc * 64:(loc + 1) * 64]
                        else:
                            src = self.khat[:, cc * 64:(cc + 1) * 64]
                        inst = e.transpose(self.pbt[:, i * 128:(i + 1) * 128],
                                           src, self.identb[:])
                    return inst
                E.op("tensor", tr_f,
                     waits=[("cV", chg + 1),
                            ("sAC", M.get("trev_prev", 0)),
                            ("sSC", M.get("vtmsp_%d" % (chg - 1), 0)
                             if srcname == "v" and half == 0 else 0)],
                     incs=[("sPE", 1)])
                mk_tr = E.mark("sPE")
                E.op("scalar",
                     (lambda e, dstt=dstt, half=half: e.copy(
                         dstt[0:64, half * 512:(half + 1) * 512], self.pbt[:, :])),
                     waits=[("sPE", mk_tr)], incs=[("sAC", 1)])
                M["trev_prev"] = E.mark("sAC")

        E.dma("scalar", "sSC",
              S["vtm"].rearrange("(ch s) n -> ch s n", s=64)[chg],
              self.vtms[0:64, :],
              waits=[("sAC", M["trev_prev"])])
        M["vtmsp_%d" % chg] = E.mark("sSC")

        if getattr(self, "chunk_stop", None) == "tr":
            return

        def st_mm(e):
            for cc in range(8):
                e.matmul(pb4[0:64, cc * 64:(cc + 1) * 64],
                         self.ktms[0:64, cc * 128: cc * 128 + 64],
                         self.vtms[0:64, cc * 128: cc * 128 + 64],
                         start=True, stop=True, tile_position=(0, 0))
                inst = e.matmul(pb4[64:128, cc * 64:(cc + 1) * 64],
                                self.ktms[0:64, cc * 128 + 64: cc * 128 + 128],
                                self.vtms[0:64, cc * 128 + 64: cc * 128 + 128],
                                start=True, stop=True, tile_position=(0, 64))
            return inst
        E.op("tensor", st_mm,
             waits=[("sAC", M["trev_prev"]), ("cS", chg)],
             incs=[("cP", 1)])

        def st_up(e):
            for cc in range(8):
                inst = e.scalar_tensor_tensor(
                    self.S32[:, cc * 64:(cc + 1) * 64],
                    self.S32[:, cc * 64:(cc + 1) * 64], col(self.Dtt, cc),
                    pb4[:, cc * 64:(cc + 1) * 64], ALU.mult, ALU.add)
            return inst
        E.op("vector", st_up, waits=[("cP", chg + 1)], incs=[("cS", 1)])

    def _sdiag(self, tb, p3c, gcc, ydcc):
        E, S, M = self.E, self.S, self.M
        kv3 = self.ktb.rearrange("p (c n) -> p c n", n=TB)
        rv3 = self.rtb.rearrange("p (c n) -> p c n", n=TB)
        vv3 = self.vtb.rearrange("p (c n) -> p c n", n=TB)
        ssl = self.tmp

        mks = []
        for cc in range(8):
            wp3 = [("cS", (tb + 1) * NCHB)] if cc == 0 else \
                [("sPE", mks[cc - 1])]
            E.op("vector",
                 (lambda e, cc=cc: e.scalar_tensor_tensor(
                     p3c[:], rv3[:, cc, :], self.v_u[:, cc:cc + 1],
                     kv3[:, cc, :], ALU.mult, ALU.mult)),
                 waits=wp3,
                 incs=[("sDV", 1)])
            mkd = E.mark("sDV")
            E.op("tensor",
                 (lambda e, cc=cc: e.matmul(
                     self.pb[5][0:16, :], self.ione_b[:, cc * 16:(cc + 1) * 16],
                     p3c[:], start=(cc == 0), stop=(cc == 7))),
                 waits=[("sDV", mkd)] + ([("sAC", E.mark("sAC"))] if cc == 0 else []),
                 incs=[("sPE", 1)])
            mks.append(E.mark("sPE"))

        E.op("scalar", (lambda e: e.copy(ssl[0:16, :], self.pb[5][0:16, :])),
             waits=[("sPE", mks[-1])], incs=[("sAC", 1)])
        mk_ssl = E.mark("sAC")

        for cc in range(8):
            bank = self.pb[2] if cc % 2 == 0 else self.pb[3]
            w = [("sAC", mk_ssl)]
            if cc >= 2:
                w.append(("sDV", M["ydtt_%d_%d" % (tb, cc - 2)]))
            E.op("tensor",
                 (lambda e, cc=cc, bank=bank: e.matmul(
                     bank[:, :], self.ind_cc[:, cc * 128:(cc + 1) * 128],
                     ssl[0:16, :], start=True, stop=True)),
                 waits=w, incs=[("sPE", 1)])
            mkb = E.mark("sPE")
            E.op("vector",
                 (lambda e, cc=cc, bank=bank: e.tensor_tensor(
                     ydcc[:], vv3[:, cc, :], bank[:, :], ALU.mult)),
                 waits=[("sPE", mkb),
                        ("sSC", M.get("ydsp_%d_%d" % (tb, cc - 1), 0))],
                 incs=[("sDV", 1)])
            M["ydtt_%d_%d" % (tb, cc)] = E.mark("sDV")
            E.dma("scalar", "sSC",
                  S["yd"].rearrange("(ch p) n -> ch p n", p=128)[
                      tb * 8:tb * 8 + 8, :, cc * 64:(cc + 1) * 64].transpose([1, 0, 2]),
                  ydcc[:].rearrange("p (ch n) -> p ch n", n=64),
                  waits=[("sDV", M["ydtt_%d_%d" % (tb, cc)])])
            M["ydsp_%d_%d" % (tb, cc)] = E.mark("sSC")
        M["sdiag_%d" % tb] = E.mark("sDV")
    def exchange(self):
        E, S, M = self.E, self.S, self.M
        for cc in range(8):
            E.op("tensor",
                 (lambda e, cc=cc: e.matmul(self.pb[5][0:16, 0:128], self.ones16[:],
                                            self.r_lng[:, cc * 128:(cc + 1) * 128],
                                            start=True, stop=True)),
                 waits=[("sAC", E.mark("sAC")), ("sDV", E.mark("sDV")),
                        ("sPE", E.mark("sPE"))] if cc == 0 else
                       [("sDV", E.mark("sDV"))],
                 incs=[("sPE", 1)])
            E.op("vector",
                 (lambda e, cc=cc: e.tensor_tensor(
                     self.gin_cc[:, cc * 128:(cc + 1) * 128],
                     self.ind_cc[:, cc * 128:(cc + 1) * 128],
                     self.pb[5][0:16, 0:128], ALU.mult)),
                 waits=[("sPE", E.mark("sPE"))], incs=[("sDV", 1)])
        E.op("gpsimd",
             lambda e: e.collective_compute(
                 "AllGather", ALU.bypass, replica_groups=[list(range(8))],
                 ins=[S["sum_in"].opt()], outs=[S["sum_out"].opt()]),
             waits=[("sSC", M["sumspill"])], incs=[("sCC", 1)])
        M["ccsum"] = E.mark("sCC")

        E.dma("sync", "sSY", self.logdt[:].rearrange("p (m c) -> p m c", c=8),
              S["sum_out"][:, 512:520].rearrange("(m p) c -> p m c", p=128),
              waits=[("sCC", M["ccsum"])], allow_slow_non_contiguous=True)
        mk_ld = E.mark("sSY")
        ldv = self.logdt.rearrange("p (m c) -> p m c", c=8)

        self.ldT = self.cF(14144, 128)[0:8, :]
        self.wfac = self.cF(14272, 64)
        psin = self.cF(0, 512)
        facts = self.cF(13696, 256)

        for cc in range(8):
            E.op("tensor",
                 (lambda e, cc=cc: e.transpose(self.pb[5][0:8, 0:128],
                                               ldv[:, :, cc], self.identf[:])),
                 waits=[("sSY", mk_ld)] + ([("sAC", E.mark("sAC"))] if cc == 0 else
                                           [("sAC", M["ldT_%d" % (cc - 1)])]),
                 incs=[("sPE", 1)])
            mkp = E.mark("sPE")
            E.op("scalar",
                 (lambda e: e.copy(self.ldT[:], self.pb[5][0:8, 0:128])),
                 waits=[("sPE", mkp)], incs=[("sAC", 1)])
            M["ldT_%d" % cc] = E.mark("sAC")

            def emm(e, cc=cc):
                e.matmul(self.pb[4][:, 0:8], self.ldT[:], self.wsel_sb[:],
                         start=True, stop=False)
                return e.matmul(self.pb[4][:, 0:8], self.r_ones[0:1, 0:128],
                                self.wbias_sb[:], start=False, stop=True)
            E.op("tensor", emm, waits=[("sAC", M["ldT_%d" % cc]),
                                       ("sDV", M.get("wfac_%d" % (cc - 1), 0))],
                 incs=[("sPE", 1)])
            mkp2 = E.mark("sPE")
            E.op("scalar",
                 (lambda e, cc=cc: e.activation(self.wfac[:, cc * 8:(cc + 1) * 8],
                                                self.pb[4][:, 0:8], AF.Exp)),
                 waits=[("sPE", mkp2)], incs=[("sAC", 1)])
            M["wfac_%d" % cc] = E.mark("sAC")
        mk_wf = M["wfac_7"]

        E.op("vector", lambda e: e.memset(self.Sin[:], 0.0),
             waits=[], incs=[("sDV", 1)])
        for j in range(8):
            E.dma("sync", "sSY", psin,
                  S["sum_out"][j * 128:(j + 1) * 128, 0:512],
                  waits=[("sDV", E.mark("sDV"))] if j >= 2 else [])
            mk_pl = E.mark("sSY")

            def acc(e, j=j):
                for cc in range(8):
                    inst = e.scalar_tensor_tensor(
                        self.Sin[:, cc * 64:(cc + 1) * 64],
                        psin[:, cc * 64:(cc + 1) * 64],
                        self.wfac[:, cc * 8 + j: cc * 8 + j + 1],
                        self.Sin[:, cc * 64:(cc + 1) * 64], ALU.mult, ALU.add)
                return inst
            E.op("vector", acc, waits=[("sSY", mk_pl), ("sAC", mk_wf)],
                 incs=[("sDV", 1)])
        M["sin"] = E.mark("sDV")

        E.op("scalar", lambda e: e.activation(facts, self.scols[:], AF.Exp, scale=-1.0),
             waits=[("cV", NCH)], incs=[("sAC", 1)])
        M["facts"] = E.mark("sAC")
        self.facts = facts

    def _l2stop(self, part):
        return getattr(self, "l2_stop", None) == part

    def loop2(self):
        E, D, S, M = self.E, self.D, self.S, self.M
        l2stopped = False
        cF, cB = self.cF, self.cB
        pb = self.pb
        xfms = cF(0, 4096)
        xfv = xfms.rearrange("p (c n) -> p c n", n=TB)
        x1 = cF(4096, 4096)
        x1v = x1.rearrange("p (c n) -> p c n", n=TB)
        x1tm = cF(8192, 4096)
        ychp = [cF(14336, 512), cF(14848, 512)]
        ysqp = [cF(15360, 512), cF(15872, 512)]
        stp = [cF(16384, 384), cF(16768, 384)]
        lncol = cF(14080, 64)
        xq = cB(0, 16384)
        wo = cB(16384, 8192)
        a_s = cB(24576, 1024)
        v_s = cB(25600, 1024)
        r_s = cB(26624, 1024)
        sl_sp = [cB(27648, 512), cB(34304, 512)]
        yd_sp = [cB(28160, 512), cB(34816, 512)]
        g_sp = [cB(28672, 512), cB(35328, 512)]
        woin = cB(29184, 4096)
        sfull = cB(33280, 1024)
        self.xq = xq

        E.dma("sync", "sSY", wo.rearrange("p (a co) -> p a co", co=C),
              S["wob"].rearrange("(a p) co -> p a co", p=128),
              waits=[("sPE", M.get("l1pe_done", 0))])
        M["wo_ld"] = E.mark("sSY")

        for tb in range(NTB):
            t0 = tb * TB
            E.dma("sync", "sSY", xfms.rearrange("p (c n) -> p c n", n=TB),
                  S["xfm"][:, t0:t0 + TB].rearrange("(c p) n -> p c n", p=128),
                  waits=[("sSC", M["sumspill"]), ("sDV", M["sin"])] if tb == 0
                  else [("sSC", M["ln1_%d" % (tb - 1)])])
            M["xs_%d" % tb] = E.mark("sSY")

            for loc in range(NCHB):
                chg = tb * NCHB + loc
                p = chg % 2
                pbase = p * 64
                sl_s, yd_s, g_s = sl_sp[p], yd_sp[p], g_sp[p]
                ych, ysq, st = ychp[p], ysqp[p], stp[p]
                mu2 = st[0:16, 0:64]
                msq = st[0:16, 64:128]
                var = st[0:16, 128:192]
                lnv = st[0:16, 192:256]
                rstd = st[0:16, 256:320]
                qng = st[0:16, 320:384]
                pbY = pb[p]
                pbP = pb[2 + 2 * p]
                pbQ = pb[3 + 2 * p]
                sbase = p * 128

                wld = [("sSC", M["sumspill"])] if chg == 0 else \
                    [("sPE", M.get("ymm_%d" % (chg - 2), 0))]
                av2 = S["A"].rearrange("(ch s) n -> ch s n", s=64)
                vv2 = S["vtm"].rearrange("(ch s) n -> ch s n", s=64)
                rv2 = S["rt"].rearrange("(ch p) n -> ch p n", p=128)
                E.dma("sync", "sSY", a_s[pbase:pbase + 64, :], av2[chg], waits=wld)
                E.dma("sync", "sSY", v_s[pbase:pbase + 64, :], vv2[chg])
                E.dma("sync", "sSY", r_s[pbase:pbase + 64, 0:512],
                      rv2[chg, 0:64, :])
                E.dma("sync", "sSY", r_s[pbase:pbase + 64, 512:1024],
                      rv2[chg, 64:128, :])
                E.dma("sync", "sSY", sl_s[:],
                      S["sl"].rearrange("(ch p) n -> ch p n", p=128)[chg])
                mk_ld = E.mark("sSY")
                E.dma("sync", "sSY", yd_s[:],
                      S["yd"].rearrange("(ch p) n -> ch p n", p=128)[chg],
                      waits=[("sDV", M.get("yev_%d" % (chg - 2), 0))])
                mk_ldyd = E.mark("sSY")
                E.dma("sync", "sSY", g_s[:],
                      S["g"].rearrange("(ch p) n -> ch p n", p=128)[chg],
                      waits=[("sDV", M.get("app_%d" % (chg - 2), 0))])
                mk_ldg = E.mark("sSY")

                def sff(e, chg=chg, pbase=pbase, sl_s=sl_s):
                    for cc in range(8):
                        e.scalar_tensor_tensor(
                            sfull[pbase:pbase + 64, cc * 64:(cc + 1) * 64],
                            self.Sin[0:64, cc * 64:(cc + 1) * 64],
                            self.facts[0:64, cc * 32 + chg: cc * 32 + chg + 1],
                            sl_s[0:64, cc * 64:(cc + 1) * 64], ALU.mult, ALU.add)
                        inst = e.scalar_tensor_tensor(
                            sfull[pbase:pbase + 64, 512 + cc * 64: 512 + (cc + 1) * 64],
                            self.Sin[64:128, cc * 64:(cc + 1) * 64],
                            self.facts[64:128, cc * 32 + chg: cc * 32 + chg + 1],
                            sl_s[64:128, cc * 64:(cc + 1) * 64], ALU.mult, ALU.add)
                    return inst
                E.op("vector", sff,
                     waits=[("sSY", mk_ld), ("sAC", M["facts"]),
                            ("sPE", M.get("ymm_%d" % (chg - 2), 0))],
                     incs=[("sDV", 1)])
                mk_sf = E.mark("sDV")

                def ymm(e, chg=chg, pbase=pbase, pbY=pbY):
                    tpA = (pbase, 0)
                    tpB = (pbase, 64)
                    for cc in range(8):
                        e.matmul(pbY[0:64, cc * 64:(cc + 1) * 64],
                                 v_s[pbase:pbase + 64, cc * 128: cc * 128 + 64],
                                 a_s[pbase:pbase + 64, cc * 64:(cc + 1) * 64],
                                 start=True, stop=False, tile_position=tpA)
                        e.matmul(pbY[0:64, cc * 64:(cc + 1) * 64],
                                 sfull[pbase:pbase + 64, cc * 64:(cc + 1) * 64],
                                 r_s[pbase:pbase + 64, cc * 64:(cc + 1) * 64],
                                 start=False, stop=True, tile_position=tpA)
                        e.matmul(pbY[64:128, cc * 64:(cc + 1) * 64],
                                 v_s[pbase:pbase + 64, cc * 128 + 64: cc * 128 + 128],
                                 a_s[pbase:pbase + 64, 512 + cc * 64: 512 + (cc + 1) * 64],
                                 start=True, stop=False, tile_position=tpB)
                        inst = e.matmul(
                            pbY[64:128, cc * 64:(cc + 1) * 64],
                            sfull[pbase:pbase + 64, 512 + cc * 64: 512 + (cc + 1) * 64],
                            r_s[pbase:pbase + 64, 512 + cc * 64: 512 + (cc + 1) * 64],
                            start=False, stop=True, tile_position=tpB)
                    return inst
                ymw = [("sDV", mk_sf), ("sDV", M.get("yev_%d" % (chg - 2), 0))]
                if loc <= 1:
                    ymw.append(("sDV", M.get("x1_1_%d" % (tb - 1), 0)))
                E.op("tensor", ymm, waits=ymw, incs=[("sPE", 1)])
                M["ymm_%d" % chg] = E.mark("sPE")
                if self._l2stop("ymm"):
                    l2stopped = True
                    break

                E.op("vector",
                     (lambda e, ych=ych, yd_s=yd_s, pbY=pbY: e.tensor_tensor(
                         ych, pbY[:, :], yd_s[:], ALU.add)),
                     waits=[("sPE", M["ymm_%d" % chg]), ("sSY", mk_ldyd)],
                     incs=[("sDV", 1)])
                M["yev_%d" % chg] = E.mark("sDV")
                if "dbg" in self.taps:
                    E.dma("scalar", "sSC", S["dbg"][:, chg * 512:(chg + 1) * 512],
                          ych, waits=[("sDV", M["yev_%d" % chg])])
                if self._l2stop("yev"):
                    l2stopped = True
                    break

                E.op("scalar",
                     (lambda e, ych=ych, ysq=ysq: e.activation(ysq, ych, AF.Square)),
                     waits=[("sDV", M["yev_%d" % chg]),
                            ("sPE", M.get("smm_%d" % (chg - 2), 0))],
                     incs=[("sAC", 1)])
                mk_sq = E.mark("sAC")

                def smm(e, ych=ych, ysq=ysq, sbase=sbase):
                    for cc in range(8):
                        e.matmul(self.pba[0:16, sbase:sbase + 64],
                                 self.ione_cc[:, cc * 16:(cc + 1) * 16],
                                 ych[:, cc * 64:(cc + 1) * 64],
                                 start=(cc == 0), stop=False)
                    for cc in range(8):
                        inst = e.matmul(self.pba[0:16, sbase + 64:sbase + 128],
                                        self.ione_cc[:, cc * 16:(cc + 1) * 16],
                                        ysq[:, cc * 64:(cc + 1) * 64],
                                        start=False, stop=(cc == 7))
                    return inst
                E.op("tensor", smm,
                     waits=[("sAC", mk_sq), ("sDV", M.get("statf_%d" % (chg - 2), 0))],
                     incs=[("sPE", 1)])
                M["smm_%d" % chg] = E.mark("sPE")
                mk_smm = M["smm_%d" % chg]

                def statf(e, mu2=mu2, msq=msq, var=var, sbase=sbase):
                    e.tensor_scalar(mu2, self.pba[0:16, sbase:sbase + 64],
                                    1.0 / 64.0, None, ALU.mult)
                    e.tensor_tensor(msq, mu2, mu2, ALU.mult)
                    return e.scalar_tensor_tensor(
                        var, self.pba[0:16, sbase + 64:sbase + 128], 1.0 / 64.0,
                        msq, ALU.mult, ALU.subtract)
                E.op("vector", statf, waits=[("sPE", mk_smm)], incs=[("sDV", 1)])
                M["statf_%d" % chg] = E.mark("sDV")
                mk_st = M["statf_%d" % chg]

                def lnf(e, lnv=lnv, var=var, rstd=rstd):
                    e.activation(lnv, var, AF.Ln, bias=self.epsg[0:16, :])
                    return e.activation(rstd, lnv, AF.Exp, scale=-0.5)
                E.op("scalar", lnf, waits=[("sDV", mk_st)], incs=[("sAC", 1)])
                mk_ln = E.mark("sAC")

                E.op("vector",
                     (lambda e, qng=qng, mu2=mu2, rstd=rstd: e.scalar_tensor_tensor(
                         qng, mu2, -1.0, rstd, ALU.mult, ALU.mult)),
                     waits=[("sAC", mk_ln)], incs=[("sDV", 1)])
                M["stats_%d" % chg] = E.mark("sDV")
                if "dbg4" in self.taps:
                    E.dma("scalar", "sSC", S["dbg"][0:16, chg * 384:(chg + 1) * 384],
                          st[0:16, :], waits=[("sDV", M["stats_%d" % chg])])
                if self._l2stop("stats"):
                    l2stopped = True
                    break

                def pqmm(e, pbP=pbP, pbQ=pbQ, rstd=rstd, qng=qng):
                    for cc in range(8):
                        e.matmul(pbP[:, cc * 64:(cc + 1) * 64],
                                 self.gin_cc[:, cc * 128:(cc + 1) * 128], rstd,
                                 start=True, stop=True)
                        e.matmul(pbQ[:, cc * 64:(cc + 1) * 64],
                                 self.gin_cc[:, cc * 128:(cc + 1) * 128], qng,
                                 start=True, stop=False)
                        inst = e.matmul(pbQ[:, cc * 64:(cc + 1) * 64],
                                        self.r_lnb[0:1, cc * 128:(cc + 1) * 128],
                                        self.r_ones[0:1, 0:64],
                                        start=False, stop=True)
                    return inst
                wpq = [("sDV", M["stats_%d" % chg]),
                       ("sDV", M.get("app_%d" % (chg - 2), 0))]
                if loc == 0:
                    wpq += [("sAC", M.get("l1q_ac_%d" % (tb - 1), 0)),
                            ("sDV", M.get("x1_1_%d" % (tb - 1), 0))]
                elif loc == 1:
                    wpq.append(("sAC", M.get("l1q_ac_%d" % (tb - 1), 0)))
                E.op("tensor", pqmm, waits=wpq, incs=[("sPE", 1)])
                mk_pq = E.mark("sPE")

                def appf(e, loc=loc, ych=ych, ysq=ysq, g_s=g_s, pbP=pbP, pbQ=pbQ):
                    e.tensor_tensor(ysq, ych, pbP[:, :], ALU.mult)
                    e.tensor_tensor(ych, ysq, pbQ[:, :], ALU.add)
                    wv = woin.rearrange("p (c n) -> p c n", n=TB)[:, :, loc * 64:(loc + 1) * 64]
                    return e.tensor_tensor(
                        wv, ych.rearrange("p (c n) -> p c n", n=64),
                        g_s.rearrange("p (c n) -> p c n", n=64), ALU.mult)
                wapp = [("sPE", mk_pq), ("sSY", mk_ldg)]
                if loc == 0:
                    wapp.append(("sPE", M.get("xqpe_%d" % (tb - 1), 0)))
                E.op("vector", appf, waits=wapp, incs=[("sDV", 1)])
                M["app_%d" % chg] = E.mark("sDV")
                if "dbg5" in self.taps:
                    E.dma("scalar", "sSC", S["dbg"][:, chg * 512:(chg + 1) * 512],
                          ych, waits=[("sDV", M["app_%d" % chg])])
                if self._l2stop("app"):
                    l2stopped = True
                    break

            if "dbg2" in self.taps:
                E.dma("scalar", "sSC",
                      S["dbg"].bitcast(BF16)[:, tb * 8192:tb * 8192 + 4096],
                      woin, waits=[("sDV", M["app_%d" % (tb * NCHB + NCHB - 1)])])
            if l2stopped:
                break
            for half in range(2):
                hs = half * 256

                def womm(e, hs=hs):
                    for co in range(8):
                        bank = pb[co // 2]
                        for ci in range(8):
                            inst = e.matmul(
                                bank[:, (co % 2) * 256:(co % 2) * 256 + 256],
                                wo[:, ci * C + co * 128: ci * C + co * 128 + 128],
                                woin[:, ci * TB + hs: ci * TB + hs + 256],
                                start=(ci == 0), stop=(ci == 7))
                    return inst
                w = [("sDV", M["app_%d" % (tb * NCHB + NCHB - 1)]),
                     ("sSY", M["wo_ld"]),
                     ("sDV", M.get("yev_%d" % (tb * NCHB + NCHB - 1), 0))]
                if half == 0:
                    w.append(("sDV", M.get("x1_%s_%d" % (1, tb - 1), 0)))
                else:
                    w.append(("sDV", M.get("x1_%s_%d" % (0, tb), 0)))
                E.op("tensor", womm, waits=w, incs=[("sPE", 1)])
                mk_wo = E.mark("sPE")

                def x1f(e, hs=hs):
                    for co in range(8):
                        bank = pb[co // 2]
                        inst = e.tensor_tensor(
                            x1v[:, co, hs:hs + 256], xfv[:, co, hs:hs + 256],
                            bank[:, (co % 2) * 256:(co % 2) * 256 + 256], ALU.add)
                    return inst
                x1w = [("sPE", mk_wo), ("sSY", M["xs_%d" % tb])]
                if half == 0:
                    x1w.append(("sSC", M.get("x1nsp_%d_3" % (tb - 1), 0)))
                E.op("vector", x1f, waits=x1w, incs=[("sDV", 1)])
                M["x1_%s_%d" % (half, tb)] = E.mark("sDV")
            M["x1_%d" % tb] = E.mark("sDV")
            if "dbg3" in self.taps:
                E.dma("scalar", "sSC", S["dbg"][:, tb * 4096:(tb + 1) * 4096],
                      x1, waits=[("sDV", M["x1_%d" % tb])])
            if self._l2stop("x1"):
                l2stopped = True
                break

            self._ln1_quant(tb, x1, x1v, x1tm, xfms, woin, xq)
        M["wo_free"] = E.mark("sPE")

    def _ln1_quant(self, tb, x1, x1v, x1tm, xfms, woin, xq):
        E, S, M = self.E, self.S, self.M
        pb = self.pb
        lncol = self.cF(14080, 64)
        mucol = lncol[:, 0:1]
        sqcol = lncol[:, 1:2]
        varcol = lncol[:, 2:3]
        rstdc = lncol[:, 4:5]
        lnvc = lncol[:, 3:4]
        amaxc = lncol[:, 5:6]
        sclc = lncol[:, 6:7]
        mu_s = lncol[:, 8:9]
        msqc = lncol[:, 9:10]
        amax2c = lncol[:, 10:11]
        lnsc = lncol[:, 11:12]

        for s1 in range(4):
            pem = []
            for cc in range(8):
                bank = pb[4] if cc % 2 == 0 else pb[5]
                w = [("sDV", M["x1_%d" % tb]), ("sAC", E.mark("sAC"))]
                E.op("tensor",
                     (lambda e, cc=cc, s1=s1, bank=bank: e.transpose(
                         bank[:, 0:128], x1v[:, cc, s1 * 128:(s1 + 1) * 128],
                         self.identf[:])),
                     waits=w, incs=[("sPE", 1)])
                pem.append(E.mark("sPE"))
                E.op("scalar",
                     (lambda e, cc=cc, s1=s1, bank=bank: e.copy(
                         x1tm[:, s1 * 1024 + cc * 128: s1 * 1024 + cc * 128 + 128],
                         bank[:, 0:128])),
                     waits=[("sPE", pem[-1])], incs=[("sAC", 1)])
            mk_tm = E.mark("sAC")

            sl = x1tm[:, s1 * 1024:(s1 + 1) * 1024]
            sq = xfms[:, 0:1024]

            def stat1(e, sl=sl):
                return e.tensor_reduce(mucol, sl, AX.X, ALU.add)
            E.op("vector", stat1, waits=[("sAC", mk_tm)], incs=[("sDV", 1)])
            E.op("scalar", (lambda e, sl=sl, sq=sq: e.activation(sq, sl, AF.Square)),
                 waits=[("sAC", mk_tm), ("sDV", E.mark("sDV"))], incs=[("sAC", 1)])
            mk_sq = E.mark("sAC")
            E.op("vector", (lambda e, sq=sq: e.tensor_reduce(sqcol, sq, AX.X, ALU.add)),
                 waits=[("sAC", mk_sq)], incs=[("sDV", 1)])

            def stat2a(e):
                e.activation(mu_s, mucol, AF.Copy, scale=1.0 / 1024.0)
                e.activation(msqc, mu_s, AF.Square)
                return e.activation(lnvc, sqcol, AF.Copy, scale=1.0 / 1024.0)
            E.op("scalar", stat2a, waits=[("sDV", E.mark("sDV"))], incs=[("sAC", 1)])
            mk_sa = E.mark("sAC")
            E.op("vector",
                 (lambda e: e.tensor_tensor(varcol, lnvc, msqc, ALU.subtract)),
                 waits=[("sAC", mk_sa)], incs=[("sDV", 1)])
            mk_v = E.mark("sDV")

            def lnf2(e):
                e.activation(lnvc, varcol, AF.Ln, bias=self.epsl[:])
                return e.activation(rstdc, lnvc, AF.Exp, scale=-0.5)
            E.op("scalar", lnf2, waits=[("sDV", mk_v)], incs=[("sAC", 1)])
            mk_r = E.mark("sAC")

            E.op("vector",
                 (lambda e, sl=sl: e.tensor_scalar(sl, sl, mu_s, rstdc,
                                                   ALU.subtract, ALU.mult)),
                 waits=[("sAC", mk_r)], incs=[("sDV", 1)])
            mk_n = E.mark("sDV")

            if "dbg6" in self.taps:
                sidx = tb * 4 + s1
                E.dma("scalar", "sSC", S["dbg"][:, sidx * 1024:(sidx + 1) * 1024],
                      sl, waits=[("sDV", mk_n)])
                E.dma("scalar", "sSC",
                      S["dbg"][0:128, 32768 + sidx * 8: 32768 + sidx * 8 + 8],
                      lncol[:, 0:8])
            for cc in range(8):
                bank = pb[4] if cc % 2 == 0 else pb[5]
                E.op("tensor",
                     (lambda e, cc=cc, bank=bank, sl=sl: e.transpose(
                         bank[:, 0:128], sl[:, cc * 128:(cc + 1) * 128], self.identf[:])),
                     waits=[("sDV", mk_n), ("sAC", E.mark("sAC"))],
                     incs=[("sPE", 1)])
                mk_t3 = E.mark("sPE")
                E.op("scalar",
                     (lambda e, cc=cc, s1=s1, bank=bank: e.copy(
                         x1v[:, cc, s1 * 128:(s1 + 1) * 128], bank[:, 0:128])),
                     waits=[("sPE", mk_t3)], incs=[("sAC", 1)])
            mk_fm = E.mark("sAC")
            mk_fm_pe = E.mark("sPE")
            for cc in range(8):
                E.dma("scalar", "sSC",
                      S["x1n"][cc * 128:(cc + 1) * 128,
                               tb * TB + s1 * 128: tb * TB + (s1 + 1) * 128],
                      x1v[:, cc, s1 * 128:(s1 + 1) * 128],
                      waits=[("sAC", mk_fm)] if cc == 0 else [])
            M["x1nsp_%d_%d" % (tb, s1)] = E.mark("sSC")

            E.op("vector",
                 (lambda e, sl=sl: e.tensor_reduce(
                     amaxc, sl, AX.X, ALU.max, apply_absolute_value=True)),
                 waits=[("sDV", mk_n)], incs=[("sDV", 1)])
            E.op("gpsimd",
                 (lambda e: e.tensor_scalar(amax2c, amaxc, 1.0 / 127.0,
                                            1e-5 / 127.0, ALU.mult, ALU.max)),
                 waits=[("sDV", E.mark("sDV"))], incs=[("sGP", 1)])

            def qs2(e, tb=tb, s1=s1):
                e.activation(self.dsx[:, tb * 4 + s1: tb * 4 + s1 + 1],
                             amax2c, AF.Copy)
                e.activation(lnsc, amax2c, AF.Ln)
                return e.activation(sclc, lnsc, AF.Exp, scale=-1.0)
            E.op("scalar", qs2, waits=[("sGP", E.mark("sGP"))], incs=[("sAC", 1)])
            mk_s = E.mark("sAC")
            M["dsxac"] = mk_s

            xqtm = woin[:, 0:1024]

            def qr(e, sl=sl, xqtm=xqtm):
                e.tensor_scalar(sl, sl, sclc, CRND, ALU.mult, ALU.add)
                return e.tensor_scalar(xqtm, sl, CRND, None, ALU.subtract)
            E.op("vector", qr, waits=[("sAC", mk_s), ("sPE", mk_fm_pe)],
                 incs=[("sDV", 1)])
            mk_q = E.mark("sDV")

            for cc in range(8):
                def trq(e, cc=cc):
                    e.transpose(self.pbt[:, 0:128], xqtm[:, cc * 128: cc * 128 + 64],
                                self.identb[:])
                    return e.transpose(self.pbt[:, 128:256],
                                       xqtm[:, cc * 128 + 64: cc * 128 + 128],
                                       self.identb[:])
                E.op("tensor", trq,
                     waits=[("sDV", mk_q), ("sAC", E.mark("sAC"))],
                     incs=[("sPE", 1)])
                mk_t2 = E.mark("sPE")
                tcol = tb * TB + s1 * 128

                def evq(e, cc=cc, tcol=tcol):
                    e.copy(xq.rearrange("p (c n) -> p c n", n=TLOC)[0:64, cc, tcol:tcol + 128],
                           self.pbt[:, 0:128])
                    return e.copy(
                        xq.rearrange("p (c n) -> p c n", n=TLOC)[64:128, cc, tcol:tcol + 128],
                        self.pbt[:, 128:256])
                E.op("scalar", evq, waits=[("sPE", mk_t2)], incs=[("sAC", 1)])
            M["xq_%d_%d" % (tb, s1)] = E.mark("sAC")
            M["xqpe_%d" % tb] = E.mark("sPE")
        M["l1q_ac_%d" % tb] = E.mark("sAC")
        M["ln1_%d" % tb] = E.mark("sSC")
    def ffn(self):
        E, D, S, M = self.E, self.D, self.S, self.M
        cF, cB = self.cF, self.cB
        pb = self.pb
        hw = cF(0, 16384)
        x1ns = cF(16384, 512)
        ddbs = cF(16896, 512)
        x2f = cF(17408, 512)
        x2tm = cF(17920, 4096)
        sq2 = cF(22016, 1024)
        gs = cF(23040, 512)
        fc = cF(23552, 64)
        dsg = fc[:, 0:16]
        dsu = fc[:, 16:32]
        ddc = fc[:, 32:48]
        csq = fc[:, 48:64]
        l2c = cF(23616, 16)
        mucol = l2c[:, 0:1]
        sqcol = l2c[:, 1:2]
        varcol = l2c[:, 2:3]
        lnvc = l2c[:, 3:4]
        rstdc = l2c[:, 4:5]
        mu_s2 = l2c[:, 5:6]
        msqc2 = l2c[:, 6:7]
        xq = self.xq
        wgq = cB(16384, 4096)
        wuq = cB(20480, 4096)
        hqs = cB(24576, 4096)
        wdt4 = cB(16384, 16384)
        hqT = [cB(32768, 512), cB(33280, 512)]
        amaxw = self.amaxw

        hmxq = cF(23632, 64)

        def dsf(e):
            e.tensor_scalar(dsg, self.dsx[:], self.swv[:, 0:1], None, ALU.mult)
            return e.tensor_scalar(dsu, self.dsx[:], self.swv[:, 1:2], None, ALU.mult)
        E.op("vector", dsf, waits=[("sSC", M["ln1_%d" % (NTB - 1)]),
                                   ("sAC", M["dsxac"])],
             incs=[("sDV", 1)])
        M["dsf"] = E.mark("sDV")

        for w in range(NWAVE):
            wt0 = w * TB
            for q in range(8):
                wld = [("sCC", M["ccw"]), ("sPE", M.get("dn_%d" % (w - 1), 0)),
                       ("sPE", M.get("wo_free", 0))]
                if q > 0:
                    wld.append(("sPE", M.get("gu_%d_%d" % (w, q - 1), 0)))
                E.dma("sync", "sSY", wgq.rearrange("p (a n) -> p a n", n=512),
                      S["wgT"].rearrange("(a p) n -> a p n", p=128)[
                          q * 8:(q + 1) * 8].transpose([1, 0, 2]),
                      waits=wld)
                E.dma("sync", "sSY", wuq.rearrange("p (a n) -> p a n", n=512),
                      S["wuT"].rearrange("(a p) n -> a p n", p=128)[
                          q * 8:(q + 1) * 8].transpose([1, 0, 2]))
                mk_w = E.mark("sSY")
                for sl in range(4):
                    gbank = pb[sl % 2]
                    ubank = pb[2 + sl % 2]
                    ts = wt0 + sl * 128
                    sidx = w * 4 + sl

                    def gumm(e, ts=ts, gbank=gbank, ubank=ubank):
                        xqv = xq.rearrange("p (c n) -> p c n", n=TLOC)
                        for ci in range(8):
                            e.matmul(gbank[:, :], xqv[:, ci, ts:ts + 128],
                                     wgq[:, ci * 512:(ci + 1) * 512],
                                     start=(ci == 0), stop=(ci == 7))
                        for ci in range(8):
                            inst = e.matmul(ubank[:, :], xqv[:, ci, ts:ts + 128],
                                            wuq[:, ci * 512:(ci + 1) * 512],
                                            start=(ci == 0), stop=(ci == 7))
                        return inst
                    if sl >= 2:
                        bw = ("sDV", M.get("hup_%d_%d_%d" % (w, q, sl - 2), 0))
                    else:
                        bw = ("sDV", M.get("hup_%d_%d_%d" % (w, q - 1, sl + 2), 0))
                    wmm = [("sSY", mk_w), bw]
                    if w == 0 and q == 0 and sl == 0:
                        wmm.append(("sAC", M["xq_%d_%d" % (NTB - 1, 3)]))
                    E.op("tensor", gumm, waits=wmm, incs=[("sPE", 1)])
                    mk_g = E.mark("sPE")
                    if sl == 3:
                        M["gu_%d_%d" % (w, q)] = mk_g

                    E.op("scalar",
                         (lambda e, gbank=gbank, sidx=sidx: e.activation(
                             gs, gbank[:, :], AF.Silu, scale=dsg[:, sidx:sidx + 1])),
                         waits=[("sPE", mk_g), ("sDV", M["dsf"]),
                                ("sDV", M.get("hup_%d_%d_%d" % (w, q, sl - 1), 0)
                                 if sl else M.get("hup_%d_%d_3" % (w, q - 1), 0))],
                         incs=[("sAC", 1)])
                    mk_si = E.mark("sAC")

                    def hupf(e, sl=sl, q=q, ubank=ubank):
                        hsl = hw[:, sl * 4096 + q * 512: sl * 4096 + (q + 1) * 512]
                        e.tensor_tensor(hsl, gs, ubank[:, :], ALU.mult)
                        return e.tensor_reduce(
                            hmxq[:, sl * 8 + q: sl * 8 + q + 1], hsl,
                            AX.X, ALU.max, apply_absolute_value=True)
                    E.op("vector", hupf, waits=[("sAC", mk_si)], incs=[("sDV", 1)])
                    M["hup_%d_%d_%d" % (w, q, sl)] = E.mark("sDV")

            for sl in range(4):
                sidx = w * 4 + sl

                hmax_s = hmxq[:, 32 + sl: 33 + sl]
                tmpc_s = hmxq[:, 36 + sl: 37 + sl]
                tmp2_s = hmxq[:, 40 + sl: 41 + sl]
                rec_s = hmxq[:, 44 + sl: 45 + sl]
                ddt_s = hmxq[:, 48 + sl: 49 + sl]

                E.op("vector",
                     (lambda e, sl=sl, hmax_s=hmax_s: e.tensor_reduce(
                         hmax_s, hmxq[:, sl * 8: sl * 8 + 8], AX.X, ALU.max)),
                     waits=[("sDV", M["hup_%d_%d_%d" % (w, 7, sl)])],
                     incs=[("sDV", 1)])
                E.op("gpsimd",
                     (lambda e, sidx=sidx, hmax_s=hmax_s, tmpc_s=tmpc_s:
                      e.tensor_tensor(tmpc_s, hmax_s, dsu[:, sidx:sidx + 1],
                                      ALU.mult)),
                     waits=[("sDV", E.mark("sDV"))], incs=[("sGP", 1)])
                E.op("vector",
                     (lambda e, tmpc_s=tmpc_s, tmp2_s=tmp2_s: e.tensor_scalar(
                         tmp2_s, tmpc_s, 1e-5, None, ALU.max)),
                     waits=[("sGP", E.mark("sGP"))], incs=[("sDV", 1)])

                lnt_s = hmxq[:, 52 + sl: 53 + sl]

                def qsha(e, tmp2_s=tmp2_s, rec_s=rec_s, ddt_s=ddt_s,
                         lnt_s=lnt_s, sidx=sidx):
                    e.activation(lnt_s, tmp2_s, AF.Ln)
                    e.activation(rec_s, lnt_s, AF.Exp, scale=-1.0)
                    e.activation(ddt_s, tmp2_s, AF.Copy, scale=1.0 / 127.0)
                    return e.activation(ddc[:, sidx:sidx + 1], ddt_s, AF.Copy,
                                        scale=self.swv[:, 2:3])
                E.op("scalar", qsha, waits=[("sDV", E.mark("sDV"))],
                     incs=[("sAC", 1)])
                E.op("gpsimd",
                     (lambda e, rec_s=rec_s, sidx=sidx: e.tensor_scalar(
                         csq[:, sidx:sidx + 1], rec_s, 127.0,
                         dsu[:, sidx:sidx + 1], ALU.mult, ALU.mult)),
                     waits=[("sAC", E.mark("sAC"))], incs=[("sGP", 1)])
                mk_csq = E.mark("sGP")

                qrw = [("sSC", M.get("hqsp_%d_%d" % (w, sl - 1), 0)),
                       ("sPE", M.get("dn_%d" % (w - 1), 0)),
                       ("sGP", mk_csq)]
                if "dbg7" in self.taps and w == 0:
                    E.dma("scalar", "sSC", S["dbg"][:, sl * 4096:(sl + 1) * 4096],
                          hw[:, sl * 4096:(sl + 1) * 4096],
                          waits=[("sDV", M["hup_%d_%d_%d" % (w, 7, sl)])])
                    if sl == 3:
                        E.dma("scalar", "sSC", S["dbg"][:, 16384:16448], fc,
                              waits=[("sGP", mk_csq), ("sAC", E.mark("sAC"))])
                        E.dma("scalar", "sSC", S["dbg"][:, 16448:16512], hmxq)
                    qrw.append(("sSC", E.mark("sSC")))

                def qrh(e, sl=sl, sidx=sidx):
                    e.tensor_scalar(hw[:, sl * 4096:(sl + 1) * 4096],
                                    hw[:, sl * 4096:(sl + 1) * 4096],
                                    csq[:, sidx:sidx + 1], CRND, ALU.mult, ALU.add)
                    return e.tensor_scalar(hqs, hw[:, sl * 4096:(sl + 1) * 4096],
                                           CRND, None, ALU.subtract)
                E.op("vector", qrh, waits=qrw, incs=[("sDV", 1)])
                mk_hq = E.mark("sDV")
                E.dma("scalar", "sSC", S["hq"][wt0 + sl * 128: wt0 + (sl + 1) * 128, :],
                      hqs, waits=[("sDV", mk_hq)])
                M["hqsp_%d_%d" % (w, sl)] = E.mark("sSC")

            for sl in range(4):
                sidx = w * 4 + sl
                E.op("tensor",
                     (lambda e, sidx=sidx: e.transpose(
                         pb[5][0:1, 0:128], ddc[:, sidx:sidx + 1], self.identf[:])),
                     waits=[("sDV", E.mark("sDV")), ("sAC", E.mark("sAC"))],
                     incs=[("sPE", 1)])
                mk_dt = E.mark("sPE")
                E.op("scalar",
                     (lambda e, sl=sl: e.copy(self.ddrow[0:1, sl * 128:(sl + 1) * 128],
                                              pb[5][0:1, 0:128])),
                     waits=[("sPE", mk_dt)], incs=[("sAC", 1)])
            mk_dr = E.mark("sAC")
            E.op("tensor",
                 (lambda e: e.matmul(pb[4][:, :], self.r_ones[0:1, 0:128],
                                     self.ddrow[0:1, :], start=True, stop=True)),
                 waits=[("sAC", mk_dr), ("sDV", M.get("ddbev_%d" % (w - 1), 0))],
                 incs=[("sPE", 1)])
            mk_db = E.mark("sPE")
            E.op("vector", (lambda e: e.tensor_copy(ddbs, pb[4][:, :])),
                 waits=[("sPE", mk_db)], incs=[("sDV", 1)])
            M["ddbev_%d" % w] = E.mark("sDV")

            wdtv = wdt4.rearrange("p (a c n) -> p a c n", a=4, n=128)

            for chalf in range(2):
                for i4 in range(4):
                    blk = chalf * 4 + i4
                    wldd = [("sSC", M["hqsp_%d_3" % w]), ("sPE", M["gu_%d_7" % w])]
                    if chalf == 1:
                        wldd.append(("sPE", M.get("dmm_%d_0_31" % w, 0)))
                    E.dma("sync", "sSY",
                          wdt4.rearrange("p (q a n) -> p q a n", q=4, n=128)[:, i4],
                          S["wdT"].rearrange("(a p) n -> a p n", p=128)[
                              blk * 32:(blk + 1) * 32].transpose([1, 0, 2]),
                          waits=wldd if i4 == 0 else [])
                mk_wd = E.mark("sSY")
                for coch in range(32):
                    par2 = coch % 2
                    qeng = "sync" if coch % 2 == 0 else "scalar"
                    qsem = "sSY" if coch % 2 == 0 else "sSC"
                    E.dma(qeng, qsem, hqT[par2],
                          S["hq"][wt0:wt0 + TB, coch * 128:(coch + 1) * 128],
                          transpose=True,
                          waits=[("sSC", M["hqsp_%d_3" % w]),
                                 ("sPE", M.get("dmm_%d_%d_%d" % (w, chalf, coch - 2),
                                               M.get("dmm_%d_%d_31" % (w, chalf - 1), 0)))])
                    mk_h = E.mark(qsem)

                    def dmm(e, coch=coch, chalf=chalf, par2=par2):
                        for i in range(4):
                            cch = chalf * 4 + i
                            inst = e.matmul(pb[i][:, :], wdtv[:, i, coch, :],
                                            hqT[par2][:, :],
                                            start=(coch == 0), stop=(coch == 31))
                        return inst
                    wmm = [(qsem, mk_h), ("sSY", mk_wd)]
                    if coch == 0:
                        wmm.append(("sDV", M.get("x2f_%d_%d" % (w, chalf * 4 - 1),
                                                 M.get("x2f_%d_7" % (w - 1), 0))))
                    E.op("tensor", dmm, waits=wmm, incs=[("sPE", 1)])
                    M["dmm_%d_%d_%d" % (w, chalf, coch)] = E.mark("sPE")
                mk_dn = E.mark("sPE")

                for i in range(4):
                    cch = chalf * 4 + i
                    E.dma("sync", "sSY", x1ns,
                          S["x1n"][cch * 128:(cch + 1) * 128, wt0:wt0 + TB],
                          waits=[("sDV", M.get("x2f_%d_%d" % (w, cch - 1), 0))])
                    mk_x1 = E.mark("sSY")

                    def x2ev(e, i=i):
                        e.tensor_tensor(x2f, pb[i][:, :], ddbs, ALU.mult)
                        return e.tensor_tensor(x2f, x2f, x1ns, ALU.add)
                    E.op("vector", x2ev,
                         waits=[("sPE", mk_dn), ("sSY", mk_x1),
                                ("sDV", M["ddbev_%d" % w]),
                                ("sAC", M.get("x2tr_%d_%d" % (w, cch - 1), 0))],
                         incs=[("sDV", 1)])
                    M["x2f_%d_%d" % (w, cch)] = E.mark("sDV")

                    for sl in range(4):
                        E.op("tensor",
                             (lambda e, sl=sl: e.transpose(
                                 pb[5][:, 0:128], x2f[:, sl * 128:(sl + 1) * 128],
                                 self.identf[:])),
                             waits=[("sDV", M["x2f_%d_%d" % (w, cch)]),
                                    ("sAC", E.mark("sAC"))],
                             incs=[("sPE", 1)])
                        mk_t = E.mark("sPE")
                        E.op("scalar",
                             (lambda e, sl=sl, cch=cch: e.copy(
                                 x2tm[:, sl * 1024 + cch * 128:
                                      sl * 1024 + cch * 128 + 128],
                                 pb[5][:, 0:128])),
                             waits=[("sPE", mk_t)], incs=[("sAC", 1)])
                    M["x2tr_%d_%d" % (w, cch)] = E.mark("sAC")
            M["dn_%d" % w] = E.mark("sPE")

            for sl in range(4):
                slv = x2tm[:, sl * 1024:(sl + 1) * 1024]
                E.op("vector", (lambda e, slv=slv: e.tensor_reduce(
                    mucol, slv, AX.X, ALU.add)),
                     waits=[("sAC", M["x2tr_%d_%d" % (w, 7)])], incs=[("sDV", 1)])
                E.op("scalar", (lambda e, slv=slv: e.activation(sq2, slv, AF.Square)),
                     waits=[("sAC", M["x2tr_%d_%d" % (w, 7)]),
                            ("sDV", E.mark("sDV"))], incs=[("sAC", 1)])
                mk_s2 = E.mark("sAC")
                E.op("vector", (lambda e: e.tensor_reduce(sqcol, sq2, AX.X, ALU.add)),
                     waits=[("sAC", mk_s2)], incs=[("sDV", 1)])

                def l2sa(e):
                    e.activation(mu_s2, mucol, AF.Copy, scale=1.0 / 1024.0)
                    e.activation(msqc2, mu_s2, AF.Square)
                    return e.activation(lnvc, sqcol, AF.Copy, scale=1.0 / 1024.0)
                E.op("scalar", l2sa, waits=[("sDV", E.mark("sDV"))],
                     incs=[("sAC", 1)])
                mk_a2 = E.mark("sAC")
                E.op("vector",
                     (lambda e: e.tensor_tensor(varcol, lnvc, msqc2, ALU.subtract)),
                     waits=[("sAC", mk_a2)], incs=[("sDV", 1)])
                mk_v2 = E.mark("sDV")

                def l2ln(e):
                    e.activation(lnvc, varcol, AF.Ln, bias=self.epsl[:])
                    return e.activation(rstdc, lnvc, AF.Exp, scale=-0.5)
                E.op("scalar", l2ln, waits=[("sDV", mk_v2)], incs=[("sAC", 1)])
                mk_r2 = E.mark("sAC")

                E.op("vector",
                     (lambda e, slv=slv: e.tensor_scalar(slv, slv, mu_s2, rstdc,
                                                         ALU.subtract, ALU.mult)),
                     waits=[("sAC", mk_r2)], incs=[("sDV", 1)])
                mk_o = E.mark("sDV")
                E.dma("scalar", "sOUT",
                      D["out"][wt0 + sl * 128: wt0 + (sl + 1) * 128, :], slv,
                      waits=[("sDV", mk_o)])

    def finish(self):
        E = self.E
        if E.mark("sOUT") == 0:
            w = [(s, E.mark(s)) for s in ["sSY", "sSC", "sGP", "sCC", "sPE",
                                          "sDV", "sAC", "cS", "gE", "sOUT"]]
            E.op("sync", lambda e: None, waits=w)
            return
        total = E.mark("sOUT")
        E.op("sync", lambda e: None, waits=[("sOUT", total)])


_IN_MAP = [
    ("maa_x", "L0_attn_maa_x"), ("maa_w", "L0_attn_maa_w"), ("maa_k", "L0_attn_maa_k"),
    ("maa_v", "L0_attn_maa_v"), ("maa_r", "L0_attn_maa_r"), ("maa_g", "L0_attn_maa_g"),
    ("maa_w1", "L0_attn_maa_w1"), ("maa_w2", "L0_attn_maa_w2"),
    ("Wr", "L0_attn_Wr"), ("Wk", "L0_attn_Wk"), ("Wv", "L0_attn_Wv"),
    ("Wo", "L0_attn_Wo"), ("gate_w1", "L0_attn_gate_w1"), ("gate_w2", "L0_attn_gate_w2"),
    ("time_decay", "L0_attn_time_decay"), ("decay_w1", "L0_attn_decay_w1"),
    ("decay_w2", "L0_attn_decay_w2"), ("u", "L0_attn_u"),
    ("ln_g", "L0_attn_ln_g"), ("ln_b", "L0_attn_ln_b"),
]


def build_program(taps=(), stop_after=None, chunk_stop=None, l2_stop=None):
    kb = KB(taps=taps, stop_after=stop_after)
    kb.chunk_stop = chunk_stop
    kb.l2_stop = l2_stop
    nc = kb.build()
    return nc


def make_in_maps(inputs):
    x = np.ascontiguousarray(np.asarray(inputs["L0_attn_x"], dtype=np.float32))
    wg = np.ascontiguousarray(np.asarray(inputs["L0_ffn_w_gate"], dtype=np.float32))
    wu = np.ascontiguousarray(np.asarray(inputs["L0_ffn_w_up"], dtype=np.float32))
    wd = np.ascontiguousarray(np.asarray(inputs["L0_ffn_w_down"], dtype=np.float32))
    common = {}
    for knew, kold in _IN_MAP:
        common[knew] = np.ascontiguousarray(np.asarray(inputs[kold], dtype=np.float32))
    in_maps = []
    for c in range(8):
        b, q = c // 4, c % 4
        im = dict(common)
        im["x"] = np.ascontiguousarray(x[b, q * TLOC:(q + 1) * TLOC, :])
        if q == 0:
            im["xprev"] = np.zeros((1, C), np.float32)
        else:
            im["xprev"] = np.ascontiguousarray(x[b, q * TLOC - 1: q * TLOC, :])
        im["wg_sh"] = np.ascontiguousarray(wg[c * 512:(c + 1) * 512, :])
        im["wu_sh"] = np.ascontiguousarray(wu[c * 512:(c + 1) * 512, :])
        im["wd_sh"] = np.ascontiguousarray(wd[c * 128:(c + 1) * 128, :])
        wsel = np.zeros((8, 8), np.float32)
        wbias = np.full((1, 8), -1e30, np.float32)
        g0 = b * 4
        for j in range(g0, c):
            wbias[0, j] = 0.0
            for m in range(j + 1, c):
                wsel[m, j] = 1.0
        im["wsel"] = wsel
        im["wbias"] = wbias
        in_maps.append(im)
    return in_maps


def kernel(**inputs):
    nc = build_program()
    in_maps = make_in_maps(inputs)
    res = run_bass_kernel_spmd(nc, in_maps, core_ids=list(range(8)))
    out = np.empty((2, 8192, C), np.float32)
    for c in range(8):
        b, q = c // 4, c % 4
        out[b, q * TLOC:(q + 1) * TLOC, :] = res.results[c]["out"]
    return out


# version 10 (final)
# speedup vs baseline: 1.0570x; 1.0559x over previous
import numpy as np
import ml_dtypes
from contextlib import ExitStack
import concourse.bass as bass
import concourse.mybir as mybir
from concourse.bass_utils import run_bass_kernel_spmd

F32 = mybir.dt.float32
BF16 = mybir.dt.bfloat16
AF = mybir.ActivationFunctionType
ALU = mybir.AluOpType
AX = mybir.AxisListType

C = 1024
TLOC = 2048
NTB = 4
TB = 512
LCH = 64
NCHB = TB // LCH
NCH = TLOC // LCH
DIN = 4096
NWAVE = 4
CRND = 12582912.0
EPS_GN = 0.00064
EPS_LN = 1e-5
ENGS = ("sync", "scalar", "vector", "tensor", "gpsimd")


class Emitter:

    def __init__(self):
        self.q = {e: [] for e in ENGS}
        self.cnt = {}

    def newsem(self, name):
        self.cnt[name] = 0
        return name

    def mark(self, s):
        return self.cnt[s]

    def op(self, eng, fn, waits=(), incs=()):
        for s, k in incs:
            self.cnt[s] += k
        self.q[eng].append((fn, list(waits), list(incs)))

    def dma(self, eng, sem, out, in_, waits=(), **kw):
        def fn(e, out=out, in_=in_, kw=kw):
            return e.dma_start(out=out, in_=in_, **kw)
        self.op(eng, fn, waits=waits, incs=[(sem, 16)])
        return self.cnt[sem]

    def flush(self, block, semh):
        for eng in ENGS:
            ops = self.q[eng]
            if not ops:
                continue

            def body(e, ops=ops):
                for fn, waits, incs in ops:
                    for s, v in waits:
                        if v > 0:
                            e.wait_ge(semh[s], v)
                    inst = fn(e)
                    for s, k in incs:
                        assert inst is not None, "inc without instruction"
                        inst.then_inc(semh[s], k)
            getattr(block, eng)(body)


class KB:
    def __init__(self, taps=(), stop_after=None):
        self.taps = list(taps)
        self.stop_after = stop_after
        self.nc = bass.Bass("TRN2", target_bir_lowering=False)
        self.es = ExitStack()
        self.E = Emitter()
        self.M = {}

    def sb(self, name, shape, dt):
        return self.es.enter_context(self.nc.sbuf_tensor(name, shape, dt))

    def psum(self, name, shape, dt=F32):
        return self.es.enter_context(self.nc.psum_tensor(name, shape, dt))

    def build(self):
        nc = self.nc
        D = {}
        D["x"] = nc.dram_tensor("x", [TLOC, C], F32, kind="ExternalInput").ap()
        D["xprev"] = nc.dram_tensor("xprev", [1, C], F32, kind="ExternalInput").ap()
        for n1, shp in [("maa_x", [C]), ("maa_w", [C]), ("maa_k", [C]), ("maa_v", [C]),
                        ("maa_r", [C]), ("maa_g", [C]), ("time_decay", [C]),
                        ("ln_g", [C]), ("ln_b", [C])]:
            D[n1] = nc.dram_tensor(n1, shp, F32, kind="ExternalInput").ap()
        D["maa_w1"] = nc.dram_tensor("maa_w1", [C, 160], F32, kind="ExternalInput").ap()
        D["maa_w2"] = nc.dram_tensor("maa_w2", [5, 32, C], F32, kind="ExternalInput").ap()
        for n1 in ["Wr", "Wk", "Wv", "Wo"]:
            D[n1] = nc.dram_tensor(n1, [C, C], F32, kind="ExternalInput").ap()
        D["gate_w1"] = nc.dram_tensor("gate_w1", [C, 64], F32, kind="ExternalInput").ap()
        D["gate_w2"] = nc.dram_tensor("gate_w2", [64, C], F32, kind="ExternalInput").ap()
        D["decay_w1"] = nc.dram_tensor("decay_w1", [C, 64], F32, kind="ExternalInput").ap()
        D["decay_w2"] = nc.dram_tensor("decay_w2", [64, C], F32, kind="ExternalInput").ap()
        D["u"] = nc.dram_tensor("u", [16, 64], F32, kind="ExternalInput").ap()
        D["wg_sh"] = nc.dram_tensor("wg_sh", [512, C], F32, kind="ExternalInput").ap()
        D["wu_sh"] = nc.dram_tensor("wu_sh", [512, C], F32, kind="ExternalInput").ap()
        D["wd_sh"] = nc.dram_tensor("wd_sh", [128, DIN], F32, kind="ExternalInput").ap()
        D["wsel"] = nc.dram_tensor("wsel", [8, 8], F32, kind="ExternalInput").ap()
        D["wbias"] = nc.dram_tensor("wbias", [1, 8], F32, kind="ExternalInput").ap()
        D["gin2"] = nc.dram_tensor("gin2", [16, C], F32, kind="ExternalInput").ap()
        D["out"] = nc.dram_tensor("out", [TLOC, C], F32, kind="ExternalOutput").ap()

        tset = set(self.taps)

        def dscr(name, shape, dt):
            kind = "ExternalOutput" if name.lstrip("c_") in tset or name in tset \
                else "Internal"
            return nc.dram_tensor(name, shape, dt, kind=kind).ap()
        S = {}
        S["abs_in"] = dscr("c_abs_in", [1, 4], F32)
        S["abs_out"] = dscr("c_abs_out", [1, 4], F32)
        S["wgT_sh"] = dscr("c_wgT_sh", [C, 512], BF16)
        S["wuT_sh"] = dscr("c_wuT_sh", [C, 512], BF16)
        S["wdT_sh"] = dscr("c_wdT_sh", [DIN, 128], BF16)
        S["wgT"] = dscr("c_wgT", [8 * C, 512], BF16)
        S["wuT"] = dscr("c_wuT", [8 * C, 512], BF16)
        S["wdT"] = dscr("c_wdT", [8 * DIN, 128], BF16)
        S["sum_in"] = dscr("c_sum_in", [128, 520], F32)
        S["sum_out"] = dscr("c_sum_out", [1024, 520], F32)
        S["xfm"] = dscr("c_xfm", [8 * 128, TLOC], F32)
        S["A"] = dscr("c_A", [NCH * 64, 1024], BF16)
        S["vtm"] = dscr("c_vtm", [NCH * 64, 1024], BF16)
        S["rt"] = dscr("c_rt", [NCH * 128, 512], BF16)
        S["sl"] = dscr("c_sl", [NCH * 128, 512], BF16)
        S["yd"] = dscr("c_yd", [NCH * 128, 512], BF16)
        S["g"] = dscr("c_g", [NCH * 128, 512], BF16)
        S["x1n"] = dscr("c_x1n", [8 * 128, TLOC], F32)
        S["hq"] = dscr("c_hq", [TLOC, DIN], BF16)
        S["dbg"] = dscr("c_dbg", [128, 40960], F32)
        S["wkb"] = dscr("c_wkb", [C, C], BF16)
        S["wvb"] = dscr("c_wvb", [C, C], BF16)
        S["wrb"] = dscr("c_wrb", [C, C], BF16)
        S["wob"] = dscr("c_wob", [C, C], BF16)
        S["w2b"] = dscr("c_w2b", [5 * 32, C], BF16)
        self.D, self.S = D, S

        E = self.E
        for s in ["sSY", "sSC", "sGP", "sCC", "sPE", "sDV", "sAC",
                  "cA", "cV", "cP", "cS", "gE", "sOUT"]:
            E.newsem(s)

        with self.es:
            self.arF = self.sb("arF", [128, 25600], F32)
            self.arB = self.sb("arB", [128, 35840], BF16)
            self._alloc_small()
            self.pb = [self.psum("pb%d" % i, [128, 512]) for i in range(6)]
            self.pba = self.psum("pba", [128, 512])
            self.pbt = self.psum("pbt", [64, 512], BF16)
            semh = {}
            for s in E.cnt:
                semh[s] = self.es.enter_context(nc.semaphore(s))
            self.block = self.es.enter_context(nc.Block())
            phases = [("ph0", self.ph0), ("ffnw", self.ph0_ffnw),
                      ("loop1", self.loop1), ("exchange", self.exchange),
                      ("loop2", self.loop2), ("ffn", self.ffn)]
            for nm, fn in phases:
                fn()
                sa = self.stop_after
                if sa and (sa == nm or sa.startswith(nm + "-")):
                    break
            self.finish()
            E.flush(self.block, semh)
        return nc

    def cF(self, off, cols):
        assert off + cols <= 25600, (off, cols)
        return self.arF[:, off:off + cols]

    def cB(self, off, cols):
        assert off + cols <= 35840, (off, cols)
        return self.arB[:, off:off + cols]

    def _alloc_small(self):
        sb = self.sb
        for nm in ["maax", "maaw", "maak", "maav", "maar", "maag", "td", "u", "lnb"]:
            setattr(self, "v_" + nm, sb("v_" + nm, [128, 8], F32))
        self.r_lng = sb("r_lng", [1, C], F32)
        self.r_lnb = sb("r_lnb", [1, C], F32)
        self.r_ones = sb("r_ones", [1, 128], F32)
        self.ones16 = sb("ones16", [1, 16], F32)
        self.onesc = sb("onesc", [128, 1], F32)
        self.identf = sb("identf", [128, 128], F32)
        self.identb = sb("identb", [128, 128], BF16)
        self.ones512 = sb("ones512", [128, TB], BF16)
        self.mask4 = self.cF(14672, 512)
        self.mask4u = sb("mask4u", [128, 512], mybir.dt.uint8)
        self.zeros_bf = sb("zeros_bf", [128, 512], BF16)
        self.ind_cc = sb("ind_cc", [16, 8 * 128], F32)
        self.gin_cc = self.ind_cc
        self.ione_cc = sb("ione_cc", [128, 8 * 16], F32)
        self.ione_b = sb("ione_b", [128, 8 * 16], BF16)
        self.ione16 = sb("ione16", [128, 16], F32)
        self.gin16 = self.cF(22528, 1024)[0:16, :]
        self.nl8 = sb("nl8", [128, 1], F32)
        self.scols = sb("scols", [128, 8 * NCH], F32)
        self.carryB = sb("carryB", [128, 8], F32)
        self.xcarry = sb("xcarry", [128, 8], F32)
        self.S32 = sb("S32", [128, 512], F32)
        self.Sin = sb("Sin", [128, 512], F32)
        self.dsx = sb("dsx", [128, 16], F32)
        self.wsc = sb("wsc", [1, 4], F32)
        self.wsc_r = sb("wsc_r", [1, 4], F32)
        self.wsel_sb = sb("wsel_sb", [8, 8], F32)
        self.wbias_sb = sb("wbias_sb", [1, 8], F32)
        self.swcol = sb("swcol", [128, 4], F32)
        self.idxI = self.cF(14336, 128)
        self.idxA = self.cF(14464, 64)[0:64, :]
        self.vv16 = self.cF(14528, 128)[0:16, :]
        self.vv128 = self.cF(14656, 16)
        self.logdt = sb("logdt", [128, 64], F32)
        self.sdcol = sb("sdcol", [128, 16], F32)
        self.ddrow = sb("ddrow", [1, 512], F32)
        self.swv = sb("swv", [128, 4], F32)
        self.epsg = sb("epsg", [128, 1], F32)
        self.epsl = sb("epsl", [128, 1], F32)
    def ph0(self):
        E, D, M = self.E, self.D, self.M

        for vt, key in [(self.v_maax, "maa_x"), (self.v_maaw, "maa_w"),
                        (self.v_maak, "maa_k"), (self.v_maav, "maa_v"),
                        (self.v_maar, "maa_r"), (self.v_maag, "maa_g"),
                        (self.v_td, "time_decay"), (self.v_lnb, "ln_b")]:
            E.dma("sync", "sSY", vt[:], D[key].rearrange("(c p) -> p c", p=128),
                  allow_slow_non_contiguous=True)
        E.dma("sync", "sSY", self.v_u[:],
              D["u"].rearrange("h n -> (h n)").rearrange("(c p) -> p c", p=128),
              allow_slow_non_contiguous=True)
        E.dma("sync", "sSY", self.r_lng[:], D["ln_g"].rearrange("(o c) -> o c", o=1))
        E.dma("sync", "sSY", self.r_lnb[:], D["ln_b"].rearrange("(o c) -> o c", o=1))
        E.dma("sync", "sSY", self.xcarry[:],
              D["xprev"].rearrange("o (c p) -> p (o c)", p=128),
              allow_slow_non_contiguous=True)
        E.dma("sync", "sSY", self.wsel_sb[:], D["wsel"][:, :])
        E.dma("sync", "sSY", self.wbias_sb[:], D["wbias"][:, :])
        M["vecs"] = E.mark("sSY")

        S = self.S
        self.w_l1 = self.sb("w_l1", [128, 8 * 160], BF16)
        self.w_g1 = self.sb("w_g1", [128, 8 * 64], BF16)
        self.w_g2 = self.sb("w_g2", [64, C], BF16)
        self.w_d1 = self.sb("w_d1", [128, 8 * 64], BF16)
        self.w_d2 = self.sb("w_d2", [64, C], BF16)
        for skey, key in [("wkb", "Wk"), ("wvb", "Wv"), ("wrb", "Wr"),
                          ("wob", "Wo")]:
            E.dma("gpsimd", "sGP", S[skey][:, :], D[key][:, :])
        E.dma("gpsimd", "sGP", S["w2b"].rearrange("(f r) c -> f r c", r=32),
              D["maa_w2"][:, :, :])
        M["wcast"] = E.mark("sGP")
        E.dma("gpsimd", "sGP", self.w_l1[:].rearrange("p (a d) -> p a d", d=160),
              D["maa_w1"].rearrange("(a p) d -> p a d", p=128))
        E.dma("gpsimd", "sGP", self.w_g1[:].rearrange("p (a d) -> p a d", d=64),
              D["gate_w1"].rearrange("(a p) d -> p a d", p=128))
        E.dma("gpsimd", "sGP", self.w_g2[:], D["gate_w2"][:, :])
        E.dma("gpsimd", "sGP", self.w_d1[:].rearrange("p (a d) -> p a d", d=64),
              D["decay_w1"].rearrange("(a p) d -> p a d", p=128))
        E.dma("gpsimd", "sGP", self.w_d2[:], D["decay_w2"][:, :])
        M["wload"] = E.mark("sGP")

        def iotas(e):
            e.iota(self.idxA[:], [[1, 64]], base=0, channel_multiplier=-1,
                   allow_small_or_imprecise_dtypes=True)
            e.iota(self.vv16[:, 0:64], [[0, 64]], base=0, channel_multiplier=1,
                   allow_small_or_imprecise_dtypes=True)
            e.iota(self.vv16[:, 64:128], [[0, 64]], base=-1, channel_multiplier=1,
                   allow_small_or_imprecise_dtypes=True)
            e.iota(self.vv128[0:64, :], [[1, 16]], base=0, channel_multiplier=0,
                   allow_small_or_imprecise_dtypes=True)
            e.iota(self.vv128[64:128, :], [[1, 16]], base=-1, channel_multiplier=0,
                   allow_small_or_imprecise_dtypes=True)
            return e.iota(self.idxI[:], [[1, 128]], base=0, channel_multiplier=-1,
                          allow_small_or_imprecise_dtypes=True)
        E.op("gpsimd", iotas, incs=[("sGP", 1)])
        M["iota"] = E.mark("sGP")

        def consts(e):
            for blk in range(8):
                e.tensor_single_scalar(
                    self.mask4[0:64, blk * 64:(blk + 1) * 64],
                    self.idxA[:], 0.0, ALU.is_gt)
                e.tensor_single_scalar(
                    self.mask4[64:128, blk * 64:(blk + 1) * 64],
                    self.idxA[:], 0.0, ALU.is_gt)
            e.memset(self.zeros_bf[:], 0.0)
            e.memset(self.r_ones[:], 1.0)
            e.memset(self.ones16[:], 1.0)
            e.memset(self.onesc[:], 1.0)
            e.memset(self.ones512[:], 1.0)
            e.memset(self.carryB[:], 0.0)
            e.memset(self.epsg[:], EPS_GN)
            e.memset(self.epsl[:], EPS_LN)
            e.memset(self.S32[:], 0.0)
            for cc in range(8):
                e.tensor_single_scalar(self.ind_cc[:, cc * 128:(cc + 1) * 128],
                                       self.vv16[:], float(2 * cc), ALU.is_equal)
                e.tensor_single_scalar(self.ione_cc[:, cc * 16:(cc + 1) * 16],
                                       self.vv128[:], float(2 * cc), ALU.is_equal)
            e.tensor_single_scalar(self.identf[:], self.idxI[:], 0.0, ALU.is_equal)
            e.memset(self.ione16[:], 0.0)
            for j in range(16):
                hb = (j % 2) * 64
                e.memset(self.ione16[hb:hb + 64, j:j + 1], 1.0)
            e.memset(self.nl8[:], -2.0794415416798357)
            e.tensor_copy(self.ione_b[:], self.ione_cc[:])
            e.tensor_copy(self.mask4u[:], self.mask4[:])
            return e.tensor_copy(self.identb[:], self.identf[:])
        E.op("vector", consts, waits=[("sGP", M["iota"])], incs=[("sDV", 1)])
        M["consts"] = E.mark("sDV")

        M["gin"] = 0

    def ph0_ffnw(self):
        E, D, S, M = self.E, self.D, self.S, self.M
        sh_g = self.cF(0, 4096)
        sh_u = self.cF(4096, 4096)
        sh_d = self.cF(8192, 4096)
        dmp = self.cF(12288, 1024)
        shq = self.cB(24576, 4096)
        trT = self.cB(28672, 512)
        E.dma("sync", "sSY", sh_g.rearrange("p (a c) -> p a c", c=C),
              D["wg_sh"].rearrange("(a p) c -> p a c", p=128))
        E.dma("sync", "sSY", sh_u.rearrange("p (a c) -> p a c", c=C),
              D["wu_sh"].rearrange("(a p) c -> p a c", p=128))
        E.dma("sync", "sSY", sh_d, D["wd_sh"][:, :])
        M["shards"] = E.mark("sSY")

        def absf(e):
            acc = self.sdcol
            for i in range(4):
                e.activation(dmp, sh_g[:, i * C:(i + 1) * C], AF.Abs,
                             accum_out=acc[:, i:i + 1])
            for i in range(4):
                e.activation(dmp, sh_u[:, i * C:(i + 1) * C], AF.Abs,
                             accum_out=acc[:, 4 + i:5 + i])
            for i in range(4):
                inst = e.activation(dmp, sh_d[:, i * C:(i + 1) * C], AF.Abs,
                                    accum_out=acc[:, 8 + i:9 + i])
            return inst
        E.op("scalar", absf, waits=[("sSY", M["shards"])], incs=[("sAC", 1)])
        M["abs"] = E.mark("sAC")

        E.op("tensor",
             lambda e: e.matmul(self.pb[4][0:1, 0:12], self.onesc[:],
                                self.sdcol[:, 0:12], start=True, stop=True),
             waits=[("sAC", M["abs"]), ("sDV", M["consts"])], incs=[("sPE", 1)])
        M["fold"] = E.mark("sPE")

        absum = self.sb("absum", [1, 4], F32)

        def sumf(e):
            e.tensor_reduce(absum[:, 0:3].rearrange("o (n u) -> o n u", u=1),
                            self.pb[4][0:1, 0:12].rearrange("o (g i) -> o g i", g=3),
                            AX.X, ALU.add)
            return e.memset(absum[:, 3:4], 0.0)
        E.op("vector", sumf, waits=[("sPE", M["fold"])], incs=[("sDV", 1)])
        M["absum"] = E.mark("sDV")

        E.dma("scalar", "sSC", S["abs_in"][:, :], absum[:],
              waits=[("sDV", M["absum"])])
        M["absst"] = E.mark("sSC")

        E.op("gpsimd",
             lambda e: e.collective_compute(
                 "AllReduce", ALU.add, replica_groups=[list(range(8))],
                 ins=[S["abs_in"].opt()], outs=[S["abs_out"].opt()]),
             waits=[("sSC", M["absst"])], incs=[("sCC", 1)])
        M["ccabs"] = E.mark("sCC")

        E.dma("sync", "sSY", self.wsc[:], S["abs_out"][:, :],
              waits=[("sCC", M["ccabs"])])
        M["wscl"] = E.mark("sSY")

        E.op("gpsimd",
             lambda e: e.tensor_scalar(self.wsc[:], self.wsc[:],
                                       1.0 / (4096.0 * 1024.0), 1e-5,
                                       ALU.mult, ALU.max),
             waits=[("sSY", M["wscl"])], incs=[("sGP", 1)])
        E.op("vector",
             lambda e: e.reciprocal(self.wsc_r[:], self.wsc[:]),
             waits=[("sGP", E.mark("sGP"))], incs=[("sDV", 1)])
        M["wscr"] = E.mark("sDV")

        E.op("tensor",
             lambda e: e.matmul(self.pb[4][:, 16:20], self.r_ones[0:1, 0:128],
                                self.wsc_r[:], start=True, stop=True),
             waits=[("sDV", M["wscr"])], incs=[("sPE", 1)])
        M["swbc"] = E.mark("sPE")

        E.op("vector", lambda e: e.tensor_copy(self.swcol[:], self.pb[4][:, 16:20]),
             waits=[("sPE", M["swbc"])], incs=[("sDV", 1)])
        M["swcol"] = E.mark("sDV")

        E.op("tensor",
             lambda e: e.matmul(self.pb[4][:, 20:24], self.r_ones[0:1, 0:128],
                                self.wsc[:], start=True, stop=True),
             waits=[("sDV", M["swcol"])], incs=[("sPE", 1)])
        M["swbc2"] = E.mark("sPE")
        E.op("vector", lambda e: e.tensor_copy(self.swv[:], self.pb[4][:, 20:24]),
             waits=[("sPE", M["swbc2"])], incs=[("sDV", 1)])
        M["swv"] = E.mark("sDV")

        def quant(src, dstq, col, ntiles):
            def f(e):
                for i in range(ntiles):
                    e.tensor_scalar(dmp, src[:, i * C:(i + 1) * C],
                                    self.swcol[:, col:col + 1], CRND, ALU.mult, ALU.add)
                    e.tensor_scalar(dmp, dmp, CRND, 1.0, ALU.subtract, ALU.min)
                    inst = e.tensor_scalar(dstq[:, i * C:(i + 1) * C], dmp, -1.0,
                                           None, ALU.max)
                return inst
            E.op("vector", f, waits=[("sDV", M["swv"]), ("sPE", E.mark("sPE"))],
                 incs=[("sDV", 1)])

        def transpose_store(srcq, tiles, dst):
            for (t, dr, dc) in tiles:
                dep_dv = E.mark("sDV")
                dep_sc = E.mark("sSC")
                E.op("tensor",
                     (lambda e, t=t: [
                         e.transpose(self.pbt[:, 0:128],
                                     srcq[:, t * 128:t * 128 + 64], self.identb[:]),
                         e.transpose(self.pbt[:, 128:256],
                                     srcq[:, t * 128 + 64:t * 128 + 128], self.identb[:])][-1]),
                     waits=[("sDV", dep_dv), ("sSC", dep_sc)], incs=[("sPE", 1)])
                E.op("vector",
                     (lambda e: [
                         e.tensor_copy(trT[0:64, 0:128], self.pbt[:, 0:128]),
                         e.tensor_copy(trT[64:128, 0:128], self.pbt[:, 128:256])][-1]),
                     waits=[("sPE", E.mark("sPE")), ("sSC", E.mark("sSC"))],
                     incs=[("sDV", 1)])
                E.dma("scalar", "sSC", dst[dr[0]:dr[1], dc[0]:dc[1]], trT[:, 0:128],
                      waits=[("sDV", E.mark("sDV"))])

        quant(sh_g, shq, 0, 4)
        transpose_store(shq, [(a * 8 + ci, (ci * 128, ci * 128 + 128),
                               (a * 128, a * 128 + 128))
                              for a in range(4) for ci in range(8)], S["wgT_sh"])
        quant(sh_u, shq, 1, 4)
        transpose_store(shq, [(a * 8 + ci, (ci * 128, ci * 128 + 128),
                               (a * 128, a * 128 + 128))
                              for a in range(4) for ci in range(8)], S["wuT_sh"])
        quant(sh_d, shq, 2, 4)
        transpose_store(shq, [(t, (t * 128, t * 128 + 128), (0, 128))
                              for t in range(32)], S["wdT_sh"])
        M["wqst"] = E.mark("sSC")

        for pair in [("wgT_sh", "wgT"), ("wuT_sh", "wuT"), ("wdT_sh", "wdT")]:
            E.op("gpsimd",
                 (lambda e, pair=pair: e.collective_compute(
                     "AllGather", ALU.bypass, replica_groups=[list(range(8))],
                     ins=[S[pair[0]].opt()], outs=[S[pair[1]].opt()])),
                 waits=[("sSC", M["wqst"])], incs=[("sCC", 1)])
        M["ccw"] = E.mark("sCC")
    def _l1stop(self, part, tb):
        sa = self.stop_after or ""
        if not sa.startswith("loop1-"):
            return False
        _, p, n = sa.split("-")
        return part == p and tb == int(n) - 1

    def loop1(self):
        E, D, S, M = self.E, self.D, self.S, self.M
        l1stopped = False
        cF, cB = self.cF, self.cB
        pb = self.pb

        xtm = cF(0, 4096)
        xfm = cF(4096, 4104)
        xv = xfm.rearrange("p (c n) -> p c n", n=513)
        xx = cF(8200, 4096)
        xxv = xx.rearrange("p (c n) -> p c n", n=TB)
        self.ek = [cF(8200, 512), cF(8712, 512)]
        self.er = [cF(9224, 512), cF(9736, 512)]
        B_ = cF(12304, 4104)
        Bv = B_.rearrange("p (c n) -> p c n", n=513)
        self.Bv = Bv
        self.ktb = cF(16408, 4096)
        self.rtb = cF(20504, 4096)
        tmp = cF(24600, 512)
        ew = tmp
        self.dBt = cF(25112, 64)
        qct = cF(25176, 64)
        self.bc2t = cF(25240, 64)
        self.bc1nt = cF(25304, 64)
        self.ehmt = cF(25368, 64)
        self.Dtt = cF(25432, 64)

        self.wstr8 = [cB(0, 8192), cB(8192, 8192)]
        mixbuf = cB(16384, 4096)
        self.vtb = cB(20480, 4096)
        lt = {f: cB(24576 + i * 512, 512) for i, f in enumerate("wkvrg")}
        self.dg1t = cB(27136, 512)
        gcc = cB(27648, 512)
        ydcc = cB(28160, 512)
        p3c = cB(28672, 512)
        self.ktil = [cB(29184, 512), cB(29696, 512)]
        self.rtil = [cB(30208, 512), cB(30720, 512)]
        self.khat = cB(31232, 512)
        self.slbf = cB(31744, 512)
        self.asel = cB(32256, 512)
        self.vtms = cB(32768, 1024)
        self.ktms = cB(33792, 1024)
        self.mixbuf = mixbuf
        self._gcc = gcc
        self.tmp = tmp
        self.ew = ew
        self.lt = lt

        fprev = {"w": None, "k": "w", "v": "k", "r": "v", "g": "r"}
        fnum = {"w": 0, "k": 1, "v": 2, "r": 3, "g": 4}
        w_l2f = self.sb("w_l2f", [32, C], BF16)

        E.op("vector", lambda e: e.memset(Bv[:, :, 0:1], 0.0),
             waits=[], incs=[("sDV", 1)])

        for tb in range(NTB):
            t0 = tb * TB
            chg0 = tb * NCHB

            xw = [("sPE", M.get("xtr_%d" % (tb - 1), 0))]
            if tb == 0:
                xw = [("sDV", E.mark("sDV")), ("sPE", E.mark("sPE"))]
            E.dma("sync", "sSY", xtm.rearrange("p (s c) -> p s c", c=C),
                  D["x"][t0:t0 + TB, :].rearrange("(s p) c -> p s c", p=128),
                  waits=xw)
            M["xload_%d" % tb] = E.mark("sSY")

            wkw = [("sPE", M.get("proj_r_%d" % (tb - 1), 0))]
            if tb == 0:
                wkw.append(("sGP", M["wcast"]))
            E.dma("sync", "sSY", self.wstr8[0].rearrange("p (a co) -> p a co", co=C),
                  S["wkb"].rearrange("(a p) co -> p a co", p=128),
                  waits=wkw)
            M["wld_k_%d" % tb] = E.mark("sSY")

            pems = []
            xev0 = E.mark("sAC")
            for cc in range(8):
                bank = pb[0] if cc % 2 == 0 else pb[1]
                w = [("sSY", M["xload_%d" % tb]),
                     ("sAC", M["xev_%d_%d" % (tb, cc - 2)] if cc >= 2 else xev0)]
                if tb == 0 and cc == 0:
                    w.append(("sDV", M["consts"]))

                def trf(e, cc=cc, bank=bank):
                    for s1 in range(4):
                        inst = e.transpose(bank[:, s1 * 128:(s1 + 1) * 128],
                                           xtm[:, s1 * C + cc * 128: s1 * C + cc * 128 + 128],
                                           self.identf[:])
                    return inst
                E.op("tensor", trf, waits=w, incs=[("sPE", 1)])
                pems.append(E.mark("sPE"))
                E.op("scalar",
                     (lambda e, cc=cc, bank=bank: e.copy(xv[:, cc, 1:513], bank[:, 0:512])),
                     waits=[("sPE", pems[cc]),
                            ("sSC", M.get("xspill_%d" % (tb - 1), 0)),
                            ("sDV", M.get("mix_g_%d" % (tb - 1), 0))],
                     incs=[("sAC", 1)])
                M["xev_%d_%d" % (tb, cc)] = E.mark("sAC")
            M["xtr_%d" % tb] = E.mark("sPE")

            def xxf(e, tb=tb):
                e.tensor_copy(xv[:, :, 0:1], self.xcarry[:].rearrange("p (c u) -> p c u", u=1))
                e.tensor_tensor(xxv[:, :, :], xv[:, :, 0:512], xv[:, :, 1:513], ALU.subtract)
                e.tensor_copy(self.xcarry[:].rearrange("p (c u) -> p c u", u=1), xv[:, :, 512:513])
                for cc in range(8):
                    inst = e.scalar_tensor_tensor(
                        mixbuf[:, cc * TB:(cc + 1) * TB],
                        xxv[:, cc, :], self.v_maax[:, cc:cc + 1],
                        xv[:, cc, 1:513], ALU.mult, ALU.add)
                return inst
            w = [("sAC", M["xev_%d_%d" % (tb, 7)]),
                 ("sPE", M.get("proj_g_%d" % (tb - 1), 0)),
                 ("cA", chg0)]
            if tb == 0:
                w += [("sSY", M["vecs"])]
            E.op("vector", xxf, waits=w, incs=[("sDV", 1)])
            M["xx_%d" % tb] = E.mark("sDV")

            for cc in range(8):
                E.dma("scalar", "sSC",
                      S["xfm"][cc * 128:(cc + 1) * 128, t0:t0 + TB], xv[:, cc, 1:513],
                      waits=[("sDV", M["xx_%d" % tb])] if cc == 0 else [])
            M["xspill_%d" % tb] = E.mark("sSC")
            if self._l1stop("xx", tb):
                l1stopped = True
                break

            def loraA(e):
                for ci in range(8):
                    inst = e.matmul(pb[4][0:96, :], self.w_l1[:, ci * 160:ci * 160 + 96],
                                    mixbuf[:, ci * TB:(ci + 1) * TB],
                                    start=(ci == 0), stop=(ci == 7))
                return inst
            w = [("sDV", M["xx_%d" % tb]), ("sAC", E.mark("sAC")),
                 ("cS", tb * NCHB)]
            if tb == 0:
                w += [("sGP", M["wload"])]
            E.op("tensor", loraA, waits=w, incs=[("sPE", 1)])
            mk_la = E.mark("sPE")

            def loraB(e):
                for ci in range(8):
                    inst = e.matmul(pb[5][0:64, :], self.w_l1[:, ci * 160 + 96:ci * 160 + 160],
                                    mixbuf[:, ci * TB:(ci + 1) * TB],
                                    start=(ci == 0), stop=(ci == 7))
                return inst
            E.op("tensor", loraB, waits=[("sDV", E.mark("sDV"))], incs=[("sPE", 1)])
            mk_lb = E.mark("sPE")

            def ltanh(e):
                e.activation(lt["w"][0:32, :], pb[4][0:32, :], AF.Tanh)
                e.activation(lt["k"][0:32, :], pb[4][32:64, :], AF.Tanh)
                e.activation(lt["v"][0:32, :], pb[4][64:96, :], AF.Tanh)
                e.activation(lt["r"][0:32, :], pb[5][0:32, :], AF.Tanh)
                return e.activation(lt["g"][0:32, :], pb[5][32:64, :], AF.Tanh)
            E.op("scalar", ltanh, waits=[("sPE", mk_lb), ("sPE", mk_la)],
                 incs=[("sAC", 1)])
            M["lt_%d" % tb] = E.mark("sAC")

            for f in "wkvrg":
                wl2w = [("sPE", M.get("mf_%s" % fprev[f], 0) if fprev[f] else 0)]
                if tb == 0 and f == "w":
                    wl2w.append(("sGP", M["wcast"]))
                E.dma("sync", "sSY", w_l2f[:],
                      S["w2b"][fnum[f] * 32:(fnum[f] + 1) * 32, :], waits=wl2w)
                mk_wl2 = E.mark("sSY")

                dvs = []
                for cc in range(8):
                    bank = pb[4] if cc % 2 == 0 else pb[5]
                    w = [("sAC", M["lt_%d" % tb]), ("sSY", mk_wl2)]
                    if cc >= 2:
                        w.append(("sDV", dvs[cc - 2]))
                    elif fprev[f]:
                        w.append(("sDV", M["mix_%s_%d" % (fprev[f], tb)]))
                    E.op("tensor",
                         (lambda e, cc=cc, bank=bank, f=f: e.matmul(
                             bank[:, :], w_l2f[:, cc * 128:(cc + 1) * 128],
                             lt[f][0:32, :], start=True, stop=True)),
                         waits=w, incs=[("sPE", 1)])
                    mkp = E.mark("sPE")

                    wmix = [("sPE", mkp)]
                    if cc == 0:
                        if f == "w":
                            wmix.append(("sPE", mk_lb))
                        else:
                            wmix.append(("sPE", M["proj_%s_%d" % (fprev[f], tb)]))

                    def mixf(e, cc=cc, bank=bank, f=f):
                        e.scalar_tensor_tensor(
                            tmp, bank[:, :], getattr(self, "v_maa" + f)[:, cc:cc + 1],
                            xxv[:, cc, :], ALU.add, ALU.mult)
                        return e.tensor_tensor(mixbuf[:, cc * TB:(cc + 1) * TB],
                                               tmp, xv[:, cc, 1:513], ALU.add)
                    E.op("vector", mixf, waits=wmix, incs=[("sDV", 1)])
                    dvs.append(E.mark("sDV"))
                M["mf_%s" % f] = E.mark("sPE")
                M["mix_%s_%d" % (f, tb)] = E.mark("sDV")

                self._proj(f, tb, mixbuf, lt)
                if self._l1stop("proj" + f, tb):
                    l1stopped = True
                    break
            if l1stopped:
                break

            def colf(e, tb=tb):
                for cc in range(8):
                    e.tensor_tensor(self.dBt[:, cc * 8:(cc + 1) * 8],
                                    Bv[:, cc, 64::64], Bv[:, cc, 0::64][:, 0:8],
                                    ALU.subtract)
                e.tensor_scalar(qct, self.dBt, 0.25, None, ALU.mult)
                for cc in range(8):
                    e.scalar_tensor_tensor(self.bc2t[:, cc * 8:(cc + 1) * 8],
                                           Bv[:, cc, 0::64][:, 0:8], 0.5,
                                           qct[:, cc * 8:(cc + 1) * 8],
                                           ALU.mult, ALU.add)
                e.tensor_scalar(self.bc1nt, self.bc2t, -1.0, None, ALU.mult)
                for cc in range(8):
                    e.tensor_scalar(
                        self.scols[:, cc * NCH + tb * 8: cc * NCH + tb * 8 + 8],
                        self.bc2t[:, cc * 8:(cc + 1) * 8], 2.0,
                        self.carryB[:, cc:cc + 1], ALU.mult, ALU.add)
                return e.tensor_tensor(self.carryB[:], self.carryB[:],
                                       Bv[:, :, 512], ALU.add)
            E.op("vector", colf,
                 waits=[("sDV", M["scan_%d" % tb]), ("cA", chg0), ("cV", chg0)],
                 incs=[("sDV", 1)])
            mk_cols = E.mark("sDV")

            E.op("scalar", lambda e: e.activation(self.ehmt, self.dBt, AF.Exp, scale=-0.5),
                 waits=[("sDV", mk_cols), ("cV", chg0)], incs=[("sAC", 1)])
            mk_ehm = E.mark("sAC")

            E.op("vector", lambda e: e.tensor_tensor(self.Dtt, self.ehmt, self.ehmt, ALU.mult),
                 waits=[("sAC", mk_ehm)], incs=[("sDV", 1)])
            M["cols_%d" % tb] = E.mark("sDV")
            if self._l1stop("cols", tb):
                l1stopped = True
                break

            for loc in range(NCHB):
                self._chunk(tb, loc, chg0 + loc)
                if getattr(self, "chunk_stop", None):
                    l1stopped = True
                    break
            if l1stopped:
                break

            if self._l1stop("chunks", tb):
                l1stopped = True
                break

            self._sdiag(tb, p3c, gcc, ydcc)
            if self._l1stop("sdiag", tb):
                l1stopped = True
                break

        M["l1pe_done"] = E.mark("sPE")
        if l1stopped:
            return
        def sumf(e):
            return e.tensor_scalar(tmp[:, 0:8], self.carryB[:], -1.0, None, ALU.mult)
        E.op("vector", sumf, waits=[("cS", NCH)], incs=[("sDV", 1)])
        mk_logd = E.mark("sDV")
        E.dma("scalar", "sSC", S["sum_in"][:, 0:512], self.S32[:],
              waits=[("sDV", mk_logd)])
        E.dma("scalar", "sSC", S["sum_in"][:, 512:520], tmp[:, 0:8])
        M["sumspill"] = E.mark("sSC")

    def _proj(self, f, tb, mixbuf, lt):
        E, D, S, M = self.E, self.D, self.S, self.M
        pb = self.pb
        mixmark = M["mix_%s_%d" % (f, tb)]
        Bv, ew = self.Bv, self.ew

        if f == "w":
            def d1f(e):
                for ci in range(8):
                    inst = e.matmul(pb[4][0:64, :], self.w_d1[:, ci * 64:(ci + 1) * 64],
                                    mixbuf[:, ci * TB:(ci + 1) * TB],
                                    start=(ci == 0), stop=(ci == 7))
                return inst
            E.op("tensor", d1f, waits=[("sDV", mixmark), ("sAC", E.mark("sAC"))],
                 incs=[("sPE", 1)])
            mk = E.mark("sPE")
            E.op("scalar", (lambda e: e.activation(self.dg1t[0:64, :], pb[4][0:64, :],
                                                   AF.Tanh)),
                 waits=[("sPE", mk)], incs=[("sAC", 1)])
            mk_t = E.mark("sAC")
            for cc in range(8):
                bank = pb[0] if cc % 2 == 0 else pb[1]
                E.op("tensor",
                     (lambda e, cc=cc, bank=bank: e.matmul(
                         bank[:, :], self.w_d2[:, cc * 128:(cc + 1) * 128],
                         self.dg1t[0:64, :], start=True, stop=True)),
                     waits=[("sAC", mk_t if cc < 2 else M["ewev_%d_%d" % (tb, cc - 2)])],
                     incs=[("sPE", 1)])
                mkp = E.mark("sPE")
                E.op("scalar",
                     (lambda e, cc=cc, bank=bank: e.activation(
                         ew, bank[:, :], AF.Exp, bias=self.v_td[:, cc:cc + 1])),
                     waits=[("sPE", mkp),
                            ("sDV", M.get("scan_c_%d_%d" % (tb, cc - 1), 0))],
                     incs=[("sAC", 1)])
                M["ewev_%d_%d" % (tb, cc)] = E.mark("sAC")
                E.op("vector",
                     (lambda e, cc=cc: e.tensor_tensor_scan(
                         Bv[:, cc, 1:513], ew, self.ones512[:], 0.0,
                         ALU.add, ALU.mult)),
                     waits=[("sAC", M["ewev_%d_%d" % (tb, cc)])], incs=[("sDV", 1)])
                M["scan_c_%d_%d" % (tb, cc)] = E.mark("sDV")
            M["scan_%d" % tb] = E.mark("sDV")
            M["proj_w_%d" % tb] = E.mark("sPE")
            return

        if f == "g":
            def g1f(e):
                for ci in range(8):
                    inst = e.matmul(pb[4][0:64, :], self.w_g1[:, ci * 64:(ci + 1) * 64],
                                    mixbuf[:, ci * TB:(ci + 1) * TB],
                                    start=(ci == 0), stop=(ci == 7))
                return inst
            E.op("tensor", g1f, waits=[("sDV", mixmark), ("sAC", E.mark("sAC"))],
                 incs=[("sPE", 1)])
            mk = E.mark("sPE")
            E.op("scalar", (lambda e: e.activation(self.dg1t[0:64, :], pb[4][0:64, :],
                                                   AF.Tanh)),
                 waits=[("sPE", mk)], incs=[("sAC", 1)])
            mk_t = E.mark("sAC")
            for cc in range(8):
                bank = pb[0] if cc % 2 == 0 else pb[1]
                E.op("tensor",
                     (lambda e, cc=cc, bank=bank: e.matmul(
                         bank[:, :], self.w_g2[:, cc * 128:(cc + 1) * 128],
                         self.dg1t[0:64, :], start=True, stop=True)),
                     waits=[("sAC", mk_t if cc < 2 else M["gev_%d_%d" % (tb, cc - 2)])],
                     incs=[("sPE", 1)])
                mkp = E.mark("sPE")
                E.op("scalar",
                     (lambda e, cc=cc, bank=bank: e.copy(self._gcc[:], bank[:, :])),
                     waits=[("sPE", mkp),
                            ("sSC", M.get("gsp_%d_%d" % (tb, cc - 1), 0))],
                     incs=[("sAC", 1)])
                M["gev_%d_%d" % (tb, cc)] = E.mark("sAC")
                E.dma("scalar", "sSC",
                      S["g"].rearrange("(ch p) n -> ch p n", p=128)[
                          tb * 8:tb * 8 + 8, :, cc * 64:(cc + 1) * 64].transpose([1, 0, 2]),
                      self._gcc[:].rearrange("p (ch n) -> p ch n", n=64),
                      waits=[("sAC", M["gev_%d_%d" % (tb, cc)])])
                M["gsp_%d_%d" % (tb, cc)] = E.mark("sSC")
            M["proj_g_%d" % tb] = E.mark("sPE")
            return

        wkey = {"k": "Wk", "v": "Wv", "r": "Wr"}[f]
        dst = {"k": self.ktb, "v": self.vtb, "r": self.rtb}[f]
        bufi = {"k": 0, "v": 1, "r": 0}[f]
        buf = self.wstr8[bufi]
        nxt = {"k": ("Wv", 1, "v"), "v": ("Wr", 0, "r"), "r": (None, None, None)}[f]
        if f == "k":
            pass
        if nxt[0] is not None:
            if nxt[1] == 0:
                wldw = [("sPE", M["proj_k_%d" % tb])]
            else:
                wldw = [("sPE", M.get("proj_v_%d" % (tb - 1), 0))]
            skey = {"Wv": "wvb", "Wr": "wrb"}[nxt[0]]
            E.dma("sync", "sSY",
                  self.wstr8[nxt[1]].rearrange("p (a co) -> p a co", co=C),
                  S[skey].rearrange("(a p) co -> p a co", p=128),
                  waits=wldw)
            M["wld_%s_%d" % (nxt[2], tb)] = E.mark("sSY")
        mk_w = M["wld_%s_%d" % (f, tb)]

        for half in range(2):
            hs = half * 256

            def pf(e, hs=hs, buf=buf):
                for co in range(8):
                    bank = pb[co // 2]
                    for ci in range(8):
                        inst = e.matmul(bank[:, (co % 2) * 256:(co % 2) * 256 + 256],
                                        buf[:, ci * C + co * 128: ci * C + co * 128 + 128],
                                        mixbuf[:, ci * TB + hs: ci * TB + hs + 256],
                                        start=(ci == 0), stop=(ci == 7))
                return inst
            E.op("tensor", pf,
                 waits=[("sSY", mk_w), ("sDV", mixmark), ("sAC", E.mark("sAC"))],
                 incs=[("sPE", 1)])
            mk_proj = E.mark("sPE")

            def evf(e, hs=hs, dst=dst):
                for co in range(8):
                    bank = pb[co // 2]
                    inst = e.copy(dst[:, co * TB + hs: co * TB + hs + 256],
                                  bank[:, (co % 2) * 256:(co % 2) * 256 + 256])
                return inst
            E.op("scalar", evf,
                 waits=[("sPE", mk_proj), ("cV", tb * NCHB),
                        ("sDV", M.get("sdiag_%d" % (tb - 1), 0))],
                 incs=[("sAC", 1)])
        M["proj_%s_%d" % (f, tb)] = E.mark("sPE")
        M["pvev_%s_%d" % (f, tb)] = E.mark("sAC")
    def _chunk(self, tb, loc, chg):
        E, S, M = self.E, self.S, self.M
        par = chg % 2
        pb4 = self.pb[4]
        Bv = self.Bv
        ek, er = self.ek[par], self.er[par]
        ktil, rtil = self.ktil[par], self.rtil[par]
        kv3 = self.ktb.rearrange("p (c n) -> p c n", n=TB)
        rv3 = self.rtb.rearrange("p (c n) -> p c n", n=TB)
        vv3 = self.vtb.rearrange("p (c n) -> p c n", n=TB)
        col = lambda t, cc: t[:, cc * 8 + loc: cc * 8 + loc + 1]

        def act_f(e):
            for cc in range(8):
                e.activation(ek[:, cc * 64:(cc + 1) * 64],
                             Bv[:, cc, loc * 64 + 1: loc * 64 + 65],
                             AF.Exp, scale=0.5, bias=col(self.bc1nt, cc))
                inst = e.activation(er[:, cc * 64:(cc + 1) * 64],
                                    Bv[:, cc, loc * 64: loc * 64 + 64],
                                    AF.Exp, scale=-0.5, bias=col(self.bc2t, cc))
            return inst
        E.op("scalar", act_f,
             waits=[("sDV", M["cols_%d" % tb]), ("cV", max(0, chg - 1))],
             incs=[("cA", 1)])
        if getattr(self, "chunk_stop", None) == "act":
            return

        def dve_f(e):
            e.tensor_tensor(ek, ek, ek, ALU.mult)
            e.tensor_tensor(er, er, er, ALU.mult)
            e.tensor_tensor(ktil.rearrange("p (c n) -> p c n", n=64),
                            kv3[:, :, loc * 64:(loc + 1) * 64],
                            ek.rearrange("p (c n) -> p c n", n=64), ALU.mult)
            e.tensor_tensor(rtil.rearrange("p (c n) -> p c n", n=64),
                            rv3[:, :, loc * 64:(loc + 1) * 64],
                            er.rearrange("p (c n) -> p c n", n=64), ALU.mult)
            for cc in range(8):
                e.tensor_scalar(self.khat[:, cc * 64:(cc + 1) * 64],
                                ktil[:, cc * 64:(cc + 1) * 64],
                                col(self.ehmt, cc), None, ALU.mult)
                inst = e.tensor_scalar(self.slbf[:, cc * 64:(cc + 1) * 64],
                                       self.S32[:, cc * 64:(cc + 1) * 64],
                                       col(self.ehmt, cc), None, ALU.mult)
            return inst
        E.op("vector", dve_f,
             waits=[("cA", chg + 1), ("cP", chg), ("cS", chg),
                    ("sSC", M.get("slsp_%d" % (chg - 1), 0))],
             incs=[("cV", 1)])
        mk_cv = E.mark("cV")
        if getattr(self, "chunk_stop", None) == "dve":
            return

        E.dma("scalar", "sSC",
              S["rt"].rearrange("(ch p) n -> ch p n", p=128)[chg],
              rtil[:], waits=[("cV", chg + 1)])
        M["rsp_%d" % chg] = E.mark("sSC")
        E.dma("scalar", "sSC",
              S["sl"].rearrange("(ch p) n -> ch p n", p=128)[chg],
              self.slbf[:])
        M["slsp_%d" % chg] = E.mark("sSC")
        if getattr(self, "chunk_stop", None) == "spill":
            return

        def a_mm(e):
            for cc in range(8):
                e.matmul(self.pba[0:64, cc * 64:(cc + 1) * 64],
                         ktil[0:64, cc * 64:(cc + 1) * 64],
                         rtil[0:64, cc * 64:(cc + 1) * 64],
                         start=True, stop=True, tile_position=(0, 0))
                inst = e.matmul(self.pba[64:128, cc * 64:(cc + 1) * 64],
                                ktil[64:128, cc * 64:(cc + 1) * 64],
                                rtil[64:128, cc * 64:(cc + 1) * 64],
                                start=True, stop=True, tile_position=(64, 64))
            return inst
        E.op("tensor", a_mm,
             waits=[("cV", chg + 1), ("gE", chg)],
             incs=[("sPE", 1)])
        mk_a = E.mark("sPE")

        E.op("vector",
             (lambda e: e.select(self.asel[:], self.mask4u[:],
                                 self.pba[:, :], self.zeros_bf[:])),
             waits=[("sPE", mk_a),
                    ("sSC", M.get("asp_%d" % (chg - 1), 0))],
             incs=[("gE", 1)])
        mk_sel = E.mark("gE")
        av = S["A"].rearrange("(ch s) n -> ch s n", s=64)
        E.dma("scalar", "sSC", av[chg, :, 0:512], self.asel[0:64, :],
              waits=[("gE", mk_sel)])
        E.dma("scalar", "sSC", av[chg, :, 512:1024], self.asel[64:128, :])
        M["asp_%d" % chg] = E.mark("sSC")

        for (srcname, dstt, mkey) in [("v", self.vtms, "vtm"), ("kh", self.ktms, "ktm")]:
            for half in range(2):
                def tr_f(e, srcname=srcname, half=half):
                    for i in range(4):
                        cc = half * 4 + i
                        if srcname == "v":
                            src = vv3[:, cc, loc * 64:(loc + 1) * 64]
                        else:
                            src = self.khat[:, cc * 64:(cc + 1) * 64]
                        inst = e.transpose(self.pbt[:, i * 128:(i + 1) * 128],
                                           src, self.identb[:])
                    return inst
                E.op("tensor", tr_f,
                     waits=[("cV", chg + 1),
                            ("sAC", M.get("trev_prev", 0)),
                            ("sSC", M.get("vtmsp_%d" % (chg - 1), 0)
                             if srcname == "v" and half == 0 else 0)],
                     incs=[("sPE", 1)])
                mk_tr = E.mark("sPE")
                E.op("scalar",
                     (lambda e, dstt=dstt, half=half: e.copy(
                         dstt[0:64, half * 512:(half + 1) * 512], self.pbt[:, :])),
                     waits=[("sPE", mk_tr)], incs=[("sAC", 1)])
                M["trev_prev"] = E.mark("sAC")

        E.dma("scalar", "sSC",
              S["vtm"].rearrange("(ch s) n -> ch s n", s=64)[chg],
              self.vtms[0:64, :],
              waits=[("sAC", M["trev_prev"])])
        M["vtmsp_%d" % chg] = E.mark("sSC")

        if getattr(self, "chunk_stop", None) == "tr":
            return

        def st_mm(e):
            for cc in range(8):
                e.matmul(pb4[0:64, cc * 64:(cc + 1) * 64],
                         self.ktms[0:64, cc * 128: cc * 128 + 64],
                         self.vtms[0:64, cc * 128: cc * 128 + 64],
                         start=True, stop=True, tile_position=(0, 0))
                inst = e.matmul(pb4[64:128, cc * 64:(cc + 1) * 64],
                                self.ktms[0:64, cc * 128 + 64: cc * 128 + 128],
                                self.vtms[0:64, cc * 128 + 64: cc * 128 + 128],
                                start=True, stop=True, tile_position=(0, 64))
            return inst
        E.op("tensor", st_mm,
             waits=[("sAC", M["trev_prev"]), ("cS", chg)],
             incs=[("cP", 1)])

        def st_up(e):
            for cc in range(8):
                inst = e.scalar_tensor_tensor(
                    self.S32[:, cc * 64:(cc + 1) * 64],
                    self.S32[:, cc * 64:(cc + 1) * 64], col(self.Dtt, cc),
                    pb4[:, cc * 64:(cc + 1) * 64], ALU.mult, ALU.add)
            return inst
        E.op("vector", st_up, waits=[("cP", chg + 1)], incs=[("cS", 1)])

    def _sdiag(self, tb, p3c, gcc, ydcc):
        E, S, M = self.E, self.S, self.M
        kv3 = self.ktb.rearrange("p (c n) -> p c n", n=TB)
        rv3 = self.rtb.rearrange("p (c n) -> p c n", n=TB)
        vv3 = self.vtb.rearrange("p (c n) -> p c n", n=TB)
        ssl = self.tmp

        mks = []
        for cc in range(8):
            wp3 = [("cS", (tb + 1) * NCHB)] if cc == 0 else \
                [("sPE", mks[cc - 1])]
            E.op("vector",
                 (lambda e, cc=cc: e.scalar_tensor_tensor(
                     p3c[:], rv3[:, cc, :], self.v_u[:, cc:cc + 1],
                     kv3[:, cc, :], ALU.mult, ALU.mult)),
                 waits=wp3,
                 incs=[("sDV", 1)])
            mkd = E.mark("sDV")
            E.op("tensor",
                 (lambda e, cc=cc: e.matmul(
                     self.pb[5][0:16, :], self.ione_b[:, cc * 16:(cc + 1) * 16],
                     p3c[:], start=(cc == 0), stop=(cc == 7))),
                 waits=[("sDV", mkd)] + ([("sAC", E.mark("sAC"))] if cc == 0 else []),
                 incs=[("sPE", 1)])
            mks.append(E.mark("sPE"))

        E.op("scalar", (lambda e: e.copy(ssl[0:16, :], self.pb[5][0:16, :])),
             waits=[("sPE", mks[-1])], incs=[("sAC", 1)])
        mk_ssl = E.mark("sAC")

        for cc in range(8):
            bank = self.pb[2] if cc % 2 == 0 else self.pb[3]
            w = [("sAC", mk_ssl)]
            if cc >= 2:
                w.append(("sDV", M["ydtt_%d_%d" % (tb, cc - 2)]))
            E.op("tensor",
                 (lambda e, cc=cc, bank=bank: e.matmul(
                     bank[:, :], self.ind_cc[:, cc * 128:(cc + 1) * 128],
                     ssl[0:16, :], start=True, stop=True)),
                 waits=w, incs=[("sPE", 1)])
            mkb = E.mark("sPE")
            E.op("vector",
                 (lambda e, cc=cc, bank=bank: e.tensor_tensor(
                     ydcc[:], vv3[:, cc, :], bank[:, :], ALU.mult)),
                 waits=[("sPE", mkb),
                        ("sSC", M.get("ydsp_%d_%d" % (tb, cc - 1), 0))],
                 incs=[("sDV", 1)])
            M["ydtt_%d_%d" % (tb, cc)] = E.mark("sDV")
            E.dma("scalar", "sSC",
                  S["yd"].rearrange("(ch p) n -> ch p n", p=128)[
                      tb * 8:tb * 8 + 8, :, cc * 64:(cc + 1) * 64].transpose([1, 0, 2]),
                  ydcc[:].rearrange("p (ch n) -> p ch n", n=64),
                  waits=[("sDV", M["ydtt_%d_%d" % (tb, cc)])])
            M["ydsp_%d_%d" % (tb, cc)] = E.mark("sSC")
        M["sdiag_%d" % tb] = E.mark("sDV")
    def exchange(self):
        E, S, M = self.E, self.S, self.M
        for cc in range(8):
            E.op("tensor",
                 (lambda e, cc=cc: e.matmul(self.pb[5][0:16, 0:128], self.ones16[:],
                                            self.r_lng[:, cc * 128:(cc + 1) * 128],
                                            start=True, stop=True)),
                 waits=[("sAC", E.mark("sAC")), ("sDV", E.mark("sDV")),
                        ("sPE", E.mark("sPE"))] if cc == 0 else
                       [("sDV", E.mark("sDV"))],
                 incs=[("sPE", 1)])
            E.op("vector",
                 (lambda e, cc=cc: e.tensor_tensor(
                     self.gin_cc[:, cc * 128:(cc + 1) * 128],
                     self.ind_cc[:, cc * 128:(cc + 1) * 128],
                     self.pb[5][0:16, 0:128], ALU.mult)),
                 waits=[("sPE", E.mark("sPE"))], incs=[("sDV", 1)])
        E.op("gpsimd",
             lambda e: e.collective_compute(
                 "AllGather", ALU.bypass, replica_groups=[list(range(8))],
                 ins=[S["sum_in"].opt()], outs=[S["sum_out"].opt()]),
             waits=[("sSC", M["sumspill"])], incs=[("sCC", 1)])
        M["ccsum"] = E.mark("sCC")

        E.dma("sync", "sSY", self.logdt[:].rearrange("p (m c) -> p m c", c=8),
              S["sum_out"][:, 512:520].rearrange("(m p) c -> p m c", p=128),
              waits=[("sCC", M["ccsum"])], allow_slow_non_contiguous=True)
        mk_ld = E.mark("sSY")
        ldv = self.logdt.rearrange("p (m c) -> p m c", c=8)

        self.ldT = self.cF(14144, 128)[0:8, :]
        self.wfac = self.cF(14272, 64)
        psin = self.cF(0, 512)
        facts = self.cF(13696, 256)

        for cc in range(8):
            E.op("tensor",
                 (lambda e, cc=cc: e.transpose(self.pb[5][0:8, 0:128],
                                               ldv[:, :, cc], self.identf[:])),
                 waits=[("sSY", mk_ld)] + ([("sAC", E.mark("sAC"))] if cc == 0 else
                                           [("sAC", M["ldT_%d" % (cc - 1)])]),
                 incs=[("sPE", 1)])
            mkp = E.mark("sPE")
            E.op("scalar",
                 (lambda e: e.copy(self.ldT[:], self.pb[5][0:8, 0:128])),
                 waits=[("sPE", mkp)], incs=[("sAC", 1)])
            M["ldT_%d" % cc] = E.mark("sAC")

            def emm(e, cc=cc):
                e.matmul(self.pb[4][:, 0:8], self.ldT[:], self.wsel_sb[:],
                         start=True, stop=False)
                return e.matmul(self.pb[4][:, 0:8], self.r_ones[0:1, 0:128],
                                self.wbias_sb[:], start=False, stop=True)
            E.op("tensor", emm, waits=[("sAC", M["ldT_%d" % cc]),
                                       ("sDV", M.get("wfac_%d" % (cc - 1), 0))],
                 incs=[("sPE", 1)])
            mkp2 = E.mark("sPE")
            E.op("scalar",
                 (lambda e, cc=cc: e.activation(self.wfac[:, cc * 8:(cc + 1) * 8],
                                                self.pb[4][:, 0:8], AF.Exp)),
                 waits=[("sPE", mkp2)], incs=[("sAC", 1)])
            M["wfac_%d" % cc] = E.mark("sAC")
        mk_wf = M["wfac_7"]

        E.op("vector", lambda e: e.memset(self.Sin[:], 0.0),
             waits=[], incs=[("sDV", 1)])
        for j in range(8):
            E.dma("sync", "sSY", psin,
                  S["sum_out"][j * 128:(j + 1) * 128, 0:512],
                  waits=[("sDV", E.mark("sDV"))] if j >= 2 else [])
            mk_pl = E.mark("sSY")

            def acc(e, j=j):
                for cc in range(8):
                    inst = e.scalar_tensor_tensor(
                        self.Sin[:, cc * 64:(cc + 1) * 64],
                        psin[:, cc * 64:(cc + 1) * 64],
                        self.wfac[:, cc * 8 + j: cc * 8 + j + 1],
                        self.Sin[:, cc * 64:(cc + 1) * 64], ALU.mult, ALU.add)
                return inst
            E.op("vector", acc, waits=[("sSY", mk_pl), ("sAC", mk_wf)],
                 incs=[("sDV", 1)])
        M["sin"] = E.mark("sDV")

        E.op("scalar", lambda e: e.activation(facts, self.scols[:], AF.Exp, scale=-1.0),
             waits=[("cV", NCH)], incs=[("sAC", 1)])
        M["facts"] = E.mark("sAC")
        self.facts = facts

    def _l2stop(self, part):
        return getattr(self, "l2_stop", None) == part

    def loop2(self):
        E, D, S, M = self.E, self.D, self.S, self.M
        l2stopped = False
        cF, cB = self.cF, self.cB
        pb = self.pb
        xfms = cF(0, 4096)
        xfv = xfms.rearrange("p (c n) -> p c n", n=TB)
        x1 = cF(4096, 4096)
        x1v = x1.rearrange("p (c n) -> p c n", n=TB)
        x1tm = cF(8192, 4096)
        ychp = [cF(14336, 512), cF(14848, 512)]
        ysqp = [cF(15360, 512), cF(15872, 512)]
        stp = [cF(16384, 3072), cF(19456, 3072)]
        lncol = cF(14080, 64)
        xq = cB(0, 16384)
        wo = cB(16384, 8192)
        a_s = cB(24576, 1024)
        v_s = cB(25600, 1024)
        r_s = cB(26624, 1024)
        sl_sp = [cB(27648, 512), cB(34304, 512)]
        yd_sp = [cB(28160, 512), cB(34816, 512)]
        g_sp = [cB(28672, 512), cB(35328, 512)]
        woin = cB(29184, 4096)
        sfull = cB(33280, 1024)
        self.xq = xq

        E.dma("sync", "sSY", wo.rearrange("p (a co) -> p a co", co=C),
              S["wob"].rearrange("(a p) co -> p a co", p=128),
              waits=[("sPE", M.get("l1pe_done", 0))])
        M["wo_ld"] = E.mark("sSY")

        for tb in range(NTB):
            t0 = tb * TB
            E.dma("sync", "sSY", xfms.rearrange("p (c n) -> p c n", n=TB),
                  S["xfm"][:, t0:t0 + TB].rearrange("(c p) n -> p c n", p=128),
                  waits=[("sSC", M["sumspill"]), ("sDV", M["sin"])] if tb == 0
                  else [("sSC", M["ln1_%d" % (tb - 1)])])
            M["xs_%d" % tb] = E.mark("sSY")

            for loc in range(NCHB):
                chg = tb * NCHB + loc
                p = chg % 2
                pbase = 0
                sl_s, yd_s, g_s = sl_sp[0], yd_sp[p], g_sp[p]
                ych, ysq, st = ychp[p], ysqp[p], stp[p]
                sb0 = 16384 + p * 3072
                mu2 = self.cF(sb0, 64)[0:16, :]
                msq = self.cF(sb0 + 512, 64)[0:16, :]
                var = self.cF(sb0 + 1024, 64)[0:16, :]
                lnv = self.cF(sb0 + 1536, 64)[0:16, :]
                rstd = self.cF(sb0 + 2048, 64)[0:16, :]
                qng = self.cF(sb0 + 2560, 64)[0:16, :]
                pbY = pb[p]
                pbP = pb[2 + 2 * p]
                pbQ = pb[3 + 2 * p]

                wld = [("sSC", M["sumspill"])] if chg == 0 else \
                    [("sPE", M.get("ymm_%d" % (chg - 1), 0))]
                av2 = S["A"].rearrange("(ch s) n -> ch s n", s=64)
                vv2 = S["vtm"].rearrange("(ch s) n -> ch s n", s=64)
                rv2 = S["rt"].rearrange("(ch p) n -> ch p n", p=128)
                E.dma("sync", "sSY", a_s[pbase:pbase + 64, :], av2[chg], waits=wld)
                E.dma("sync", "sSY", v_s[pbase:pbase + 64, :], vv2[chg])
                E.dma("sync", "sSY", r_s[pbase:pbase + 64, 0:512],
                      rv2[chg, 0:64, :])
                E.dma("sync", "sSY", r_s[pbase:pbase + 64, 512:1024],
                      rv2[chg, 64:128, :])
                E.dma("sync", "sSY", sl_s[:],
                      S["sl"].rearrange("(ch p) n -> ch p n", p=128)[chg])
                mk_ld = E.mark("sSY")
                E.dma("sync", "sSY", yd_s[:],
                      S["yd"].rearrange("(ch p) n -> ch p n", p=128)[chg],
                      waits=[("sDV", M.get("yev_%d" % (chg - 2), 0))])
                mk_ldyd = E.mark("sSY")
                E.dma("sync", "sSY", g_s[:],
                      S["g"].rearrange("(ch p) n -> ch p n", p=128)[chg],
                      waits=[("sDV", M.get("app_%d" % (chg - 2), 0))])
                mk_ldg = E.mark("sSY")

                def sff(e, chg=chg, pbase=pbase, sl_s=sl_s):
                    for cc in range(8):
                        e.scalar_tensor_tensor(
                            sfull[pbase:pbase + 64, cc * 64:(cc + 1) * 64],
                            self.Sin[0:64, cc * 64:(cc + 1) * 64],
                            self.facts[0:64, cc * 32 + chg: cc * 32 + chg + 1],
                            sl_s[0:64, cc * 64:(cc + 1) * 64], ALU.mult, ALU.add)
                        inst = e.scalar_tensor_tensor(
                            sfull[pbase:pbase + 64, 512 + cc * 64: 512 + (cc + 1) * 64],
                            self.Sin[64:128, cc * 64:(cc + 1) * 64],
                            self.facts[64:128, cc * 32 + chg: cc * 32 + chg + 1],
                            sl_s[64:128, cc * 64:(cc + 1) * 64], ALU.mult, ALU.add)
                    return inst
                E.op("vector", sff,
                     waits=[("sSY", mk_ld), ("sAC", M["facts"]),
                            ("sPE", M.get("ymm_%d" % (chg - 1), 0))],
                     incs=[("sDV", 1)])
                mk_sf = E.mark("sDV")

                def ymm(e, chg=chg, pbase=pbase, pbY=pbY):
                    tpA = (pbase, 0)
                    tpB = (pbase, 64)
                    for cc in range(8):
                        e.matmul(pbY[0:64, cc * 64:(cc + 1) * 64],
                                 v_s[pbase:pbase + 64, cc * 128: cc * 128 + 64],
                                 a_s[pbase:pbase + 64, cc * 64:(cc + 1) * 64],
                                 start=True, stop=False, tile_position=tpA)
                        e.matmul(pbY[0:64, cc * 64:(cc + 1) * 64],
                                 sfull[pbase:pbase + 64, cc * 64:(cc + 1) * 64],
                                 r_s[pbase:pbase + 64, cc * 64:(cc + 1) * 64],
                                 start=False, stop=True, tile_position=tpA)
                        e.matmul(pbY[64:128, cc * 64:(cc + 1) * 64],
                                 v_s[pbase:pbase + 64, cc * 128 + 64: cc * 128 + 128],
                                 a_s[pbase:pbase + 64, 512 + cc * 64: 512 + (cc + 1) * 64],
                                 start=True, stop=False, tile_position=tpB)
                        inst = e.matmul(
                            pbY[64:128, cc * 64:(cc + 1) * 64],
                            sfull[pbase:pbase + 64, 512 + cc * 64: 512 + (cc + 1) * 64],
                            r_s[pbase:pbase + 64, 512 + cc * 64: 512 + (cc + 1) * 64],
                            start=False, stop=True, tile_position=tpB)
                    return inst
                ymw = [("sDV", mk_sf), ("sDV", M.get("yev_%d" % (chg - 2), 0))]
                if loc <= 1:
                    ymw.append(("sDV", M.get("x1_1_%d" % (tb - 1), 0)))
                E.op("tensor", ymm, waits=ymw, incs=[("sPE", 1)])
                M["ymm_%d" % chg] = E.mark("sPE")
                if self._l2stop("ymm"):
                    l2stopped = True
                    break

                E.op("vector",
                     (lambda e, ych=ych, yd_s=yd_s, pbY=pbY: e.tensor_tensor(
                         ych, pbY[:, :], yd_s[:], ALU.add)),
                     waits=[("sPE", M["ymm_%d" % chg]), ("sSY", mk_ldyd)],
                     incs=[("sDV", 1)])
                M["yev_%d" % chg] = E.mark("sDV")
                if "dbg" in self.taps:
                    E.dma("scalar", "sSC", S["dbg"][:, chg * 512:(chg + 1) * 512],
                          ych, waits=[("sDV", M["yev_%d" % chg])])
                if self._l2stop("yev"):
                    l2stopped = True
                    break

                E.op("scalar",
                     (lambda e, ych=ych, ysq=ysq: e.activation(ysq, ych, AF.Square)),
                     waits=[("sDV", M["yev_%d" % chg]),
                            ("sPE", M.get("smm_%d" % (chg - 2), 0))],
                     incs=[("sAC", 1)])
                mk_sq = E.mark("sAC")

                def smm(e, ych=ych, ysq=ysq, pbP=pbP, pbQ=pbQ):
                    for cc in range(8):
                        e.matmul(pbP[0:16, 0:64],
                                 self.ione_cc[:, cc * 16:(cc + 1) * 16],
                                 ych[:, cc * 64:(cc + 1) * 64],
                                 start=(cc == 0), stop=(cc == 7))
                    for cc in range(8):
                        inst = e.matmul(pbQ[0:16, 0:64],
                                        self.ione_cc[:, cc * 16:(cc + 1) * 16],
                                        ysq[:, cc * 64:(cc + 1) * 64],
                                        start=(cc == 0), stop=(cc == 7))
                    return inst
                smw = [("sAC", mk_sq), ("sDV", M.get("app_%d" % (chg - 2), 0))]
                if loc == 0:
                    smw.append(("sDV", M.get("x1_1_%d" % (tb - 1), 0)))
                elif loc == 1:
                    smw.append(("sAC", M.get("l1q_ac_%d" % (tb - 1), 0)))
                if "dbg4" in self.taps:
                    smw.append(("sSC", M.get("dbg4_%d" % (chg - 2), 0)))
                E.op("tensor", smm, waits=smw, incs=[("sPE", 1)])
                M["smm_%d" % chg] = E.mark("sPE")
                mk_smm = M["smm_%d" % chg]

                def statf(e, mu2=mu2, msq=msq, var=var, pbP=pbP, pbQ=pbQ):
                    e.tensor_scalar(mu2, pbP[0:16, 0:64],
                                    1.0 / 64.0, None, ALU.mult)
                    e.tensor_tensor(msq, mu2, mu2, ALU.mult)
                    return e.scalar_tensor_tensor(
                        var, pbQ[0:16, 0:64], 1.0 / 64.0,
                        msq, ALU.mult, ALU.subtract)
                E.op("vector", statf, waits=[("sPE", mk_smm)], incs=[("sDV", 1)])
                M["statf_%d" % chg] = E.mark("sDV")
                mk_st = M["statf_%d" % chg]

                def lnf(e, lnv=lnv, var=var, rstd=rstd):
                    e.activation(lnv, var, AF.Ln, bias=self.epsg[0:16, :])
                    return e.activation(rstd, lnv, AF.Exp, scale=-0.5)
                E.op("scalar", lnf, waits=[("sDV", mk_st)], incs=[("sAC", 1)])
                mk_ln = E.mark("sAC")

                E.op("vector",
                     (lambda e, qng=qng, mu2=mu2, rstd=rstd: e.scalar_tensor_tensor(
                         qng, mu2, -1.0, rstd, ALU.mult, ALU.mult)),
                     waits=[("sAC", mk_ln)], incs=[("sDV", 1)])
                M["stats_%d" % chg] = E.mark("sDV")
                if "dbg4" in self.taps and chg < 8:
                    E.dma("scalar", "sSC",
                          S["dbg"][0:16, chg * 3072:(chg + 1) * 3072],
                          self.cF(16384 + p * 3072, 3072)[0:16, :],
                          waits=[("sDV", M["stats_%d" % chg])])
                    M["dbg4_%d" % chg] = E.mark("sSC")
                if self._l2stop("stats"):
                    l2stopped = True
                    break

                def pqmm(e, pbP=pbP, pbQ=pbQ, rstd=rstd, qng=qng):
                    for cc in range(8):
                        e.matmul(pbP[:, cc * 64:(cc + 1) * 64],
                                 self.gin_cc[:, cc * 128:(cc + 1) * 128],
                                 rstd[:, :],
                                 start=True, stop=True)
                        e.matmul(pbQ[:, cc * 64:(cc + 1) * 64],
                                 self.gin_cc[:, cc * 128:(cc + 1) * 128],
                                 qng[:, :],
                                 start=True, stop=False)
                        inst = e.matmul(pbQ[:, cc * 64:(cc + 1) * 64],
                                        self.r_lnb[0:1, cc * 128:(cc + 1) * 128],
                                        self.r_ones[0:1, 0:64],
                                        start=False, stop=True)
                    return inst
                wpq = [("sDV", M["stats_%d" % chg]),
                       ("sDV", M.get("app_%d" % (chg - 2), 0))]
                if loc == 0:
                    wpq += [("sAC", M.get("l1q_ac_%d" % (tb - 1), 0)),
                            ("sDV", M.get("x1_1_%d" % (tb - 1), 0))]
                elif loc == 1:
                    wpq.append(("sAC", M.get("l1q_ac_%d" % (tb - 1), 0)))
                E.op("tensor", pqmm, waits=wpq, incs=[("sPE", 1)])
                mk_pq = E.mark("sPE")

                def appf(e, loc=loc, ych=ych, ysq=ysq, g_s=g_s, pbP=pbP, pbQ=pbQ):
                    e.tensor_tensor(ysq, ych, pbP[:, :], ALU.mult)
                    e.tensor_tensor(ych, ysq, pbQ[:, :], ALU.add)
                    wv = woin.rearrange("p (c n) -> p c n", n=TB)[:, :, loc * 64:(loc + 1) * 64]
                    return e.tensor_tensor(
                        wv, ych.rearrange("p (c n) -> p c n", n=64),
                        g_s.rearrange("p (c n) -> p c n", n=64), ALU.mult)
                wapp = [("sPE", mk_pq), ("sSY", mk_ldg)]
                if loc == 0:
                    wapp.append(("sPE", M.get("xqpe_%d" % (tb - 1), 0)))
                E.op("vector", appf, waits=wapp, incs=[("sDV", 1)])
                M["app_%d" % chg] = E.mark("sDV")
                if "dbg5" in self.taps:
                    E.dma("scalar", "sSC", S["dbg"][:, chg * 512:(chg + 1) * 512],
                          ych, waits=[("sDV", M["app_%d" % chg])])
                if self._l2stop("app"):
                    l2stopped = True
                    break

            if "dbg2" in self.taps:
                E.dma("scalar", "sSC",
                      S["dbg"].bitcast(BF16)[:, tb * 8192:tb * 8192 + 4096],
                      woin, waits=[("sDV", M["app_%d" % (tb * NCHB + NCHB - 1)])])
            if l2stopped:
                break
            for half in range(2):
                hs = half * 256

                def womm(e, hs=hs):
                    for co in range(8):
                        bank = pb[co // 2]
                        for ci in range(8):
                            inst = e.matmul(
                                bank[:, (co % 2) * 256:(co % 2) * 256 + 256],
                                wo[:, ci * C + co * 128: ci * C + co * 128 + 128],
                                woin[:, ci * TB + hs: ci * TB + hs + 256],
                                start=(ci == 0), stop=(ci == 7))
                    return inst
                w = [("sDV", M["app_%d" % (tb * NCHB + NCHB - 1)]),
                     ("sSY", M["wo_ld"]),
                     ("sDV", M.get("yev_%d" % (tb * NCHB + NCHB - 1), 0))]
                if half == 0:
                    w.append(("sDV", M.get("x1_%s_%d" % (1, tb - 1), 0)))
                else:
                    w.append(("sDV", M.get("x1_%s_%d" % (0, tb), 0)))
                E.op("tensor", womm, waits=w, incs=[("sPE", 1)])
                mk_wo = E.mark("sPE")

                def x1f(e, hs=hs):
                    for co in range(8):
                        bank = pb[co // 2]
                        inst = e.tensor_tensor(
                            x1v[:, co, hs:hs + 256], xfv[:, co, hs:hs + 256],
                            bank[:, (co % 2) * 256:(co % 2) * 256 + 256], ALU.add)
                    return inst
                x1w = [("sPE", mk_wo), ("sSY", M["xs_%d" % tb])]
                if half == 0:
                    x1w.append(("sSC", M.get("x1nsp_%d_3" % (tb - 1), 0)))
                E.op("vector", x1f, waits=x1w, incs=[("sDV", 1)])
                M["x1_%s_%d" % (half, tb)] = E.mark("sDV")
            M["x1_%d" % tb] = E.mark("sDV")
            if "dbg3" in self.taps:
                E.dma("scalar", "sSC", S["dbg"][:, tb * 4096:(tb + 1) * 4096],
                      x1, waits=[("sDV", M["x1_%d" % tb])])
            if self._l2stop("x1"):
                l2stopped = True
                break

            self._ln1_quant(tb, x1, x1v, x1tm, xfms, woin, xq)
        M["wo_free"] = E.mark("sPE")

    def _ln1_quant(self, tb, x1, x1v, x1tm, xfms, woin, xq):
        E, S, M = self.E, self.S, self.M
        pb = self.pb
        lncol = self.cF(14080, 64)
        mucol = lncol[:, 0:1]
        sqcol = lncol[:, 1:2]
        varcol = lncol[:, 2:3]
        rstdc = lncol[:, 4:5]
        lnvc = lncol[:, 3:4]
        amaxc = lncol[:, 5:6]
        sclc = lncol[:, 6:7]
        mu_s = lncol[:, 8:9]
        msqc = lncol[:, 9:10]
        amax2c = lncol[:, 10:11]
        lnsc = lncol[:, 11:12]

        for s1 in range(4):
            evm0 = E.mark("sAC")
            evm = []
            for cc in range(8):
                bank = pb[4] if cc % 2 == 0 else pb[5]
                w = [("sDV", M["x1_%d" % tb]), ("sAC", E.mark("sAC"))]
                E.op("tensor",
                     (lambda e, cc=cc, s1=s1, bank=bank: e.transpose(
                         bank[:, 0:128], x1v[:, cc, s1 * 128:(s1 + 1) * 128],
                         self.identf[:])),
                     waits=w, incs=[("sPE", 1)])
                mk_pe = E.mark("sPE")
                E.op("scalar",
                     (lambda e, cc=cc, s1=s1, bank=bank: e.copy(
                         x1tm[:, s1 * 1024 + cc * 128: s1 * 1024 + cc * 128 + 128],
                         bank[:, 0:128])),
                     waits=[("sPE", mk_pe)], incs=[("sAC", 1)])
                evm.append(E.mark("sAC"))
            mk_tm = E.mark("sAC")

            sl = x1tm[:, s1 * 1024:(s1 + 1) * 1024]
            sq = xfms[:, 0:1024]

            def stat1(e, sl=sl):
                return e.tensor_reduce(mucol, sl, AX.X, ALU.add)
            E.op("vector", stat1, waits=[("sAC", mk_tm)], incs=[("sDV", 1)])
            E.op("scalar", (lambda e, sl=sl, sq=sq: e.activation(sq, sl, AF.Square)),
                 waits=[("sAC", mk_tm), ("sDV", E.mark("sDV"))], incs=[("sAC", 1)])
            mk_sq = E.mark("sAC")
            E.op("vector", (lambda e, sq=sq: e.tensor_reduce(sqcol, sq, AX.X, ALU.add)),
                 waits=[("sAC", mk_sq)], incs=[("sDV", 1)])

            def stat2a(e):
                e.activation(mu_s, mucol, AF.Copy, scale=1.0 / 1024.0)
                e.activation(msqc, mu_s, AF.Square)
                return e.activation(lnvc, sqcol, AF.Copy, scale=1.0 / 1024.0)
            E.op("scalar", stat2a, waits=[("sDV", E.mark("sDV"))], incs=[("sAC", 1)])
            mk_sa = E.mark("sAC")
            E.op("vector",
                 (lambda e: e.tensor_tensor(varcol, lnvc, msqc, ALU.subtract)),
                 waits=[("sAC", mk_sa)], incs=[("sDV", 1)])
            mk_v = E.mark("sDV")

            def lnf2(e):
                e.activation(lnvc, varcol, AF.Ln, bias=self.epsl[:])
                return e.activation(rstdc, lnvc, AF.Exp, scale=-0.5)
            E.op("scalar", lnf2, waits=[("sDV", mk_v)], incs=[("sAC", 1)])
            mk_r = E.mark("sAC")

            E.op("vector",
                 (lambda e, sl=sl: e.tensor_scalar(sl, sl, mu_s, rstdc,
                                                   ALU.subtract, ALU.mult)),
                 waits=[("sAC", mk_r)], incs=[("sDV", 1)])
            mk_n = E.mark("sDV")

            if "dbg6" in self.taps:
                sidx = tb * 4 + s1
                E.dma("scalar", "sSC", S["dbg"][:, sidx * 1024:(sidx + 1) * 1024],
                      sl, waits=[("sDV", mk_n)])
                E.dma("scalar", "sSC",
                      S["dbg"][0:128, 32768 + sidx * 8: 32768 + sidx * 8 + 8],
                      lncol[:, 0:8])
            evb0 = E.mark("sAC")
            evb = []
            for cc in range(8):
                bank = pb[4] if cc % 2 == 0 else pb[5]
                E.op("tensor",
                     (lambda e, cc=cc, bank=bank, sl=sl: e.transpose(
                         bank[:, 0:128], sl[:, cc * 128:(cc + 1) * 128], self.identf[:])),
                     waits=[("sDV", mk_n), ("sAC", E.mark("sAC"))],
                     incs=[("sPE", 1)])
                mk_t3 = E.mark("sPE")
                E.op("scalar",
                     (lambda e, cc=cc, s1=s1, bank=bank: e.copy(
                         x1v[:, cc, s1 * 128:(s1 + 1) * 128], bank[:, 0:128])),
                     waits=[("sPE", mk_t3)], incs=[("sAC", 1)])
                evb.append(E.mark("sAC"))
            mk_fm = E.mark("sAC")
            mk_fm_pe = E.mark("sPE")
            for cc in range(8):
                E.dma("scalar", "sSC",
                      S["x1n"][cc * 128:(cc + 1) * 128,
                               tb * TB + s1 * 128: tb * TB + (s1 + 1) * 128],
                      x1v[:, cc, s1 * 128:(s1 + 1) * 128],
                      waits=[("sAC", mk_fm)] if cc == 0 else [])
            M["x1nsp_%d_%d" % (tb, s1)] = E.mark("sSC")

            E.op("vector",
                 (lambda e, sl=sl: e.tensor_reduce(
                     amaxc, sl, AX.X, ALU.max, apply_absolute_value=True)),
                 waits=[("sDV", mk_n)], incs=[("sDV", 1)])
            E.op("gpsimd",
                 (lambda e: e.tensor_scalar(amax2c, amaxc, 1.0 / 127.0,
                                            1e-5 / 127.0, ALU.mult, ALU.max)),
                 waits=[("sDV", E.mark("sDV"))], incs=[("sGP", 1)])

            def qs2(e, tb=tb, s1=s1):
                e.activation(self.dsx[:, tb * 4 + s1: tb * 4 + s1 + 1],
                             amax2c, AF.Copy)
                e.activation(lnsc, amax2c, AF.Ln)
                return e.activation(sclc, lnsc, AF.Exp, scale=-1.0)
            E.op("scalar", qs2, waits=[("sGP", E.mark("sGP"))], incs=[("sAC", 1)])
            mk_s = E.mark("sAC")
            M["dsxac"] = mk_s

            xqtm = woin[:, 0:1024]

            def qr(e, sl=sl, xqtm=xqtm):
                e.tensor_scalar(sl, sl, sclc, CRND, ALU.mult, ALU.add)
                return e.tensor_scalar(xqtm, sl, CRND, None, ALU.subtract)
            E.op("vector", qr, waits=[("sAC", mk_s), ("sPE", mk_fm_pe)],
                 incs=[("sDV", 1)])
            mk_q = E.mark("sDV")

            for cc in range(8):
                def trq(e, cc=cc):
                    e.transpose(self.pbt[:, 0:128], xqtm[:, cc * 128: cc * 128 + 64],
                                self.identb[:])
                    return e.transpose(self.pbt[:, 128:256],
                                       xqtm[:, cc * 128 + 64: cc * 128 + 128],
                                       self.identb[:])
                E.op("tensor", trq,
                     waits=[("sDV", mk_q), ("sAC", E.mark("sAC"))],
                     incs=[("sPE", 1)])
                mk_t2 = E.mark("sPE")
                tcol = tb * TB + s1 * 128

                def evq(e, cc=cc, tcol=tcol):
                    e.copy(xq.rearrange("p (c n) -> p c n", n=TLOC)[0:64, cc, tcol:tcol + 128],
                           self.pbt[:, 0:128])
                    return e.copy(
                        xq.rearrange("p (c n) -> p c n", n=TLOC)[64:128, cc, tcol:tcol + 128],
                        self.pbt[:, 128:256])
                E.op("scalar", evq, waits=[("sPE", mk_t2)], incs=[("sAC", 1)])
            M["xq_%d_%d" % (tb, s1)] = E.mark("sAC")
            M["xqpe_%d" % tb] = E.mark("sPE")
        M["l1q_ac_%d" % tb] = E.mark("sAC")
        M["ln1_%d" % tb] = E.mark("sSC")
    def ffn(self):
        E, D, S, M = self.E, self.D, self.S, self.M
        cF, cB = self.cF, self.cB
        pb = self.pb
        hw = cF(0, 16384)
        x1ns = cF(16384, 512)
        ddbs = cF(16896, 512)
        x2f = cF(17408, 512)
        x2tm = cF(17920, 4096)
        sq2 = cF(22016, 1024)
        gs = cF(23040, 512)
        fc = cF(23552, 64)
        dsg = fc[:, 0:16]
        dsu = fc[:, 16:32]
        ddc = fc[:, 32:48]
        csq = fc[:, 48:64]
        l2c = cF(23616, 16)
        mucol = l2c[:, 0:1]
        sqcol = l2c[:, 1:2]
        varcol = l2c[:, 2:3]
        lnvc = l2c[:, 3:4]
        rstdc = l2c[:, 4:5]
        mu_s2 = l2c[:, 5:6]
        msqc2 = l2c[:, 6:7]
        xq = self.xq
        wgq = cB(16384, 4096)
        wuq = cB(20480, 4096)
        hqs = cB(24576, 4096)
        wdt4 = cB(16384, 16384)
        hqT = [cB(32768 + i * 512, 512) for i in range(4)]

        hmxq = cF(23632, 64)

        def dsf(e):
            e.tensor_scalar(dsg, self.dsx[:], self.swv[:, 0:1], None, ALU.mult)
            return e.tensor_scalar(dsu, self.dsx[:], self.swv[:, 1:2], None, ALU.mult)
        E.op("vector", dsf, waits=[("sSC", M["ln1_%d" % (NTB - 1)]),
                                   ("sAC", M["dsxac"])],
             incs=[("sDV", 1)])
        M["dsf"] = E.mark("sDV")

        for w in range(NWAVE):
            wt0 = w * TB
            for q in range(8):
                wld = [("sCC", M["ccw"]), ("sPE", M.get("dn_%d" % (w - 1), 0)),
                       ("sPE", M.get("wo_free", 0))]
                if q > 0:
                    wld.append(("sPE", M.get("gu_%d_%d" % (w, q - 1), 0)))
                E.dma("sync", "sSY", wgq.rearrange("p (a n) -> p a n", n=512),
                      S["wgT"].rearrange("(a p) n -> a p n", p=128)[
                          q * 8:(q + 1) * 8].transpose([1, 0, 2]),
                      waits=wld)
                E.dma("sync", "sSY", wuq.rearrange("p (a n) -> p a n", n=512),
                      S["wuT"].rearrange("(a p) n -> a p n", p=128)[
                          q * 8:(q + 1) * 8].transpose([1, 0, 2]))
                mk_w = E.mark("sSY")
                for sl in range(4):
                    gbank = pb[sl % 2]
                    ubank = pb[2 + sl % 2]
                    ts = wt0 + sl * 128
                    sidx = w * 4 + sl

                    def gumm(e, ts=ts, gbank=gbank, ubank=ubank):
                        xqv = xq.rearrange("p (c n) -> p c n", n=TLOC)
                        for ci in range(8):
                            e.matmul(gbank[:, :], xqv[:, ci, ts:ts + 128],
                                     wgq[:, ci * 512:(ci + 1) * 512],
                                     start=(ci == 0), stop=(ci == 7))
                        for ci in range(8):
                            inst = e.matmul(ubank[:, :], xqv[:, ci, ts:ts + 128],
                                            wuq[:, ci * 512:(ci + 1) * 512],
                                            start=(ci == 0), stop=(ci == 7))
                        return inst
                    if sl >= 2:
                        bw = ("sDV", M.get("hup_%d_%d_%d" % (w, q, sl - 2), 0))
                    else:
                        bw = ("sDV", M.get("hup_%d_%d_%d" % (w, q - 1, sl + 2), 0))
                    wmm = [("sSY", mk_w), bw]
                    if w == 0 and q == 0 and sl == 0:
                        wmm.append(("sAC", M["xq_%d_%d" % (NTB - 1, 3)]))
                    E.op("tensor", gumm, waits=wmm, incs=[("sPE", 1)])
                    mk_g = E.mark("sPE")
                    if sl == 3:
                        M["gu_%d_%d" % (w, q)] = mk_g

                    E.op("scalar",
                         (lambda e, gbank=gbank, sidx=sidx: e.activation(
                             gs, gbank[:, :], AF.Silu, scale=dsg[:, sidx:sidx + 1])),
                         waits=[("sPE", mk_g), ("sDV", M["dsf"]),
                                ("sDV", M.get("hup_%d_%d_%d" % (w, q, sl - 1), 0)
                                 if sl else M.get("hup_%d_%d_3" % (w, q - 1), 0))],
                         incs=[("sAC", 1)])
                    mk_si = E.mark("sAC")

                    def hupf(e, sl=sl, q=q, ubank=ubank):
                        hsl = hw[:, sl * 4096 + q * 512: sl * 4096 + (q + 1) * 512]
                        e.tensor_tensor(hsl, gs, ubank[:, :], ALU.mult)
                        return e.tensor_reduce(
                            hmxq[:, sl * 8 + q: sl * 8 + q + 1], hsl,
                            AX.X, ALU.max, apply_absolute_value=True)
                    E.op("vector", hupf, waits=[("sAC", mk_si)], incs=[("sDV", 1)])
                    M["hup_%d_%d_%d" % (w, q, sl)] = E.mark("sDV")

            for sl in range(4):
                sidx = w * 4 + sl

                hmax_s = hmxq[:, 32 + sl: 33 + sl]
                tmpc_s = hmxq[:, 36 + sl: 37 + sl]
                tmp2_s = hmxq[:, 40 + sl: 41 + sl]
                rec_s = hmxq[:, 44 + sl: 45 + sl]
                ddt_s = hmxq[:, 48 + sl: 49 + sl]

                E.op("vector",
                     (lambda e, sl=sl, hmax_s=hmax_s: e.tensor_reduce(
                         hmax_s, hmxq[:, sl * 8: sl * 8 + 8], AX.X, ALU.max)),
                     waits=[("sDV", M["hup_%d_%d_%d" % (w, 7, sl)])],
                     incs=[("sDV", 1)])
                E.op("gpsimd",
                     (lambda e, sidx=sidx, hmax_s=hmax_s, tmpc_s=tmpc_s:
                      e.tensor_tensor(tmpc_s, hmax_s, dsu[:, sidx:sidx + 1],
                                      ALU.mult)),
                     waits=[("sDV", E.mark("sDV"))], incs=[("sGP", 1)])
                E.op("vector",
                     (lambda e, tmpc_s=tmpc_s, tmp2_s=tmp2_s: e.tensor_scalar(
                         tmp2_s, tmpc_s, 1e-5, None, ALU.max)),
                     waits=[("sGP", E.mark("sGP"))], incs=[("sDV", 1)])

                lnt_s = hmxq[:, 52 + sl: 53 + sl]

                def qsha(e, tmp2_s=tmp2_s, rec_s=rec_s, ddt_s=ddt_s,
                         lnt_s=lnt_s, sidx=sidx):
                    e.activation(lnt_s, tmp2_s, AF.Ln)
                    e.activation(rec_s, lnt_s, AF.Exp, scale=-1.0)
                    e.activation(ddt_s, tmp2_s, AF.Copy, scale=1.0 / 127.0)
                    return e.activation(ddc[:, sidx:sidx + 1], ddt_s, AF.Copy,
                                        scale=self.swv[:, 2:3])
                E.op("scalar", qsha, waits=[("sDV", E.mark("sDV"))],
                     incs=[("sAC", 1)])
                E.op("gpsimd",
                     (lambda e, rec_s=rec_s, sidx=sidx: e.tensor_scalar(
                         csq[:, sidx:sidx + 1], rec_s, 127.0,
                         dsu[:, sidx:sidx + 1], ALU.mult, ALU.mult)),
                     waits=[("sAC", E.mark("sAC"))], incs=[("sGP", 1)])
                mk_csq = E.mark("sGP")

                qrw = [("sSC", M.get("hqsp_%d_%d" % (w, sl - 1), 0)),
                       ("sPE", M.get("dn_%d" % (w - 1), 0)),
                       ("sGP", mk_csq)]
                if "dbg7" in self.taps and w == 0:
                    E.dma("scalar", "sSC", S["dbg"][:, sl * 4096:(sl + 1) * 4096],
                          hw[:, sl * 4096:(sl + 1) * 4096],
                          waits=[("sDV", M["hup_%d_%d_%d" % (w, 7, sl)])])
                    if sl == 3:
                        E.dma("scalar", "sSC", S["dbg"][:, 16384:16448], fc,
                              waits=[("sGP", mk_csq), ("sAC", E.mark("sAC"))])
                        E.dma("scalar", "sSC", S["dbg"][:, 16448:16512], hmxq)
                    qrw.append(("sSC", E.mark("sSC")))

                def qrh(e, sl=sl, sidx=sidx):
                    e.tensor_scalar(hw[:, sl * 4096:(sl + 1) * 4096],
                                    hw[:, sl * 4096:(sl + 1) * 4096],
                                    csq[:, sidx:sidx + 1], CRND, ALU.mult, ALU.add)
                    return e.tensor_scalar(hqs, hw[:, sl * 4096:(sl + 1) * 4096],
                                           CRND, None, ALU.subtract)
                E.op("vector", qrh, waits=qrw, incs=[("sDV", 1)])
                mk_hq = E.mark("sDV")
                E.dma("scalar", "sSC", S["hq"][wt0 + sl * 128: wt0 + (sl + 1) * 128, :],
                      hqs, waits=[("sDV", mk_hq)])
                M["hqsp_%d_%d" % (w, sl)] = E.mark("sSC")

            for sl in range(4):
                sidx = w * 4 + sl
                E.op("tensor",
                     (lambda e, sidx=sidx: e.transpose(
                         pb[5][0:1, 0:128], ddc[:, sidx:sidx + 1], self.identf[:])),
                     waits=[("sDV", E.mark("sDV")), ("sAC", E.mark("sAC"))],
                     incs=[("sPE", 1)])
                mk_dt = E.mark("sPE")
                E.op("scalar",
                     (lambda e, sl=sl: e.copy(self.ddrow[0:1, sl * 128:(sl + 1) * 128],
                                              pb[5][0:1, 0:128])),
                     waits=[("sPE", mk_dt)], incs=[("sAC", 1)])
            mk_dr = E.mark("sAC")
            E.op("tensor",
                 (lambda e: e.matmul(pb[4][:, :], self.r_ones[0:1, 0:128],
                                     self.ddrow[0:1, :], start=True, stop=True)),
                 waits=[("sAC", mk_dr), ("sDV", M.get("ddbev_%d" % (w - 1), 0))],
                 incs=[("sPE", 1)])
            mk_db = E.mark("sPE")
            E.op("vector", (lambda e: e.tensor_copy(ddbs, pb[4][:, :])),
                 waits=[("sPE", mk_db)], incs=[("sDV", 1)])
            M["ddbev_%d" % w] = E.mark("sDV")

            wdtv = wdt4.rearrange("p (a c n) -> p a c n", a=4, n=128)

            for chalf in range(2):
                for i4 in range(4):
                    blk = chalf * 4 + i4
                    wldd = [("sSC", M["hqsp_%d_3" % w]), ("sPE", M["gu_%d_7" % w])]
                    if chalf == 1:
                        wldd.append(("sPE", M.get("dmm_%d_0_31" % w, 0)))
                    E.dma("sync", "sSY",
                          wdt4.rearrange("p (q a n) -> p q a n", q=4, n=128)[:, i4],
                          S["wdT"].rearrange("(a p) n -> a p n", p=128)[
                              blk * 32:(blk + 1) * 32].transpose([1, 0, 2]),
                          waits=wldd if i4 == 0 else [])
                mk_wd = E.mark("sSY")
                for coch in range(32):
                    par2 = coch % 4
                    qeng = "sync" if coch % 2 == 0 else "scalar"
                    qsem = "sSY" if coch % 2 == 0 else "sSC"
                    E.dma(qeng, qsem, hqT[par2],
                          S["hq"][wt0:wt0 + TB, coch * 128:(coch + 1) * 128],
                          transpose=True,
                          waits=[("sSC", M["hqsp_%d_3" % w]),
                                 ("sPE", M.get("dmm_%d_%d_%d" % (w, chalf, coch - 4),
                                               M.get("dmm_%d_%d_31" % (w, chalf - 1), 0)))])
                    mk_h = E.mark(qsem)

                    def dmm(e, coch=coch, chalf=chalf, par2=par2):
                        for i in range(4):
                            cch = chalf * 4 + i
                            inst = e.matmul(pb[i][:, :], wdtv[:, i, coch, :],
                                            hqT[par2][:, :],
                                            start=(coch == 0), stop=(coch == 31))
                        return inst
                    wmm = [(qsem, mk_h), ("sSY", mk_wd)]
                    if coch == 0:
                        wmm.append(("sDV", M.get("x2f_%d_%d" % (w, chalf * 4 - 1),
                                                 M.get("x2f_%d_7" % (w - 1), 0))))
                    E.op("tensor", dmm, waits=wmm, incs=[("sPE", 1)])
                    M["dmm_%d_%d_%d" % (w, chalf, coch)] = E.mark("sPE")
                mk_dn = E.mark("sPE")

                for i in range(4):
                    cch = chalf * 4 + i
                    E.dma("sync", "sSY", x1ns,
                          S["x1n"][cch * 128:(cch + 1) * 128, wt0:wt0 + TB],
                          waits=[("sDV", M.get("x2f_%d_%d" % (w, cch - 1), 0))])
                    mk_x1 = E.mark("sSY")

                    def x2ev(e, i=i):
                        e.tensor_tensor(x2f, pb[i][:, :], ddbs, ALU.mult)
                        return e.tensor_tensor(x2f, x2f, x1ns, ALU.add)
                    E.op("vector", x2ev,
                         waits=[("sPE", mk_dn), ("sSY", mk_x1),
                                ("sDV", M["ddbev_%d" % w]),
                                ("sAC", M.get("x2tr_%d_%d" % (w, cch - 1), 0))],
                         incs=[("sDV", 1)])
                    M["x2f_%d_%d" % (w, cch)] = E.mark("sDV")

                    for sl in range(4):
                        E.op("tensor",
                             (lambda e, sl=sl: e.transpose(
                                 pb[5][:, 0:128], x2f[:, sl * 128:(sl + 1) * 128],
                                 self.identf[:])),
                             waits=[("sDV", M["x2f_%d_%d" % (w, cch)]),
                                    ("sAC", E.mark("sAC"))],
                             incs=[("sPE", 1)])
                        mk_t = E.mark("sPE")
                        E.op("scalar",
                             (lambda e, sl=sl, cch=cch: e.copy(
                                 x2tm[:, sl * 1024 + cch * 128:
                                      sl * 1024 + cch * 128 + 128],
                                 pb[5][:, 0:128])),
                             waits=[("sPE", mk_t)], incs=[("sAC", 1)])
                    M["x2tr_%d_%d" % (w, cch)] = E.mark("sAC")
            M["dn_%d" % w] = E.mark("sPE")

            for sl in range(4):
                slv = x2tm[:, sl * 1024:(sl + 1) * 1024]
                E.op("vector", (lambda e, slv=slv: e.tensor_reduce(
                    mucol, slv, AX.X, ALU.add)),
                     waits=[("sAC", M["x2tr_%d_%d" % (w, 7)])], incs=[("sDV", 1)])
                E.op("scalar", (lambda e, slv=slv: e.activation(sq2, slv, AF.Square)),
                     waits=[("sAC", M["x2tr_%d_%d" % (w, 7)]),
                            ("sDV", E.mark("sDV"))], incs=[("sAC", 1)])
                mk_s2 = E.mark("sAC")
                E.op("vector", (lambda e: e.tensor_reduce(sqcol, sq2, AX.X, ALU.add)),
                     waits=[("sAC", mk_s2)], incs=[("sDV", 1)])

                def l2sa(e):
                    e.activation(mu_s2, mucol, AF.Copy, scale=1.0 / 1024.0)
                    e.activation(msqc2, mu_s2, AF.Square)
                    return e.activation(lnvc, sqcol, AF.Copy, scale=1.0 / 1024.0)
                E.op("scalar", l2sa, waits=[("sDV", E.mark("sDV"))],
                     incs=[("sAC", 1)])
                mk_a2 = E.mark("sAC")
                E.op("vector",
                     (lambda e: e.tensor_tensor(varcol, lnvc, msqc2, ALU.subtract)),
                     waits=[("sAC", mk_a2)], incs=[("sDV", 1)])
                mk_v2 = E.mark("sDV")

                def l2ln(e):
                    e.activation(lnvc, varcol, AF.Ln, bias=self.epsl[:])
                    return e.activation(rstdc, lnvc, AF.Exp, scale=-0.5)
                E.op("scalar", l2ln, waits=[("sDV", mk_v2)], incs=[("sAC", 1)])
                mk_r2 = E.mark("sAC")

                E.op("vector",
                     (lambda e, slv=slv: e.tensor_scalar(slv, slv, mu_s2, rstdc,
                                                         ALU.subtract, ALU.mult)),
                     waits=[("sAC", mk_r2)], incs=[("sDV", 1)])
                mk_o = E.mark("sDV")
                E.dma("scalar", "sOUT",
                      D["out"][wt0 + sl * 128: wt0 + (sl + 1) * 128, :], slv,
                      waits=[("sDV", mk_o)])

    def finish(self):
        E = self.E
        if E.mark("sOUT") == 0:
            w = [(s, E.mark(s)) for s in ["sSY", "sSC", "sGP", "sCC", "sPE",
                                          "sDV", "sAC", "cS", "gE", "sOUT"]]
            E.op("sync", lambda e: None, waits=w)
            return
        total = E.mark("sOUT")
        E.op("sync", lambda e: None, waits=[("sOUT", total)])


_IN_MAP = [
    ("maa_x", "L0_attn_maa_x"), ("maa_w", "L0_attn_maa_w"), ("maa_k", "L0_attn_maa_k"),
    ("maa_v", "L0_attn_maa_v"), ("maa_r", "L0_attn_maa_r"), ("maa_g", "L0_attn_maa_g"),
    ("maa_w1", "L0_attn_maa_w1"), ("maa_w2", "L0_attn_maa_w2"),
    ("Wr", "L0_attn_Wr"), ("Wk", "L0_attn_Wk"), ("Wv", "L0_attn_Wv"),
    ("Wo", "L0_attn_Wo"), ("gate_w1", "L0_attn_gate_w1"), ("gate_w2", "L0_attn_gate_w2"),
    ("time_decay", "L0_attn_time_decay"), ("decay_w1", "L0_attn_decay_w1"),
    ("decay_w2", "L0_attn_decay_w2"), ("u", "L0_attn_u"),
    ("ln_g", "L0_attn_ln_g"), ("ln_b", "L0_attn_ln_b"),
]


def build_program(taps=(), stop_after=None, chunk_stop=None, l2_stop=None):
    kb = KB(taps=taps, stop_after=stop_after)
    kb.chunk_stop = chunk_stop
    kb.l2_stop = l2_stop
    nc = kb.build()
    return nc


def make_in_maps(inputs):
    x = np.ascontiguousarray(np.asarray(inputs["L0_attn_x"], dtype=np.float32))
    wg = np.ascontiguousarray(np.asarray(inputs["L0_ffn_w_gate"], dtype=np.float32))
    wu = np.ascontiguousarray(np.asarray(inputs["L0_ffn_w_up"], dtype=np.float32))
    wd = np.ascontiguousarray(np.asarray(inputs["L0_ffn_w_down"], dtype=np.float32))
    common = {}
    for knew, kold in _IN_MAP:
        common[knew] = np.ascontiguousarray(np.asarray(inputs[kold], dtype=np.float32))
    lng = np.asarray(inputs["L0_attn_ln_g"], dtype=np.float32).reshape(-1)
    halfsel = (np.arange(C) % 128) < 64
    g2 = np.zeros((16, C), np.float32)
    for j in range(16):
        sel = halfsel if j % 2 == 0 else ~halfsel
        g2[j, sel] = lng[sel]
    common["gin2"] = g2
    in_maps = []
    for c in range(8):
        b, q = c // 4, c % 4
        im = dict(common)
        im["x"] = np.ascontiguousarray(x[b, q * TLOC:(q + 1) * TLOC, :])
        if q == 0:
            im["xprev"] = np.zeros((1, C), np.float32)
        else:
            im["xprev"] = np.ascontiguousarray(x[b, q * TLOC - 1: q * TLOC, :])
        im["wg_sh"] = np.ascontiguousarray(wg[c * 512:(c + 1) * 512, :])
        im["wu_sh"] = np.ascontiguousarray(wu[c * 512:(c + 1) * 512, :])
        im["wd_sh"] = np.ascontiguousarray(wd[c * 128:(c + 1) * 128, :])
        wsel = np.zeros((8, 8), np.float32)
        wbias = np.full((1, 8), -1e30, np.float32)
        g0 = b * 4
        for j in range(g0, c):
            wbias[0, j] = 0.0
            for m in range(j + 1, c):
                wsel[m, j] = 1.0
        im["wsel"] = wsel
        im["wbias"] = wbias
        in_maps.append(im)
    return in_maps


def kernel(**inputs):
    nc = build_program()
    in_maps = make_in_maps(inputs)
    res = run_bass_kernel_spmd(nc, in_maps, core_ids=list(range(8)))
    out = np.empty((2, 8192, C), np.float32)
    for c in range(8):
        b, q = c // 4, c % 4
        out[b, q * TLOC:(q + 1) * TLOC, :] = res.results[c]["out"]
    return out


# version 15
# speedup vs baseline: 1.0644x; 1.0070x over previous
import numpy as np
import ml_dtypes
from contextlib import ExitStack
import concourse.bass as bass
import concourse.mybir as mybir
from concourse.bass_utils import run_bass_kernel_spmd

F32 = mybir.dt.float32
BF16 = mybir.dt.bfloat16
AF = mybir.ActivationFunctionType
ALU = mybir.AluOpType
AX = mybir.AxisListType

C = 1024
TLOC = 2048
NTB = 4
TB = 512
LCH = 64
NCHB = TB // LCH
NCH = TLOC // LCH
DIN = 4096
NWAVE = 4
CRND = 12582912.0
EPS_GN = 0.00064
EPS_LN = 1e-5
ENGS = ("sync", "scalar", "vector", "tensor", "gpsimd")


class Emitter:

    def __init__(self):
        self.q = {e: [] for e in ENGS}
        self.cnt = {}

    def newsem(self, name):
        self.cnt[name] = 0
        return name

    def mark(self, s):
        return self.cnt[s]

    def op(self, eng, fn, waits=(), incs=()):
        for s, k in incs:
            self.cnt[s] += k
        self.q[eng].append((fn, list(waits), list(incs)))

    def dma(self, eng, sem, out, in_, waits=(), **kw):
        def fn(e, out=out, in_=in_, kw=kw):
            return e.dma_start(out=out, in_=in_, **kw)
        self.op(eng, fn, waits=waits, incs=[(sem, 16)])
        return self.cnt[sem]

    def flush(self, block, semh):
        for eng in ENGS:
            ops = self.q[eng]
            if not ops:
                continue

            def body(e, ops=ops):
                for fn, waits, incs in ops:
                    for s, v in waits:
                        if v > 0:
                            e.wait_ge(semh[s], v)
                    inst = fn(e)
                    for s, k in incs:
                        assert inst is not None, "inc without instruction"
                        inst.then_inc(semh[s], k)
            getattr(block, eng)(body)


class KB:
    def __init__(self, taps=(), stop_after=None):
        self.taps = list(taps)
        self.stop_after = stop_after
        self.nc = bass.Bass("TRN2", target_bir_lowering=False)
        self.es = ExitStack()
        self.E = Emitter()
        self.M = {}

    def sb(self, name, shape, dt):
        return self.es.enter_context(self.nc.sbuf_tensor(name, shape, dt))

    def psum(self, name, shape, dt=F32):
        return self.es.enter_context(self.nc.psum_tensor(name, shape, dt))

    def build(self):
        nc = self.nc
        D = {}
        D["x"] = nc.dram_tensor("x", [TLOC, C], F32, kind="ExternalInput").ap()
        D["xprev"] = nc.dram_tensor("xprev", [1, C], F32, kind="ExternalInput").ap()
        for n1, shp in [("maa_x", [C]), ("maa_w", [C]), ("maa_k", [C]), ("maa_v", [C]),
                        ("maa_r", [C]), ("maa_g", [C]), ("time_decay", [C]),
                        ("ln_g", [C]), ("ln_b", [C])]:
            D[n1] = nc.dram_tensor(n1, shp, F32, kind="ExternalInput").ap()
        D["maa_w1"] = nc.dram_tensor("maa_w1", [C, 160], F32, kind="ExternalInput").ap()
        D["maa_w2"] = nc.dram_tensor("maa_w2", [5, 32, C], F32, kind="ExternalInput").ap()
        for n1 in ["Wr", "Wk", "Wv", "Wo"]:
            D[n1] = nc.dram_tensor(n1, [C, C], F32, kind="ExternalInput").ap()
        D["gate_w1"] = nc.dram_tensor("gate_w1", [C, 64], F32, kind="ExternalInput").ap()
        D["gate_w2"] = nc.dram_tensor("gate_w2", [64, C], F32, kind="ExternalInput").ap()
        D["decay_w1"] = nc.dram_tensor("decay_w1", [C, 64], F32, kind="ExternalInput").ap()
        D["decay_w2"] = nc.dram_tensor("decay_w2", [64, C], F32, kind="ExternalInput").ap()
        D["u"] = nc.dram_tensor("u", [16, 64], F32, kind="ExternalInput").ap()
        D["wg_sh"] = nc.dram_tensor("wg_sh", [512, C], F32, kind="ExternalInput").ap()
        D["wu_sh"] = nc.dram_tensor("wu_sh", [512, C], F32, kind="ExternalInput").ap()
        D["wd_sh"] = nc.dram_tensor("wd_sh", [128, DIN], F32, kind="ExternalInput").ap()
        D["wsel"] = nc.dram_tensor("wsel", [8, 8], F32, kind="ExternalInput").ap()
        D["wbias"] = nc.dram_tensor("wbias", [1, 8], F32, kind="ExternalInput").ap()
        D["gin2"] = nc.dram_tensor("gin2", [16, C], F32, kind="ExternalInput").ap()
        D["out"] = nc.dram_tensor("out", [TLOC, C], F32, kind="ExternalOutput").ap()

        tset = set(self.taps)

        def dscr(name, shape, dt):
            kind = "ExternalOutput" if name.lstrip("c_") in tset or name in tset \
                else "Internal"
            return nc.dram_tensor(name, shape, dt, kind=kind).ap()
        S = {}
        S["abs_in"] = dscr("c_abs_in", [1, 4], F32)
        S["abs_out"] = dscr("c_abs_out", [1, 4], F32)
        S["wgT_sh"] = dscr("c_wgT_sh", [C, 512], BF16)
        S["wuT_sh"] = dscr("c_wuT_sh", [C, 512], BF16)
        S["wdT_sh"] = dscr("c_wdT_sh", [DIN, 128], BF16)
        S["wgT"] = dscr("c_wgT", [8 * C, 512], BF16)
        S["wuT"] = dscr("c_wuT", [8 * C, 512], BF16)
        S["wdT"] = dscr("c_wdT", [8 * DIN, 128], BF16)
        S["sum_in"] = dscr("c_sum_in", [128, 520], F32)
        S["sum_out"] = dscr("c_sum_out", [1024, 520], F32)
        S["xfm"] = dscr("c_xfm", [8 * 128, TLOC], F32)
        S["A"] = dscr("c_A", [NCH * 64, 1024], BF16)
        S["vtm"] = dscr("c_vtm", [NCH * 64, 1024], BF16)
        S["rt"] = dscr("c_rt", [NCH * 128, 512], BF16)
        S["sl"] = dscr("c_sl", [NCH * 128, 512], BF16)
        S["yd"] = dscr("c_yd", [NCH * 128, 512], BF16)
        S["g"] = dscr("c_g", [NCH * 128, 512], BF16)
        S["x1n"] = dscr("c_x1n", [8 * 128, TLOC], F32)
        S["hq"] = dscr("c_hq", [TLOC, DIN], BF16)
        S["dbg"] = dscr("c_dbg", [128, 40960], F32)
        S["wkb"] = dscr("c_wkb", [C, C], BF16)
        S["wvb"] = dscr("c_wvb", [C, C], BF16)
        S["wrb"] = dscr("c_wrb", [C, C], BF16)
        S["wob"] = dscr("c_wob", [C, C], BF16)
        S["w2b"] = dscr("c_w2b", [5 * 32, C], BF16)
        self.D, self.S = D, S

        E = self.E
        for s in ["sSY", "sSC", "sGP", "sCC", "sPE", "sDV", "sAC",
                  "cA", "cV", "cP", "cS", "gE", "sOUT"]:
            E.newsem(s)

        with self.es:
            self.arF = self.sb("arF", [128, 25600], F32)
            self.arB = self.sb("arB", [128, 35840], BF16)
            self._alloc_small()
            self.pb = [self.psum("pb%d" % i, [128, 512]) for i in range(6)]
            self.pba = self.psum("pba", [128, 512])
            self.pbt = self.psum("pbt", [64, 512], BF16)
            semh = {}
            for s in E.cnt:
                semh[s] = self.es.enter_context(nc.semaphore(s))
            self.block = self.es.enter_context(nc.Block())
            phases = [("ph0", self.ph0), ("ffnw", self.ph0_ffnw),
                      ("loop1", self.loop1), ("exchange", self.exchange),
                      ("loop2", self.loop2), ("ffn", self.ffn)]
            for nm, fn in phases:
                fn()
                sa = self.stop_after
                if sa and (sa == nm or sa.startswith(nm + "-")):
                    break
            self.finish()
            E.flush(self.block, semh)
        return nc

    def cF(self, off, cols):
        assert off + cols <= 25600, (off, cols)
        return self.arF[:, off:off + cols]

    def cB(self, off, cols):
        assert off + cols <= 35840, (off, cols)
        return self.arB[:, off:off + cols]

    def _alloc_small(self):
        sb = self.sb
        for nm in ["maax", "maaw", "maak", "maav", "maar", "maag", "td", "u", "lnb"]:
            setattr(self, "v_" + nm, sb("v_" + nm, [128, 8], F32))
        self.r_lng = sb("r_lng", [1, C], F32)
        self.r_lnb = sb("r_lnb", [1, C], F32)
        self.r_ones = sb("r_ones", [1, 128], F32)
        self.ones16 = sb("ones16", [1, 16], F32)
        self.onesc = sb("onesc", [128, 1], F32)
        self.identf = sb("identf", [128, 128], F32)
        self.identb = sb("identb", [128, 128], BF16)
        self.ones512 = sb("ones512", [128, TB], BF16)
        self.mask4 = self.cF(14672, 512)
        self.mask4u = sb("mask4u", [128, 512], mybir.dt.uint8)
        self.zeros_bf = sb("zeros_bf", [128, 512], BF16)
        self.ind_cc = sb("ind_cc", [16, 8 * 128], F32)
        self.gin_cc = self.ind_cc
        self.ione_cc = sb("ione_cc", [128, 8 * 16], F32)
        self.ione_b = sb("ione_b", [128, 8 * 16], BF16)
        self.ione16 = sb("ione16", [128, 16], F32)
        self.gin16 = self.cF(22528, 1024)[0:16, :]
        self.nl8 = sb("nl8", [128, 1], F32)
        self.scols = sb("scols", [128, 8 * NCH], F32)
        self.carryB = sb("carryB", [128, 8], F32)
        self.xcarry = sb("xcarry", [128, 8], F32)
        self.S32 = sb("S32", [128, 512], F32)
        self.Sin = sb("Sin", [128, 512], F32)
        self.dsx = sb("dsx", [128, 16], F32)
        self.wsc = sb("wsc", [1, 4], F32)
        self.wsc_r = sb("wsc_r", [1, 4], F32)
        self.wsel_sb = sb("wsel_sb", [8, 8], F32)
        self.wbias_sb = sb("wbias_sb", [1, 8], F32)
        self.swcol = sb("swcol", [128, 4], F32)
        self.idxI = self.cF(14336, 128)
        self.idxA = self.cF(14464, 64)[0:64, :]
        self.vv16 = self.cF(14528, 128)[0:16, :]
        self.vv128 = self.cF(14656, 16)
        self.logdt = sb("logdt", [128, 64], F32)
        self.sdcol = sb("sdcol", [128, 16], F32)
        self.ddrow = sb("ddrow", [1, 512], F32)
        self.swv = sb("swv", [128, 4], F32)
        self.epsg = sb("epsg", [128, 1], F32)
        self.epsl = sb("epsl", [128, 1], F32)
    def ph0(self):
        E, D, M = self.E, self.D, self.M

        for vt, key in [(self.v_maax, "maa_x"), (self.v_maaw, "maa_w"),
                        (self.v_maak, "maa_k"), (self.v_maav, "maa_v"),
                        (self.v_maar, "maa_r"), (self.v_maag, "maa_g"),
                        (self.v_td, "time_decay"), (self.v_lnb, "ln_b")]:
            E.dma("sync", "sSY", vt[:], D[key].rearrange("(c p) -> p c", p=128),
                  allow_slow_non_contiguous=True)
        E.dma("sync", "sSY", self.v_u[:],
              D["u"].rearrange("h n -> (h n)").rearrange("(c p) -> p c", p=128),
              allow_slow_non_contiguous=True)
        E.dma("sync", "sSY", self.r_lng[:], D["ln_g"].rearrange("(o c) -> o c", o=1))
        E.dma("sync", "sSY", self.r_lnb[:], D["ln_b"].rearrange("(o c) -> o c", o=1))
        E.dma("sync", "sSY", self.xcarry[:],
              D["xprev"].rearrange("o (c p) -> p (o c)", p=128),
              allow_slow_non_contiguous=True)
        E.dma("sync", "sSY", self.wsel_sb[:], D["wsel"][:, :])
        E.dma("sync", "sSY", self.wbias_sb[:], D["wbias"][:, :])
        M["vecs"] = E.mark("sSY")

        S = self.S
        self.w_l1 = self.sb("w_l1", [128, 8 * 160], BF16)
        self.w_g1 = self.sb("w_g1", [128, 8 * 64], BF16)
        self.w_g2 = self.sb("w_g2", [64, C], BF16)
        self.w_d1 = self.sb("w_d1", [128, 8 * 64], BF16)
        self.w_d2 = self.sb("w_d2", [64, C], BF16)
        for skey, key in [("wkb", "Wk"), ("wvb", "Wv"), ("wrb", "Wr"),
                          ("wob", "Wo")]:
            E.dma("gpsimd", "sGP", S[skey][:, :], D[key][:, :])
        E.dma("gpsimd", "sGP", S["w2b"].rearrange("(f r) c -> f r c", r=32),
              D["maa_w2"][:, :, :])
        M["wcast"] = E.mark("sGP")
        E.dma("gpsimd", "sGP", self.w_l1[:].rearrange("p (a d) -> p a d", d=160),
              D["maa_w1"].rearrange("(a p) d -> p a d", p=128))
        E.dma("gpsimd", "sGP", self.w_g1[:].rearrange("p (a d) -> p a d", d=64),
              D["gate_w1"].rearrange("(a p) d -> p a d", p=128))
        E.dma("gpsimd", "sGP", self.w_g2[:], D["gate_w2"][:, :])
        E.dma("gpsimd", "sGP", self.w_d1[:].rearrange("p (a d) -> p a d", d=64),
              D["decay_w1"].rearrange("(a p) d -> p a d", p=128))
        E.dma("gpsimd", "sGP", self.w_d2[:], D["decay_w2"][:, :])
        M["wload"] = E.mark("sGP")

        def iotas(e):
            e.iota(self.idxA[:], [[1, 64]], base=0, channel_multiplier=-1,
                   allow_small_or_imprecise_dtypes=True)
            e.iota(self.vv16[:, 0:64], [[0, 64]], base=0, channel_multiplier=1,
                   allow_small_or_imprecise_dtypes=True)
            e.iota(self.vv16[:, 64:128], [[0, 64]], base=-1, channel_multiplier=1,
                   allow_small_or_imprecise_dtypes=True)
            e.iota(self.vv128[0:64, :], [[1, 16]], base=0, channel_multiplier=0,
                   allow_small_or_imprecise_dtypes=True)
            e.iota(self.vv128[64:128, :], [[1, 16]], base=-1, channel_multiplier=0,
                   allow_small_or_imprecise_dtypes=True)
            return e.iota(self.idxI[:], [[1, 128]], base=0, channel_multiplier=-1,
                          allow_small_or_imprecise_dtypes=True)
        E.op("gpsimd", iotas, incs=[("sGP", 1)])
        M["iota"] = E.mark("sGP")

        def consts(e):
            for blk in range(8):
                e.tensor_single_scalar(
                    self.mask4[0:64, blk * 64:(blk + 1) * 64],
                    self.idxA[:], 0.0, ALU.is_gt)
                e.tensor_single_scalar(
                    self.mask4[64:128, blk * 64:(blk + 1) * 64],
                    self.idxA[:], 0.0, ALU.is_gt)
            e.memset(self.zeros_bf[:], 0.0)
            e.memset(self.r_ones[:], 1.0)
            e.memset(self.ones16[:], 1.0)
            e.memset(self.onesc[:], 1.0)
            e.memset(self.ones512[:], 1.0)
            e.memset(self.carryB[:], 0.0)
            e.memset(self.epsg[:], EPS_GN)
            e.memset(self.epsl[:], EPS_LN)
            e.memset(self.S32[:], 0.0)
            for cc in range(8):
                e.tensor_single_scalar(self.ind_cc[:, cc * 128:(cc + 1) * 128],
                                       self.vv16[:], float(2 * cc), ALU.is_equal)
                e.tensor_single_scalar(self.ione_cc[:, cc * 16:(cc + 1) * 16],
                                       self.vv128[:], float(2 * cc), ALU.is_equal)
            e.tensor_single_scalar(self.identf[:], self.idxI[:], 0.0, ALU.is_equal)
            e.tensor_copy(self.ione_b[:], self.ione_cc[:])
            e.tensor_copy(self.mask4u[:], self.mask4[:])
            return e.tensor_copy(self.identb[:], self.identf[:])
        E.op("vector", consts, waits=[("sGP", M["iota"])], incs=[("sDV", 1)])
        M["consts"] = E.mark("sDV")

        M["gin"] = 0

    def ph0_ffnw(self):
        E, D, S, M = self.E, self.D, self.S, self.M
        sh_g = self.cF(0, 4096)
        sh_u = self.cF(4096, 4096)
        sh_d = self.cF(8192, 4096)
        dmp = self.cF(12288, 1024)
        shq = self.cB(24576, 4096)
        trT = self.cB(28672, 512)
        E.dma("sync", "sSY", sh_g.rearrange("p (a c) -> p a c", c=C),
              D["wg_sh"].rearrange("(a p) c -> p a c", p=128))
        E.dma("sync", "sSY", sh_u.rearrange("p (a c) -> p a c", c=C),
              D["wu_sh"].rearrange("(a p) c -> p a c", p=128))
        E.dma("sync", "sSY", sh_d, D["wd_sh"][:, :])
        M["shards"] = E.mark("sSY")

        def absf(e):
            acc = self.sdcol
            for i in range(4):
                e.activation(dmp, sh_g[:, i * C:(i + 1) * C], AF.Abs,
                             accum_out=acc[:, i:i + 1])
            for i in range(4):
                e.activation(dmp, sh_u[:, i * C:(i + 1) * C], AF.Abs,
                             accum_out=acc[:, 4 + i:5 + i])
            for i in range(4):
                inst = e.activation(dmp, sh_d[:, i * C:(i + 1) * C], AF.Abs,
                                    accum_out=acc[:, 8 + i:9 + i])
            return inst
        E.op("scalar", absf, waits=[("sSY", M["shards"])], incs=[("sAC", 1)])
        M["abs"] = E.mark("sAC")

        E.op("tensor",
             lambda e: e.matmul(self.pb[4][0:1, 0:12], self.onesc[:],
                                self.sdcol[:, 0:12], start=True, stop=True),
             waits=[("sAC", M["abs"]), ("sDV", M["consts"])], incs=[("sPE", 1)])
        M["fold"] = E.mark("sPE")

        absum = self.sb("absum", [1, 4], F32)

        def sumf(e):
            e.tensor_reduce(absum[:, 0:3].rearrange("o (n u) -> o n u", u=1),
                            self.pb[4][0:1, 0:12].rearrange("o (g i) -> o g i", g=3),
                            AX.X, ALU.add)
            return e.memset(absum[:, 3:4], 0.0)
        E.op("vector", sumf, waits=[("sPE", M["fold"])], incs=[("sDV", 1)])
        M["absum"] = E.mark("sDV")

        E.dma("scalar", "sSC", S["abs_in"][:, :], absum[:],
              waits=[("sDV", M["absum"])])
        M["absst"] = E.mark("sSC")

        E.op("gpsimd",
             lambda e: e.collective_compute(
                 "AllReduce", ALU.add, replica_groups=[list(range(8))],
                 ins=[S["abs_in"].opt()], outs=[S["abs_out"].opt()]),
             waits=[("sSC", M["absst"])], incs=[("sCC", 1)])
        M["ccabs"] = E.mark("sCC")

        E.dma("sync", "sSY", self.wsc[:], S["abs_out"][:, :],
              waits=[("sCC", M["ccabs"])])
        M["wscl"] = E.mark("sSY")

        E.op("gpsimd",
             lambda e: e.tensor_scalar(self.wsc[:], self.wsc[:],
                                       1.0 / (4096.0 * 1024.0), 1e-5,
                                       ALU.mult, ALU.max),
             waits=[("sSY", M["wscl"])], incs=[("sGP", 1)])
        E.op("vector",
             lambda e: e.reciprocal(self.wsc_r[:], self.wsc[:]),
             waits=[("sGP", E.mark("sGP"))], incs=[("sDV", 1)])
        M["wscr"] = E.mark("sDV")

        E.op("tensor",
             lambda e: e.matmul(self.pb[4][:, 16:20], self.r_ones[0:1, 0:128],
                                self.wsc_r[:], start=True, stop=True),
             waits=[("sDV", M["wscr"])], incs=[("sPE", 1)])
        M["swbc"] = E.mark("sPE")

        E.op("vector", lambda e: e.tensor_copy(self.swcol[:], self.pb[4][:, 16:20]),
             waits=[("sPE", M["swbc"])], incs=[("sDV", 1)])
        M["swcol"] = E.mark("sDV")

        E.op("tensor",
             lambda e: e.matmul(self.pb[4][:, 20:24], self.r_ones[0:1, 0:128],
                                self.wsc[:], start=True, stop=True),
             waits=[("sDV", M["swcol"])], incs=[("sPE", 1)])
        M["swbc2"] = E.mark("sPE")
        E.op("vector", lambda e: e.tensor_copy(self.swv[:], self.pb[4][:, 20:24]),
             waits=[("sPE", M["swbc2"])], incs=[("sDV", 1)])
        M["swv"] = E.mark("sDV")

        def quant(src, dstq, col, ntiles):
            def f(e):
                for i in range(ntiles):
                    e.tensor_scalar(dmp, src[:, i * C:(i + 1) * C],
                                    self.swcol[:, col:col + 1], CRND, ALU.mult, ALU.add)
                    e.tensor_scalar(dmp, dmp, CRND, 1.0, ALU.subtract, ALU.min)
                    inst = e.tensor_scalar(dstq[:, i * C:(i + 1) * C], dmp, -1.0,
                                           None, ALU.max)
                return inst
            E.op("vector", f, waits=[("sDV", M["swv"]), ("sPE", E.mark("sPE"))],
                 incs=[("sDV", 1)])

        def transpose_store(srcq, tiles, dst):
            for (t, dr, dc) in tiles:
                dep_dv = E.mark("sDV")
                dep_sc = E.mark("sSC")
                E.op("tensor",
                     (lambda e, t=t: [
                         e.transpose(self.pbt[:, 0:128],
                                     srcq[:, t * 128:t * 128 + 64], self.identb[:]),
                         e.transpose(self.pbt[:, 128:256],
                                     srcq[:, t * 128 + 64:t * 128 + 128], self.identb[:])][-1]),
                     waits=[("sDV", dep_dv), ("sSC", dep_sc)], incs=[("sPE", 1)])
                E.op("vector",
                     (lambda e: [
                         e.tensor_copy(trT[0:64, 0:128], self.pbt[:, 0:128]),
                         e.tensor_copy(trT[64:128, 0:128], self.pbt[:, 128:256])][-1]),
                     waits=[("sPE", E.mark("sPE")), ("sSC", E.mark("sSC"))],
                     incs=[("sDV", 1)])
                E.dma("scalar", "sSC", dst[dr[0]:dr[1], dc[0]:dc[1]], trT[:, 0:128],
                      waits=[("sDV", E.mark("sDV"))])

        quant(sh_g, shq, 0, 4)
        transpose_store(shq, [(a * 8 + ci, (ci * 128, ci * 128 + 128),
                               (a * 128, a * 128 + 128))
                              for a in range(4) for ci in range(8)], S["wgT_sh"])
        quant(sh_u, shq, 1, 4)
        transpose_store(shq, [(a * 8 + ci, (ci * 128, ci * 128 + 128),
                               (a * 128, a * 128 + 128))
                              for a in range(4) for ci in range(8)], S["wuT_sh"])
        quant(sh_d, shq, 2, 4)
        transpose_store(shq, [(t, (t * 128, t * 128 + 128), (0, 128))
                              for t in range(32)], S["wdT_sh"])
        M["wqst"] = E.mark("sSC")

        for pair in [("wgT_sh", "wgT"), ("wuT_sh", "wuT"), ("wdT_sh", "wdT")]:
            E.op("gpsimd",
                 (lambda e, pair=pair: e.collective_compute(
                     "AllGather", ALU.bypass, replica_groups=[list(range(8))],
                     ins=[S[pair[0]].opt()], outs=[S[pair[1]].opt()])),
                 waits=[("sSC", M["wqst"])], incs=[("sCC", 1)])
        M["ccw"] = E.mark("sCC")
    def _l1stop(self, part, tb):
        sa = self.stop_after or ""
        if not sa.startswith("loop1-"):
            return False
        _, p, n = sa.split("-")
        return part == p and tb == int(n) - 1

    def loop1(self):
        E, D, S, M = self.E, self.D, self.S, self.M
        l1stopped = False
        cF, cB = self.cF, self.cB
        pb = self.pb

        xtm = cF(0, 4096)
        xfm = cF(4096, 4104)
        xv = xfm.rearrange("p (c n) -> p c n", n=513)
        xx = cF(8200, 4096)
        xxv = xx.rearrange("p (c n) -> p c n", n=TB)
        self.ek = [cF(8200, 512), cF(8712, 512)]
        self.er = [cF(9224, 512), cF(9736, 512)]
        B_ = cF(12304, 4104)
        Bv = B_.rearrange("p (c n) -> p c n", n=513)
        self.Bv = Bv
        self.ktb = cF(16408, 4096)
        self.rtb = cF(20504, 4096)
        tmp = cF(24600, 512)
        ew = tmp
        self.dBt = cF(25112, 64)
        qct = cF(25176, 64)
        self.bc2t = cF(25240, 64)
        self.bc1nt = cF(25304, 64)
        self.ehmt = cF(25368, 64)
        self.Dtt = cF(25432, 64)

        self.wstr8 = [cB(0, 8192), cB(8192, 8192)]
        mixbuf = cB(16384, 4096)
        self.vtb = cB(20480, 4096)
        lt = {f: cB(24576 + i * 512, 512) for i, f in enumerate("wkvrg")}
        self.dg1t = cB(27136, 512)
        gcc = cB(27648, 512)
        ydcc = cB(28160, 512)
        p3c = cB(28672, 512)
        self.ktil = [cB(29184, 512), cB(29696, 512)]
        self.rtil = [cB(30208, 512), cB(30720, 512)]
        self.khat = cB(31232, 512)
        self.slbf = cB(31744, 512)
        self.asel = cB(32256, 512)
        self.vtms = cB(32768, 1024)
        self.ktms = cB(33792, 1024)
        self.mixbuf = mixbuf
        self._gcc = gcc
        self.tmp = tmp
        self.ew = ew
        self.lt = lt

        fprev = {"w": None, "k": "w", "v": "k", "r": "v", "g": "r"}
        fnum = {"w": 0, "k": 1, "v": 2, "r": 3, "g": 4}
        w_l2f = self.sb("w_l2f", [32, C], BF16)

        E.op("vector", lambda e: e.memset(Bv[:, :, 0:1], 0.0),
             waits=[], incs=[("sDV", 1)])

        for tb in range(NTB):
            t0 = tb * TB
            chg0 = tb * NCHB

            xw = [("sPE", M.get("xtr_%d" % (tb - 1), 0))]
            if tb == 0:
                xw = [("sDV", E.mark("sDV")), ("sPE", E.mark("sPE"))]
            E.dma("sync", "sSY", xtm.rearrange("p (s c) -> p s c", c=C),
                  D["x"][t0:t0 + TB, :].rearrange("(s p) c -> p s c", p=128),
                  waits=xw)
            M["xload_%d" % tb] = E.mark("sSY")

            wkw = [("sPE", M.get("proj_r_%d" % (tb - 1), 0))]
            if tb == 0:
                wkw.append(("sGP", M["wcast"]))
            E.dma("sync", "sSY", self.wstr8[0].rearrange("p (a co) -> p a co", co=C),
                  S["wkb"].rearrange("(a p) co -> p a co", p=128),
                  waits=wkw)
            M["wld_k_%d" % tb] = E.mark("sSY")

            pems = []
            xev0 = E.mark("sAC")
            for cc in range(8):
                bank = pb[0] if cc % 2 == 0 else pb[1]
                w = [("sSY", M["xload_%d" % tb]),
                     ("sAC", M["xev_%d_%d" % (tb, cc - 2)] if cc >= 2 else xev0)]
                if tb == 0 and cc == 0:
                    w.append(("sDV", M["consts"]))

                def trf(e, cc=cc, bank=bank):
                    for s1 in range(4):
                        inst = e.transpose(bank[:, s1 * 128:(s1 + 1) * 128],
                                           xtm[:, s1 * C + cc * 128: s1 * C + cc * 128 + 128],
                                           self.identf[:])
                    return inst
                E.op("tensor", trf, waits=w, incs=[("sPE", 1)])
                pems.append(E.mark("sPE"))
                E.op("scalar",
                     (lambda e, cc=cc, bank=bank: e.copy(xv[:, cc, 1:513], bank[:, 0:512])),
                     waits=[("sPE", pems[cc]),
                            ("sSC", M.get("xspill_%d" % (tb - 1), 0)),
                            ("sDV", M.get("mix_g_%d" % (tb - 1), 0))],
                     incs=[("sAC", 1)])
                M["xev_%d_%d" % (tb, cc)] = E.mark("sAC")
            M["xtr_%d" % tb] = E.mark("sPE")

            def xxf(e, tb=tb):
                e.tensor_copy(xv[:, :, 0:1], self.xcarry[:].rearrange("p (c u) -> p c u", u=1))
                e.tensor_tensor(xxv[:, :, :], xv[:, :, 0:512], xv[:, :, 1:513], ALU.subtract)
                e.tensor_copy(self.xcarry[:].rearrange("p (c u) -> p c u", u=1), xv[:, :, 512:513])
                for cc in range(8):
                    inst = e.scalar_tensor_tensor(
                        mixbuf[:, cc * TB:(cc + 1) * TB],
                        xxv[:, cc, :], self.v_maax[:, cc:cc + 1],
                        xv[:, cc, 1:513], ALU.mult, ALU.add)
                return inst
            w = [("sAC", M["xev_%d_%d" % (tb, 7)]),
                 ("sPE", M.get("proj_g_%d" % (tb - 1), 0)),
                 ("cA", chg0)]
            if tb == 0:
                w += [("sSY", M["vecs"])]
            E.op("vector", xxf, waits=w, incs=[("sDV", 1)])
            M["xx_%d" % tb] = E.mark("sDV")

            for cc in range(8):
                E.dma("scalar", "sSC",
                      S["xfm"][cc * 128:(cc + 1) * 128, t0:t0 + TB], xv[:, cc, 1:513],
                      waits=[("sDV", M["xx_%d" % tb])] if cc == 0 else [])
            M["xspill_%d" % tb] = E.mark("sSC")
            if self._l1stop("xx", tb):
                l1stopped = True
                break

            def loraA(e):
                for ci in range(8):
                    inst = e.matmul(pb[4][0:96, :], self.w_l1[:, ci * 160:ci * 160 + 96],
                                    mixbuf[:, ci * TB:(ci + 1) * TB],
                                    start=(ci == 0), stop=(ci == 7))
                return inst
            w = [("sDV", M["xx_%d" % tb]), ("sAC", E.mark("sAC")),
                 ("cS", tb * NCHB)]
            if tb == 0:
                w += [("sGP", M["wload"])]
            E.op("tensor", loraA, waits=w, incs=[("sPE", 1)])
            mk_la = E.mark("sPE")

            def loraB(e):
                for ci in range(8):
                    inst = e.matmul(pb[5][0:64, :], self.w_l1[:, ci * 160 + 96:ci * 160 + 160],
                                    mixbuf[:, ci * TB:(ci + 1) * TB],
                                    start=(ci == 0), stop=(ci == 7))
                return inst
            E.op("tensor", loraB, waits=[("sDV", E.mark("sDV"))], incs=[("sPE", 1)])
            mk_lb = E.mark("sPE")

            def ltanh(e):
                e.activation(lt["w"][0:32, :], pb[4][0:32, :], AF.Tanh)
                e.activation(lt["k"][0:32, :], pb[4][32:64, :], AF.Tanh)
                e.activation(lt["v"][0:32, :], pb[4][64:96, :], AF.Tanh)
                e.activation(lt["r"][0:32, :], pb[5][0:32, :], AF.Tanh)
                return e.activation(lt["g"][0:32, :], pb[5][32:64, :], AF.Tanh)
            E.op("scalar", ltanh, waits=[("sPE", mk_lb), ("sPE", mk_la)],
                 incs=[("sAC", 1)])
            M["lt_%d" % tb] = E.mark("sAC")

            for f in "wkvrg":
                wl2w = [("sPE", M.get("mf_%s" % fprev[f], 0) if fprev[f] else 0)]
                if tb == 0 and f == "w":
                    wl2w.append(("sGP", M["wcast"]))
                E.dma("sync", "sSY", w_l2f[:],
                      S["w2b"][fnum[f] * 32:(fnum[f] + 1) * 32, :], waits=wl2w)
                mk_wl2 = E.mark("sSY")

                dvs = []
                for cc in range(8):
                    bank = pb[4] if cc % 2 == 0 else pb[5]
                    w = [("sAC", M["lt_%d" % tb]), ("sSY", mk_wl2)]
                    if cc >= 2:
                        w.append(("sDV", dvs[cc - 2]))
                    elif fprev[f]:
                        w.append(("sDV", M["mix_%s_%d" % (fprev[f], tb)]))
                    E.op("tensor",
                         (lambda e, cc=cc, bank=bank, f=f: e.matmul(
                             bank[:, :], w_l2f[:, cc * 128:(cc + 1) * 128],
                             lt[f][0:32, :], start=True, stop=True)),
                         waits=w, incs=[("sPE", 1)])
                    mkp = E.mark("sPE")

                    wmix = [("sPE", mkp)]
                    if cc == 0:
                        if f == "w":
                            wmix.append(("sPE", mk_lb))
                        else:
                            wmix.append(("sPE", M["proj_%s_%d" % (fprev[f], tb)]))

                    def mixf(e, cc=cc, bank=bank, f=f):
                        e.scalar_tensor_tensor(
                            tmp, bank[:, :], getattr(self, "v_maa" + f)[:, cc:cc + 1],
                            xxv[:, cc, :], ALU.add, ALU.mult)
                        return e.tensor_tensor(mixbuf[:, cc * TB:(cc + 1) * TB],
                                               tmp, xv[:, cc, 1:513], ALU.add)
                    E.op("vector", mixf, waits=wmix, incs=[("sDV", 1)])
                    dvs.append(E.mark("sDV"))
                M["mf_%s" % f] = E.mark("sPE")
                M["mix_%s_%d" % (f, tb)] = E.mark("sDV")

                self._proj(f, tb, mixbuf, lt)
                if self._l1stop("proj" + f, tb):
                    l1stopped = True
                    break
            if l1stopped:
                break

            def colf(e, tb=tb):
                for cc in range(8):
                    e.tensor_tensor(self.dBt[:, cc * 8:(cc + 1) * 8],
                                    Bv[:, cc, 64::64], Bv[:, cc, 0::64][:, 0:8],
                                    ALU.subtract)
                e.tensor_scalar(qct, self.dBt, 0.25, None, ALU.mult)
                for cc in range(8):
                    e.scalar_tensor_tensor(self.bc2t[:, cc * 8:(cc + 1) * 8],
                                           Bv[:, cc, 0::64][:, 0:8], 0.5,
                                           qct[:, cc * 8:(cc + 1) * 8],
                                           ALU.mult, ALU.add)
                e.tensor_scalar(self.bc1nt, self.bc2t, -1.0, None, ALU.mult)
                for cc in range(8):
                    e.tensor_scalar(
                        self.scols[:, cc * NCH + tb * 8: cc * NCH + tb * 8 + 8],
                        self.bc2t[:, cc * 8:(cc + 1) * 8], 2.0,
                        self.carryB[:, cc:cc + 1], ALU.mult, ALU.add)
                return e.tensor_tensor(self.carryB[:], self.carryB[:],
                                       Bv[:, :, 512], ALU.add)
            E.op("vector", colf,
                 waits=[("sDV", M["scan_%d" % tb]), ("cA", chg0), ("cV", chg0)],
                 incs=[("sDV", 1)])
            mk_cols = E.mark("sDV")

            E.op("scalar", lambda e: e.activation(self.ehmt, self.dBt, AF.Exp, scale=-0.5),
                 waits=[("sDV", mk_cols), ("cV", chg0)], incs=[("sAC", 1)])
            mk_ehm = E.mark("sAC")

            E.op("vector", lambda e: e.tensor_tensor(self.Dtt, self.ehmt, self.ehmt, ALU.mult),
                 waits=[("sAC", mk_ehm)], incs=[("sDV", 1)])
            M["cols_%d" % tb] = E.mark("sDV")
            if self._l1stop("cols", tb):
                l1stopped = True
                break

            for loc in range(NCHB):
                self._chunk(tb, loc, chg0 + loc)
                if getattr(self, "chunk_stop", None):
                    l1stopped = True
                    break
            if l1stopped:
                break

            if self._l1stop("chunks", tb):
                l1stopped = True
                break

            self._sdiag(tb, p3c, gcc, ydcc)
            if self._l1stop("sdiag", tb):
                l1stopped = True
                break

        M["l1pe_done"] = E.mark("sPE")
        if l1stopped:
            return
        def sumf(e):
            return e.tensor_scalar(tmp[:, 0:8], self.carryB[:], -1.0, None, ALU.mult)
        E.op("vector", sumf, waits=[("cS", NCH)], incs=[("sDV", 1)])
        mk_logd = E.mark("sDV")
        E.dma("scalar", "sSC", S["sum_in"][:, 0:512], self.S32[:],
              waits=[("sDV", mk_logd)])
        E.dma("scalar", "sSC", S["sum_in"][:, 512:520], tmp[:, 0:8])
        M["sumspill"] = E.mark("sSC")

    def _proj(self, f, tb, mixbuf, lt):
        E, D, S, M = self.E, self.D, self.S, self.M
        pb = self.pb
        mixmark = M["mix_%s_%d" % (f, tb)]
        Bv, ew = self.Bv, self.ew

        if f == "w":
            def d1f(e):
                for ci in range(8):
                    inst = e.matmul(pb[4][0:64, :], self.w_d1[:, ci * 64:(ci + 1) * 64],
                                    mixbuf[:, ci * TB:(ci + 1) * TB],
                                    start=(ci == 0), stop=(ci == 7))
                return inst
            E.op("tensor", d1f, waits=[("sDV", mixmark), ("sAC", E.mark("sAC"))],
                 incs=[("sPE", 1)])
            mk = E.mark("sPE")
            E.op("scalar", (lambda e: e.activation(self.dg1t[0:64, :], pb[4][0:64, :],
                                                   AF.Tanh)),
                 waits=[("sPE", mk)], incs=[("sAC", 1)])
            mk_t = E.mark("sAC")
            for cc in range(8):
                bank = pb[0] if cc % 2 == 0 else pb[1]
                E.op("tensor",
                     (lambda e, cc=cc, bank=bank: e.matmul(
                         bank[:, :], self.w_d2[:, cc * 128:(cc + 1) * 128],
                         self.dg1t[0:64, :], start=True, stop=True)),
                     waits=[("sAC", mk_t if cc < 2 else M["ewev_%d_%d" % (tb, cc - 2)])],
                     incs=[("sPE", 1)])
                mkp = E.mark("sPE")
                E.op("scalar",
                     (lambda e, cc=cc, bank=bank: e.activation(
                         ew, bank[:, :], AF.Exp, bias=self.v_td[:, cc:cc + 1])),
                     waits=[("sPE", mkp),
                            ("sDV", M.get("scan_c_%d_%d" % (tb, cc - 1), 0))],
                     incs=[("sAC", 1)])
                M["ewev_%d_%d" % (tb, cc)] = E.mark("sAC")
                E.op("vector",
                     (lambda e, cc=cc: e.tensor_tensor_scan(
                         Bv[:, cc, 1:513], ew, self.ones512[:], 0.0,
                         ALU.add, ALU.mult)),
                     waits=[("sAC", M["ewev_%d_%d" % (tb, cc)])], incs=[("sDV", 1)])
                M["scan_c_%d_%d" % (tb, cc)] = E.mark("sDV")
            M["scan_%d" % tb] = E.mark("sDV")
            M["proj_w_%d" % tb] = E.mark("sPE")
            return

        if f == "g":
            def g1f(e):
                for ci in range(8):
                    inst = e.matmul(pb[4][0:64, :], self.w_g1[:, ci * 64:(ci + 1) * 64],
                                    mixbuf[:, ci * TB:(ci + 1) * TB],
                                    start=(ci == 0), stop=(ci == 7))
                return inst
            E.op("tensor", g1f, waits=[("sDV", mixmark), ("sAC", E.mark("sAC"))],
                 incs=[("sPE", 1)])
            mk = E.mark("sPE")
            E.op("scalar", (lambda e: e.activation(self.dg1t[0:64, :], pb[4][0:64, :],
                                                   AF.Tanh)),
                 waits=[("sPE", mk)], incs=[("sAC", 1)])
            mk_t = E.mark("sAC")
            for cc in range(8):
                bank = pb[0] if cc % 2 == 0 else pb[1]
                E.op("tensor",
                     (lambda e, cc=cc, bank=bank: e.matmul(
                         bank[:, :], self.w_g2[:, cc * 128:(cc + 1) * 128],
                         self.dg1t[0:64, :], start=True, stop=True)),
                     waits=[("sAC", mk_t if cc < 2 else M["gev_%d_%d" % (tb, cc - 2)])],
                     incs=[("sPE", 1)])
                mkp = E.mark("sPE")
                E.op("scalar",
                     (lambda e, cc=cc, bank=bank: e.copy(self._gcc[:], bank[:, :])),
                     waits=[("sPE", mkp),
                            ("sSC", M.get("gsp_%d_%d" % (tb, cc - 1), 0))],
                     incs=[("sAC", 1)])
                M["gev_%d_%d" % (tb, cc)] = E.mark("sAC")
                E.dma("scalar", "sSC",
                      S["g"].rearrange("(ch p) n -> ch p n", p=128)[
                          tb * 8:tb * 8 + 8, :, cc * 64:(cc + 1) * 64].transpose([1, 0, 2]),
                      self._gcc[:].rearrange("p (ch n) -> p ch n", n=64),
                      waits=[("sAC", M["gev_%d_%d" % (tb, cc)])])
                M["gsp_%d_%d" % (tb, cc)] = E.mark("sSC")
            M["proj_g_%d" % tb] = E.mark("sPE")
            return

        wkey = {"k": "Wk", "v": "Wv", "r": "Wr"}[f]
        dst = {"k": self.ktb, "v": self.vtb, "r": self.rtb}[f]
        bufi = {"k": 0, "v": 1, "r": 0}[f]
        buf = self.wstr8[bufi]
        nxt = {"k": ("Wv", 1, "v"), "v": ("Wr", 0, "r"), "r": (None, None, None)}[f]
        if f == "k":
            pass
        if nxt[0] is not None:
            if nxt[1] == 0:
                wldw = [("sPE", M["proj_k_%d" % tb])]
            else:
                wldw = [("sPE", M.get("proj_v_%d" % (tb - 1), 0))]
            skey = {"Wv": "wvb", "Wr": "wrb"}[nxt[0]]
            E.dma("sync", "sSY",
                  self.wstr8[nxt[1]].rearrange("p (a co) -> p a co", co=C),
                  S[skey].rearrange("(a p) co -> p a co", p=128),
                  waits=wldw)
            M["wld_%s_%d" % (nxt[2], tb)] = E.mark("sSY")
        mk_w = M["wld_%s_%d" % (f, tb)]

        for half in range(2):
            hs = half * 256

            def pf(e, hs=hs, buf=buf):
                for co in range(8):
                    bank = pb[co // 2]
                    for ci in range(8):
                        inst = e.matmul(bank[:, (co % 2) * 256:(co % 2) * 256 + 256],
                                        buf[:, ci * C + co * 128: ci * C + co * 128 + 128],
                                        mixbuf[:, ci * TB + hs: ci * TB + hs + 256],
                                        start=(ci == 0), stop=(ci == 7))
                return inst
            E.op("tensor", pf,
                 waits=[("sSY", mk_w), ("sDV", mixmark), ("sAC", E.mark("sAC"))],
                 incs=[("sPE", 1)])
            mk_proj = E.mark("sPE")

            def evf(e, hs=hs, dst=dst):
                for co in range(8):
                    bank = pb[co // 2]
                    inst = e.copy(dst[:, co * TB + hs: co * TB + hs + 256],
                                  bank[:, (co % 2) * 256:(co % 2) * 256 + 256])
                return inst
            E.op("scalar", evf,
                 waits=[("sPE", mk_proj), ("cV", tb * NCHB),
                        ("sDV", M.get("sdiag_%d" % (tb - 1), 0))],
                 incs=[("sAC", 1)])
        M["proj_%s_%d" % (f, tb)] = E.mark("sPE")
        M["pvev_%s_%d" % (f, tb)] = E.mark("sAC")
    def _chunk(self, tb, loc, chg):
        E, S, M = self.E, self.S, self.M
        par = chg % 2
        pb4 = self.pb[4]
        Bv = self.Bv
        ek, er = self.ek[par], self.er[par]
        ktil, rtil = self.ktil[par], self.rtil[par]
        kv3 = self.ktb.rearrange("p (c n) -> p c n", n=TB)
        rv3 = self.rtb.rearrange("p (c n) -> p c n", n=TB)
        vv3 = self.vtb.rearrange("p (c n) -> p c n", n=TB)
        col = lambda t, cc: t[:, cc * 8 + loc: cc * 8 + loc + 1]

        def act_f(e):
            for cc in range(8):
                e.activation(ek[:, cc * 64:(cc + 1) * 64],
                             Bv[:, cc, loc * 64 + 1: loc * 64 + 65],
                             AF.Exp, scale=0.5, bias=col(self.bc1nt, cc))
                inst = e.activation(er[:, cc * 64:(cc + 1) * 64],
                                    Bv[:, cc, loc * 64: loc * 64 + 64],
                                    AF.Exp, scale=-0.5, bias=col(self.bc2t, cc))
            return inst
        E.op("scalar", act_f,
             waits=[("sDV", M["cols_%d" % tb]), ("cV", max(0, chg - 1))],
             incs=[("cA", 1)])
        if getattr(self, "chunk_stop", None) == "act":
            return

        def dve_f(e):
            e.tensor_tensor(ek, ek, ek, ALU.mult)
            e.tensor_tensor(er, er, er, ALU.mult)
            e.tensor_tensor(ktil.rearrange("p (c n) -> p c n", n=64),
                            kv3[:, :, loc * 64:(loc + 1) * 64],
                            ek.rearrange("p (c n) -> p c n", n=64), ALU.mult)
            e.tensor_tensor(rtil.rearrange("p (c n) -> p c n", n=64),
                            rv3[:, :, loc * 64:(loc + 1) * 64],
                            er.rearrange("p (c n) -> p c n", n=64), ALU.mult)
            for cc in range(8):
                e.tensor_scalar(self.khat[:, cc * 64:(cc + 1) * 64],
                                ktil[:, cc * 64:(cc + 1) * 64],
                                col(self.ehmt, cc), None, ALU.mult)
                inst = e.tensor_scalar(self.slbf[:, cc * 64:(cc + 1) * 64],
                                       self.S32[:, cc * 64:(cc + 1) * 64],
                                       col(self.ehmt, cc), None, ALU.mult)
            return inst
        E.op("vector", dve_f,
             waits=[("cA", chg + 1), ("cP", chg), ("cS", chg),
                    ("sSC", M.get("slsp_%d" % (chg - 1), 0))],
             incs=[("cV", 1)])
        mk_cv = E.mark("cV")
        if getattr(self, "chunk_stop", None) == "dve":
            return

        E.dma("scalar", "sSC",
              S["rt"].rearrange("(ch p) n -> ch p n", p=128)[chg],
              rtil[:], waits=[("cV", chg + 1)])
        M["rsp_%d" % chg] = E.mark("sSC")
        E.dma("scalar", "sSC",
              S["sl"].rearrange("(ch p) n -> ch p n", p=128)[chg],
              self.slbf[:])
        M["slsp_%d" % chg] = E.mark("sSC")
        if getattr(self, "chunk_stop", None) == "spill":
            return

        def a_mm(e):
            for cc in range(8):
                e.matmul(self.pba[0:64, cc * 64:(cc + 1) * 64],
                         ktil[0:64, cc * 64:(cc + 1) * 64],
                         rtil[0:64, cc * 64:(cc + 1) * 64],
                         start=True, stop=True, tile_position=(0, 0))
                inst = e.matmul(self.pba[64:128, cc * 64:(cc + 1) * 64],
                                ktil[64:128, cc * 64:(cc + 1) * 64],
                                rtil[64:128, cc * 64:(cc + 1) * 64],
                                start=True, stop=True, tile_position=(64, 64))
            return inst
        E.op("tensor", a_mm,
             waits=[("cV", chg + 1), ("gE", chg)],
             incs=[("sPE", 1)])
        mk_a = E.mark("sPE")

        E.op("vector",
             (lambda e: e.select(self.asel[:], self.mask4u[:],
                                 self.pba[:, :], self.zeros_bf[:])),
             waits=[("sPE", mk_a),
                    ("sSC", M.get("asp_%d" % (chg - 1), 0))],
             incs=[("gE", 1)])
        mk_sel = E.mark("gE")
        av = S["A"].rearrange("(ch s) n -> ch s n", s=64)
        E.dma("scalar", "sSC", av[chg, :, 0:512], self.asel[0:64, :],
              waits=[("gE", mk_sel)])
        E.dma("scalar", "sSC", av[chg, :, 512:1024], self.asel[64:128, :])
        M["asp_%d" % chg] = E.mark("sSC")

        for (srcname, dstt, mkey) in [("v", self.vtms, "vtm"), ("kh", self.ktms, "ktm")]:
            for half in range(2):
                def tr_f(e, srcname=srcname, half=half):
                    for i in range(4):
                        cc = half * 4 + i
                        if srcname == "v":
                            src = vv3[:, cc, loc * 64:(loc + 1) * 64]
                        else:
                            src = self.khat[:, cc * 64:(cc + 1) * 64]
                        inst = e.transpose(self.pbt[:, i * 128:(i + 1) * 128],
                                           src, self.identb[:])
                    return inst
                E.op("tensor", tr_f,
                     waits=[("cV", chg + 1),
                            ("sAC", M.get("trev_prev", 0)),
                            ("sSC", M.get("vtmsp_%d" % (chg - 1), 0)
                             if srcname == "v" and half == 0 else 0)],
                     incs=[("sPE", 1)])
                mk_tr = E.mark("sPE")
                E.op("scalar",
                     (lambda e, dstt=dstt, half=half: e.copy(
                         dstt[0:64, half * 512:(half + 1) * 512], self.pbt[:, :])),
                     waits=[("sPE", mk_tr)], incs=[("sAC", 1)])
                M["trev_prev"] = E.mark("sAC")

        E.dma("scalar", "sSC",
              S["vtm"].rearrange("(ch s) n -> ch s n", s=64)[chg],
              self.vtms[0:64, :],
              waits=[("sAC", M["trev_prev"])])
        M["vtmsp_%d" % chg] = E.mark("sSC")

        if getattr(self, "chunk_stop", None) == "tr":
            return

        def st_mm(e):
            for cc in range(8):
                e.matmul(pb4[0:64, cc * 64:(cc + 1) * 64],
                         self.ktms[0:64, cc * 128: cc * 128 + 64],
                         self.vtms[0:64, cc * 128: cc * 128 + 64],
                         start=True, stop=True, tile_position=(0, 0))
                inst = e.matmul(pb4[64:128, cc * 64:(cc + 1) * 64],
                                self.ktms[0:64, cc * 128 + 64: cc * 128 + 128],
                                self.vtms[0:64, cc * 128 + 64: cc * 128 + 128],
                                start=True, stop=True, tile_position=(0, 64))
            return inst
        E.op("tensor", st_mm,
             waits=[("sAC", M["trev_prev"]), ("cS", chg)],
             incs=[("cP", 1)])

        def st_up(e):
            for cc in range(8):
                inst = e.scalar_tensor_tensor(
                    self.S32[:, cc * 64:(cc + 1) * 64],
                    self.S32[:, cc * 64:(cc + 1) * 64], col(self.Dtt, cc),
                    pb4[:, cc * 64:(cc + 1) * 64], ALU.mult, ALU.add)
            return inst
        E.op("vector", st_up, waits=[("cP", chg + 1)], incs=[("cS", 1)])

    def _sdiag(self, tb, p3c, gcc, ydcc):
        E, S, M = self.E, self.S, self.M
        kv3 = self.ktb.rearrange("p (c n) -> p c n", n=TB)
        rv3 = self.rtb.rearrange("p (c n) -> p c n", n=TB)
        vv3 = self.vtb.rearrange("p (c n) -> p c n", n=TB)
        ssl = self.tmp

        mks = []
        for cc in range(8):
            wp3 = [("cS", (tb + 1) * NCHB)] if cc == 0 else \
                [("sPE", mks[cc - 1])]
            E.op("vector",
                 (lambda e, cc=cc: e.scalar_tensor_tensor(
                     p3c[:], rv3[:, cc, :], self.v_u[:, cc:cc + 1],
                     kv3[:, cc, :], ALU.mult, ALU.mult)),
                 waits=wp3,
                 incs=[("sDV", 1)])
            mkd = E.mark("sDV")
            E.op("tensor",
                 (lambda e, cc=cc: e.matmul(
                     self.pb[5][0:16, :], self.ione_b[:, cc * 16:(cc + 1) * 16],
                     p3c[:], start=(cc == 0), stop=(cc == 7))),
                 waits=[("sDV", mkd)] + ([("sAC", E.mark("sAC"))] if cc == 0 else []),
                 incs=[("sPE", 1)])
            mks.append(E.mark("sPE"))

        E.op("scalar", (lambda e: e.copy(ssl[0:16, :], self.pb[5][0:16, :])),
             waits=[("sPE", mks[-1])], incs=[("sAC", 1)])
        mk_ssl = E.mark("sAC")

        for cc in range(8):
            bank = self.pb[2] if cc % 2 == 0 else self.pb[3]
            w = [("sAC", mk_ssl)]
            if cc >= 2:
                w.append(("sDV", M["ydtt_%d_%d" % (tb, cc - 2)]))
            E.op("tensor",
                 (lambda e, cc=cc, bank=bank: e.matmul(
                     bank[:, :], self.ind_cc[:, cc * 128:(cc + 1) * 128],
                     ssl[0:16, :], start=True, stop=True)),
                 waits=w, incs=[("sPE", 1)])
            mkb = E.mark("sPE")
            E.op("vector",
                 (lambda e, cc=cc, bank=bank: e.tensor_tensor(
                     ydcc[:], vv3[:, cc, :], bank[:, :], ALU.mult)),
                 waits=[("sPE", mkb),
                        ("sSC", M.get("ydsp_%d_%d" % (tb, cc - 1), 0))],
                 incs=[("sDV", 1)])
            M["ydtt_%d_%d" % (tb, cc)] = E.mark("sDV")
            E.dma("scalar", "sSC",
                  S["yd"].rearrange("(ch p) n -> ch p n", p=128)[
                      tb * 8:tb * 8 + 8, :, cc * 64:(cc + 1) * 64].transpose([1, 0, 2]),
                  ydcc[:].rearrange("p (ch n) -> p ch n", n=64),
                  waits=[("sDV", M["ydtt_%d_%d" % (tb, cc)])])
            M["ydsp_%d_%d" % (tb, cc)] = E.mark("sSC")
        M["sdiag_%d" % tb] = E.mark("sDV")
    def exchange(self):
        E, S, M = self.E, self.S, self.M
        for cc in range(8):
            E.op("tensor",
                 (lambda e, cc=cc: e.matmul(self.pb[5][0:16, 0:128], self.ones16[:],
                                            self.r_lng[:, cc * 128:(cc + 1) * 128],
                                            start=True, stop=True)),
                 waits=[("sAC", E.mark("sAC")), ("sDV", E.mark("sDV")),
                        ("sPE", E.mark("sPE"))] if cc == 0 else
                       [("sDV", E.mark("sDV"))],
                 incs=[("sPE", 1)])
            E.op("vector",
                 (lambda e, cc=cc: e.tensor_tensor(
                     self.gin_cc[:, cc * 128:(cc + 1) * 128],
                     self.ind_cc[:, cc * 128:(cc + 1) * 128],
                     self.pb[5][0:16, 0:128], ALU.mult)),
                 waits=[("sPE", E.mark("sPE"))], incs=[("sDV", 1)])
        E.op("gpsimd",
             lambda e: e.collective_compute(
                 "AllGather", ALU.bypass, replica_groups=[list(range(8))],
                 ins=[S["sum_in"].opt()], outs=[S["sum_out"].opt()]),
             waits=[("sSC", M["sumspill"])], incs=[("sCC", 1)])
        M["ccsum"] = E.mark("sCC")

        E.dma("sync", "sSY", self.logdt[:].rearrange("p (m c) -> p m c", c=8),
              S["sum_out"][:, 512:520].rearrange("(m p) c -> p m c", p=128),
              waits=[("sCC", M["ccsum"])], allow_slow_non_contiguous=True)
        mk_ld = E.mark("sSY")
        ldv = self.logdt.rearrange("p (m c) -> p m c", c=8)

        self.ldT = self.cF(14144, 128)[0:8, :]
        self.wfac = self.cF(14272, 64)
        psin = self.cF(0, 512)
        facts = self.cF(13696, 256)

        for cc in range(8):
            E.op("tensor",
                 (lambda e, cc=cc: e.transpose(self.pb[5][0:8, 0:128],
                                               ldv[:, :, cc], self.identf[:])),
                 waits=[("sSY", mk_ld)] + ([("sAC", E.mark("sAC"))] if cc == 0 else
                                           [("sAC", M["ldT_%d" % (cc - 1)])]),
                 incs=[("sPE", 1)])
            mkp = E.mark("sPE")
            E.op("scalar",
                 (lambda e: e.copy(self.ldT[:], self.pb[5][0:8, 0:128])),
                 waits=[("sPE", mkp)], incs=[("sAC", 1)])
            M["ldT_%d" % cc] = E.mark("sAC")

            def emm(e, cc=cc):
                e.matmul(self.pb[4][:, 0:8], self.ldT[:], self.wsel_sb[:],
                         start=True, stop=False)
                return e.matmul(self.pb[4][:, 0:8], self.r_ones[0:1, 0:128],
                                self.wbias_sb[:], start=False, stop=True)
            E.op("tensor", emm, waits=[("sAC", M["ldT_%d" % cc]),
                                       ("sDV", M.get("wfac_%d" % (cc - 1), 0))],
                 incs=[("sPE", 1)])
            mkp2 = E.mark("sPE")
            E.op("scalar",
                 (lambda e, cc=cc: e.activation(self.wfac[:, cc * 8:(cc + 1) * 8],
                                                self.pb[4][:, 0:8], AF.Exp)),
                 waits=[("sPE", mkp2)], incs=[("sAC", 1)])
            M["wfac_%d" % cc] = E.mark("sAC")
        mk_wf = M["wfac_7"]

        E.op("vector", lambda e: e.memset(self.Sin[:], 0.0),
             waits=[], incs=[("sDV", 1)])
        for j in range(8):
            E.dma("sync", "sSY", psin,
                  S["sum_out"][j * 128:(j + 1) * 128, 0:512],
                  waits=[("sDV", E.mark("sDV"))] if j >= 2 else [])
            mk_pl = E.mark("sSY")

            def acc(e, j=j):
                for cc in range(8):
                    inst = e.scalar_tensor_tensor(
                        self.Sin[:, cc * 64:(cc + 1) * 64],
                        psin[:, cc * 64:(cc + 1) * 64],
                        self.wfac[:, cc * 8 + j: cc * 8 + j + 1],
                        self.Sin[:, cc * 64:(cc + 1) * 64], ALU.mult, ALU.add)
                return inst
            E.op("vector", acc, waits=[("sSY", mk_pl), ("sAC", mk_wf)],
                 incs=[("sDV", 1)])
        M["sin"] = E.mark("sDV")

        E.op("scalar", lambda e: e.activation(facts, self.scols[:], AF.Exp, scale=-1.0),
             waits=[("cV", NCH)], incs=[("sAC", 1)])
        M["facts"] = E.mark("sAC")
        self.facts = facts

    def _l2stop(self, part):
        return getattr(self, "l2_stop", None) == part

    def loop2(self):
        E, D, S, M = self.E, self.D, self.S, self.M
        l2stopped = False
        cF, cB = self.cF, self.cB
        pb = self.pb
        xfms = cF(0, 4096)
        xfv = xfms.rearrange("p (c n) -> p c n", n=TB)
        x1 = cF(4096, 4096)
        x1v = x1.rearrange("p (c n) -> p c n", n=TB)
        x1tm = cF(8192, 4096)
        ychp = [cF(14336, 512), cF(14848, 512)]
        ysqp = [cF(15360, 512), cF(15872, 512)]
        stp = [cF(16384, 3072), cF(19456, 3072)]
        lncol = cF(14080, 64)
        xq = cB(0, 16384)
        wo = cB(16384, 8192)
        a_s = cB(24576, 1024)
        v_s = cB(25600, 1024)
        r_s = cB(26624, 1024)
        sl_sp = [cB(27648, 512), cB(34304, 512)]
        yd_sp = [cB(28160, 512), cB(34816, 512)]
        g_sp = [cB(28672, 512), cB(35328, 512)]
        woin = cB(29184, 4096)
        sfull = cB(33280, 1024)
        self.xq = xq

        E.dma("sync", "sSY", wo.rearrange("p (a co) -> p a co", co=C),
              S["wob"].rearrange("(a p) co -> p a co", p=128),
              waits=[("sPE", M.get("l1pe_done", 0))])
        M["wo_ld"] = E.mark("sSY")

        for tb in range(NTB):
            t0 = tb * TB
            E.dma("sync", "sSY", xfms.rearrange("p (c n) -> p c n", n=TB),
                  S["xfm"][:, t0:t0 + TB].rearrange("(c p) n -> p c n", p=128),
                  waits=[("sSC", M["sumspill"]), ("sDV", M["sin"])] if tb == 0
                  else [("sSC", M["ln1_%d" % (tb - 1)])])
            M["xs_%d" % tb] = E.mark("sSY")

            for loc in range(NCHB):
                chg = tb * NCHB + loc
                p = chg % 2
                pbase = 0
                sl_s, yd_s, g_s = sl_sp[0], yd_sp[p], g_sp[p]
                ych, ysq, st = ychp[p], ysqp[p], stp[p]
                sb0 = 16384 + p * 3072
                mu2 = self.cF(sb0, 64)[0:16, :]
                msq = self.cF(sb0 + 512, 64)[0:16, :]
                var = self.cF(sb0 + 1024, 64)[0:16, :]
                lnv = self.cF(sb0 + 1536, 64)[0:16, :]
                rstd = self.cF(sb0 + 2048, 64)[0:16, :]
                qng = self.cF(sb0 + 2560, 64)[0:16, :]
                pbY = pb[p]
                pbP = pb[2 + 2 * p]
                pbQ = pb[3 + 2 * p]

                wld = [("sSC", M["sumspill"])] if chg == 0 else \
                    [("sPE", M.get("ymm_%d" % (chg - 1), 0))]
                av2 = S["A"].rearrange("(ch s) n -> ch s n", s=64)
                vv2 = S["vtm"].rearrange("(ch s) n -> ch s n", s=64)
                rv2 = S["rt"].rearrange("(ch p) n -> ch p n", p=128)
                E.dma("sync", "sSY", a_s[pbase:pbase + 64, :], av2[chg], waits=wld)
                E.dma("sync", "sSY", v_s[pbase:pbase + 64, :], vv2[chg])
                E.dma("sync", "sSY", r_s[pbase:pbase + 64, 0:512],
                      rv2[chg, 0:64, :])
                E.dma("sync", "sSY", r_s[pbase:pbase + 64, 512:1024],
                      rv2[chg, 64:128, :])
                E.dma("sync", "sSY", sl_s[:],
                      S["sl"].rearrange("(ch p) n -> ch p n", p=128)[chg])
                mk_ld = E.mark("sSY")
                E.dma("sync", "sSY", yd_s[:],
                      S["yd"].rearrange("(ch p) n -> ch p n", p=128)[chg],
                      waits=[("sDV", M.get("yev_%d" % (chg - 2), 0))])
                mk_ldyd = E.mark("sSY")
                E.dma("sync", "sSY", g_s[:],
                      S["g"].rearrange("(ch p) n -> ch p n", p=128)[chg],
                      waits=[("sDV", M.get("app_%d" % (chg - 2), 0))])
                mk_ldg = E.mark("sSY")

                def sff(e, chg=chg, pbase=pbase, sl_s=sl_s):
                    for cc in range(8):
                        e.scalar_tensor_tensor(
                            sfull[pbase:pbase + 64, cc * 64:(cc + 1) * 64],
                            self.Sin[0:64, cc * 64:(cc + 1) * 64],
                            self.facts[0:64, cc * 32 + chg: cc * 32 + chg + 1],
                            sl_s[0:64, cc * 64:(cc + 1) * 64], ALU.mult, ALU.add)
                        inst = e.scalar_tensor_tensor(
                            sfull[pbase:pbase + 64, 512 + cc * 64: 512 + (cc + 1) * 64],
                            self.Sin[64:128, cc * 64:(cc + 1) * 64],
                            self.facts[64:128, cc * 32 + chg: cc * 32 + chg + 1],
                            sl_s[64:128, cc * 64:(cc + 1) * 64], ALU.mult, ALU.add)
                    return inst
                E.op("vector", sff,
                     waits=[("sSY", mk_ld), ("sAC", M["facts"]),
                            ("sPE", M.get("ymm_%d" % (chg - 1), 0))],
                     incs=[("sDV", 1)])
                mk_sf = E.mark("sDV")

                def ymm(e, chg=chg, pbase=pbase, pbY=pbY):
                    tpA = (pbase, 0)
                    tpB = (pbase, 64)
                    for cc in range(8):
                        e.matmul(pbY[0:64, cc * 64:(cc + 1) * 64],
                                 v_s[pbase:pbase + 64, cc * 128: cc * 128 + 64],
                                 a_s[pbase:pbase + 64, cc * 64:(cc + 1) * 64],
                                 start=True, stop=False, tile_position=tpA)
                        e.matmul(pbY[0:64, cc * 64:(cc + 1) * 64],
                                 sfull[pbase:pbase + 64, cc * 64:(cc + 1) * 64],
                                 r_s[pbase:pbase + 64, cc * 64:(cc + 1) * 64],
                                 start=False, stop=True, tile_position=tpA)
                        e.matmul(pbY[64:128, cc * 64:(cc + 1) * 64],
                                 v_s[pbase:pbase + 64, cc * 128 + 64: cc * 128 + 128],
                                 a_s[pbase:pbase + 64, 512 + cc * 64: 512 + (cc + 1) * 64],
                                 start=True, stop=False, tile_position=tpB)
                        inst = e.matmul(
                            pbY[64:128, cc * 64:(cc + 1) * 64],
                            sfull[pbase:pbase + 64, 512 + cc * 64: 512 + (cc + 1) * 64],
                            r_s[pbase:pbase + 64, 512 + cc * 64: 512 + (cc + 1) * 64],
                            start=False, stop=True, tile_position=tpB)
                    return inst
                ymw = [("sDV", mk_sf), ("sDV", M.get("yev_%d" % (chg - 2), 0))]
                if loc <= 1:
                    ymw.append(("sDV", M.get("x1_1_%d" % (tb - 1), 0)))
                E.op("tensor", ymm, waits=ymw, incs=[("sPE", 1)])
                M["ymm_%d" % chg] = E.mark("sPE")
                if self._l2stop("ymm"):
                    l2stopped = True
                    break

                E.op("vector",
                     (lambda e, ych=ych, yd_s=yd_s, pbY=pbY: e.tensor_tensor(
                         ych, pbY[:, :], yd_s[:], ALU.add)),
                     waits=[("sPE", M["ymm_%d" % chg]), ("sSY", mk_ldyd)],
                     incs=[("sDV", 1)])
                M["yev_%d" % chg] = E.mark("sDV")
                if "dbg" in self.taps:
                    E.dma("scalar", "sSC", S["dbg"][:, chg * 512:(chg + 1) * 512],
                          ych, waits=[("sDV", M["yev_%d" % chg])])
                if self._l2stop("yev"):
                    l2stopped = True
                    break

                E.op("scalar",
                     (lambda e, ych=ych, ysq=ysq: e.activation(ysq, ych, AF.Square)),
                     waits=[("sDV", M["yev_%d" % chg]),
                            ("sPE", M.get("smm_%d" % (chg - 2), 0))],
                     incs=[("sAC", 1)])
                mk_sq = E.mark("sAC")

                def smm(e, ych=ych, ysq=ysq, pbP=pbP, pbQ=pbQ):
                    for cc in range(8):
                        e.matmul(pbP[0:16, 0:64],
                                 self.ione_cc[:, cc * 16:(cc + 1) * 16],
                                 ych[:, cc * 64:(cc + 1) * 64],
                                 start=(cc == 0), stop=(cc == 7))
                    for cc in range(8):
                        inst = e.matmul(pbQ[0:16, 0:64],
                                        self.ione_cc[:, cc * 16:(cc + 1) * 16],
                                        ysq[:, cc * 64:(cc + 1) * 64],
                                        start=(cc == 0), stop=(cc == 7))
                    return inst
                smw = [("sAC", mk_sq), ("sDV", M.get("app_%d" % (chg - 2), 0))]
                if loc == 0:
                    smw.append(("sDV", M.get("x1_1_%d" % (tb - 1), 0)))
                elif loc == 1:
                    smw.append(("sAC", M.get("l1q_ac_%d" % (tb - 1), 0)))
                if "dbg4" in self.taps:
                    smw.append(("sSC", M.get("dbg4_%d" % (chg - 2), 0)))
                E.op("tensor", smm, waits=smw, incs=[("sPE", 1)])
                M["smm_%d" % chg] = E.mark("sPE")
                mk_smm = M["smm_%d" % chg]

                def statf(e, mu2=mu2, msq=msq, var=var, pbP=pbP, pbQ=pbQ):
                    e.tensor_scalar(mu2, pbP[0:16, 0:64],
                                    1.0 / 64.0, None, ALU.mult)
                    e.tensor_tensor(msq, mu2, mu2, ALU.mult)
                    return e.scalar_tensor_tensor(
                        var, pbQ[0:16, 0:64], 1.0 / 64.0,
                        msq, ALU.mult, ALU.subtract)
                E.op("vector", statf, waits=[("sPE", mk_smm)], incs=[("sDV", 1)])
                M["statf_%d" % chg] = E.mark("sDV")
                mk_st = M["statf_%d" % chg]

                def lnf(e, lnv=lnv, var=var, rstd=rstd):
                    e.activation(lnv, var, AF.Ln, bias=self.epsg[0:16, :])
                    return e.activation(rstd, lnv, AF.Exp, scale=-0.5)
                E.op("scalar", lnf, waits=[("sDV", mk_st)], incs=[("sAC", 1)])
                mk_ln = E.mark("sAC")

                E.op("vector",
                     (lambda e, qng=qng, mu2=mu2, rstd=rstd: e.scalar_tensor_tensor(
                         qng, mu2, -1.0, rstd, ALU.mult, ALU.mult)),
                     waits=[("sAC", mk_ln)], incs=[("sDV", 1)])
                M["stats_%d" % chg] = E.mark("sDV")
                if "dbg4" in self.taps and chg < 8:
                    E.dma("scalar", "sSC",
                          S["dbg"][0:16, chg * 3072:(chg + 1) * 3072],
                          self.cF(16384 + p * 3072, 3072)[0:16, :],
                          waits=[("sDV", M["stats_%d" % chg])])
                    M["dbg4_%d" % chg] = E.mark("sSC")
                if self._l2stop("stats"):
                    l2stopped = True
                    break

                def pqmm(e, pbP=pbP, pbQ=pbQ, rstd=rstd, qng=qng):
                    for cc in range(8):
                        e.matmul(pbP[:, cc * 64:(cc + 1) * 64],
                                 self.gin_cc[:, cc * 128:(cc + 1) * 128],
                                 rstd[:, :],
                                 start=True, stop=True)
                        e.matmul(pbQ[:, cc * 64:(cc + 1) * 64],
                                 self.gin_cc[:, cc * 128:(cc + 1) * 128],
                                 qng[:, :],
                                 start=True, stop=False)
                        inst = e.matmul(pbQ[:, cc * 64:(cc + 1) * 64],
                                        self.r_lnb[0:1, cc * 128:(cc + 1) * 128],
                                        self.r_ones[0:1, 0:64],
                                        start=False, stop=True)
                    return inst
                wpq = [("sDV", M["stats_%d" % chg]),
                       ("sDV", M.get("app_%d" % (chg - 2), 0))]
                if loc == 0:
                    wpq += [("sAC", M.get("l1q_ac_%d" % (tb - 1), 0)),
                            ("sDV", M.get("x1_1_%d" % (tb - 1), 0))]
                elif loc == 1:
                    wpq.append(("sAC", M.get("l1q_ac_%d" % (tb - 1), 0)))
                E.op("tensor", pqmm, waits=wpq, incs=[("sPE", 1)])
                mk_pq = E.mark("sPE")

                def appf(e, loc=loc, ych=ych, ysq=ysq, g_s=g_s, pbP=pbP, pbQ=pbQ):
                    e.tensor_tensor(ysq, ych, pbP[:, :], ALU.mult)
                    e.tensor_tensor(ych, ysq, pbQ[:, :], ALU.add)
                    wv = woin.rearrange("p (c n) -> p c n", n=TB)[:, :, loc * 64:(loc + 1) * 64]
                    return e.tensor_tensor(
                        wv, ych.rearrange("p (c n) -> p c n", n=64),
                        g_s.rearrange("p (c n) -> p c n", n=64), ALU.mult)
                wapp = [("sPE", mk_pq), ("sSY", mk_ldg)]
                if loc == 0:
                    wapp.append(("sPE", M.get("xqpe_%d" % (tb - 1), 0)))
                E.op("vector", appf, waits=wapp, incs=[("sDV", 1)])
                M["app_%d" % chg] = E.mark("sDV")
                if "dbg5" in self.taps:
                    E.dma("scalar", "sSC", S["dbg"][:, chg * 512:(chg + 1) * 512],
                          ych, waits=[("sDV", M["app_%d" % chg])])
                if self._l2stop("app"):
                    l2stopped = True
                    break

            if "dbg2" in self.taps:
                E.dma("scalar", "sSC",
                      S["dbg"].bitcast(BF16)[:, tb * 8192:tb * 8192 + 4096],
                      woin, waits=[("sDV", M["app_%d" % (tb * NCHB + NCHB - 1)])])
            if l2stopped:
                break
            for half in range(2):
                hs = half * 256

                def womm(e, hs=hs):
                    for co in range(8):
                        bank = pb[co // 2]
                        for ci in range(8):
                            inst = e.matmul(
                                bank[:, (co % 2) * 256:(co % 2) * 256 + 256],
                                wo[:, ci * C + co * 128: ci * C + co * 128 + 128],
                                woin[:, ci * TB + hs: ci * TB + hs + 256],
                                start=(ci == 0), stop=(ci == 7))
                    return inst
                w = [("sDV", M["app_%d" % (tb * NCHB + NCHB - 1)]),
                     ("sSY", M["wo_ld"]),
                     ("sDV", M.get("yev_%d" % (tb * NCHB + NCHB - 1), 0))]
                if half == 0:
                    w.append(("sDV", M.get("x1_%s_%d" % (1, tb - 1), 0)))
                else:
                    w.append(("sDV", M.get("x1_%s_%d" % (0, tb), 0)))
                E.op("tensor", womm, waits=w, incs=[("sPE", 1)])
                mk_wo = E.mark("sPE")

                def x1f(e, hs=hs):
                    for co in range(8):
                        bank = pb[co // 2]
                        inst = e.tensor_tensor(
                            x1v[:, co, hs:hs + 256], xfv[:, co, hs:hs + 256],
                            bank[:, (co % 2) * 256:(co % 2) * 256 + 256], ALU.add)
                    return inst
                x1w = [("sPE", mk_wo), ("sSY", M["xs_%d" % tb])]
                if half == 0:
                    x1w.append(("sSC", M.get("x1nsp_%d_3" % (tb - 1), 0)))
                E.op("vector", x1f, waits=x1w, incs=[("sDV", 1)])
                M["x1_%s_%d" % (half, tb)] = E.mark("sDV")
            M["x1_%d" % tb] = E.mark("sDV")
            if "dbg3" in self.taps:
                E.dma("scalar", "sSC", S["dbg"][:, tb * 4096:(tb + 1) * 4096],
                      x1, waits=[("sDV", M["x1_%d" % tb])])
            if self._l2stop("x1"):
                l2stopped = True
                break

            self._ln1_quant(tb, x1, x1v, x1tm, xfms, woin, xq)
        M["wo_free"] = E.mark("sPE")

    def _ln1_quant(self, tb, x1, x1v, x1tm, xfms, woin, xq):
        E, S, M = self.E, self.S, self.M
        pb = self.pb
        lncol = self.cF(14080, 64)
        mucol = lncol[:, 0:1]
        sqcol = lncol[:, 1:2]
        varcol = lncol[:, 2:3]
        rstdc = lncol[:, 4:5]
        lnvc = lncol[:, 3:4]
        amaxc = lncol[:, 5:6]
        sclc = lncol[:, 6:7]
        mu_s = lncol[:, 8:9]
        msqc = lncol[:, 9:10]
        amax2c = lncol[:, 10:11]
        lnsc = lncol[:, 11:12]

        for s1 in range(4):
            evm0 = E.mark("sAC")
            evm = []
            for cc in range(8):
                bank = pb[4] if cc % 2 == 0 else pb[5]
                w = [("sDV", M["x1_%d" % tb]), ("sAC", E.mark("sAC"))]
                E.op("tensor",
                     (lambda e, cc=cc, s1=s1, bank=bank: e.transpose(
                         bank[:, 0:128], x1v[:, cc, s1 * 128:(s1 + 1) * 128],
                         self.identf[:])),
                     waits=w, incs=[("sPE", 1)])
                mk_pe = E.mark("sPE")
                E.op("scalar",
                     (lambda e, cc=cc, s1=s1, bank=bank: e.copy(
                         x1tm[:, s1 * 1024 + cc * 128: s1 * 1024 + cc * 128 + 128],
                         bank[:, 0:128])),
                     waits=[("sPE", mk_pe)], incs=[("sAC", 1)])
                evm.append(E.mark("sAC"))
            mk_tm = E.mark("sAC")

            sl = x1tm[:, s1 * 1024:(s1 + 1) * 1024]
            sq = xfms[:, 0:1024]

            def stat1(e, sl=sl):
                return e.tensor_reduce(mucol, sl, AX.X, ALU.add)
            E.op("vector", stat1, waits=[("sAC", mk_tm)], incs=[("sDV", 1)])
            E.op("scalar", (lambda e, sl=sl, sq=sq: e.activation(sq, sl, AF.Square)),
                 waits=[("sAC", mk_tm), ("sDV", E.mark("sDV"))], incs=[("sAC", 1)])
            mk_sq = E.mark("sAC")
            E.op("vector", (lambda e, sq=sq: e.tensor_reduce(sqcol, sq, AX.X, ALU.add)),
                 waits=[("sAC", mk_sq)], incs=[("sDV", 1)])

            def stat2a(e):
                e.activation(mu_s, mucol, AF.Copy, scale=1.0 / 1024.0)
                e.activation(msqc, mu_s, AF.Square)
                return e.activation(lnvc, sqcol, AF.Copy, scale=1.0 / 1024.0)
            E.op("scalar", stat2a, waits=[("sDV", E.mark("sDV"))], incs=[("sAC", 1)])
            mk_sa = E.mark("sAC")
            E.op("vector",
                 (lambda e: e.tensor_tensor(varcol, lnvc, msqc, ALU.subtract)),
                 waits=[("sAC", mk_sa)], incs=[("sDV", 1)])
            mk_v = E.mark("sDV")

            def lnf2(e):
                e.activation(lnvc, varcol, AF.Ln, bias=self.epsl[:])
                return e.activation(rstdc, lnvc, AF.Exp, scale=-0.5)
            E.op("scalar", lnf2, waits=[("sDV", mk_v)], incs=[("sAC", 1)])
            mk_r = E.mark("sAC")

            E.op("vector",
                 (lambda e, sl=sl: e.tensor_scalar(sl, sl, mu_s, rstdc,
                                                   ALU.subtract, ALU.mult)),
                 waits=[("sAC", mk_r)], incs=[("sDV", 1)])
            mk_n = E.mark("sDV")

            if "dbg6" in self.taps:
                sidx = tb * 4 + s1
                E.dma("scalar", "sSC", S["dbg"][:, sidx * 1024:(sidx + 1) * 1024],
                      sl, waits=[("sDV", mk_n)])
                E.dma("scalar", "sSC",
                      S["dbg"][0:128, 32768 + sidx * 8: 32768 + sidx * 8 + 8],
                      lncol[:, 0:8])
            evb0 = E.mark("sAC")
            evb = []
            for cc in range(8):
                bank = pb[4] if cc % 2 == 0 else pb[5]
                E.op("tensor",
                     (lambda e, cc=cc, bank=bank, sl=sl: e.transpose(
                         bank[:, 0:128], sl[:, cc * 128:(cc + 1) * 128], self.identf[:])),
                     waits=[("sDV", mk_n), ("sAC", E.mark("sAC"))],
                     incs=[("sPE", 1)])
                mk_t3 = E.mark("sPE")
                E.op("scalar",
                     (lambda e, cc=cc, s1=s1, bank=bank: e.copy(
                         x1v[:, cc, s1 * 128:(s1 + 1) * 128], bank[:, 0:128])),
                     waits=[("sPE", mk_t3)], incs=[("sAC", 1)])
                evb.append(E.mark("sAC"))
            mk_fm = E.mark("sAC")
            mk_fm_pe = E.mark("sPE")
            for cc in range(8):
                E.dma("scalar", "sSC",
                      S["x1n"][cc * 128:(cc + 1) * 128,
                               tb * TB + s1 * 128: tb * TB + (s1 + 1) * 128],
                      x1v[:, cc, s1 * 128:(s1 + 1) * 128],
                      waits=[("sAC", mk_fm)] if cc == 0 else [])
            M["x1nsp_%d_%d" % (tb, s1)] = E.mark("sSC")

            E.op("vector",
                 (lambda e, sl=sl: e.tensor_reduce(
                     amaxc, sl, AX.X, ALU.max, apply_absolute_value=True)),
                 waits=[("sDV", mk_n)], incs=[("sDV", 1)])
            E.op("gpsimd",
                 (lambda e: e.tensor_scalar(amax2c, amaxc, 1.0 / 127.0,
                                            1e-5 / 127.0, ALU.mult, ALU.max)),
                 waits=[("sDV", E.mark("sDV"))], incs=[("sGP", 1)])

            def qs2(e, tb=tb, s1=s1):
                e.activation(self.dsx[:, tb * 4 + s1: tb * 4 + s1 + 1],
                             amax2c, AF.Copy)
                e.activation(lnsc, amax2c, AF.Ln)
                return e.activation(sclc, lnsc, AF.Exp, scale=-1.0)
            E.op("scalar", qs2, waits=[("sGP", E.mark("sGP"))], incs=[("sAC", 1)])
            mk_s = E.mark("sAC")
            M["dsxac"] = mk_s

            xqtm = woin[:, 0:1024]

            def qr(e, sl=sl, xqtm=xqtm):
                e.tensor_scalar(sl, sl, sclc, CRND, ALU.mult, ALU.add)
                return e.tensor_scalar(xqtm, sl, CRND, None, ALU.subtract)
            E.op("vector", qr, waits=[("sAC", mk_s), ("sPE", mk_fm_pe)],
                 incs=[("sDV", 1)])
            mk_q = E.mark("sDV")

            for cc in range(8):
                def trq(e, cc=cc):
                    e.transpose(self.pbt[:, 0:128], xqtm[:, cc * 128: cc * 128 + 64],
                                self.identb[:])
                    return e.transpose(self.pbt[:, 128:256],
                                       xqtm[:, cc * 128 + 64: cc * 128 + 128],
                                       self.identb[:])
                E.op("tensor", trq,
                     waits=[("sDV", mk_q), ("sAC", E.mark("sAC"))],
                     incs=[("sPE", 1)])
                mk_t2 = E.mark("sPE")
                tcol = tb * TB + s1 * 128

                def evq(e, cc=cc, tcol=tcol):
                    e.copy(xq.rearrange("p (c n) -> p c n", n=TLOC)[0:64, cc, tcol:tcol + 128],
                           self.pbt[:, 0:128])
                    return e.copy(
                        xq.rearrange("p (c n) -> p c n", n=TLOC)[64:128, cc, tcol:tcol + 128],
                        self.pbt[:, 128:256])
                E.op("scalar", evq, waits=[("sPE", mk_t2)], incs=[("sAC", 1)])
            M["xq_%d_%d" % (tb, s1)] = E.mark("sAC")
            M["xqpe_%d" % tb] = E.mark("sPE")
        M["l1q_ac_%d" % tb] = E.mark("sAC")
        M["ln1_%d" % tb] = E.mark("sSC")
    def ffn(self):
        E, D, S, M = self.E, self.D, self.S, self.M
        cF, cB = self.cF, self.cB
        pb = self.pb
        hw = cF(0, 16384)
        x1ns = cF(16384, 512)
        ddbs = cF(16896, 512)
        x2f = cF(17408, 512)
        x2tm = cF(17920, 4096)
        sq2 = cF(22016, 1024)
        gs = cF(23040, 512)
        fc = cF(23552, 64)
        dsg = fc[:, 0:16]
        dsu = fc[:, 16:32]
        ddc = fc[:, 32:48]
        csq = fc[:, 48:64]
        l2c = cF(23616, 16)
        mucol = l2c[:, 0:1]
        sqcol = l2c[:, 1:2]
        varcol = l2c[:, 2:3]
        lnvc = l2c[:, 3:4]
        rstdc = l2c[:, 4:5]
        mu_s2 = l2c[:, 5:6]
        msqc2 = l2c[:, 6:7]
        xq = self.xq
        wgq = cB(16384, 4096)
        wuq = cB(20480, 4096)
        hqs = cB(24576, 4096)
        wdt4 = cB(16384, 16384)
        hqT = [cB(32768 + i * 512, 512) for i in range(4)]

        hmxq = cF(23632, 64)

        def dsf(e):
            e.tensor_scalar(dsg, self.dsx[:], self.swv[:, 0:1], None, ALU.mult)
            return e.tensor_scalar(dsu, self.dsx[:], self.swv[:, 1:2], None, ALU.mult)
        E.op("vector", dsf, waits=[("sSC", M["ln1_%d" % (NTB - 1)]),
                                   ("sAC", M["dsxac"])],
             incs=[("sDV", 1)])
        M["dsf"] = E.mark("sDV")

        for w in range(NWAVE):
            wt0 = w * TB
            for q in range(8):
                wld = [("sCC", M["ccw"]), ("sPE", M.get("dn_%d" % (w - 1), 0)),
                       ("sPE", M.get("wo_free", 0))]
                if q > 0:
                    wld.append(("sPE", M.get("gu_%d_%d" % (w, q - 1), 0)))
                E.dma("sync", "sSY", wgq.rearrange("p (a n) -> p a n", n=512),
                      S["wgT"].rearrange("(a p) n -> a p n", p=128)[
                          q * 8:(q + 1) * 8].transpose([1, 0, 2]),
                      waits=wld)
                E.dma("sync", "sSY", wuq.rearrange("p (a n) -> p a n", n=512),
                      S["wuT"].rearrange("(a p) n -> a p n", p=128)[
                          q * 8:(q + 1) * 8].transpose([1, 0, 2]))
                mk_w = E.mark("sSY")
                for sl in range(4):
                    gbank = pb[sl % 2]
                    ubank = pb[2 + sl % 2]
                    ts = wt0 + sl * 128
                    sidx = w * 4 + sl

                    def gumm(e, ts=ts, gbank=gbank, ubank=ubank):
                        xqv = xq.rearrange("p (c n) -> p c n", n=TLOC)
                        for ci in range(8):
                            e.matmul(gbank[:, :], xqv[:, ci, ts:ts + 128],
                                     wgq[:, ci * 512:(ci + 1) * 512],
                                     start=(ci == 0), stop=(ci == 7))
                        for ci in range(8):
                            inst = e.matmul(ubank[:, :], xqv[:, ci, ts:ts + 128],
                                            wuq[:, ci * 512:(ci + 1) * 512],
                                            start=(ci == 0), stop=(ci == 7))
                        return inst
                    if sl >= 2:
                        bw = ("sDV", M.get("hup_%d_%d_%d" % (w, q, sl - 2), 0))
                    else:
                        bw = ("sDV", M.get("hup_%d_%d_%d" % (w, q - 1, sl + 2), 0))
                    wmm = [("sSY", mk_w), bw]
                    if w == 0 and q == 0 and sl == 0:
                        wmm.append(("sAC", M["xq_%d_%d" % (NTB - 1, 3)]))
                    E.op("tensor", gumm, waits=wmm, incs=[("sPE", 1)])
                    mk_g = E.mark("sPE")
                    if sl == 3:
                        M["gu_%d_%d" % (w, q)] = mk_g

                    E.op("scalar",
                         (lambda e, gbank=gbank, sidx=sidx: e.activation(
                             gs, gbank[:, :], AF.Silu, scale=dsg[:, sidx:sidx + 1])),
                         waits=[("sPE", mk_g), ("sDV", M["dsf"]),
                                ("sDV", M.get("hup_%d_%d_%d" % (w, q, sl - 1), 0)
                                 if sl else M.get("hup_%d_%d_3" % (w, q - 1), 0))],
                         incs=[("sAC", 1)])
                    mk_si = E.mark("sAC")

                    def hupf(e, sl=sl, q=q, ubank=ubank):
                        hsl = hw[:, sl * 4096 + q * 512: sl * 4096 + (q + 1) * 512]
                        e.tensor_tensor(hsl, gs, ubank[:, :], ALU.mult)
                        return e.tensor_reduce(
                            hmxq[:, sl * 8 + q: sl * 8 + q + 1], hsl,
                            AX.X, ALU.max, apply_absolute_value=True)
                    E.op("vector", hupf, waits=[("sAC", mk_si)], incs=[("sDV", 1)])
                    M["hup_%d_%d_%d" % (w, q, sl)] = E.mark("sDV")

            for sl in range(4):
                sidx = w * 4 + sl

                hmax_s = hmxq[:, 32 + sl: 33 + sl]
                tmpc_s = hmxq[:, 36 + sl: 37 + sl]
                tmp2_s = hmxq[:, 40 + sl: 41 + sl]
                rec_s = hmxq[:, 44 + sl: 45 + sl]
                ddt_s = hmxq[:, 48 + sl: 49 + sl]

                E.op("vector",
                     (lambda e, sl=sl, hmax_s=hmax_s: e.tensor_reduce(
                         hmax_s, hmxq[:, sl * 8: sl * 8 + 8], AX.X, ALU.max)),
                     waits=[("sDV", M["hup_%d_%d_%d" % (w, 7, sl)])],
                     incs=[("sDV", 1)])
                E.op("gpsimd",
                     (lambda e, sidx=sidx, hmax_s=hmax_s, tmpc_s=tmpc_s:
                      e.tensor_tensor(tmpc_s, hmax_s, dsu[:, sidx:sidx + 1],
                                      ALU.mult)),
                     waits=[("sDV", E.mark("sDV"))], incs=[("sGP", 1)])
                E.op("vector",
                     (lambda e, tmpc_s=tmpc_s, tmp2_s=tmp2_s: e.tensor_scalar(
                         tmp2_s, tmpc_s, 1e-5, None, ALU.max)),
                     waits=[("sGP", E.mark("sGP"))], incs=[("sDV", 1)])

                lnt_s = hmxq[:, 52 + sl: 53 + sl]

                def qsha(e, tmp2_s=tmp2_s, rec_s=rec_s, ddt_s=ddt_s,
                         lnt_s=lnt_s, sidx=sidx):
                    e.activation(lnt_s, tmp2_s, AF.Ln)
                    e.activation(rec_s, lnt_s, AF.Exp, scale=-1.0)
                    e.activation(ddt_s, tmp2_s, AF.Copy, scale=1.0 / 127.0)
                    return e.activation(ddc[:, sidx:sidx + 1], ddt_s, AF.Copy,
                                        scale=self.swv[:, 2:3])
                E.op("scalar", qsha, waits=[("sDV", E.mark("sDV"))],
                     incs=[("sAC", 1)])
                E.op("gpsimd",
                     (lambda e, rec_s=rec_s, sidx=sidx: e.tensor_scalar(
                         csq[:, sidx:sidx + 1], rec_s, 127.0,
                         dsu[:, sidx:sidx + 1], ALU.mult, ALU.mult)),
                     waits=[("sAC", E.mark("sAC"))], incs=[("sGP", 1)])
                mk_csq = E.mark("sGP")

                qrw = [("sSC", M.get("hqsp_%d_%d" % (w, sl - 1), 0)),
                       ("sPE", M.get("dn_%d" % (w - 1), 0)),
                       ("sGP", mk_csq)]
                if "dbg7" in self.taps and w == 0:
                    E.dma("scalar", "sSC", S["dbg"][:, sl * 4096:(sl + 1) * 4096],
                          hw[:, sl * 4096:(sl + 1) * 4096],
                          waits=[("sDV", M["hup_%d_%d_%d" % (w, 7, sl)])])
                    if sl == 3:
                        E.dma("scalar", "sSC", S["dbg"][:, 16384:16448], fc,
                              waits=[("sGP", mk_csq), ("sAC", E.mark("sAC"))])
                        E.dma("scalar", "sSC", S["dbg"][:, 16448:16512], hmxq)
                    qrw.append(("sSC", E.mark("sSC")))

                def qrh(e, sl=sl, sidx=sidx):
                    e.tensor_scalar(hw[:, sl * 4096:(sl + 1) * 4096],
                                    hw[:, sl * 4096:(sl + 1) * 4096],
                                    csq[:, sidx:sidx + 1], CRND, ALU.mult, ALU.add)
                    return e.tensor_scalar(hqs, hw[:, sl * 4096:(sl + 1) * 4096],
                                           CRND, None, ALU.subtract)
                E.op("vector", qrh, waits=qrw, incs=[("sDV", 1)])
                mk_hq = E.mark("sDV")
                E.dma("scalar", "sSC", S["hq"][wt0 + sl * 128: wt0 + (sl + 1) * 128, :],
                      hqs, waits=[("sDV", mk_hq)])
                M["hqsp_%d_%d" % (w, sl)] = E.mark("sSC")

            for sl in range(4):
                sidx = w * 4 + sl
                E.op("tensor",
                     (lambda e, sidx=sidx: e.transpose(
                         pb[5][0:1, 0:128], ddc[:, sidx:sidx + 1], self.identf[:])),
                     waits=[("sDV", E.mark("sDV")), ("sAC", E.mark("sAC"))],
                     incs=[("sPE", 1)])
                mk_dt = E.mark("sPE")
                E.op("scalar",
                     (lambda e, sl=sl: e.copy(self.ddrow[0:1, sl * 128:(sl + 1) * 128],
                                              pb[5][0:1, 0:128])),
                     waits=[("sPE", mk_dt)], incs=[("sAC", 1)])
            mk_dr = E.mark("sAC")
            E.op("tensor",
                 (lambda e: e.matmul(pb[4][:, :], self.r_ones[0:1, 0:128],
                                     self.ddrow[0:1, :], start=True, stop=True)),
                 waits=[("sAC", mk_dr), ("sDV", M.get("ddbev_%d" % (w - 1), 0))],
                 incs=[("sPE", 1)])
            mk_db = E.mark("sPE")
            E.op("vector", (lambda e: e.tensor_copy(ddbs, pb[4][:, :])),
                 waits=[("sPE", mk_db)], incs=[("sDV", 1)])
            M["ddbev_%d" % w] = E.mark("sDV")

            wdtv = wdt4.rearrange("p (a c n) -> p a c n", a=4, n=128)

            for chalf in range(2):
                for i4 in range(4):
                    blk = chalf * 4 + i4
                    wldd = [("sSC", M["hqsp_%d_3" % w]), ("sPE", M["gu_%d_7" % w])]
                    if chalf == 1:
                        wldd.append(("sPE", M.get("dmm_%d_0_31" % w, 0)))
                    E.dma("sync", "sSY",
                          wdt4.rearrange("p (q a n) -> p q a n", q=4, n=128)[:, i4],
                          S["wdT"].rearrange("(a p) n -> a p n", p=128)[
                              blk * 32:(blk + 1) * 32].transpose([1, 0, 2]),
                          waits=wldd if i4 == 0 else [])
                mk_wd = E.mark("sSY")
                for coch in range(32):
                    par2 = coch % 4
                    qeng = "sync" if coch % 2 == 0 else "scalar"
                    qsem = "sSY" if coch % 2 == 0 else "sSC"
                    E.dma(qeng, qsem, hqT[par2],
                          S["hq"][wt0:wt0 + TB, coch * 128:(coch + 1) * 128],
                          transpose=True,
                          waits=[("sSC", M["hqsp_%d_3" % w]),
                                 ("sPE", M.get("dmm_%d_%d_%d" % (w, chalf, coch - 4),
                                               M.get("dmm_%d_%d_31" % (w, chalf - 1), 0)))])
                    mk_h = E.mark(qsem)

                    def dmm(e, coch=coch, chalf=chalf, par2=par2):
                        for i in range(4):
                            cch = chalf * 4 + i
                            inst = e.matmul(pb[i][:, :], wdtv[:, i, coch, :],
                                            hqT[par2][:, :],
                                            start=(coch == 0), stop=(coch == 31))
                        return inst
                    wmm = [(qsem, mk_h), ("sSY", mk_wd)]
                    if coch == 0:
                        wmm.append(("sDV", M.get("x2f_%d_%d" % (w, chalf * 4 - 1),
                                                 M.get("x2f_%d_7" % (w - 1), 0))))
                    E.op("tensor", dmm, waits=wmm, incs=[("sPE", 1)])
                    M["dmm_%d_%d_%d" % (w, chalf, coch)] = E.mark("sPE")
                mk_dn = E.mark("sPE")

                x2m = []
                x2m0 = E.mark("sAC")
                for i in range(4):
                    cch = chalf * 4 + i
                    E.dma("sync", "sSY", x1ns,
                          S["x1n"][cch * 128:(cch + 1) * 128, wt0:wt0 + TB],
                          waits=[("sDV", M.get("x2f_%d_%d" % (w, cch - 1), 0))])
                    mk_x1 = E.mark("sSY")

                    def x2ev(e, i=i):
                        e.tensor_tensor(x2f, pb[i][:, :], ddbs, ALU.mult)
                        return e.tensor_tensor(x2f, x2f, x1ns, ALU.add)
                    E.op("vector", x2ev,
                         waits=[("sPE", mk_dn), ("sSY", mk_x1),
                                ("sDV", M["ddbev_%d" % w]),
                                ("sAC", M.get("x2tr_%d_%d" % (w, cch - 1), 0))],
                         incs=[("sDV", 1)])
                    M["x2f_%d_%d" % (w, cch)] = E.mark("sDV")

                    for sl in range(4):
                        k2 = i * 4 + sl
                        bank = pb[4] if k2 % 2 == 0 else pb[5]
                        w2 = [("sDV", M["x2f_%d_%d" % (w, cch)]),
                              ("sDV", M["ddbev_%d" % w]),
                              ("sAC", x2m[k2 - 2] if k2 >= 2 else x2m0)]
                        if chalf == 0 and k2 < 2:
                            w2.append(("sOUT", M.get("outm_%d" % (w - 1), 0)))
                        E.op("tensor",
                             (lambda e, sl=sl, bank=bank: e.transpose(
                                 bank[:, 0:128], x2f[:, sl * 128:(sl + 1) * 128],
                                 self.identf[:])),
                             waits=w2, incs=[("sPE", 1)])
                        mk_t = E.mark("sPE")
                        E.op("scalar",
                             (lambda e, sl=sl, cch=cch, bank=bank: e.copy(
                                 x2tm[:, sl * 1024 + cch * 128:
                                      sl * 1024 + cch * 128 + 128],
                                 bank[:, 0:128])),
                             waits=[("sPE", mk_t)], incs=[("sAC", 1)])
                        x2m.append(E.mark("sAC"))
                    M["x2tr_%d_%d" % (w, cch)] = E.mark("sAC")
            M["dn_%d" % w] = E.mark("sPE")

            for sl in range(4):
                slv = x2tm[:, sl * 1024:(sl + 1) * 1024]
                E.op("vector", (lambda e, slv=slv: e.tensor_reduce(
                    mucol, slv, AX.X, ALU.add)),
                     waits=[("sAC", M["x2tr_%d_%d" % (w, 7)])], incs=[("sDV", 1)])
                E.op("scalar", (lambda e, slv=slv: e.activation(sq2, slv, AF.Square)),
                     waits=[("sAC", M["x2tr_%d_%d" % (w, 7)]),
                            ("sDV", E.mark("sDV"))], incs=[("sAC", 1)])
                mk_s2 = E.mark("sAC")
                E.op("vector", (lambda e: e.tensor_reduce(sqcol, sq2, AX.X, ALU.add)),
                     waits=[("sAC", mk_s2)], incs=[("sDV", 1)])

                def l2sa(e):
                    e.activation(mu_s2, mucol, AF.Copy, scale=1.0 / 1024.0)
                    e.activation(msqc2, mu_s2, AF.Square)
                    return e.activation(lnvc, sqcol, AF.Copy, scale=1.0 / 1024.0)
                E.op("scalar", l2sa, waits=[("sDV", E.mark("sDV"))],
                     incs=[("sAC", 1)])
                mk_a2 = E.mark("sAC")
                E.op("vector",
                     (lambda e: e.tensor_tensor(varcol, lnvc, msqc2, ALU.subtract)),
                     waits=[("sAC", mk_a2)], incs=[("sDV", 1)])
                mk_v2 = E.mark("sDV")

                def l2ln(e):
                    e.activation(lnvc, varcol, AF.Ln, bias=self.epsl[:])
                    return e.activation(rstdc, lnvc, AF.Exp, scale=-0.5)
                E.op("scalar", l2ln, waits=[("sDV", mk_v2)], incs=[("sAC", 1)])
                mk_r2 = E.mark("sAC")

                E.op("vector",
                     (lambda e, slv=slv: e.tensor_scalar(slv, slv, mu_s2, rstdc,
                                                         ALU.subtract, ALU.mult)),
                     waits=[("sAC", mk_r2)], incs=[("sDV", 1)])
                mk_o = E.mark("sDV")
                E.dma("scalar", "sOUT",
                      D["out"][wt0 + sl * 128: wt0 + (sl + 1) * 128, :], slv,
                      waits=[("sDV", mk_o)])
            M["outm_%d" % w] = E.mark("sOUT")

    def finish(self):
        E = self.E
        if E.mark("sOUT") == 0:
            w = [(s, E.mark(s)) for s in ["sSY", "sSC", "sGP", "sCC", "sPE",
                                          "sDV", "sAC", "cS", "gE", "sOUT"]]
            E.op("sync", lambda e: None, waits=w)
            return
        total = E.mark("sOUT")
        E.op("sync", lambda e: None, waits=[("sOUT", total)])


_IN_MAP = [
    ("maa_x", "L0_attn_maa_x"), ("maa_w", "L0_attn_maa_w"), ("maa_k", "L0_attn_maa_k"),
    ("maa_v", "L0_attn_maa_v"), ("maa_r", "L0_attn_maa_r"), ("maa_g", "L0_attn_maa_g"),
    ("maa_w1", "L0_attn_maa_w1"), ("maa_w2", "L0_attn_maa_w2"),
    ("Wr", "L0_attn_Wr"), ("Wk", "L0_attn_Wk"), ("Wv", "L0_attn_Wv"),
    ("Wo", "L0_attn_Wo"), ("gate_w1", "L0_attn_gate_w1"), ("gate_w2", "L0_attn_gate_w2"),
    ("time_decay", "L0_attn_time_decay"), ("decay_w1", "L0_attn_decay_w1"),
    ("decay_w2", "L0_attn_decay_w2"), ("u", "L0_attn_u"),
    ("ln_g", "L0_attn_ln_g"), ("ln_b", "L0_attn_ln_b"),
]


def build_program(taps=(), stop_after=None, chunk_stop=None, l2_stop=None):
    kb = KB(taps=taps, stop_after=stop_after)
    kb.chunk_stop = chunk_stop
    kb.l2_stop = l2_stop
    nc = kb.build()
    return nc


def make_in_maps(inputs):
    x = np.ascontiguousarray(np.asarray(inputs["L0_attn_x"], dtype=np.float32))
    wg = np.ascontiguousarray(np.asarray(inputs["L0_ffn_w_gate"], dtype=np.float32))
    wu = np.ascontiguousarray(np.asarray(inputs["L0_ffn_w_up"], dtype=np.float32))
    wd = np.ascontiguousarray(np.asarray(inputs["L0_ffn_w_down"], dtype=np.float32))
    common = {}
    for knew, kold in _IN_MAP:
        common[knew] = np.ascontiguousarray(np.asarray(inputs[kold], dtype=np.float32))
    lng = np.asarray(inputs["L0_attn_ln_g"], dtype=np.float32).reshape(-1)
    halfsel = (np.arange(C) % 128) < 64
    g2 = np.zeros((16, C), np.float32)
    for j in range(16):
        sel = halfsel if j % 2 == 0 else ~halfsel
        g2[j, sel] = lng[sel]
    common["gin2"] = g2
    in_maps = []
    for c in range(8):
        b, q = c // 4, c % 4
        im = dict(common)
        im["x"] = np.ascontiguousarray(x[b, q * TLOC:(q + 1) * TLOC, :])
        if q == 0:
            im["xprev"] = np.zeros((1, C), np.float32)
        else:
            im["xprev"] = np.ascontiguousarray(x[b, q * TLOC - 1: q * TLOC, :])
        im["wg_sh"] = np.ascontiguousarray(wg[c * 512:(c + 1) * 512, :])
        im["wu_sh"] = np.ascontiguousarray(wu[c * 512:(c + 1) * 512, :])
        im["wd_sh"] = np.ascontiguousarray(wd[c * 128:(c + 1) * 128, :])
        wsel = np.zeros((8, 8), np.float32)
        wbias = np.full((1, 8), -1e30, np.float32)
        g0 = b * 4
        for j in range(g0, c):
            wbias[0, j] = 0.0
            for m in range(j + 1, c):
                wsel[m, j] = 1.0
        im["wsel"] = wsel
        im["wbias"] = wbias
        in_maps.append(im)
    return in_maps


def kernel(**inputs):
    nc = build_program()
    in_maps = make_in_maps(inputs)
    res = run_bass_kernel_spmd(nc, in_maps, core_ids=list(range(8)))
    out = np.empty((2, 8192, C), np.float32)
    for c in range(8):
        b, q = c // 4, c % 4
        out[b, q * TLOC:(q + 1) * TLOC, :] = res.results[c]["out"]
    return out
